# version 06
# baseline (speedup 1.0000x reference)
import os
import numpy as np
import ml_dtypes
from contextlib import ExitStack

DBG_NOMASK = os.environ.get("DBG_NOMASK", "0") == "1"
DBG_L1STOP = int(os.environ.get("DBG_L1STOP", "99"))
DBG_L1FSTOP = int(os.environ.get("DBG_L1FSTOP", "99"))
DBG_NOSCAT = os.environ.get("DBG_NOSCAT", "0") == "1"
DBG_NOEVAC = os.environ.get("DBG_NOEVAC", "0") == "1"
DBG_XNT = None

import concourse.bass as bass
import concourse.mybir as mybir
from concourse.bass_utils import run_bass_kernel_spmd

F32 = mybir.dt.float32
BF16 = mybir.dt.bfloat16
I32 = mybir.dt.int32
AF = mybir.ActivationFunctionType
ALU = mybir.AluOpType
BT = ml_dtypes.bfloat16

NCORE = 8
TOK = 1024
DM = 2048
NTT = TOK // 128
NDC = DM // 128
MAGIC = 1234567891
EPS_LN = 1e-5

ENGS = ("sync", "vector", "scalar", "tensor", "gpsimd")


def ceil_div(a, b):
    return (a + b - 1) // b


class Ch:

    def __init__(self, es, nc, name):
        self.sem = es.enter_context(nc.semaphore(name))
        self.n = 0
        self._waited = {}

    def reserve(self, k):
        self.n += k
        return self.n

    def wait_emit(self, eng, v):
        if v <= 0:
            return
        key = id(eng)
        if self._waited.get(key, -1) >= v:
            return
        eng.wait_ge(self.sem, v)
        self._waited[key] = v


class Rec:

    def __init__(self, ctx):
        self.ctx = ctx
        self.q = {e: [] for e in ENGS}

    def op(self, eng, fn, waits=(), inc=None, k=1):
        ws = [(c, v) for (c, v) in waits]
        mark = None
        sem = None
        if inc is not None:
            mark = inc.reserve(k)
            sem = inc.sem
        kk = k

        def run(h):
            for c, v in ws:
                c.wait_emit(h, v)
            inst = fn(h)
            if sem is not None:
                inst.then_inc(sem, kk)
        self.q[eng].append(run)
        return mark

    def raw(self, eng, fn):
        self.q[eng].append(fn)

    def barrier(self, chans):
        ws = [(c, c.n) for c in chans]
        for e in ENGS:
            def run(h, ws=ws):
                for c, v in ws:
                    c.wait_emit(h, v)
            self.q[e].append(run)

    def flush(self):
        b = self.ctx.block
        for name in ENGS:
            fns = self.q[name]
            if fns:
                getattr(b, name)(lambda h, fns=fns: [f(h) for f in fns])
        self.q = {e: [] for e in ENGS}


class Ctx:
    def __init__(self, nc, es, block):
        self.nc = nc
        self.es = es
        self.block = block
        self._chans = {}
        self._uid = 0

    def chan(self, name):
        if name not in self._chans:
            self._chans[name] = Ch(self.es, self.nc, name)
        return self._chans[name]

    @property
    def chans(self):
        return list(self._chans.values())

    def uid(self):
        self._uid += 1
        return self._uid

    def sb(self, name, shape, dtype, scope=None, side=None):
        s = scope if scope is not None else self.es
        return s.enter_context(self.nc.sbuf_tensor(f"{name}_{self.uid()}", shape, dtype, side=side))

    def ps(self, name, shape, dtype=F32, scope=None):
        s = scope if scope is not None else self.es
        return s.enter_context(self.nc.psum_tensor(f"{name}_{self.uid()}", shape, dtype))

    def wwait(self, ap):
        m = getattr(self, "apmark", {}).get(id(ap))
        return [m] if m else []



def w_chunk_ap(w_hbm, o0, o1):
    return w_hbm[:, o0:o1].rearrange("(k p) n -> p k n", p=128)


def emit_ln_phase(ctx, rec, src_x, scr_hbm, xnT, *, kind="ln", eps=EPS_LN,
                  u_keep=None, tag="ln", scope=None, u_hbm=None, rms2=False):
    nc = ctx.nc
    ld = ctx.chan(tag + "_ld")
    cv = ctx.chan(tag + "_v")
    ca = ctx.chan(tag + "_a")
    cw = ctx.chan(tag + "_w")
    tr = ctx.chan(tag + "_t")
    xt = ctx.sb("lnx", [128, 2, DM], F32, scope=scope)
    xb = ctx.sb("lnxb", [128, 2, DM], BF16, scope=scope)
    st = ctx.sb("lnst", [128, 32], F32, scope=scope)
    ust = ctx.sb("lnu", [128, 2, DM], F32, scope=scope) if rms2 else None
    cu = ctx.chan(tag + "_u")
    free_marks = [0] * NTT
    prev_tail = 0
    for i in range(NTT):
        sl = i % 2
        lm = rec.op("sync", lambda h, i=i, sl=sl: h.dma_start(out=xt[:, sl, :], in_=src_x[i * 128:(i + 1) * 128, :]),
                    waits=[(cv, free_marks[i - 2])] if i >= 2 else [], inc=ld, k=16)
        m1 = None
        for g in range(4):
            m1 = rec.op("vector", lambda h, sl=sl, g=g: h.bn_stats(out=st[:, g * 6:(g + 1) * 6],
                                                                   in_=xt[:, sl, g * 512:(g + 1) * 512]),
                        waits=[(ld, lm), (cv, prev_tail)] if g == 0 else [(cv, m1)], inc=cv)
        m2 = rec.op("vector", lambda h: h.bn_aggr(out=st[:, 24:26], in_=st[:, 0:24]),
                    waits=[(cv, m1)], inc=cv)
        if kind == "rms":
            m3 = rec.op("vector", lambda h: h.tensor_scalar(out=st[:, 26:27], in0=st[:, 24:25], scalar1=st[:, 24:25],
                                                            scalar2=None, op0=ALU.mult), waits=[(cv, m2)], inc=cv)
            m4 = rec.op("vector", lambda h: h.tensor_tensor(out=st[:, 27:28], in0=st[:, 26:27], in1=st[:, 25:26],
                                                            op=ALU.add), waits=[(cv, m3)], inc=cv)
            vcol = st[:, 27:28]
            mzero = True
        else:
            m4 = m2
            vcol = st[:, 25:26]
            mzero = False
        m5 = rec.op("vector", lambda h, vcol=vcol: h.tensor_scalar(out=st[:, 28:29], in0=vcol, scalar1=float(eps),
                                                                   scalar2=None, op0=ALU.add), waits=[(cv, m4)], inc=cv)
        m6 = rec.op("scalar", lambda h: h.activation(out=st[:, 29:30], in_=st[:, 28:29], func=AF.Sqrt),
                    waits=[(cv, m5)], inc=ca)
        m7 = rec.op("vector", lambda h: h.reciprocal(out=st[:, 28:29], in_=st[:, 29:30]),
                    waits=[(ca, m6)], inc=cv)
        if rms2:
            m8u = rec.op("vector", lambda h, sl=sl: h.tensor_scalar(out=ust[:, sl, :], in0=xt[:, sl, :],
                                                                    scalar1=st[:, 24:25], scalar2=st[:, 28:29],
                                                                    op0=ALU.subtract, op1=ALU.mult),
                         waits=[(cv, m7)], inc=cv)
            rec.op("sync", lambda h, i=i, sl=sl: h.dma_start(out=u_hbm[i * 128:(i + 1) * 128, :],
                                                             in_=ust[:, sl, :]),
                   waits=[(cv, m8u)], inc=cu, k=16)
            m1b = None
            for g in range(4):
                m1b = rec.op("vector", lambda h, sl=sl, g=g: h.bn_stats(out=st[:, g * 6:(g + 1) * 6],
                                                                        in_=ust[:, sl, g * 512:(g + 1) * 512]),
                             waits=[(cv, m8u)] if g == 0 else [(cv, m1b)], inc=cv)
            m2b = rec.op("vector", lambda h: h.bn_aggr(out=st[:, 24:26], in_=st[:, 0:24]),
                         waits=[(cv, m1b)], inc=cv)
            m3b = rec.op("vector", lambda h: h.tensor_scalar(out=st[:, 26:27], in0=st[:, 24:25], scalar1=st[:, 24:25],
                                                             scalar2=None, op0=ALU.mult), waits=[(cv, m2b)], inc=cv)
            m4b = rec.op("vector", lambda h: h.tensor_tensor(out=st[:, 27:28], in0=st[:, 26:27], in1=st[:, 25:26],
                                                             op=ALU.add), waits=[(cv, m3b)], inc=cv)
            m5b = rec.op("vector", lambda h: h.tensor_scalar(out=st[:, 28:29], in0=st[:, 27:28], scalar1=float(eps),
                                                             scalar2=None, op0=ALU.add), waits=[(cv, m4b)], inc=cv)
            m6b = rec.op("scalar", lambda h: h.activation(out=st[:, 29:30], in_=st[:, 28:29], func=AF.Sqrt),
                         waits=[(cv, m5b)], inc=ca)
            m7b = rec.op("vector", lambda h: h.reciprocal(out=st[:, 28:29], in_=st[:, 29:30]),
                         waits=[(ca, m6b)], inc=cv)
            m8 = rec.op("vector", lambda h, sl=sl: h.tensor_scalar(out=xb[:, sl, :], in0=ust[:, sl, :],
                                                                   scalar1=st[:, 28:29], scalar2=None, op0=ALU.mult),
                        waits=[(cv, m7b)], inc=cv)
        elif mzero:
            m8 = rec.op("vector", lambda h, sl=sl: h.tensor_scalar(out=xb[:, sl, :], in0=xt[:, sl, :],
                                                                   scalar1=st[:, 28:29], scalar2=None, op0=ALU.mult),
                        waits=[(cv, m7)], inc=cv)
        else:
            m8 = rec.op("vector", lambda h, sl=sl: h.tensor_scalar(out=xb[:, sl, :], in0=xt[:, sl, :],
                                                                   scalar1=st[:, 24:25], scalar2=st[:, 28:29],
                                                                   op0=ALU.subtract, op1=ALU.mult),
                        waits=[(cv, m7)], inc=cv)
        if u_keep is not None:
            if mzero:
                m8b = rec.op("vector", lambda h, sl=sl, i=i: h.tensor_scalar(out=u_keep[:, i, :], in0=xt[:, sl, :],
                                                                             scalar1=st[:, 28:29], scalar2=None,
                                                                             op0=ALU.mult), waits=[(cv, m7)], inc=cv)
            else:
                m8b = rec.op("vector", lambda h, sl=sl, i=i: h.tensor_scalar(out=u_keep[:, i, :], in0=xt[:, sl, :],
                                                                             scalar1=st[:, 24:25], scalar2=st[:, 28:29],
                                                                             op0=ALU.subtract, op1=ALU.mult),
                             waits=[(cv, m7)], inc=cv)
            m8 = m8b
        free_marks[i] = m8
        prev_tail = m8
        rec.op("sync", lambda h, i=i, sl=sl: h.dma_start(out=scr_hbm[i * 128:(i + 1) * 128, :], in_=xb[:, sl, :]),
               waits=[(cv, m8)], inc=cw, k=16)
    wall = cw.n
    tm = []
    for k in range(NDC):
        tm.append(rec.op("sync", lambda h, k=k: h.dma_start_transpose(out=xnT[:, k, :],
                                                                      in_=scr_hbm[:, k * 128:(k + 1) * 128]),
                         waits=[(cw, wall)], inc=tr, k=16))
    return tr, tm[-1]



def make_off_evac(evac, tt0):
    def e2(rec, o, tt, ps, pm):
        return evac(rec, o, tt + tt0, ps, pm)
    return e2


def mm_group(rec, pe_chan, mms, waits=()):
    n = len(mms)
    mark = None
    for i, fn in enumerate(mms):
        first, last = i == 0, i == n - 1
        mark = rec.op("tensor", (lambda h, fn=fn, first=first, last=last: fn(h, first, last)),
                      waits=waits if first else (), inc=pe_chan if last else None)
    return mark


def proj_state(ctx, scope, tag, nps=4, nk=NDC, cw=512, ksplit=1):
    return {
        "wbuf": ctx.sb(tag + "_w", [128, 2, nk // ksplit, cw], BF16, scope=scope),
        "w_free": [None, None],
        "pss": [ctx.ps(tag + "_ps", [128, 512], F32, scope=scope) for _ in range(nps)],
        "ps_free": [None] * nps,
        "nk": nk, "cw": cw,
    }


def emit_proj(ctx, rec, xnT, w_hbm, n_out, evac, *, tag, ready, pst, bias=None, consts=None,
              pe=None, wld=None, ntt=NTT, t0=0, ksplit=1, xwaits=()):
    wld = wld or ctx.chan(tag + "_wl")
    pe = pe or ctx.chan(tag + "_pe")
    nk = pst["nk"]
    CW = pst["cw"]
    NO = ceil_div(n_out, CW)
    nkh = nk // ksplit if ksplit > 1 else nk
    wbuf = pst["wbuf"]
    pss = pst["pss"]
    ps_free = pst["ps_free"]
    w_free = pst["w_free"]
    assert ksplit == 1 or len(pss) >= ntt
    for o in range(NO):
        o0, o1 = o * CW, min((o + 1) * CW, n_out)
        ow = o1 - o0
        pm_of = {}
        for kh in range(ksplit):
            sl = (o * ksplit + kh) % 2
            ws = [w_free[sl]] if w_free[sl] is not None else []
            k0 = kh * nkh
            wm = rec.op("scalar", lambda h, o0=o0, sl=sl, ow=ow, k0=k0:
                        h.dma_start(out=wbuf[:, sl, 0:nkh, 0:ow],
                                    in_=w_hbm[k0 * 128:(k0 + nkh) * 128, o0:o0 + ow]
                                    .rearrange("(k p) n -> p k n", p=128)),
                        waits=ws + ctx.wwait(w_hbm), inc=wld, k=16)
            last_pm = None
            for tt in range(ntt):
                pi = (o * ntt + tt) % len(pss) if ksplit == 1 else tt
                ps = pss[pi]
                waits = [(wld, wm), ready] + list(xwaits)
                if kh == 0 and ps_free[pi] is not None:
                    waits.append(ps_free[pi])
                nmm = nkh + (1 if (bias is not None and kh == ksplit - 1) else 0)
                mark = None
                for k in range(nkh):
                    first = (kh == 0 and k == 0)
                    lastm = (kh == ksplit - 1 and k == nmm - 1)
                    lastk = (k == nkh - 1)
                    mark = rec.op("tensor", lambda h, k=k, tt=tt, sl=sl, ps=ps, ow=ow, k0=k0,
                                  first=first, lastm=lastm:
                                  h.matmul(out=ps[:, 0:ow], lhsT=xnT[:, k0 + k, t0 + tt * 128:t0 + (tt + 1) * 128],
                                           rhs=wbuf[:, sl, k, 0:ow], start=first, stop=lastm),
                                  waits=waits if k == 0 else (), inc=pe if (lastm or lastk) else None)
                if bias is not None and kh == ksplit - 1:
                    bsb, boff = bias
                    mark = rec.op("tensor", lambda h, ps=ps, ow=ow, o0=o0, bsb=bsb, boff=boff:
                                  h.matmul(out=ps[:, 0:ow], lhsT=consts["ones1"][0:1, 0:128],
                                           rhs=bsb[0:1, boff + o0:boff + o0 + ow], start=False, stop=True),
                                  inc=pe)
                last_pm = mark
                if kh == ksplit - 1:
                    pm_of[tt] = mark
                    if ksplit == 1:
                        ps_free[pi] = evac(rec, o, tt, ps[:, 0:ow], mark)
            w_free[sl] = (pe, last_pm)
        if ksplit > 1:
            for tt in range(ntt):
                ps_free[tt] = evac(rec, o, tt, pss[tt][:, 0:ow], pm_of[tt])



def emit_l0_attn(ctx, rec, io, W, C):
    nc = ctx.nc
    HN = 16
    bias_scope = ExitStack()
    bias_sb = ctx.sb("a_bias", [1, 4 * DM], BF16, scope=bias_scope, side="right")
    kv_scope = ExitStack()
    ktok = ctx.sb("a_ktok", [128, NTT, DM], BF16, scope=kv_scope, side="right")
    vtok = ctx.sb("a_vtok", [128, NTT, DM], BF16, scope=kv_scope, side="right")

    cv = ctx.chan("l0_v")
    ca = ctx.chan("l0_a")
    cw = ctx.chan("l0_sw")
    pe = ctx.chan("l0_pe")
    wld = ctx.chan("l0_wl")
    trq = ctx.chan("l0_trq")

    scopeA = ExitStack()
    xnT = ctx.sb("a_xnT", [128, NDC, TOK], BF16, scope=scopeA)
    elu_tmp = ctx.sb("a_elut", [128, 4, 512], F32, scope=scopeA)
    elu_e = ctx.sb("a_elue", [128, 4, 512], F32, scope=scopeA)
    qstage = ctx.sb("a_qst", [128, 4, 512], BF16, scope=scopeA)
    pst = proj_state(ctx, scopeA, "l0p")
    trc, trm = emit_ln_phase(ctx, rec, io["src"], io["scr_a"], xnT, kind="ln", tag="l0ln", scope=scopeA)
    ready = (trc, trm)
    wldg = ctx.chan("l0_wlg")
    bm_ = rec.op("gpsimd", lambda h: h.dma_start(out=bias_sb[:], in_=W["l0_biases"]), inc=wldg, k=16)
    bias_rdy = (wldg, bm_)

    elu_slot_mark = {}

    def elu_evac(dst_fn, store_fn):
        def evac(rec, o, tt, ps, pm):
            i = (o * NTT + tt) % 4
            ws = [(pe, pm)]
            if i in elu_slot_mark:
                ws.append(elu_slot_mark[i])
            if ("q", i) in elu_slot_mark:
                ws.append(elu_slot_mark[("q", i)])
            ow = ps.shape[-1]
            m1 = rec.op("vector", lambda h, ps=ps, i=i, ow=ow: h.tensor_scalar(
                out=elu_tmp[:, i, 0:ow], in0=ps, scalar1=0.0, scalar2=None, op0=ALU.min),
                waits=ws, inc=cv)
            m2 = rec.op("scalar", lambda h, i=i, ow=ow: h.activation(
                out=elu_e[:, i, 0:ow], in_=elu_tmp[:, i, 0:ow], func=AF.Exp),
                waits=[(cv, m1)], inc=ca)
            m3 = rec.op("vector", lambda h, ps=ps, i=i, o=o, tt=tt, ow=ow: h.scalar_tensor_tensor(
                out=dst_fn(o, tt, i), in0=ps, scalar=0.0, in1=elu_e[:, i, 0:ow],
                op0=ALU.max, op1=ALU.add), waits=[(ca, m2)], inc=cv)
            elu_slot_mark[i] = (cv, m3)
            sf = store_fn(o, tt, i)
            if sf is not None:
                hbm_ap, src_ap = sf
                sm = rec.op("sync", lambda h, hbm_ap=hbm_ap, src_ap=src_ap:
                            h.dma_start(out=hbm_ap, in_=src_ap),
                            waits=[(cv, m3), ready], inc=cw, k=16)
                elu_slot_mark[("q", i)] = (cw, sm)
            return (cv, m3)
        return evac

    kevac = elu_evac(lambda o, tt, i: ktok[:, tt, o * 512:(o + 1) * 512],
                     lambda o, tt, i: ((io["scr_b"][tt * 128:(tt + 1) * 128, :], ktok[:, tt, :])
                                       if o == 3 else None))
    emit_proj(ctx, rec, xnT, W["Wk"], DM, kevac, tag="l0k", ready=ready, pst=pst, xwaits=[bias_rdy],
              bias=(bias_sb, DM), consts=C, pe=pe, wld=wld)
    kdone = (cv, cv.n)

    def vevac(rec, o, tt, ps, pm):
        m = rec.op("scalar", lambda h, ps=ps, o=o, tt=tt: h.activation(
            out=vtok[:, tt, o * 512:(o + 1) * 512], in_=ps, func=AF.Copy), waits=[(pe, pm)], inc=ca)
        return (ca, m)
    emit_proj(ctx, rec, xnT, W["Wv"], DM, vevac, tag="l0v", ready=ready, pst=pst, xwaits=[bias_rdy],
              bias=(bias_sb, 2 * DM), consts=C, pe=pe, wld=wld)
    vdone = (ca, ca.n)

    qevac = elu_evac(lambda o, tt, i: qstage[:, i, :],
                     lambda o, tt, i: (io["scr_a"][tt * 128:(tt + 1) * 128, o * 512:(o + 1) * 512],
                                       qstage[:, i, :]))
    emit_proj(ctx, rec, xnT, W["Wq"], DM, qevac, tag="l0q", ready=ready, pst=pst, xwaits=[bias_rdy],
              bias=(bias_sb, 0), consts=C, pe=pe, wld=wld)
    q_stores = cw.n
    rec.barrier(ctx.chans)
    scopeA.close()

    scopeB = ExitStack()
    kfT = ctx.sb("a_kfT", [128, NDC, TOK], BF16, scope=scopeB)
    qfT = ctx.sb("a_qfT", [128, NDC, TOK], BF16, scope=scopeB)
    sendb = ctx.sb("a_send", [128, HN * 129], BF16, scope=scopeB)
    seedb = ctx.sb("a_seed", [128, HN * 129], BF16, scope=scopeB)
    KVrun = ctx.sb("a_kvr", [128, HN * 129], F32, scope=scopeB)
    KVbf = ctx.sb("a_kvb", [128, HN * 129], BF16, scope=scopeB)
    At = ctx.sb("a_At", [128, 2, 128], BF16, scope=scopeB)
    den = ctx.sb("a_den", [128, 4], F32, scope=scopeB)
    attno = ctx.sb("a_attno", [128, NTT, DM], BF16, scope=scopeB)

    st_scope = ExitStack()
    psst = [ctx.ps("l0_psst", [128, 132], F32, scope=st_scope) for _ in range(2)]
    st_free = [None, None]
    for hh in range(HN):
        psp = psst[hh % 2]
        mms = []
        for n in range(NTT):
            mms.append(lambda h, s, e, n=n, hh=hh, psp=psp: h.matmul(
                out=psp[:, 0:128], lhsT=ktok[:, n, hh * 128:(hh + 1) * 128],
                rhs=vtok[:, n, hh * 128:(hh + 1) * 128], start=s, stop=False))
            mms.append(lambda h, s, e, n=n, hh=hh, psp=psp: h.matmul(
                out=psp[:, 128:129], lhsT=ktok[:, n, hh * 128:(hh + 1) * 128],
                rhs=C["onescol"][0:128, 0:1], start=False, stop=e))
        ws = [kdone, vdone] + ([st_free[hh % 2]] if st_free[hh % 2] else [])
        pm = mm_group(rec, pe, mms, waits=ws)
        em = rec.op("scalar", lambda h, hh=hh, psp=psp: h.activation(
            out=sendb[:, hh * 129:(hh + 1) * 129], in_=psp[:, 0:129], func=AF.Copy),
            waits=[(pe, pm)], inc=ca)
        st_free[hh % 2] = (ca, em)
    st_all = ca.n

    sh_pay = io["sh_l0"]
    sh_flag = io["sh_l0f"]
    snd = ctx.chan("l0_snd")
    rcv = ctx.chan("l0_rcv")
    seeded = ctx.chan("l0_seeded")
    snd_v = st_all

    def g_exchange(h):
        with h.register("a_pp") as pp:
            h.reg_load(pp, C["par_sb"][0:1, 0:1])
            with h.If_eq(pp, 0):
                h.wait_ge(ca.sem, snd_v)
                h.dma_start(out=sh_pay[:], in_=sendb[:]).then_inc(snd.sem, 16)
                h.wait_ge(snd.sem, 16)
                h.dma_start(out=sh_flag[:], in_=C["nonce_sb"][0:1, 0:4]).then_inc(snd.sem, 16)
            with h.Else():
                with h.register("a_pc") as pc, h.register("a_pex") as pex, h.register("a_f0") as f0, \
                     h.register("a_nn") as nn:
                    h.reg_load(nn, C["nonce_sb"][0:1, 0:1])
                    h.reg_mov(pc, 1)
                    h.reg_mov(pex, 0)
                    with h.While(pc):
                        h.reg_alu(pex, pex, 16, ALU.add)
                        h.dma_start(out=C["flag_sb"][0:1, 0:4], in_=sh_flag[0:1, :]).then_inc(snd.sem, 16)
                        h.wait_ge(snd.sem, pex)
                        h.reg_load(f0, C["flag_sb"][0:1, 0:1])
                        h.reg_alu(pc, f0, nn, ALU.not_equal)
                h.dma_start(out=seedb[:], in_=sh_pay[:]).then_inc(rcv.sem, 16)
    rec.raw("gpsimd", g_exchange)

    seeded.reserve(1)

    def v_seed(h):
        with h.register("a_pv") as pv:
            h.reg_load(pv, C["par_sb"][0:1, 0:1])
            with h.If_eq(pv, 0):
                h.memset(KVrun[:], 0.0).then_inc(seeded.sem, 1)
            with h.Else():
                h.wait_ge(rcv.sem, 16)
                h.tensor_copy(out=KVrun[:], in_=seedb[:]).then_inc(seeded.sem, 1)
    rec.raw("vector", v_seed)
    mkb = rec.op("vector", lambda h: h.tensor_copy(out=KVbf[:], in_=KVrun[:]),
                 waits=[(seeded, 1)], inc=cv)

    for k in range(NDC):
        rec.op("sync", lambda h, k=k: h.dma_start_transpose(out=kfT[:, k, :],
                                                            in_=io["scr_b"][:, k * 128:(k + 1) * 128]),
               waits=[(cw, q_stores)], inc=trq, k=16)
        rec.op("sync", lambda h, k=k: h.dma_start_transpose(out=qfT[:, k, :],
                                                            in_=io["scr_a"][:, k * 128:(k + 1) * 128]),
               waits=[(cw, q_stores)], inc=trq, k=16)
    trdone = (trq, trq.n)
    rec.barrier(ctx.chans)
    st_scope.close()

    ap_scope = ExitStack()
    psA = [ctx.ps("l0_psA", [128, 128], F32, scope=ap_scope) for _ in range(2)]
    psY = [ctx.ps("l0_psY", [128, 132], F32, scope=ap_scope) for _ in range(2)]
    psK = [ctx.ps("l0_psK", [128, 132], F32, scope=ap_scope) for _ in range(2)]
    psA_free = [None, None]
    psY_free = [None, None]
    psK_free = [None, None]
    At_free = [None, None]
    kv_mark = {}
    for hh in range(HN):
        for n in range(NTT):
            sl = (hh * NTT + n) % 2
            wsA = [trdone] + ([psA_free[sl]] if psA_free[sl] else [])
            pmA = mm_group(rec, pe, [lambda h, s, e, hh=hh, n=n, sl=sl: h.matmul(
                out=psA[sl][:, 0:128], lhsT=kfT[:, hh, n * 128:(n + 1) * 128],
                rhs=qfT[:, hh, n * 128:(n + 1) * 128], start=s, stop=e)], waits=wsA)
            wsAt = [(pe, pmA)] + ([At_free[sl]] if At_free[sl] else [])
            mAt = rec.op("vector", lambda h, sl=sl: h.tensor_tensor(
                out=At[:, sl, :], in0=psA[sl][:, 0:128], in1=C["triU"][:, 0:128], op=ALU.mult),
                waits=wsAt, inc=cv)
            psA_free[sl] = (cv, mAt)
            wsY = [(cv, mAt), (seeded, 1), (cv, mkb)]
            if hh in kv_mark:
                wsY.append(kv_mark[hh])
            if psY_free[sl]:
                wsY.append(psY_free[sl])
            mms = [
                lambda h, s, e, sl=sl, hh=hh, n=n: h.matmul(
                    out=psY[sl][:, 0:128], lhsT=At[:, sl, :],
                    rhs=vtok[:, n, hh * 128:(hh + 1) * 128], start=s, stop=False),
                lambda h, s, e, sl=sl, hh=hh, n=n: h.matmul(
                    out=psY[sl][:, 128:129], lhsT=At[:, sl, :],
                    rhs=C["onescol"][0:128, 0:1], start=False, stop=False),
                lambda h, s, e, sl=sl, hh=hh, n=n: h.matmul(
                    out=psY[sl][:, 0:129], lhsT=qfT[:, hh, n * 128:(n + 1) * 128],
                    rhs=KVbf[:, hh * 129:(hh + 1) * 129], start=False, stop=e),
            ]
            pmY = mm_group(rec, pe, mms, waits=wsY)
            At_free[sl] = (pe, pmY)
            m1 = rec.op("vector", lambda h, sl=sl: h.tensor_scalar(
                out=den[:, sl:sl + 1], in0=psY[sl][:, 128:129], scalar1=1e-6, scalar2=None,
                op0=ALU.add), waits=[(pe, pmY)], inc=cv)
            m2 = rec.op("vector", lambda h, sl=sl: h.reciprocal(
                out=den[:, 2 + sl:3 + sl], in_=den[:, sl:sl + 1]), waits=[(cv, m1)], inc=cv)
            m3 = rec.op("vector", lambda h, sl=sl, hh=hh, n=n: h.tensor_scalar(
                out=attno[:, n, hh * 128:(hh + 1) * 128], in0=psY[sl][:, 0:128],
                scalar1=den[:, 2 + sl:3 + sl], scalar2=None, op0=ALU.mult),
                waits=[(cv, m2)], inc=cv)
            psY_free[sl] = (cv, m3)
            if n < NTT - 1:
                wsK = [kdone, vdone] + ([psK_free[sl]] if psK_free[sl] else [])
                pmK = mm_group(rec, pe, [
                    lambda h, s, e, sl=sl, hh=hh, n=n: h.matmul(
                        out=psK[sl][:, 0:128], lhsT=ktok[:, n, hh * 128:(hh + 1) * 128],
                        rhs=vtok[:, n, hh * 128:(hh + 1) * 128], start=s, stop=False),
                    lambda h, s, e, sl=sl, hh=hh, n=n: h.matmul(
                        out=psK[sl][:, 128:129], lhsT=ktok[:, n, hh * 128:(hh + 1) * 128],
                        rhs=C["onescol"][0:128, 0:1], start=False, stop=e)], waits=wsK)
                mk1 = rec.op("vector", lambda h, sl=sl, hh=hh: h.tensor_tensor(
                    out=KVrun[:, hh * 129:(hh + 1) * 129], in0=KVrun[:, hh * 129:(hh + 1) * 129],
                    in1=psK[sl][:, 0:129], op=ALU.add), waits=[(pe, pmK), (pe, pmY), (seeded, 1)], inc=cv)
                mk2 = rec.op("vector", lambda h, hh=hh: h.tensor_copy(
                    out=KVbf[:, hh * 129:(hh + 1) * 129], in_=KVrun[:, hh * 129:(hh + 1) * 129]),
                    waits=[(cv, mk1)], inc=cv)
                psK_free[sl] = (cv, mk1)
                kv_mark[hh] = (cv, mk2)

    attn_all = cv.n
    for tt in range(NTT):
        rec.op("sync", lambda h, tt=tt: h.dma_start(out=io["scr_b"][tt * 128:(tt + 1) * 128, :],
                                                    in_=attno[:, tt, :]),
               waits=[(cv, attn_all), (trq, trq.n)], inc=cw, k=16)
    ao_stores = cw.n
    rec.barrier(ctx.chans)
    ap_scope.close()
    scopeB.close()
    kv_scope.close()

    scopeC = ExitStack()
    aoT = ctx.sb("a_aoT", [128, NDC, TOK], BF16, scope=scopeC)
    of32 = ctx.sb("a_of32", [128, NTT, DM], F32, scope=scopeC)
    pst_o = proj_state(ctx, scopeC, "l0po")
    for k in range(NDC):
        rec.op("sync", lambda h, k=k: h.dma_start_transpose(out=aoT[:, k, :],
                                                            in_=io["scr_b"][:, k * 128:(k + 1) * 128]),
               waits=[(cw, ao_stores)], inc=trq, k=16)
    aoT_ready = (trq, trq.n)

    def oevac(rec, o, tt, ps, pm):
        m = rec.op("scalar", lambda h, ps=ps, o=o, tt=tt: h.activation(
            out=of32[:, tt, o * 512:(o + 1) * 512], in_=ps, func=AF.Copy), waits=[(pe, pm)], inc=ca)
        return (ca, m)
    emit_proj(ctx, rec, aoT, W["Wo"], DM, oevac, tag="l0o", ready=aoT_ready, pst=pst_o, xwaits=[bias_rdy],
              bias=(bias_sb, 3 * DM), consts=C, pe=pe, wld=wld)
    odone = (ca, ca.n)

    emit_ln_residual_store(ctx, rec, of32, io["src"], io["dst"], odone, tag="l0fin", scope=scopeC)
    rec.barrier(ctx.chans)
    scopeC.close()
    bias_scope.close()


def emit_ln_residual_store(ctx, rec, of32, src_x, dst_x, ready, *, tag, eps=EPS_LN, scope=None,
                           ntt=NTT, row0=0, pre_add_hbm=None):
    ld = ctx.chan(tag + "_ld")
    cv = ctx.chan(tag + "_v")
    ca = ctx.chan(tag + "_a")
    st = ctx.chan(tag + "_st")
    xt = ctx.sb(tag + "_x", [128, 2, DM], F32, scope=scope)
    ot = ctx.sb(tag + "_o", [128, 2, DM], F32, scope=scope)
    sst = ctx.sb(tag + "_s", [128, 32], F32, scope=scope)
    ut = ctx.sb(tag + "_u", [128, 2, DM], F32, scope=scope) if pre_add_hbm is not None else None
    free = [0, 0]
    prev_tail = 0
    for i in range(ntt):
        sl = i % 2
        r0 = row0 + i * 128
        lm = rec.op("sync", lambda h, r0=r0, sl=sl: h.dma_start(out=xt[:, sl, :],
                                                                in_=src_x[r0:r0 + 128, :]),
                    waits=[(cv, free[sl])], inc=ld, k=16)
        if pre_add_hbm is not None:
            lmu = rec.op("sync", lambda h, r0=r0, sl=sl: h.dma_start(out=ut[:, sl, :],
                                                                     in_=pre_add_hbm[r0:r0 + 128, :]),
                         waits=[(cv, free[sl])], inc=ld, k=16)
            pa = rec.op("vector", lambda h, i=i, sl=sl: h.tensor_tensor(
                out=of32[:, i, :], in0=of32[:, i, :], in1=ut[:, sl, :], op=ALU.add),
                waits=[ready, (ld, lmu), (cv, free[sl]), (cv, prev_tail)], inc=cv)
            stats_w = [(cv, pa)]
        else:
            stats_w = [ready, (cv, free[sl]), (cv, prev_tail)]
        m1 = None
        for g in range(4):
            m1 = rec.op("vector", lambda h, i=i, g=g: h.bn_stats(out=sst[:, g * 6:(g + 1) * 6],
                                                                 in_=of32[:, i, g * 512:(g + 1) * 512]),
                        waits=stats_w if g == 0 else [(cv, m1)], inc=cv)
        m2 = rec.op("vector", lambda h: h.bn_aggr(out=sst[:, 24:26], in_=sst[:, 0:24]),
                    waits=[(cv, m1)], inc=cv)
        m3 = rec.op("vector", lambda h: h.tensor_scalar(out=sst[:, 26:27], in0=sst[:, 25:26],
                                                        scalar1=float(eps), scalar2=None, op0=ALU.add),
                    waits=[(cv, m2)], inc=cv)
        m4 = rec.op("scalar", lambda h: h.activation(out=sst[:, 27:28], in_=sst[:, 26:27], func=AF.Sqrt),
                    waits=[(cv, m3)], inc=ca)
        m5 = rec.op("vector", lambda h: h.reciprocal(out=sst[:, 26:27], in_=sst[:, 27:28]),
                    waits=[(ca, m4)], inc=cv)
        m6 = rec.op("vector", lambda h, i=i, sl=sl: h.tensor_scalar(
            out=ot[:, sl, :], in0=of32[:, i, :], scalar1=sst[:, 24:25], scalar2=sst[:, 26:27],
            op0=ALU.subtract, op1=ALU.mult), waits=[(cv, m5)], inc=cv)
        m7 = rec.op("vector", lambda h, sl=sl: h.tensor_tensor(out=ot[:, sl, :], in0=ot[:, sl, :],
                                                               in1=xt[:, sl, :], op=ALU.add),
                    waits=[(cv, m6), (ld, lm)], inc=cv)
        free[sl] = m7
        prev_tail = m6
        rec.op("sync", lambda h, r0=r0, sl=sl: h.dma_start(out=dst_x[r0:r0 + 128, :],
                                                           in_=ot[:, sl, :]),
               waits=[(cv, m7)], inc=st, k=16)
    return st


def emit_l0_ffn(ctx, rec, io, W, C):
    nc = ctx.nc
    NI = 8192
    NIC = NI // 512
    cv = ctx.chan("f0_v")
    ca = ctx.chan("f0_a")
    cp = ctx.chan("f0_p")
    cw = ctx.chan("f0_sw")
    pe = ctx.chan("f0_pe")
    wld = ctx.chan("f0_wl")
    trh = ctx.chan("f0_tr")

    blk = ExitStack()
    xnT = ctx.sb("f_xnT", [128, NDC, TOK], BF16, side="right", scope=blk)
    of32 = ctx.sb("f_of32", [128, NTT, DM], F32, side="right", scope=blk)
    ln_scope = ExitStack()
    trc, trm = emit_ln_phase(ctx, rec, io["src"], io["scr_a"], xnT, kind="ln", tag="f0ln", scope=ln_scope)
    ready = (trc, trm)
    rec.barrier(ctx.chans)
    ln_scope.close()

    main_scope = ExitStack()
    wg = ctx.sb("f_wg", [128, 2, NDC, 512], BF16, scope=main_scope)
    wu = ctx.sb("f_wu", [128, 2, NDC, 512], BF16, scope=main_scope)
    wd = ctx.sb("f_wd", [128, 2, 4, DM // 4], BF16, scope=main_scope)
    sg = ctx.sb("f_sg", [128, 2, 512], F32, scope=main_scope)
    hbuf = ctx.sb("f_hbuf", [128, 4, 512], F32, scope=main_scope)
    ab = ctx.sb("f_ab", [128, 1, 512], F32, scope=main_scope)
    thr = ctx.sb("f_thr", [128, 6, 128], F32, scope=main_scope)
    hsst = ctx.sb("f_hsst", [128, 4, 512], BF16, scope=main_scope)
    hsT = ctx.sb("f_hsT", [128, 4, TOK], BF16, scope=main_scope)
    psg = [ctx.ps("f_psg", [128, 512], F32, scope=main_scope) for _ in range(2)]
    psu = [ctx.ps("f_psu", [128, 512], F32, scope=main_scope) for _ in range(2)]
    psd = [ctx.ps("f_psd", [128, 512], F32, scope=main_scope) for _ in range(4)]
    zm = rec.op("vector", lambda h: h.memset(of32[:, 0:4, :], 0.0), inc=cv)
    zm = rec.op("vector", lambda h: h.memset(of32[:, 4:8, :], 0.0), waits=[(cv, zm)], inc=cv)
    of_zero = (cv, zm)

    wfree = {"g": [None, None], "u": [None, None], "d": [None, None]}
    psg_free = [None, None]
    psu_free = [None, None]
    psd_free = [None] * 4
    sg_free = [None] * 4
    hb_free = [None] * 4
    hs_free = [None] * 4
    hsT_free = [None]
    ab_free = [None, None]
    thr_free = None
    of_add = {}
    st = {}

    def emit_gu(ic):
        nonlocal thr_free
        sl = ic % 2
        wmg = rec.op("scalar", lambda h, ic=ic, sl=sl: h.dma_start(
            out=wg[:, sl, :, :], in_=w_chunk_ap(W["w_gate"], ic * 512, (ic + 1) * 512)),
            waits=([wfree["g"][sl]] if wfree["g"][sl] else []) + ctx.wwait(W["w_gate"]),
            inc=wld, k=16)
        wmu = rec.op("scalar", lambda h, ic=ic, sl=sl: h.dma_start(
            out=wu[:, sl, :, :], in_=w_chunk_ap(W["w_up"], ic * 512, (ic + 1) * 512)),
            waits=([wfree["u"][sl]] if wfree["u"][sl] else []) + ctx.wwait(W["w_up"]),
            inc=wld, k=16)
        hs_marks = []
        for tt in range(NTT):
            pg = tt % 2
            wsg_ = [(wld, wmg), ready] + ([psg_free[pg]] if psg_free[pg] else [])
            gm = mm_group(rec, pe, [
                lambda h, s, e, k=k, tt=tt, sl=sl, pg=pg: h.matmul(
                    out=psg[pg][:], lhsT=xnT[:, k, tt * 128:(tt + 1) * 128],
                    rhs=wg[:, sl, k, :], start=s, stop=e) for k in range(NDC)], waits=wsg_)
            wsu_ = [(wld, wmu)] + ([psu_free[pg]] if psu_free[pg] else [])
            um = mm_group(rec, pe, [
                lambda h, s, e, k=k, tt=tt, sl=sl, pg=pg: h.matmul(
                    out=psu[pg][:], lhsT=xnT[:, k, tt * 128:(tt + 1) * 128],
                    rhs=wu[:, sl, k, :], start=s, stop=e) for k in range(NDC)], waits=wsu_)
            sgi = tt % 2
            wss = [(pe, gm)] + ([sg_free[sgi]] if sg_free[sgi] else [])
            sm = rec.op("scalar", lambda h, sgi=sgi, pg=pg: h.activation(
                out=sg[:, sgi, :], in_=psg[pg][:], func=AF.Silu), waits=wss, inc=ca)
            psg_free[pg] = (ca, sm)
            hsl = (ic * NTT + tt) % 4
            hw = [(ca, sm), (pe, um)] + ([hb_free[hsl]] if hb_free[hsl] else [])
            hm = rec.op("vector", lambda h, sgi=sgi, pg=pg, hsl=hsl: h.tensor_tensor(
                out=hbuf[:, hsl, :], in0=sg[:, sgi, :], in1=psu[pg][:], op=ALU.mult),
                waits=hw, inc=cv)
            psu_free[pg] = (cv, hm)
            sg_free[sgi] = (cv, hm)
            hv = hbuf[:, hsl, :]
            h4 = hv.rearrange("p (g f) -> p g f", f=4)
            asl = 0
            am = rec.op("scalar", lambda h, hv=hv, asl=asl: h.activation(
                out=ab[:, asl, :], in_=hv, func=AF.Abs),
                waits=[(cv, hm)] + ([ab_free[asl]] if ab_free[asl] else []), inc=ca)
            a4 = ab[:, asl, :].rearrange("p (g f) -> p g f", f=4)
            t_ = thr
            m_ = rec.op("vector", lambda h, a4=a4: h.tensor_tensor(
                out=t_[:, 0, :], in0=a4[:, :, 0], in1=a4[:, :, 1], op=ALU.max),
                waits=[(ca, am)] + ([thr_free] if thr_free else []), inc=cv)
            m_ = rec.op("vector", lambda h, a4=a4: h.tensor_tensor(
                out=t_[:, 1, :], in0=a4[:, :, 0], in1=a4[:, :, 1], op=ALU.min),
                waits=[(cv, m_)], inc=cv)
            m_ = rec.op("vector", lambda h, a4=a4: h.tensor_tensor(
                out=t_[:, 2, :], in0=a4[:, :, 2], in1=a4[:, :, 3], op=ALU.max),
                waits=[(cv, m_)], inc=cv)
            m_ = rec.op("vector", lambda h, a4=a4: h.tensor_tensor(
                out=t_[:, 3, :], in0=a4[:, :, 2], in1=a4[:, :, 3], op=ALU.min),
                waits=[(cv, m_)], inc=cv)
            m_ = rec.op("vector", lambda h: h.tensor_tensor(
                out=t_[:, 4, :], in0=t_[:, 0, :], in1=t_[:, 2, :], op=ALU.min),
                waits=[(cv, m_)], inc=cv)
            m_ = rec.op("vector", lambda h: h.tensor_tensor(
                out=t_[:, 5, :], in0=t_[:, 1, :], in1=t_[:, 3, :], op=ALU.max),
                waits=[(cv, m_)], inc=cv)
            m_ = rec.op("vector", lambda h: h.tensor_tensor(
                out=t_[:, 4, :], in0=t_[:, 4, :], in1=t_[:, 5, :], op=ALU.max),
                waits=[(cv, m_)], inc=cv)
            km = []
            for qi in range(4):
                m_ = rec.op("vector", lambda h, qi=qi, a4=a4: h.tensor_tensor(
                    out=t_[:, qi, :], in0=a4[:, :, qi], in1=t_[:, 4, :], op=ALU.is_ge),
                    waits=[(cv, m_)], inc=cv)
                km.append(m_)
            hsv = hsst[:, hsl, :].rearrange("p (g f) -> p g f", f=4)
            pm_ = None
            for qi in range(4):
                wsp = [(cv, km[qi])] + ([hs_free[hsl]] if (hs_free[hsl] and qi == 0) else [])
                pm_ = rec.op("gpsimd", lambda h, qi=qi, h4=h4, hsv=hsv: h.tensor_tensor(
                    out=hsv[:, :, qi], in0=h4[:, :, qi], in1=t_[:, qi, :], op=ALU.mult),
                    waits=wsp + ([(cp, pm_)] if pm_ else []), inc=cp)
            thr_free = (cp, pm_)
            hb_free[hsl] = (cp, pm_)
            ab_free[asl] = (cv, km[3])
            sm2 = rec.op("sync", lambda h, tt=tt, hsl=hsl, ic=ic: h.dma_start(
                out=io["scr_big"][tt * 128:(tt + 1) * 128, ic * 512:(ic + 1) * 512],
                in_=hsst[:, hsl, :]), waits=[(cp, pm_)], inc=cw, k=16)
            hs_marks.append(sm2)
            hs_free[hsl] = (cw, sm2)
        st[ic] = hs_marks[-1]

    def emit_dn(ic):
        sl = ic % 2
        tms = []
        for j4 in range(4):
            wst = [(cw, st[ic])] + ([hsT_free[0]] if (hsT_free[0] and j4 == 0) else [])
            tmk = rec.op("sync", lambda h, ic=ic, j4=j4: h.dma_start_transpose(
                out=hsT[:, j4, :],
                in_=io["scr_big"][0:TOK, ic * 512 + j4 * 128:ic * 512 + (j4 + 1) * 128]),
                waits=wst, inc=trh, k=16)
            tms.append(tmk)
        tr_ic = tms[-1]
        for dh in range(4):
            wmd = rec.op("scalar", lambda h, ic=ic, sl=sl, dh=dh: h.dma_start(
                out=wd[:, sl, :, :],
                in_=W["w_down"][ic * 512:(ic + 1) * 512, dh * 512:(dh + 1) * 512]
                .rearrange("(k p) n -> p k n", p=128)),
                waits=(([wfree["d"][sl]] if wfree["d"][sl] else []) if dh == 0 else [(pe, pe.n)])
                + ctx.wwait(W["w_down"]), inc=wld, k=16)
            for tt in range(NTT):
                if True:
                    ob = dh
                    pdi = tt % 4
                    wsd_ = [(trh, tr_ic), (wld, wmd)] + ([psd_free[pdi]] if psd_free[pdi] else [])
                    dm = mm_group(rec, pe, [
                        lambda h, s, e, j4=j4, tt=tt, sl=sl, pdi=pdi: h.matmul(
                            out=psd[pdi][:], lhsT=hsT[:, j4, tt * 128:(tt + 1) * 128],
                            rhs=wd[:, sl, j4, 0:512], start=s, stop=e)
                        for j4 in range(4)], waits=wsd_)
                    wse_ = [(pe, dm), of_zero] + ([of_add[(tt, ob)]] if (tt, ob) in of_add else [])
                    em = rec.op("vector", lambda h, tt=tt, ob=ob, pdi=pdi: h.tensor_tensor(
                        out=of32[:, tt, ob * 512:(ob + 1) * 512],
                        in0=of32[:, tt, ob * 512:(ob + 1) * 512],
                        in1=psd[pdi][:], op=ALU.add), waits=wse_, inc=cv)
                    psd_free[pdi] = (cv, em)
                    of_add[(tt, ob)] = (cv, em)
        hsT_free[0] = (pe, pe.n)
        wfree["d"][sl] = (pe, pe.n)
        wfree["g"][sl] = (pe, pe.n)
        wfree["u"][sl] = (pe, pe.n)

    for ic in range(NIC + 1):
        if ic < NIC:
            emit_gu(ic)
        if ic >= 1:
            emit_dn(ic - 1)
        dn_done = (cv, cv.n)
    rec.barrier(ctx.chans)
    main_scope.close()
    fin_scope = ExitStack()
    emit_ln_residual_store(ctx, rec, of32, io["src"], io["dst"], dn_done, tag="f0fin",
                           scope=fin_scope)
    rec.barrier(ctx.chans)
    fin_scope.close()
    blk.close()



def emit_l1_attn(ctx, rec, io, W, C):
    nc = ctx.nc
    NH, DH, DS, DIN = 48, 64, 128, 3072
    NCC = DIN // 128
    cv = ctx.chan("m_v")
    ca = ctx.chan("m_a")
    cp = ctx.chan("m_p")
    cw = ctx.chan("m_sw")
    pe = ctx.chan("m_pe")
    wld = ctx.chan("m_wl")
    trm = ctx.chan("m_tr")
    snd = ctx.chan("m_snd")
    rcv1 = ctx.chan("m_rcv1")
    rcv2 = ctx.chan("m_rcv2")
    sdd = ctx.chan("m_seed")
    poll1 = ctx.chan("m_poll1")
    poll2 = ctx.chan("m_poll2")

    blk = ExitStack()
    small = ctx.sb("m_small", [128, 80], F32, side="right", scope=blk)
    LcT = ctx.sb("m_LcT", [128, NTT, NH], F32, side="right", scope=blk)
    eUT = ctx.sb("m_eUT", [128, NTT, NH], F32, side="right", scope=blk)
    expLcT = ctx.sb("m_expLcT", [128, NTT, NH], F32, side="right", scope=blk)
    dAe = ctx.sb("m_dAe", [128, NTT * NH], F32, side="right", scope=blk)
    Db = ctx.sb("m_Db", [128, NH], F32, side="right", scope=blk)
    T0c = ctx.sb("m_T0", [128, 128], F32, side="right", scope=blk)
    eyeb = ctx.sb("m_eyeb", [128, 128], BF16, side="right", scope=blk)
    eyef = ctx.sb("m_eyef", [128, 128], F32, side="right", scope=blk)
    zro = ctx.sb("m_zro", [128, 128], F32, side="right", scope=blk)
    ctail = ctx.sb("m_ctail", [128, NCC, 3], BF16, side="right", scope=blk)
    convst = ctx.sb("m_cvst", [128, NCC, 3], BF16, side="right", scope=blk)
    scope2 = ExitStack()
    xn3T = ctx.sb("m_xnT", [128, NDC, TOK], BF16, side="right", scope=scope2)
    scope3 = ExitStack()
    Lc = ctx.sb("m_Lc", [128, TOK], F32, side="right", scope=scope3)
    Bf = ctx.sb("m_Bf", [128, TOK], BF16, side="right", scope=scope3)
    Cf = ctx.sb("m_Cf", [128, TOK], BF16, side="right", scope=scope3)
    Btok = ctx.sb("m_Btok", [128, NTT, DS], BF16, side="right", scope=scope3)
    Ssb = ctx.sb("m_Ssb", [128, NH * DH], F32, side="right", scope=scope3)
    Sbf = ctx.sb("m_Sbf", [128, NH * DH], BF16, side="right", scope=scope3)
    sendS = ctx.sb("m_sendS", [128, NH * DH], BF16, side="right", scope=scope3)
    ssm0 = ctx.sb("m_ssm0", [128, NH * DH], F32, side="right", scope=scope3)
    scope4 = ExitStack()
    xcf = ctx.sb("m_xcf", [128, NCC, TOK], BF16, side="right", scope=scope4)
    scope3b = ExitStack()
    dtl = ctx.sb("m_dt", [128, TOK], F32, side="right", scope=scope3b)
    ldec = ctx.sb("m_ldec", [128, TOK], F32, side="right", scope=scope3b)
    eU = ctx.sb("m_eU", [128, TOK], F32, side="right", scope=scope3b)
    expLc = ctx.sb("m_expLc", [128, TOK], F32, side="right", scope=scope3b)

    cst = ctx.chan("m_cst")
    rec.op("sync", lambda h: h.dma_start(out=T0c[:], in_=W["c_T0"]), inc=cst, k=16)
    gcst = ctx.chan("m_gcst")
    rec.op("gpsimd", lambda h: h.dma_start(out=eyeb[:], in_=W["c_eyef"]), inc=gcst, k=16)
    rec.op("sync", lambda h: h.dma_start(out=eyef[:], in_=W["c_eyef"]), inc=cst, k=16)
    rec.op("sync", lambda h: h.dma_start(out=small[0:48, 0:1], in_=W["dt_bias_r"]), inc=cst, k=16)
    rec.op("sync", lambda h: h.dma_start(out=small[0:48, 49:50], in_=W["A_log_c"]), inc=cst, k=16)
    cvm_ = rec.op("gpsimd", lambda h: h.dma_start(
        out=convst[:], in_=W["conv_state"].rearrange("(cc p) k -> p cc k", p=128)), inc=gcst, k=16)
    rec.op("sync", lambda h: h.dma_start(out=ssm0[:], in_=W["ssm0_t"]), inc=cst, k=16)
    mz = rec.op("vector", lambda h: h.memset(zro[:], 0.0), inc=cv)
    mo = rec.op("vector", lambda h: h.memset(small[:, 48:49], 1.0), waits=[(cst, cst.n)], inc=cv)
    cst_all = (cst, cst.n)

    ln_scope = ExitStack()
    trc0, trm0 = emit_ln_phase(ctx, rec, io["src"], io["scr_a"], xn3T, kind="ln", tag="m_ln",
                               scope=ln_scope, u_hbm=io["scr_u"], rms2=True)
    wcol = ctx.sb("m_wcol", [128, NDC], F32, scope=ln_scope)
    wcm = rec.op("sync", lambda h: h.dma_start(out=wcol[:], in_=W["norm_w_c"]), inc=cst, k=16)
    sc_m = None
    for k in range(NDC):
        sc_m = rec.op("vector", lambda h, k=k: h.tensor_scalar(
            out=xn3T[:, k, :], in0=xn3T[:, k, :], scalar1=wcol[:, k:k + 1], scalar2=None,
            op0=ALU.mult), waits=[(trc0, trm0), (cst, wcm)] if k == 0 else [(cv, sc_m)], inc=cv)
    xn3_ready = (cv, sc_m)
    rec.barrier(ctx.chans)
    ln_scope.close()

    fp_scope = ExitStack()
    wti = ctx.sb("m_wti", [128, 2, NDC, 128], BF16, scope=fp_scope)
    psf = [ctx.ps("m_psf", [128, 512], F32, scope=fp_scope) for _ in range(4)]
    wfree = [None, None]
    psfree = [None] * 4
    units = [("xc", cc, DIN + cc * 128) for cc in range(NCC)]
    units += [("B", 0, 2 * DIN), ("C", 0, 2 * DIN + DS), ("dt", 0, 2 * DIN + 2 * DS)]
    for ui, (kind, cc, col0) in enumerate(units):
        sl = ui % 2
        ncol = 48 if kind == "dt" else 128
        ws = [wfree[sl]] if wfree[sl] else []
        wm = rec.op("scalar", lambda h, sl=sl, col0=col0, ncol=ncol: h.dma_start(
            out=wti[:, sl, :, 0:ncol],
            in_=W["W_inT"][:, col0:col0 + ncol].rearrange("(k p) n -> p k n", p=128)),
            waits=ws + ctx.wwait(W["W_inT"]), inc=wld, k=16)
        for ts in range(2):
            pi = (ui * 2 + ts) % 4
            waits = [(wld, wm), xn3_ready, cst_all] + ([psfree[pi]] if psfree[pi] else [])
            mark = None
            for k in range(NDC):
                mark = rec.op("tensor", lambda h, k=k, ts=ts, sl=sl, pi=pi, ncol=ncol: h.matmul(
                    out=psf[pi][0:ncol, 0:512], lhsT=wti[:, sl, k, 0:ncol],
                    rhs=xn3T[:, k, ts * 512:(ts + 1) * 512],
                    start=(k == 0), stop=(k == NDC - 1)), waits=waits if k == 0 else (),
                    inc=pe if k == NDC - 1 else None)
            if kind == "xc":
                em = rec.op("vector", lambda h, pi=pi, cc=cc, ts=ts: h.tensor_copy(
                    out=xcf[:, cc, ts * 512:(ts + 1) * 512], in_=psf[pi][:, 0:512]),
                    waits=[(pe, mark)], inc=cv)
            elif kind == "B":
                em = rec.op("vector", lambda h, pi=pi, ts=ts: h.tensor_copy(
                    out=Bf[:, ts * 512:(ts + 1) * 512], in_=psf[pi][:, 0:512]),
                    waits=[(pe, mark)], inc=cv)
            elif kind == "C":
                em = rec.op("vector", lambda h, pi=pi, ts=ts: h.tensor_copy(
                    out=Cf[:, ts * 512:(ts + 1) * 512], in_=psf[pi][:, 0:512]),
                    waits=[(pe, mark)], inc=cv)
            else:
                em = rec.op("vector", lambda h, pi=pi, ts=ts: h.tensor_copy(
                    out=dtl[0:48, ts * 512:(ts + 1) * 512], in_=psf[pi][0:48, 0:512]),
                    waits=[(pe, mark)], inc=cv)
            psfree[pi] = (cv, em)
        wfree[sl] = (pe, mark)
    xc_done = (cv, cv.n)
    rec.barrier(ctx.chans)
    fp_scope.close()
    if DBG_L1STOP <= 2:
        rec.barrier(ctx.chans)
        return

    def g_tail(h):
        with h.register("t_pp") as pp:
            h.reg_load(pp, C["par_sb"][0:1, 0:1])
            with h.If_eq(pp, 0):
                h.wait_ge(cv.sem, xc_done[1])
                h.dma_start(out=io["sh_l1t"][:, 0:NCC * 3].rearrange("p (a b) -> p a b", b=3),
                            in_=xcf[:, :, TOK - 3:TOK]).then_inc(snd.sem, 16)
                h.wait_ge(snd.sem, 16)
                h.dma_start(out=io["sh_l1tf"][:], in_=C["nonce_sb"][0:1, 0:4]).then_inc(snd.sem, 16)
            with h.Else():
                with h.register("t_pc") as pc, h.register("t_pex") as pex, h.register("t_f0") as f0, \
                     h.register("t_nn") as nn:
                    h.reg_load(nn, C["nonce_sb"][0:1, 0:1])
                    h.reg_mov(pc, 1)
                    h.reg_mov(pex, 0)
                    with h.While(pc):
                        h.reg_alu(pex, pex, 16, ALU.add)
                        h.dma_start(out=C["flag_sb"][0:1, 0:4], in_=io["sh_l1tf"][0:1, :]).then_inc(poll1.sem, 16)
                        h.wait_ge(poll1.sem, pex)
                        h.reg_load(f0, C["flag_sb"][0:1, 0:1])
                        h.reg_alu(pc, f0, nn, ALU.not_equal)
                h.dma_start(out=ctail[:],
                            in_=io["sh_l1t"][:, 0:NCC * 3].rearrange("p (a b) -> p a b", b=3)).then_inc(rcv1.sem, 16)
    rec.raw("gpsimd", g_tail)
    sdd.reserve(1)

    def v_tail(h):
        with h.register("t_pv") as pv:
            h.reg_load(pv, C["par_sb"][0:1, 0:1])
            with h.If_eq(pv, 0):
                h.wait_ge(gcst.sem, cvm_)
                h.tensor_copy(out=ctail[:], in_=convst[:]).then_inc(sdd.sem, 1)
            with h.Else():
                h.wait_ge(rcv1.sem, 16)
                h.memset(small[:, 79:80], 0.0).then_inc(sdd.sem, 1)
    rec.raw("vector", v_tail)
    tail_ready = (sdd, 1)
    if DBG_L1STOP <= 3:
        rec.barrier(ctx.chans)
        return

    d0 = rec.op("vector", lambda h: h.tensor_scalar(
        out=dtl[0:48, :], in0=dtl[0:48, :], scalar1=small[0:48, 0:1], scalar2=None, op0=ALU.add),
        waits=[xc_done, cst_all], inc=cv)
    d1 = rec.op("scalar", lambda h: h.activation(out=ldec[0:48, :], in_=dtl[0:48, :], func=AF.Abs),
                waits=[(cv, d0)], inc=ca)
    d2 = rec.op("scalar", lambda h: h.activation(out=eU[0:48, :], in_=ldec[0:48, :], func=AF.Exp,
                                                 scale=-1.0), waits=[(ca, d1)], inc=ca)
    d3 = rec.op("scalar", lambda h: h.activation(out=eU[0:48, :], in_=eU[0:48, :], func=AF.Ln,
                                                 bias=small[0:48, 48:49]), waits=[(ca, d2)], inc=ca)
    d4 = rec.op("vector", lambda h: h.tensor_scalar(
        out=Lc[0:48, :], in0=dtl[0:48, :], scalar1=0.0, scalar2=None, op0=ALU.max),
        waits=[(ca, d3), (ca, d1)], inc=cv)
    d5 = rec.op("vector", lambda h: h.tensor_tensor(
        out=dtl[0:48, :], in0=Lc[0:48, :], in1=eU[0:48, :], op=ALU.add), waits=[(cv, d4)], inc=cv)
    if DBG_L1STOP <= 31:
        rec.barrier(ctx.chans)
        return
    a0 = rec.op("scalar", lambda h: h.activation(out=small[0:48, 50:51], in_=small[0:48, 49:50],
                                                 func=AF.Exp), waits=[cst_all], inc=ca)
    a1 = rec.op("vector", lambda h: h.tensor_scalar(
        out=small[0:48, 50:51], in0=small[0:48, 50:51], scalar1=-1.0, scalar2=None, op0=ALU.mult),
        waits=[(ca, a0)], inc=cv)
    l0 = rec.op("vector", lambda h: h.tensor_scalar(
        out=ldec[0:48, :], in0=dtl[0:48, :], scalar1=small[0:48, 50:51], scalar2=None, op0=ALU.mult),
        waits=[(cv, a1), (cv, d5)], inc=cv)
    sc = l0
    for n in range(NTT):
        sc = rec.op("vector", lambda h, n=n: h.tensor_tensor_scan(
            out=Lc[0:48, n * 128:(n + 1) * 128], data0=ldec[0:48, n * 128:(n + 1) * 128],
            data1=zro[0:48, 0:128], initial=0.0, op0=ALU.add, op1=ALU.add),
            waits=[(cv, sc), (cv, mz)], inc=cv)
    if DBG_L1STOP <= 32:
        rec.barrier(ctx.chans)
        return
    e0 = rec.op("scalar", lambda h: h.activation(out=expLc[0:48, :], in_=Lc[0:48, :], func=AF.Exp),
                waits=[(cv, sc)], inc=ca)
    eu = sc
    for n in range(NTT):
        eu = rec.op("vector", lambda h, n=n: h.tensor_scalar(
            out=eU[0:48, n * 128:(n + 1) * 128], in0=Lc[0:48, n * 128:(n + 1) * 128],
            scalar1=Lc[0:48, n * 128 + 127:n * 128 + 128], scalar2=None, op0=ALU.subtract),
            waits=[(cv, eu), (ca, e0)], inc=cv)
    e1 = rec.op("scalar", lambda h: h.activation(out=eU[0:48, :], in_=eU[0:48, :], func=AF.Exp,
                                                 scale=-1.0), waits=[(cv, eu)], inc=ca)
    if DBG_L1STOP <= 33:
        rec.barrier(ctx.chans)
        return
    dec_done = (ca, ca.n)

    tr_scope = ExitStack()
    pst_r = ctx.ps("m_pstr", [128, 64], F32, scope=tr_scope)
    ps_bc = ctx.ps("m_psbc", [128, 512], F32, scope=tr_scope)
    tm_ = None
    te_free = None
    for (srcT, dstT) in ((Lc, LcT), (eU, eUT), (expLc, expLcT)):
        for n in range(NTT):
            wsx = [dec_done] + ([te_free] if te_free else [])
            tm_ = rec.op("tensor", lambda h, srcT=srcT, n=n: h.transpose(
                out=pst_r[:, 0:48], in_=srcT[0:48, n * 128:(n + 1) * 128], identity=eyef[0:48, 0:48]),
                waits=wsx, inc=pe)
            tm_ = rec.op("scalar", lambda h, dstT=dstT, n=n: h.activation(
                out=dstT[:, n, :], in_=pst_r[:, 0:48], func=AF.Copy), waits=[(pe, tm_)], inc=ca)
            te_free = (ca, tm_)
    if DBG_L1STOP <= 34:
        rec.barrier(ctx.chans)
        return
    row1 = ctx.sb("m_row1", [1, 384], F32, scope=tr_scope)
    ec = rec.op("sync", lambda h: h.dma_start(
        out=io["scr_small"][0:48, 0:8],
        in_=Lc[0:48, :].rearrange("p (n c) -> p n c", c=128)[:, :, 127:128].rearrange("p n c -> p (n c)"),
        allow_slow_non_contiguous=True), waits=[dec_done], inc=cw, k=16)
    er = rec.op("sync", lambda h: h.dma_start(
        out=row1[:].rearrange("p (a b) -> p a b", a=8),
        in_=io["scr_small"][0:48, 0:8].transpose((1, 0)).unsqueeze(0),
        allow_slow_non_contiguous=True), waits=[(cw, ec)], inc=cw, k=16)
    bm1 = rec.op("tensor", lambda h: h.matmul(out=ps_bc[:, 0:384], lhsT=C["onesf"][0:1, 0:128],
                                              rhs=row1[:], start=True, stop=True),
                 waits=[(cw, er), cst_all], inc=pe)
    bm2 = rec.op("scalar", lambda h: h.activation(out=dAe[:, 0:384], in_=ps_bc[:, 0:384],
                                                  func=AF.Exp), waits=[(pe, bm1)], inc=ca)
    if DBG_L1STOP <= 35:
        rec.barrier(ctx.chans)
        return
    drow = ctx.sb("m_drow", [1, 48], F32, scope=tr_scope)
    dm0 = rec.op("sync", lambda h: h.dma_start(out=drow[:], in_=W["D_row"]), inc=cst, k=16)
    dm1 = rec.op("tensor", lambda h: h.matmul(out=ps_bc[:, 384:432], lhsT=C["onesf"][0:1, 0:128],
                                              rhs=drow[:], start=True, stop=True),
                 waits=[(cst, dm0), (ca, bm2), cst_all], inc=pe)
    dm2 = rec.op("scalar", lambda h: h.activation(out=Db[:], in_=ps_bc[:, 384:432], func=AF.Copy),
                 waits=[(pe, dm1)], inc=ca)
    bc_done = (ca, ca.n)
    rec.barrier(ctx.chans)
    tr_scope.close()
    scope3b.close()
    if DBG_L1STOP <= 4:
        rec.barrier(ctx.chans)
        return

    cvp_scope = ExitStack()
    cpad = ctx.sb("m_cpad", [128, 2, TOK + 3], F32, scope=cvp_scope)
    ctmp = ctx.sb("m_ctmp", [128, 2, TOK], F32, scope=cvp_scope)
    xsil = ctx.sb("m_xsil", [128, 2, TOK], BF16, scope=cvp_scope)
    cw4 = ctx.sb("m_cw4", [128, NCC, 4], F32, scope=cvp_scope)
    cb1 = ctx.sb("m_cb1", [128, NCC], F32, scope=cvp_scope)
    cwm = rec.op("sync", lambda h: h.dma_start(
        out=cw4[:], in_=W["conv_w"].rearrange("(cc p) k -> p cc k", p=128)), inc=cst, k=16)
    cbm = rec.op("sync", lambda h: h.dma_start(out=cb1[:], in_=W["conv_b_c"]), inc=cst, k=16)
    cp_free = [(cv, 0), (cv, 0)]
    for cc in range(NCC):
        sl = cc % 2
        m0 = rec.op("vector", lambda h, cc=cc, sl=sl: h.tensor_copy(
            out=cpad[:, sl, 0:3], in_=ctail[:, cc, :]),
            waits=[tail_ready, xc_done, (cst, max(cwm, cbm)), cp_free[sl]], inc=cv)
        m1 = rec.op("vector", lambda h, cc=cc, sl=sl: h.tensor_copy(
            out=cpad[:, sl, 3:TOK + 3], in_=xcf[:, cc, :]), waits=[(cv, m0)], inc=cv)
        m2 = rec.op("vector", lambda h, cc=cc, sl=sl: h.tensor_scalar(
            out=ctmp[:, sl, :], in0=cpad[:, sl, 0:TOK], scalar1=cw4[:, cc, 0:1], scalar2=None,
            op0=ALU.mult), waits=[(cv, m1)], inc=cv)
        for kk in range(1, 4):
            m2 = rec.op("vector", lambda h, cc=cc, sl=sl, kk=kk: h.scalar_tensor_tensor(
                out=ctmp[:, sl, :], in0=cpad[:, sl, kk:kk + TOK], scalar=cw4[:, cc, kk:kk + 1],
                in1=ctmp[:, sl, :], op0=ALU.mult, op1=ALU.add), waits=[(cv, m2)], inc=cv)
        m3 = rec.op("vector", lambda h, cc=cc, sl=sl: h.tensor_scalar(
            out=ctmp[:, sl, :], in0=ctmp[:, sl, :], scalar1=cb1[:, cc:cc + 1], scalar2=None,
            op0=ALU.add), waits=[(cv, m2)], inc=cv)
        m4 = rec.op("scalar", lambda h, cc=cc, sl=sl: h.activation(
            out=xsil[:, sl, :], in_=ctmp[:, sl, :], func=AF.Silu), waits=[(cv, m3)], inc=ca)
        sm_ = rec.op("sync", lambda h, cc=cc, sl=sl: h.dma_start(
            out=io["scr_m"][cc * 128:(cc + 1) * 128, 0:TOK], in_=xsil[:, sl, :]),
            waits=[(ca, m4)], inc=cw, k=16)
        cp_free[sl] = (cw, sm_)
    xf_st = cw.n
    rec.barrier(ctx.chans)
    scope4.close()
    cvp_scope.close()
    if DBG_L1STOP <= 5:
        rec.barrier(ctx.chans)
        return

    yt_scope = ExitStack()
    Ytok = ctx.sb("m_Ytok", [128, NTT, DIN], BF16, scope=yt_scope)
    xt_scope = ExitStack()
    Xchunk = ctx.sb("m_Xch", [128, 2, DIN], BF16, scope=xt_scope)
    rec.op("sync", lambda h: h.dma_start(out=io["scr_m"][DIN:DIN + 128, 0:TOK], in_=Bf[:]),
           waits=[xc_done], inc=cw, k=16)
    bf_st = cw.n
    for tt in range(NTT):
        rec.op("sync", lambda h, tt=tt: h.dma_start_transpose(
            out=Btok[:, tt, :], in_=io["scr_m"][DIN:DIN + 128, tt * 128:(tt + 1) * 128]),
            waits=[(cw, bf_st)], inc=trm, k=16)
    tok_ready = (trm, trm.n)
    hself = ctx.sb("m_hself", [48, NH * 128], F32, scope=xt_scope)
    hsm = rec.op("sync", lambda h: h.dma_start(out=hself[:], in_=W["c_hsel"]), inc=cst, k=16)
    if DBG_L1STOP <= 6:
        rec.barrier(ctx.chans)
        return


    sp_scope = ExitStack()
    psS = [ctx.ps("m_psS", [128, 64], F32, scope=sp_scope) for _ in range(2)]
    xeu = ctx.sb("m_xeu", [128, 2, DH], BF16, scope=sp_scope)
    s0m = rec.op("vector", lambda h: h.tensor_copy(out=Ssb[:], in_=ssm0[:]),
                 waits=[cst_all], inc=cv)

    def state_pass(seeded_mark, tag):
        psS_free = [None, None]
        xeu_free = [None, None]
        xch_last = [None, None]
        rm = {}
        for n in range(NTT):
            sx = n % 2
            wsl = [(cw, xf_st)] + ([xch_last[sx]] if xch_last[sx] else [])
            xlm = rec.op("sync", lambda h, n=n, sx=sx: h.dma_start_transpose(
                out=Xchunk[:, sx, :], in_=io["scr_m"][0:DIN, n * 128:(n + 1) * 128]),
                waits=wsl, inc=trm, k=16)
            hh_marks = []
            for hh in range(NH):
                sl = (n * NH + hh) % 2
                wsx = [tok_ready, bc_done, seeded_mark, (trm, xlm)] + ([xeu_free[sl]] if xeu_free[sl] else [])
                xm = rec.op("vector", lambda h, n=n, hh=hh, sl=sl, sx=sx: h.tensor_scalar(
                    out=xeu[:, sl, :], in0=Xchunk[:, sx, hh * DH:(hh + 1) * DH],
                    scalar1=eUT[:, n, hh:hh + 1], scalar2=None, op0=ALU.mult),
                    waits=wsx, inc=cv)
                hh_marks.append(xm)
                wsp = [(cv, xm)] + ([psS_free[sl]] if psS_free[sl] else [])
                pm = rec.op("tensor", lambda h, n=n, sl=sl: h.matmul(
                    out=psS[sl][:], lhsT=Btok[:, n, :], rhs=xeu[:, sl, :], start=True, stop=True),
                    waits=wsp, inc=pe)
                wsr = [(pe, pm)] + ([rm[hh]] if hh in rm else [])
                rmk = rec.op("vector", lambda h, n=n, hh=hh, sl=sl: h.scalar_tensor_tensor(
                    out=Ssb[:, hh * DH:(hh + 1) * DH], in0=Ssb[:, hh * DH:(hh + 1) * DH],
                    scalar=dAe[:, n * NH + hh:n * NH + hh + 1], in1=psS[sl][:],
                    op0=ALU.mult, op1=ALU.add), waits=wsr, inc=cv)
                rm[hh] = (cv, rmk)
                psS_free[sl] = (cv, rmk)
                xeu_free[sl] = (pe, pm)
            xch_last[sx] = (cv, cv.n)
        return cv.n
    sp_end = state_pass((cv, s0m), "sp")
    sendm = rec.op("vector", lambda h: h.tensor_copy(out=sendS[:], in_=Ssb[:]),
                   waits=[(cv, sp_end)], inc=cv)
    if DBG_L1STOP <= 7:
        rec.barrier(ctx.chans)
        return

    def g_state_x(h):
        with h.register("s_pp") as pp:
            h.reg_load(pp, C["par_sb"][0:1, 0:1])
            with h.If_eq(pp, 0):
                h.wait_ge(cv.sem, sendm)
                h.dma_start(out=io["sh_l1s"][:], in_=sendS[:]).then_inc(snd.sem, 16)
                h.wait_ge(snd.sem, 48)
                h.dma_start(out=io["sh_l1sf"][:], in_=C["nonce_sb"][0:1, 0:4]).then_inc(snd.sem, 16)
            with h.Else():
                h.wait_ge(cv.sem, sendm)
                with h.register("s_pc") as pc, h.register("s_pex") as pex, h.register("s_f0") as f0, \
                     h.register("s_nn") as nn:
                    h.reg_load(nn, C["nonce_sb"][0:1, 0:1])
                    h.reg_mov(pc, 1)
                    h.reg_mov(pex, 0)
                    with h.While(pc):
                        h.reg_alu(pex, pex, 16, ALU.add)
                        h.dma_start(out=C["flag_sb"][0:1, 0:4], in_=io["sh_l1sf"][0:1, :]).then_inc(poll2.sem, 16)
                        h.wait_ge(poll2.sem, pex)
                        h.reg_load(f0, C["flag_sb"][0:1, 0:1])
                        h.reg_alu(pc, f0, nn, ALU.not_equal)
                h.dma_start(out=sendS[:], in_=io["sh_l1s"][:]).then_inc(rcv2.sem, 16)
    rec.raw("gpsimd", g_state_x)
    sdd.reserve(1)

    def v_seed2(h):
        with h.register("s_pv") as pv:
            h.reg_load(pv, C["par_sb"][0:1, 0:1])
            with h.If_eq(pv, 0):
                h.tensor_copy(out=Ssb[:], in_=ssm0[:]).then_inc(sdd.sem, 1)
            with h.Else():
                h.wait_ge(rcv2.sem, 16)
                h.tensor_copy(out=Ssb[:], in_=sendS[:]).then_inc(sdd.sem, 1)
    rec.raw("vector", v_seed2)
    seed2 = (sdd, 2)
    sb0 = rec.op("vector", lambda h: h.tensor_copy(out=Sbf[:], in_=Ssb[:]), waits=[seed2], inc=cv)
    rec.barrier(ctx.chans)
    sp_scope.close()
    if DBG_L1STOP <= 8:
        rec.barrier(ctx.chans)
        return

    yl_scope = ExitStack()
    dtile = ctx.sb("m_dtile", [128, 4, 128], F32, scope=yl_scope)
    dexp = ctx.sb("m_dexp", [128, 4, 128], BF16, scope=yl_scope)
    Mt = ctx.sb("m_Mt", [128, 4, 128], BF16, scope=yl_scope)
    BCs = ctx.sb("m_BCs", [128, 128], F32, scope=yl_scope)
    xeu2 = ctx.sb("m_xeu2", [128, 4, DH], BF16, scope=yl_scope)
    xdd = ctx.sb("m_xdd", [128, 4, DH], BF16, scope=yl_scope)
    ytmp = ctx.sb("m_ytmp", [128, 4, DH], F32, scope=yl_scope)
    psBC = ctx.ps("m_psBC", [128, 128], F32, scope=yl_scope)
    psM = [ctx.ps("m_psM", [128, 128], F32, scope=yl_scope) for _ in range(2)]
    psYi = [ctx.ps("m_psYi", [128, 64], F32, scope=yl_scope) for _ in range(2)]
    psYe = [ctx.ps("m_psYe", [128, 64], F32, scope=yl_scope) for _ in range(2)]
    psS2 = ctx.ps("m_psS2", [128, 64], F32, scope=yl_scope)
    psM_free = [None, None]
    psYi_free = [None, None]
    psYe_free = [None, None]
    dt_free = [None] * 4
    de_free = [None] * 4
    mt_free = [None] * 4
    xe_free = [None] * 4
    xd_free = [None] * 4
    yt_free = [None] * 4
    bc_ev = None
    ps2_free = None
    srm = {}

    xch2_last = [None, None]

    def do_chunk(n):
        nonlocal bc_ev, ps2_free
        sx = n % 2
        wsl = [(cw, xf_st)] + ([xch2_last[sx]] if xch2_last[sx] else [])
        xlm = rec.op("sync", lambda h, n=n, sx=sx: h.dma_start_transpose(
            out=Xchunk[:, sx, :], in_=io["scr_m"][0:DIN, n * 128:(n + 1) * 128]),
            waits=wsl, inc=trm, k=16)
        bcw = [tok_ready, xc_done] + ([bc_ev] if bc_ev else [])
        bm = rec.op("tensor", lambda h, n=n: h.matmul(
            out=psBC[:], lhsT=Bf[:, n * 128:(n + 1) * 128], rhs=Cf[:, n * 128:(n + 1) * 128],
            start=True, stop=True), waits=bcw, inc=pe)
        bce = rec.op("scalar", lambda h: h.activation(out=BCs[:], in_=psBC[:], func=AF.Copy),
                     waits=[(pe, bm)], inc=ca)
        bc_ev = (ca, bce)
        for hh in range(NH):
            sl = (n * NH + hh) % 2
            s4 = (n * NH + hh) % 4
            wsm = [(cv, sc), (cst, hsm)] + ([psM_free[sl]] if psM_free[sl] else [])
            mm0 = rec.op("tensor", lambda h, n=n, hh=hh, sl=sl: h.matmul(
                out=psM[sl][:], lhsT=hself[0:48, hh * 128:(hh + 1) * 128],
                rhs=Lc[0:48, n * 128:(n + 1) * 128], start=True, stop=True), waits=wsm, inc=pe)
            wsd = [(pe, mm0), cst_all, bc_done] + ([dt_free[s4]] if dt_free[s4] else [])
            md = rec.op("vector", lambda h, n=n, hh=hh, sl=sl, s4=s4: h.scalar_tensor_tensor(
                out=dtile[:, s4, :], in0=psM[sl][:], scalar=LcT[:, n, hh:hh + 1], in1=T0c[:],
                op0=ALU.subtract, op1=ALU.add), waits=wsd, inc=cv)
            psM_free[sl] = (cv, md)
            wse = [(cv, md)] + ([de_free[s4]] if de_free[s4] else [])
            me = rec.op("scalar", lambda h, s4=s4: h.activation(
                out=dexp[:, s4, :], in_=dtile[:, s4, :], func=AF.Exp), waits=wse, inc=ca)
            dt_free[s4] = (ca, me)
            wst = [(ca, me), bc_ev] + ([mt_free[s4]] if mt_free[s4] else [])
            mt = rec.op("gpsimd", lambda h, s4=s4: h.tensor_tensor(
                out=Mt[:, s4, :], in0=BCs[:], in1=dexp[:, s4, :], op=ALU.mult), waits=wst, inc=cp)
            de_free[s4] = (cp, mt)
            wxd = [tok_ready, bc_done, (trm, xlm)] + ([xd_free[s4]] if xd_free[s4] else [])
            xd = rec.op("vector", lambda h, hh=hh, s4=s4, sx=sx: h.tensor_scalar(
                out=xdd[:, s4, :], in0=Xchunk[:, sx, hh * DH:(hh + 1) * DH],
                scalar1=Db[:, hh:hh + 1], scalar2=None, op0=ALU.mult), waits=wxd, inc=cv)
            wyi = [(cp, mt), (cv, xd)] + ([psYi_free[sl]] if psYi_free[sl] else [])
            yi1 = rec.op("tensor", lambda h, hh=hh, sl=sl, s4=s4, sx=sx: h.matmul(
                out=psYi[sl][:], lhsT=Mt[:, s4, :], rhs=Xchunk[:, sx, hh * DH:(hh + 1) * DH],
                start=True, stop=False), waits=wyi)
            yi2 = rec.op("tensor", lambda h, sl=sl, s4=s4: h.matmul(
                out=psYi[sl][:], lhsT=eyeb[:], rhs=xdd[:, s4, :], start=False, stop=True), inc=pe)
            mt_free[s4] = (pe, yi2)
            xd_free[s4] = (pe, yi2)
            wye = [(cv, sb0), xc_done] + ([srm[hh]] if hh in srm else []) + \
                  ([psYe_free[sl]] if psYe_free[sl] else [])
            ye = rec.op("tensor", lambda h, n=n, hh=hh, sl=sl: h.matmul(
                out=psYe[sl][:], lhsT=Cf[:, n * 128:(n + 1) * 128],
                rhs=Sbf[:, hh * DH:(hh + 1) * DH], start=True, stop=True), waits=wye, inc=pe)
            wv1 = [(pe, ye)] + ([yt_free[s4]] if yt_free[s4] else [])
            yv1 = rec.op("vector", lambda h, n=n, hh=hh, sl=sl, s4=s4: h.tensor_scalar(
                out=ytmp[:, s4, :], in0=psYe[sl][:], scalar1=expLcT[:, n, hh:hh + 1],
                scalar2=None, op0=ALU.mult), waits=wv1, inc=cv)
            psYe_free[sl] = (cv, yv1)
            yv2 = rec.op("vector", lambda h, n=n, hh=hh, sl=sl, s4=s4: h.tensor_tensor(
                out=Ytok[:, n, hh * DH:(hh + 1) * DH], in0=ytmp[:, s4, :], in1=psYi[sl][:],
                op=ALU.add), waits=[(cv, yv1), (pe, yi2)], inc=cv)
            psYi_free[sl] = (cv, yv2)
            yt_free[s4] = (cv, yv2)
            if n < NTT - 1:
                wxe = [tok_ready, bc_done] + ([xe_free[s4]] if xe_free[s4] else [])
                xm = rec.op("vector", lambda h, n=n, hh=hh, s4=s4, sx=sx: h.tensor_scalar(
                    out=xeu2[:, s4, :], in0=Xchunk[:, sx, hh * DH:(hh + 1) * DH],
                    scalar1=eUT[:, n, hh:hh + 1], scalar2=None, op0=ALU.mult), waits=wxe, inc=cv)
                wsp2 = [(cv, xm), (pe, ye)] + ([ps2_free] if ps2_free else [])
                pm2 = rec.op("tensor", lambda h, n=n, s4=s4: h.matmul(
                    out=psS2[:], lhsT=Btok[:, n, :], rhs=xeu2[:, s4, :], start=True, stop=True),
                    waits=wsp2, inc=pe)
                xe_free[s4] = (pe, pm2)
                rmk = rec.op("vector", lambda h, n=n, hh=hh: h.scalar_tensor_tensor(
                    out=Ssb[:, hh * DH:(hh + 1) * DH], in0=Ssb[:, hh * DH:(hh + 1) * DH],
                    scalar=dAe[:, n * NH + hh:n * NH + hh + 1], in1=psS2[:],
                    op0=ALU.mult, op1=ALU.add), waits=[(pe, pm2), seed2, bc_done], inc=cv)
                ps2_free = (cv, rmk)
                sbm = rec.op("vector", lambda h, hh=hh: h.tensor_copy(
                    out=Sbf[:, hh * DH:(hh + 1) * DH], in_=Ssb[:, hh * DH:(hh + 1) * DH]),
                    waits=[(cv, rmk)], inc=cv)
                srm[hh] = (cv, sbm)

    for n in range(NTT):
        do_chunk(n)
        xch2_last[n % 2] = (pe, pe.n)
    y_done = (cv, cv.n)
    if DBG_L1STOP <= 9:
        rec.barrier(ctx.chans)
        return
    rec.barrier(ctx.chans)
    yl_scope.close()
    xt_scope.close()
    scope3.close()

    zg_scope = ExitStack()
    zs = ctx.sb("m_zs", [128, NTT, DIN], BF16, scope=zg_scope)
    wib = ctx.sb("m_wib", [128, DIN], F32, scope=zg_scope)
    wirow = ctx.sb("m_wirow", [1, DIN], F32, scope=zg_scope)
    gst = ctx.sb("m_gst", [128, 40], F32, scope=zg_scope)
    gtmp = ctx.sb("m_gtmp", [128, 2, DIN], F32, scope=zg_scope)
    pst_z = proj_state(ctx, zg_scope, "m_z", nps=4, nk=NDC, cw=512, ksplit=2)
    psw = pst_z["pss"][0]

    def zevac(rec, o, tt, ps, pm):
        m = rec.op("scalar", lambda h, ps=ps, o=o, tt=tt: h.activation(
            out=zs[:, tt, o * 512:(o + 1) * 512], in_=ps, func=AF.Silu), waits=[(pe, pm)], inc=ca)
        return (ca, m)
    emit_proj(ctx, rec, xn3T, W["W_inT"], DIN, make_off_evac(zevac, 0), tag="m_zp0",
              ready=xn3_ready, pst=pst_z, consts=C, pe=pe, wld=wld, ksplit=2, ntt=4, t0=0)
    emit_proj(ctx, rec, xn3T, W["W_inT"], DIN, make_off_evac(zevac, 4), tag="m_zp1",
              ready=xn3_ready, pst=pst_z, consts=C, pe=pe, wld=wld, ksplit=2, ntt=4, t0=512)
    z_done = (ca, ca.n)
    wm0 = rec.op("sync", lambda h: h.dma_start(out=wirow[:], in_=W["inner_w_row"]), inc=cst, k=16)
    wbm = None
    for g6 in range(6):
        wbm = rec.op("tensor", lambda h, g6=g6: h.matmul(
            out=psw[:, 0:512], lhsT=C["onesf"][0:1, 0:128], rhs=wirow[0:1, g6 * 512:(g6 + 1) * 512],
            start=True, stop=True), waits=[(cst, wm0), z_done] if g6 == 0 else [(ca, wbm)], inc=pe)
        wbm = rec.op("scalar", lambda h, g6=g6: h.activation(
            out=wib[:, g6 * 512:(g6 + 1) * 512], in_=psw[:, 0:512], func=AF.Copy),
            waits=[(pe, wbm)], inc=ca)
    wib_done = (ca, wbm)
    gt_free = [None, None]
    ym_prev = 0
    for tt in range(NTT):
        sl = tt % 2
        g1 = None
        for g6 in range(6):
            g1 = rec.op("vector", lambda h, tt=tt, g6=g6: h.bn_stats(
                out=gst[:, g6 * 6:(g6 + 1) * 6], in_=Ytok[:, tt, g6 * 512:(g6 + 1) * 512]),
                waits=([y_done, (cv, ym_prev)] if g6 == 0 else [(cv, g1)]), inc=cv)
        g2 = rec.op("vector", lambda h: h.bn_aggr(out=gst[:, 36:38], in_=gst[:, 0:36]),
                    waits=[(cv, g1)], inc=cv)
        g3 = rec.op("vector", lambda h: h.tensor_scalar(out=gst[:, 38:39], in0=gst[:, 36:37],
                                                        scalar1=gst[:, 36:37], scalar2=None,
                                                        op0=ALU.mult), waits=[(cv, g2)], inc=cv)
        g4 = rec.op("vector", lambda h: h.tensor_tensor(out=gst[:, 38:39], in0=gst[:, 38:39],
                                                        in1=gst[:, 37:38], op=ALU.add),
                    waits=[(cv, g3)], inc=cv)
        g5 = rec.op("vector", lambda h: h.tensor_scalar(out=gst[:, 38:39], in0=gst[:, 38:39],
                                                        scalar1=1e-5, scalar2=None, op0=ALU.add),
                    waits=[(cv, g4)], inc=cv)
        g6m = rec.op("scalar", lambda h: h.activation(out=gst[:, 39:40], in_=gst[:, 38:39],
                                                      func=AF.Sqrt), waits=[(cv, g5)], inc=ca)
        g7 = rec.op("vector", lambda h: h.reciprocal(out=gst[:, 38:39], in_=gst[:, 39:40]),
                    waits=[(ca, g6m)], inc=cv)
        wsg = [(cv, g7)] + ([gt_free[sl]] if gt_free[sl] else [])
        g8 = rec.op("vector", lambda h, tt=tt, sl=sl: h.tensor_scalar(
            out=gtmp[:, sl, :], in0=Ytok[:, tt, :], scalar1=gst[:, 38:39], scalar2=None,
            op0=ALU.mult), waits=wsg, inc=cv)
        g9 = rec.op("vector", lambda h, sl=sl: h.tensor_tensor(
            out=gtmp[:, sl, :], in0=gtmp[:, sl, :], in1=wib[:], op=ALU.mult),
            waits=[(cv, g8), wib_done], inc=cv)
        g10 = rec.op("vector", lambda h, tt=tt, sl=sl: h.tensor_tensor(
            out=gtmp[:, sl, :], in0=gtmp[:, sl, :], in1=zs[:, tt, :], op=ALU.mult),
            waits=[(cv, g9), z_done], inc=cv)
        g11 = rec.op("vector", lambda h, tt=tt, sl=sl: h.tensor_copy(
            out=zs[:, tt, :], in_=gtmp[:, sl, :]), waits=[(cv, g10)], inc=cv)
        ym_prev = g11
        gt_free[sl] = (cv, g11)
        rec.op("sync", lambda h, tt=tt, sl=sl: h.dma_start(
            out=io["scr_big"][tt * 128:(tt + 1) * 128, 0:DIN], in_=zs[:, tt, :]),
            waits=[(cv, g11)], inc=cw, k=16)
    yg_st = cw.n
    if DBG_L1STOP <= 10:
        rec.barrier(ctx.chans)
        return
    rec.barrier(ctx.chans)
    zg_scope.close()
    yt_scope.close()
    scope2.close()

    fin_scope = ExitStack()
    ygT = ctx.sb("m_ygT", [128, NCC, TOK], BF16, scope=fin_scope)
    of32 = ctx.sb("m_of32", [128, NTT, DM], F32, scope=fin_scope)
    pst_o = proj_state(ctx, fin_scope, "m_o", nps=4, nk=NCC, cw=512, ksplit=2)
    for cc in range(NCC):
        rec.op("sync", lambda h, cc=cc: h.dma_start_transpose(
            out=ygT[:, cc, :], in_=io["scr_big"][0:TOK, cc * 128:(cc + 1) * 128]),
            waits=[(cw, yg_st)], inc=trm, k=16)
    ygT_ready = (trm, trm.n)

    def oevac(rec, o, tt, ps, pm):
        m = rec.op("scalar", lambda h, ps=ps, o=o, tt=tt: h.activation(
            out=of32[:, tt, o * 512:(o + 1) * 512], in_=ps, func=AF.Copy), waits=[(pe, pm)], inc=ca)
        return (ca, m)
    emit_proj(ctx, rec, ygT, W["out_projT"], DM, make_off_evac(oevac, 0), tag="m_op0",
              ready=ygT_ready, pst=pst_o, consts=C, pe=pe, wld=wld, ksplit=2, ntt=4, t0=0)
    emit_proj(ctx, rec, ygT, W["out_projT"], DM, make_off_evac(oevac, 4), tag="m_op1",
              ready=ygT_ready, pst=pst_o, consts=C, pe=pe, wld=wld, ksplit=2, ntt=4, t0=512)
    odone = (ca, ca.n)
    emit_ln_residual_store(ctx, rec, of32, io["src"], io["dst"], odone, tag="m_fin",
                           scope=fin_scope, pre_add_hbm=io["scr_u"])
    rec.barrier(ctx.chans)
    fin_scope.close()
    blk.close()



NSP = 20480
NJC = NSP // 128
NJRB = 10


def emit_l1_ffn(ctx, rec, io, W, C):
    nc = ctx.nc
    cv = ctx.chan("q_v")
    ca = ctx.chan("q_a")
    cw = ctx.chan("q_sw")
    pe = ctx.chan("q_pe")
    wld = ctx.chan("q_wl")
    cst = ctx.chan("q_cst")

    blk = ExitStack()
    of32 = ctx.sb("q_of32", [128, NTT, DM], F32, side="right", scope=blk)
    cwT = ctx.sb("q_cwT", [8, TOK], BF16, side="right", scope=blk)
    lgts = ctx.sb("q_lg", [128, NTT, 8], F32, side="right", scope=blk)
    lst = ctx.sb("q_lst", [128, 24], F32, side="right", scope=blk)
    scope2 = ExitStack()
    xn4T = ctx.sb("q_xnT", [128, NDC, TOK], BF16, side="right", scope=scope2)

    trc0, trm0 = emit_ln_phase(ctx, rec, io["src"], io["scr_a"], xn4T, kind="ln", tag="q_ln",
                               scope=scope2)
    ready = (trc0, trm0)
    for k in range(NDC):
        rec.op("sync", lambda h, k=k: h.dma_start(
            out=io["scr_m"][k * 128:(k + 1) * 128, 0:TOK], in_=xn4T[:, k, :]),
            waits=[ready], inc=cw, k=16)
    gsrc_ready = cw.n
    if DBG_L1FSTOP <= 1:
        rec.barrier(ctx.chans)
        return

    wp_scope = ExitStack()
    pst_w = proj_state(ctx, wp_scope, "q_w", nps=4, nk=NDC, cw=512)

    def wevac(rec, o, tt, ps, pm):
        m = rec.op("scalar", lambda h, ps=ps, o=o, tt=tt: h.activation(
            out=of32[:, tt, o * 512:(o + 1) * 512], in_=ps, func=AF.Copy), waits=[(pe, pm)], inc=ca)
        return (ca, m)
    emit_proj(ctx, rec, xn4T, W["WT"], DM, wevac, tag="q_wp", ready=ready, pst=pst_w,
              consts=C, pe=pe, wld=wld)
    w_done = (ca, ca.n)
    if DBG_L1FSTOP <= 2:
        rec.barrier(ctx.chans)
        return

    gwb = ctx.sb("q_gw", [128, NDC, 8], BF16, scope=wp_scope)
    wldg = ctx.chan("q_wlg")
    gm0 = rec.op("gpsimd", lambda h: h.dma_start(
        out=gwb[:], in_=W["gate_wT"].rearrange("(k p) n -> p k n", p=128)), inc=wldg, k=16)
    psl = pst_w["pss"][0]
    lg_ev = None
    for tt in range(NTT):
        mark = None
        for k in range(NDC):
            w0 = [(wldg, gm0), ready, w_done] + ([(cv, lg_ev)] if lg_ev else [])
            mark = rec.op("tensor", lambda h, k=k, tt=tt: h.matmul(
                out=psl[:, 0:8], lhsT=xn4T[:, k, tt * 128:(tt + 1) * 128], rhs=gwb[:, k, :],
                start=(k == 0), stop=(k == NDC - 1)),
                waits=w0 if k == 0 else (), inc=pe if k == NDC - 1 else None)
        lg_ev = rec.op("vector", lambda h, tt=tt: h.tensor_copy(out=lgts[:, tt, :], in_=psl[:, 0:8]),
                       waits=[(pe, mark)] + ([(pe, mark)] if lg_ev is None else [(cv, lg_ev)]), inc=cv)
    if DBG_L1FSTOP <= 3:
        rec.barrier(ctx.chans)
        return
    c1m = rec.op("vector", lambda h: h.tensor_scalar(
        out=lgts[:].rearrange("p a b -> p (a b)"), in0=lgts[:].rearrange("p a b -> p (a b)"),
        scalar1=50.0, scalar2=-50.0, op0=ALU.min, op1=ALU.max), waits=[(cv, lg_ev)], inc=cv)
    cwf = ctx.sb("q_cwf", [128, NTT, 8], F32, scope=wp_scope)
    mx8 = ctx.sb("q_mx8", [128, 8], F32, scope=wp_scope)
    cw_ev = None
    for tt in range(NTT):
        m1 = rec.op("vector", lambda h, tt=tt: h.max(out=mx8[:], in_=lgts[:, tt, :]),
                    waits=[(cv, c1m)] + ([(cv, cw_ev)] if cw_ev else []), inc=cv)
        m2 = rec.op("vector", lambda h: h.tensor_scalar(
            out=lst[:, 0:1], in0=mx8[:, 0:1], scalar1=mx8[:, 1:2], scalar2=None, op0=ALU.subtract),
            waits=[(cv, m1)], inc=cv)
        m3 = rec.op("scalar", lambda h: h.activation(out=lst[:, 1:2], in_=lst[:, 0:1],
                                                     func=AF.Sigmoid), waits=[(cv, m2)], inc=ca)
        m4 = rec.op("vector", lambda h: h.tensor_scalar(
            out=lst[:, 2:3], in0=lst[:, 1:2], scalar1=-1.0, scalar2=1.0, op0=ALU.mult, op1=ALU.add),
            waits=[(ca, m3)], inc=cv)
        m5 = rec.op("vector", lambda h, tt=tt: h.tensor_scalar(
            out=cwf[:, tt, :], in0=lgts[:, tt, :], scalar1=mx8[:, 0:1], scalar2=None,
            op0=ALU.is_equal), waits=[(cv, m4)], inc=cv)
        m6 = rec.op("vector", lambda h, tt=tt: h.tensor_scalar(
            out=cwf[:, tt, :], in0=cwf[:, tt, :], scalar1=lst[:, 1:2], scalar2=None, op0=ALU.mult),
            waits=[(cv, m5)], inc=cv)
        m7 = rec.op("vector", lambda h, tt=tt: h.tensor_scalar(
            out=mx8[:, 0:8], in0=lgts[:, tt, :], scalar1=mx8[:, 1:2], scalar2=None,
            op0=ALU.is_equal), waits=[(cv, m6)], inc=cv)
        m8 = rec.op("vector", lambda h, tt=tt: h.scalar_tensor_tensor(
            out=cwf[:, tt, :], in0=mx8[:, 0:8], scalar=lst[:, 2:3], in1=cwf[:, tt, :],
            op0=ALU.mult, op1=ALU.add), waits=[(cv, m7)], inc=cv)
        cw_ev = m8
    psq = pst_w["pss"][1]
    ct_ev = None
    pe_tr_prev = None
    for tt in range(NTT):
        wst = [(cv, cw_ev)] + ([ct_ev] if ct_ev else [])
        t1 = rec.op("tensor", lambda h, tt=tt: h.transpose(
            out=psq[0:8, 0:128], in_=cwf[:, tt, :], identity=C["eyef"][:]), waits=wst, inc=pe)
        t2 = rec.op("scalar", lambda h, tt=tt: h.activation(
            out=cwT[0:8, tt * 128:(tt + 1) * 128], in_=psq[0:8, 0:128], func=AF.Copy),
            waits=[(pe, t1)], inc=ca)
        ct_ev = (ca, t2)
    cwT_done = (ca, ca.n)
    if DBG_L1FSTOP <= 4:
        rec.barrier(ctx.chans)
        return

    wgt = ctx.sb("q_wgt", [8, NSP], BF16, side="right", scope=scope2)
    eawb = ctx.sb("q_eaw", [8, 64], F32, scope=wp_scope)
    eawt = ctx.sb("q_eawt", [8, 72], F32, scope=wp_scope)
    PTb = ctx.sb("q_PT", [64, 8], BF16, scope=wp_scope)
    em0 = rec.op("sync", lambda h: h.dma_start(out=eawb[:], in_=W["eaw"]), inc=cst, k=16)
    e1 = rec.op("vector", lambda h: h.tensor_reduce(out=eawt[0:8, 64:65], in_=eawb[0:8, :],
                                                    axis=mybir.AxisListType.X, op=ALU.max),
                waits=[(cst, em0)], inc=cv)
    e2 = rec.op("vector", lambda h: h.tensor_scalar(out=eawt[0:8, 0:64], in0=eawb[0:8, :],
                                                    scalar1=eawt[0:8, 64:65], scalar2=None,
                                                    op0=ALU.subtract), waits=[(cv, e1)], inc=cv)
    e3 = rec.op("scalar", lambda h: h.activation(out=eawt[0:8, 0:64], in_=eawt[0:8, 0:64],
                                                 func=AF.Exp), waits=[(cv, e2)], inc=ca)
    e4 = rec.op("vector", lambda h: h.tensor_reduce(out=eawt[0:8, 65:66], in_=eawt[0:8, 0:64],
                                                    axis=mybir.AxisListType.X, op=ALU.add),
                waits=[(ca, e3)], inc=cv)
    e5 = rec.op("vector", lambda h: h.reciprocal(out=eawt[0:8, 66:67], in_=eawt[0:8, 65:66]),
                waits=[(cv, e4)], inc=cv)
    e6 = rec.op("vector", lambda h: h.tensor_scalar(out=eawt[0:8, 0:64], in0=eawt[0:8, 0:64],
                                                    scalar1=eawt[0:8, 66:67], scalar2=None,
                                                    op0=ALU.mult), waits=[(cv, e5)], inc=cv)
    t3 = rec.op("tensor", lambda h: h.transpose(out=psq[0:64, 0:8], in_=eawt[0:8, 0:64],
                                                identity=C["eyef"][0:8, 0:8]),
                waits=[(cv, e6), cwT_done], inc=pe)
    t4 = rec.op("scalar", lambda h: h.activation(out=PTb[:], in_=psq[0:64, 0:8], func=AF.Copy),
                waits=[(pe, t3)], inc=ca)
    atb = ctx.sb("q_atb", [64, 2, 512], BF16, scope=wp_scope)
    imb = ctx.sb("q_imb", [8, 2, 512], F32, scope=wp_scope)
    sgb = ctx.sb("q_sgb", [8, 2, 512], F32, scope=wp_scope)
    psd = pst_w["pss"][2]
    at_free = [None, None]
    im_free = [None, None]
    wg_prev = None
    for jc in range(NSP // 512):
        sl = jc % 2
        wa = [(ca, t4)] + ([at_free[sl]] if at_free[sl] else [])
        am = rec.op("scalar", lambda h, jc=jc, sl=sl: h.dma_start(
            out=atb[:, sl, :], in_=W["atoms_p"][:, jc * 512:(jc + 1) * 512]),
            waits=wa + ctx.wwait(W["atoms_p"]), inc=wld, k=16)
        im = rec.op("sync", lambda h, jc=jc, sl=sl: h.dma_start(
            out=imb[0:8, sl, :], in_=W["imp_p"][:, jc * 512:(jc + 1) * 512]),
            waits=[im_free[sl]] if im_free[sl] else [], inc=cst, k=16)
        dm = rec.op("tensor", lambda h, sl=sl: h.matmul(
            out=psd[0:8, 0:512], lhsT=PTb[:], rhs=atb[:, sl, :], start=True, stop=True),
            waits=[(wld, am), (ca, t4)] + ([wg_prev] if wg_prev else []), inc=pe)
        sg = rec.op("scalar", lambda h, sl=sl: h.activation(
            out=sgb[0:8, sl, :], in_=imb[0:8, sl, :], func=AF.Sigmoid), waits=[(cst, im)], inc=ca)
        wg = rec.op("vector", lambda h, jc=jc, sl=sl: h.scalar_tensor_tensor(
            out=wgt[0:8, jc * 512:(jc + 1) * 512], in0=psd[0:8, 0:512], scalar=1.0,
            in1=sgb[0:8, sl, :], op0=ALU.mult, op1=ALU.mult), waits=[(pe, dm), (ca, sg)], inc=cv)
        at_free[sl] = (pe, dm)
        im_free[sl] = (ca, sg)
        wg_prev = (cv, wg)
    wgt_done = (cv, cv.n)
    rec.barrier(ctx.chans)
    wp_scope.close()
    if DBG_L1FSTOP <= 5:
        rec.barrier(ctx.chans)
        return

    sc_scope = ExitStack()
    gidx = ctx.sb("q_gidx", [128, NJC], I32, scope=sc_scope)
    Rb = ctx.sb("q_Rb", [128, 2, NJRB, 128], BF16, scope=sc_scope)
    xg = ctx.sb("q_xg", [128, 4, TOK], BF16, scope=sc_scope)
    Gb = ctx.sb("q_Gb", [128, 4, TOK], BF16, scope=sc_scope)
    psV = [ctx.ps("q_psV", [128, 512], F32, scope=sc_scope) for _ in range(2)]
    psO = [ctx.ps("q_psO", [128, 512], F32, scope=sc_scope) for _ in range(4)]
    gi0 = rec.op("sync", lambda h: h.dma_start(out=gidx[:], in_=W["gidx"]), inc=cst, k=16)
    rb_slab_free = [None, None]
    xg_free = [None] * 4
    G_free = [None] * 4
    psV_free = [None, None]
    psO_ev = [None] * 4
    sc_done = []
    for rb in range(16):
        rsl = rb % 2
        wrb = [rb_slab_free[rsl]] if rb_slab_free[rsl] else []
        rb0 = rec.op("gpsimd", lambda h, rb=rb, rsl=rsl: h.dma_start(
            out=Rb[:, rsl, :, :],
            in_=W["Rb"][rb * NJRB * 128:(rb + 1) * NJRB * 128, :].rearrange("(j p) r -> p j r", p=128)),
            waits=wrb, inc=wldg, k=16)
        pmO = [None] * 8
        for jj in range(NJRB):
            j = rb * NJRB + jj
            sl = j % 4
            wgx = [(cst, gi0), (cw, gsrc_ready)] + ([xg_free[sl]] if xg_free[sl] else [])
            gx = rec.op("gpsimd", lambda h, j=j, sl=sl: h.indirect_dma_start(
                out=xg[:, sl, :], out_offset=None, in_=io["scr_m"][0:DM, 0:TOK],
                in_offset=bass.IndirectOffsetOnAxis(ap=gidx[:, j:j + 1], axis=0)),
                waits=wgx, inc=wldg, k=16)
            for ts in range(2):
                pv2 = j % 2
                wv = [wgt_done, cwT_done] + ([psV_free[pv2]] if psV_free[pv2] else [])
                vm = rec.op("tensor", lambda h, j=j, ts=ts, pv2=pv2: h.matmul(
                    out=psV[pv2][:], lhsT=wgt[0:8, j * 128:(j + 1) * 128],
                    rhs=cwT[0:8, ts * 512:(ts + 1) * 512], start=True, stop=True),
                    waits=wv, inc=pe)
                gm = rec.op("vector", lambda h, j=j, ts=ts, sl=sl, pv2=pv2: h.tensor_tensor(
                    out=Gb[:, sl, ts * 512:(ts + 1) * 512], in0=xg[:, sl, ts * 512:(ts + 1) * 512],
                    in1=psV[pv2][:], op=ALU.mult), waits=[(wldg, gx), (pe, vm)] +
                    ([G_free[sl]] if G_free[sl] and ts == 0 else []), inc=cv)
                psV_free[pv2] = (cv, gm)
                for t4i in range(4):
                    if DBG_NOSCAT:
                        continue
                    tt = ts * 4 + t4i
                    first = (jj == 0)
                    last = (jj == NJRB - 1)
                    wsc = [(cv, gm), (wldg, rb0)]
                    if first and ts == 0 and psO_ev[t4i] is not None:
                        wsc.append(psO_ev[t4i])
                    needs_inc = last or (ts == 1 and t4i == 3)
                    pmO[tt] = rec.op("tensor", lambda h, j=j, tt=tt, t4i=t4i, ts=ts, sl=sl,
                                     first=first, last=last: h.matmul(
                        out=psO[t4i][:, ts * 128:ts * 128 + 128],
                        lhsT=Gb[:, sl, tt * 128:(tt + 1) * 128], rhs=Rb[:, rsl, jj, :],
                        start=(first and ts == 0), stop=last),
                        waits=wsc, inc=pe if needs_inc else None)
            G_free[sl] = (pe, pmO[7]) if not DBG_NOSCAT else (cv, gm)
            xg_free[sl] = (cv, gm)
        for tt in range(8):
            if DBG_NOSCAT or DBG_NOEVAC:
                continue
            bank = tt % 4
            ts = tt // 4
            evm = rec.op("vector", lambda h, tt=tt, bank=bank, ts=ts, rb=rb: h.tensor_tensor(
                out=of32[:, tt, rb * 128:(rb + 1) * 128], in0=of32[:, tt, rb * 128:(rb + 1) * 128],
                in1=psO[bank][:, ts * 128:ts * 128 + 128], op=ALU.add),
                waits=[(pe, pmO[4 + bank]), w_done], inc=cv)
            psO_ev[bank] = (cv, evm)
            sc_done.append(evm)
        rb_slab_free[rsl] = (pe, pmO[7])
    scat_done = (cv, cv.n)
    if DBG_L1FSTOP <= 6:
        rec.barrier(ctx.chans)
        return
    rec.barrier(ctx.chans)
    sc_scope.close()
    scope2.close()

    emit_ln_residual_store(ctx, rec, of32, io["src"], io["dst"], scat_done, tag="q_fin", scope=blk)
    rec.barrier(ctx.chans)
    blk.close()



L0A_WEIGHTS = {
    "Wq": [DM, DM], "Wk": [DM, DM], "Wv": [DM, DM], "Wo": [DM, DM],
    "l0_biases": [1, 4 * DM],
}
L0F_WEIGHTS = {
    "w_gate": [DM, 8192], "w_up": [DM, 8192], "w_down": [8192, DM],
}
L1F_WEIGHTS_BASE = {
    "WT": [DM, DM], "gate_wT": [DM, 8], "eaw": [8, 64],
}
L1A_WEIGHTS = {
    "W_inT": [DM, 6448], "out_projT": [3072, DM], "conv_w": [3072, 4],
    "conv_b_c": [128, 24], "dt_bias_r": [48, 1], "A_log_c": [48, 1], "D_row": [1, 48],
    "inner_w_row": [1, 3072], "norm_w_c": [128, 16], "conv_state": [3072, 3],
    "ssm0_t": [128, 3072], "c_hsel": [48, 48 * 128], "c_T0": [128, 128], "c_eyef": [128, 128],
}


def build_program(blocks, dbg=False):
    nc = bass.Bass("TRN2", target_bir_lowering=False)
    es = ExitStack()
    xin = nc.dram_tensor("xin", [TOK, DM], F32, kind="ExternalInput").ap()
    xout = nc.dram_tensor("xout", [TOK, DM], F32, kind="ExternalOutput").ap()
    scr_a = nc.dram_tensor("scr_a", [TOK, DM], BF16, kind="Internal").ap()
    scr_b = nc.dram_tensor("scr_b", [TOK, DM], BF16, kind="Internal").ap()
    xmid = [nc.dram_tensor(f"xmid{i}", [TOK, DM], F32, kind="Internal").ap()
            for i in range(len(blocks) - 1)]
    sh_l0 = nc.dram_tensor("sh_l0", [128, 16 * 129], BF16, addr_space="Shared", kind="Internal").ap()
    sh_l0f = nc.dram_tensor("sh_l0f", [1, 4], I32, addr_space="Shared", kind="Internal").ap()

    scr_big = nc.dram_tensor("scr_big", [TOK, 8192], BF16, kind="Internal").ap()
    scr_m = nc.dram_tensor("scr_m", [3328, TOK], BF16, kind="Internal").ap()
    scr_u = nc.dram_tensor("scr_u", [TOK, DM], F32, kind="Internal").ap()
    scr_small = nc.dram_tensor("scr_small", [64, 64], F32, kind="Internal").ap()
    sh_l1t = nc.dram_tensor("sh_l1t", [128, 128], BF16, addr_space="Shared", kind="Internal").ap()
    sh_l1tf = nc.dram_tensor("sh_l1tf", [1, 4], I32, addr_space="Shared", kind="Internal").ap()
    sh_l1s = nc.dram_tensor("sh_l1s", [128, 3072], BF16, addr_space="Shared", kind="Internal").ap()
    sh_l1sf = nc.dram_tensor("sh_l1sf", [1, 4], I32, addr_space="Shared", kind="Internal").ap()
    wext = {}
    need = {}
    if "l0a" in blocks:
        need.update(L0A_WEIGHTS)
    if "l0f" in blocks:
        need.update(L0F_WEIGHTS)
    if "l1a" in blocks:
        need.update(L1A_WEIGHTS)
    if "l1f" in blocks:
        need.update(L1F_WEIGHTS_BASE)
        need["atoms_p"] = [64, NSP]
        need["imp_p"] = [8, NSP]
        need["gidx"] = [128, NSP // 128]
    for name, shape in need.items():
        dt = F32
        if name == "gidx":
            dt = I32
        wext[name] = nc.dram_tensor(name, shape, dt, kind="ExternalInput").ap()
    if "l1f" in blocks:
        wext["Rb"] = nc.dram_tensor("Rb", [NSP, 128], BF16, kind="ExternalInput").ap()
    CAST_NAMES = [n for n in ("Wk", "Wv", "Wq", "Wo", "w_gate", "w_up", "w_down",
                              "W_inT", "out_projT", "WT", "atoms_p") if n in wext]
    wtwin = {}
    for n in CAST_NAMES:
        wtwin[n] = nc.dram_tensor(n + "_bf", need[n], BF16, kind="Internal").ap()
    c_par = nc.dram_tensor("c_par", [1, 4], I32, kind="ExternalInput").ap()
    c_nonce = nc.dram_tensor("c_nonce", [1, 4], I32, kind="ExternalInput").ap()
    c_ones = nc.dram_tensor("c_ones", [1, 128], F32, kind="ExternalInput").ap()
    c_onescol = nc.dram_tensor("c_onescol", [128, 1], F32, kind="ExternalInput").ap()
    c_triu = nc.dram_tensor("c_triu", [128, 128], F32, kind="ExternalInput").ap()
    c_eye = nc.dram_tensor("c_eye", [128, 128], F32, kind="ExternalInput").ap()

    with nc.Block() as block:
        ctx = Ctx(nc, es, block)
        rec = Rec(ctx)
        C = {
            "par_sb": ctx.sb("c_par", [1, 4], I32),
            "nonce_sb": ctx.sb("c_nonce", [1, 4], I32),
            "flag_sb": ctx.sb("c_flag", [1, 4], I32),
            "ones1": ctx.sb("c_ones1", [1, 128], BF16),
            "onescol": ctx.sb("c_onescol", [128, 1], BF16),
            "triU": ctx.sb("c_triu", [128, 128], F32),
            "onesf": ctx.sb("c_onesf", [1, 128], F32),
            "eyef": ctx.sb("c_eyeg", [128, 128], F32),
        }
        C["nonce_i_sb"] = C["nonce_sb"]
        cst = ctx.chan("cst")
        rec.op("sync", lambda h: h.dma_start(out=C["par_sb"][:], in_=c_par[:]), inc=cst, k=16)
        rec.op("sync", lambda h: h.dma_start(out=C["nonce_sb"][:], in_=c_nonce[:]), inc=cst, k=16)
        gq = ctx.chan("cstg")
        rec.op("gpsimd", lambda h: h.dma_start(out=C["ones1"][:], in_=c_ones[:]), inc=gq, k=16)
        rec.op("gpsimd", lambda h: h.dma_start(out=C["onescol"][:], in_=c_onescol[:]), inc=gq, k=16)
        rec.op("sync", lambda h: h.dma_start(out=C["triU"][:], in_=c_triu[:]), inc=cst, k=16)
        rec.op("sync", lambda h: h.dma_start(out=C["onesf"][:], in_=c_ones[:]), inc=cst, k=16)
        rec.op("sync", lambda h: h.dma_start(out=C["eyef"][:], in_=c_eye[:]), inc=cst, k=16)
        rec.barrier([cst, ctx.chan("cstg")])
        wcst = ctx.chan("wcst")
        ctx.apmark = {}
        for n in CAST_NAMES:
            mm_ = rec.op("gpsimd", lambda h, dst=wtwin[n], srcp=wext[n]: h.dma_start(
                out=dst[:], in_=srcp[:]), inc=wcst, k=16)
            ctx.apmark[id(wtwin[n])] = (wcst, mm_)
            wext[n] = wtwin[n]

        cur = xin
        for j, b in enumerate(blocks):
            dst = xout if j == len(blocks) - 1 else xmid[j]
            io = {"src": cur, "dst": dst, "scr_a": scr_a, "scr_b": scr_b, "scr_big": scr_big,
                  "scr_m": scr_m, "scr_u": scr_u, "scr_small": scr_small,
                  "sh_l0": sh_l0, "sh_l0f": sh_l0f, "sh_l1t": sh_l1t, "sh_l1tf": sh_l1tf,
                  "sh_l1s": sh_l1s, "sh_l1sf": sh_l1sf}
            if b == "l0a":
                emit_l0_attn(ctx, rec, io, wext, C)
            elif b == "l0f":
                emit_l0_ffn(ctx, rec, io, wext, C)
            elif b == "l1a":
                emit_l1_attn(ctx, rec, io, wext, C)
            elif b == "l1f":
                emit_l1_ffn(ctx, rec, io, wext, C)
            else:
                raise NotImplementedError(b)
            cur = dst

        if dbg:
            dbg_hs = nc.dram_tensor("dbg_hs", [TOK, 8192], BF16, kind="ExternalOutput").ap()
            dbg_xn = nc.dram_tensor("dbg_xn", [TOK, DM], BF16, kind="ExternalOutput").ap()
            dch = ctx.chan("dbg")
            rec.op("sync", lambda h: h.dma_start(out=dbg_hs[:], in_=scr_big[:]), inc=dch, k=16)
            rec.op("sync", lambda h: h.dma_start(out=dbg_xn[:], in_=scr_a[:]), inc=dch, k=16)
            if DBG_XNT is not None:
                dbg_xnT = nc.dram_tensor("dbg_xnT", [128, NDC * TOK], BF16, kind="ExternalOutput").ap()
                rec.op("sync", lambda h: h.dma_start(out=dbg_xnT[:],
                                                     in_=DBG_XNT[0][:].rearrange("p a b -> p (a b)")),
                       inc=dch, k=16)
        rec.barrier(ctx.chans)
        rec.flush()
    return nc


def host_consts():
    tri = np.triu(np.ones((128, 128), np.float32))
    return {
        "c_ones": np.ones((1, 128), np.float32),
        "c_onescol": np.ones((128, 1), np.float32),
        "c_triu": tri,
        "c_eye": np.eye(128, dtype=np.float32),
    }


def run_blocks(blocks, inputs, x_full, dbg=False):
    consts = host_consts()
    nonce = np.random.randint(1, 2**31 - 1, size=(1, 4)).astype(np.int32)
    in_maps = []
    need = {}
    if "l0f" in blocks:
        need["w_gate"] = np.ascontiguousarray(inputs["L0_ffn_w_gate"])
        need["w_up"] = np.ascontiguousarray(inputs["L0_ffn_w_up"])
        need["w_down"] = np.ascontiguousarray(inputs["L0_ffn_w_down"])
    if "l1f" in blocks:
        import ml_dtypes as _md
        mi_ = np.asarray(inputs["L1_ffn_mask_idx"]).astype(np.int64)
        rows_ = mi_ // DM
        cols_ = mi_ % DM
        global NSP, NJC, NJRB
        cnt = np.bincount(rows_ // 128, minlength=16)
        NJRB = max(10, int(np.ceil(cnt.max() / 128.0)))
        NSP = 16 * NJRB * 128
        NJC = NSP // 128
        gidx_ = np.zeros((16, NJRB * 128), np.int32)
        Rbm_ = np.zeros((16, NJRB * 128, 128), np.float32)
        srcx = np.full((16, NJRB * 128), -1, np.int64)
        for rb in range(16):
            sel = np.nonzero((rows_ >= rb * 128) & (rows_ < (rb + 1) * 128))[0]
            srcx[rb, :len(sel)] = sel
            gidx_[rb, :len(sel)] = cols_[sel]
            Rbm_[rb, np.arange(len(sel)), rows_[sel] - rb * 128] = 1.0
        flat = srcx.reshape(-1)
        valid = flat >= 0
        atoms_p = np.zeros((64, NSP), np.float32)
        atoms_p[:, valid] = np.asarray(inputs["L1_ffn_atoms"], np.float32)[:, flat[valid]]
        imp_p = np.zeros((8, NSP), np.float32)
        imp_p[:, valid] = np.asarray(inputs["L1_ffn_importance"], np.float32)[:, flat[valid]]
        need["atoms_p"] = atoms_p
        need["imp_p"] = imp_p
        need["gidx"] = np.ascontiguousarray(gidx_.reshape(NJC, 128).T)
        need["Rb"] = Rbm_.reshape(NSP, 128).astype(_md.bfloat16)
        need["WT"] = np.ascontiguousarray(np.asarray(inputs["L1_ffn_W"], np.float32).T)
        need["gate_wT"] = np.ascontiguousarray(np.asarray(inputs["L1_ffn_gate_w"], np.float32).T)
        need["eaw"] = np.ascontiguousarray(inputs["L1_ffn_expert_atom_weights"], np.float32)
    if "l1a" in blocks:
        NH = 48
        need["W_inT"] = np.ascontiguousarray(inputs["L1_attn_in_proj_w"].T.astype(np.float32))
        need["out_projT"] = np.ascontiguousarray(inputs["L1_attn_out_proj_w"].T.astype(np.float32))
        need["conv_w"] = np.ascontiguousarray(inputs["L1_attn_conv_w"], np.float32)
        need["conv_b_c"] = np.ascontiguousarray(inputs["L1_attn_conv_b"].reshape(24, 128).T, np.float32)
        need["dt_bias_r"] = np.ascontiguousarray(inputs["L1_attn_dt_bias"].reshape(48, 1), np.float32)
        need["A_log_c"] = np.ascontiguousarray(inputs["L1_attn_A_log"].reshape(48, 1), np.float32)
        need["D_row"] = np.ascontiguousarray(inputs["L1_attn_D_param"].reshape(1, 48), np.float32)
        need["inner_w_row"] = np.ascontiguousarray(inputs["L1_attn_inner_norm_w"].reshape(1, 3072), np.float32)
        need["norm_w_c"] = np.ascontiguousarray(inputs["L1_attn_norm_w"].reshape(16, 128).T, np.float32)
        need["conv_state"] = np.ascontiguousarray(inputs["L1_attn_conv_state"][0], np.float32)
        need["ssm0_t"] = np.ascontiguousarray(
            np.transpose(np.asarray(inputs["L1_attn_ssm_state"], np.float32)[0], (1, 0, 2)).reshape(128, 3072))
        hsel = np.zeros((48, 48 * 128), np.float32)
        for hh in range(NH):
            hsel[hh, hh * 128:(hh + 1) * 128] = 1.0
        need["c_hsel"] = hsel
        tri = np.triu(np.ones((128, 128), np.float32))
        need["c_T0"] = np.where(tri > 0, 0.0, -1e9).astype(np.float32)
        need["c_eyef"] = np.eye(128, dtype=np.float32)
    if "l0a" in blocks:
        need["Wq"] = np.ascontiguousarray(inputs["L0_attn_Wq"])
        need["Wk"] = np.ascontiguousarray(inputs["L0_attn_Wk"])
        need["Wv"] = np.ascontiguousarray(inputs["L0_attn_Wv"])
        need["Wo"] = np.ascontiguousarray(inputs["L0_attn_Wo"])
        need["l0_biases"] = np.concatenate([
            inputs["L0_attn_bq"], inputs["L0_attn_bk"], inputs["L0_attn_bv"],
            inputs["L0_attn_bo"]]).reshape(1, 4 * DM).astype(np.float32)
    nc = build_program(blocks, dbg=dbg)
    for c in range(NCORE):
        b, hh = c // 2, c % 2
        m = dict(need)
        m["xin"] = np.ascontiguousarray(x_full[b, hh * TOK:(hh + 1) * TOK, :], np.float32)
        m["c_par"] = np.full((1, 4), hh, np.int32)
        m["c_nonce"] = nonce
        m.update(consts)
        in_maps.append(m)
    res = run_bass_kernel_spmd(nc, in_maps, core_ids=list(range(NCORE)))
    out = np.empty((4, 2048, DM), np.float32)
    for c in range(NCORE):
        b, hh = c // 2, c % 2
        out[b, hh * TOK:(hh + 1) * TOK, :] = res.results[c]["xout"]
    if dbg:
        return out, res
    return out


def kernel(**inputs):
    x = np.asarray(inputs["L0_attn_x"], np.float32)
    return run_blocks(["l0a", "l0f", "l1a", "l1f"], inputs, x)


# version 07
# speedup vs baseline: 1.0360x; 1.0360x over previous
import os
import numpy as np
import ml_dtypes
from contextlib import ExitStack

DBG_NOMASK = os.environ.get("DBG_NOMASK", "0") == "1"
DBG_L1STOP = int(os.environ.get("DBG_L1STOP", "99"))
DBG_L1FSTOP = int(os.environ.get("DBG_L1FSTOP", "99"))
DBG_NOSCAT = os.environ.get("DBG_NOSCAT", "0") == "1"
DBG_NOEVAC = os.environ.get("DBG_NOEVAC", "0") == "1"
DBG_XNT = None

import concourse.bass as bass
import concourse.mybir as mybir
from concourse.bass_utils import run_bass_kernel_spmd

F32 = mybir.dt.float32
BF16 = mybir.dt.bfloat16
I32 = mybir.dt.int32
AF = mybir.ActivationFunctionType
ALU = mybir.AluOpType
BT = ml_dtypes.bfloat16

NCORE = 8
TOK = 1024
DM = 2048
NTT = TOK // 128
NDC = DM // 128
MAGIC = 1234567891
EPS_LN = 1e-5

ENGS = ("sync", "vector", "scalar", "tensor", "gpsimd")


def ceil_div(a, b):
    return (a + b - 1) // b


class Ch:

    def __init__(self, es, nc, name):
        self.sem = es.enter_context(nc.semaphore(name))
        self.n = 0
        self._waited = {}

    def reserve(self, k):
        self.n += k
        return self.n

    def wait_emit(self, eng, v):
        if v <= 0:
            return
        key = id(eng)
        if self._waited.get(key, -1) >= v:
            return
        eng.wait_ge(self.sem, v)
        self._waited[key] = v


class Rec:

    def __init__(self, ctx):
        self.ctx = ctx
        self.q = {e: [] for e in ENGS}

    def op(self, eng, fn, waits=(), inc=None, k=1):
        ws = [(c, v) for (c, v) in waits]
        mark = None
        sem = None
        if inc is not None:
            mark = inc.reserve(k)
            sem = inc.sem
        kk = k

        def run(h):
            for c, v in ws:
                c.wait_emit(h, v)
            inst = fn(h)
            if sem is not None:
                inst.then_inc(sem, kk)
        self.q[eng].append(run)
        return mark

    def raw(self, eng, fn):
        self.q[eng].append(fn)

    def barrier(self, chans):
        ws = [(c, c.n) for c in chans]
        for e in ENGS:
            def run(h, ws=ws):
                for c, v in ws:
                    c.wait_emit(h, v)
            self.q[e].append(run)

    def flush(self):
        b = self.ctx.block
        for name in ENGS:
            fns = self.q[name]
            if fns:
                getattr(b, name)(lambda h, fns=fns: [f(h) for f in fns])
        self.q = {e: [] for e in ENGS}


class Ctx:
    def __init__(self, nc, es, block):
        self.nc = nc
        self.es = es
        self.block = block
        self._chans = {}
        self._uid = 0

    def chan(self, name):
        if name not in self._chans:
            self._chans[name] = Ch(self.es, self.nc, name)
        return self._chans[name]

    @property
    def chans(self):
        return list(self._chans.values())

    def uid(self):
        self._uid += 1
        return self._uid

    def sb(self, name, shape, dtype, scope=None, side=None):
        s = scope if scope is not None else self.es
        return s.enter_context(self.nc.sbuf_tensor(f"{name}_{self.uid()}", shape, dtype, side=side))

    def ps(self, name, shape, dtype=F32, scope=None):
        s = scope if scope is not None else self.es
        return s.enter_context(self.nc.psum_tensor(f"{name}_{self.uid()}", shape, dtype))

    def wwait(self, ap):
        m = getattr(self, "apmark", {}).get(id(ap))
        return [m] if m else []



def w_chunk_ap(w_hbm, o0, o1):
    return w_hbm[:, o0:o1].rearrange("(k p) n -> p k n", p=128)


def emit_ln_phase(ctx, rec, src_x, scr_hbm, xnT, *, kind="ln", eps=EPS_LN,
                  u_keep=None, tag="ln", scope=None, u_hbm=None, rms2=False):
    nc = ctx.nc
    ld = ctx.chan(tag + "_ld")
    cv = ctx.chan(tag + "_v")
    ca = ctx.chan(tag + "_a")
    cw = ctx.chan(tag + "_w")
    tr = ctx.chan(tag + "_t")
    xt = ctx.sb("lnx", [128, 2, DM], F32, scope=scope)
    xb = ctx.sb("lnxb", [128, 2, DM], BF16, scope=scope)
    st = ctx.sb("lnst", [128, 32], F32, scope=scope)
    ust = ctx.sb("lnu", [128, 2, DM], F32, scope=scope) if rms2 else None
    cu = ctx.chan(tag + "_u")
    free_marks = [0] * NTT
    prev_tail = 0
    for i in range(NTT):
        sl = i % 2
        lm = rec.op("sync", lambda h, i=i, sl=sl: h.dma_start(out=xt[:, sl, :], in_=src_x[i * 128:(i + 1) * 128, :]),
                    waits=[(cv, free_marks[i - 2])] if i >= 2 else [], inc=ld, k=16)
        m1 = None
        for g in range(4):
            m1 = rec.op("vector", lambda h, sl=sl, g=g: h.bn_stats(out=st[:, g * 6:(g + 1) * 6],
                                                                   in_=xt[:, sl, g * 512:(g + 1) * 512]),
                        waits=[(ld, lm), (cv, prev_tail)] if g == 0 else [(cv, m1)], inc=cv)
        m2 = rec.op("vector", lambda h: h.bn_aggr(out=st[:, 24:26], in_=st[:, 0:24]),
                    waits=[(cv, m1)], inc=cv)
        if kind == "rms":
            m3 = rec.op("vector", lambda h: h.tensor_scalar(out=st[:, 26:27], in0=st[:, 24:25], scalar1=st[:, 24:25],
                                                            scalar2=None, op0=ALU.mult), waits=[(cv, m2)], inc=cv)
            m4 = rec.op("vector", lambda h: h.tensor_tensor(out=st[:, 27:28], in0=st[:, 26:27], in1=st[:, 25:26],
                                                            op=ALU.add), waits=[(cv, m3)], inc=cv)
            vcol = st[:, 27:28]
            mzero = True
        else:
            m4 = m2
            vcol = st[:, 25:26]
            mzero = False
        m5 = rec.op("vector", lambda h, vcol=vcol: h.tensor_scalar(out=st[:, 28:29], in0=vcol, scalar1=float(eps),
                                                                   scalar2=None, op0=ALU.add), waits=[(cv, m4)], inc=cv)
        m6 = rec.op("scalar", lambda h: h.activation(out=st[:, 29:30], in_=st[:, 28:29], func=AF.Sqrt),
                    waits=[(cv, m5)], inc=ca)
        m7 = rec.op("vector", lambda h: h.reciprocal(out=st[:, 28:29], in_=st[:, 29:30]),
                    waits=[(ca, m6)], inc=cv)
        if rms2:
            m8u = rec.op("vector", lambda h, sl=sl: h.tensor_scalar(out=ust[:, sl, :], in0=xt[:, sl, :],
                                                                    scalar1=st[:, 24:25], scalar2=st[:, 28:29],
                                                                    op0=ALU.subtract, op1=ALU.mult),
                         waits=[(cv, m7)], inc=cv)
            rec.op("sync", lambda h, i=i, sl=sl: h.dma_start(out=u_hbm[i * 128:(i + 1) * 128, :],
                                                             in_=ust[:, sl, :]),
                   waits=[(cv, m8u)], inc=cu, k=16)
            m1b = None
            for g in range(4):
                m1b = rec.op("vector", lambda h, sl=sl, g=g: h.bn_stats(out=st[:, g * 6:(g + 1) * 6],
                                                                        in_=ust[:, sl, g * 512:(g + 1) * 512]),
                             waits=[(cv, m8u)] if g == 0 else [(cv, m1b)], inc=cv)
            m2b = rec.op("vector", lambda h: h.bn_aggr(out=st[:, 24:26], in_=st[:, 0:24]),
                         waits=[(cv, m1b)], inc=cv)
            m3b = rec.op("vector", lambda h: h.tensor_scalar(out=st[:, 26:27], in0=st[:, 24:25], scalar1=st[:, 24:25],
                                                             scalar2=None, op0=ALU.mult), waits=[(cv, m2b)], inc=cv)
            m4b = rec.op("vector", lambda h: h.tensor_tensor(out=st[:, 27:28], in0=st[:, 26:27], in1=st[:, 25:26],
                                                             op=ALU.add), waits=[(cv, m3b)], inc=cv)
            m5b = rec.op("vector", lambda h: h.tensor_scalar(out=st[:, 28:29], in0=st[:, 27:28], scalar1=float(eps),
                                                             scalar2=None, op0=ALU.add), waits=[(cv, m4b)], inc=cv)
            m6b = rec.op("scalar", lambda h: h.activation(out=st[:, 29:30], in_=st[:, 28:29], func=AF.Sqrt),
                         waits=[(cv, m5b)], inc=ca)
            m7b = rec.op("vector", lambda h: h.reciprocal(out=st[:, 28:29], in_=st[:, 29:30]),
                         waits=[(ca, m6b)], inc=cv)
            m8 = rec.op("vector", lambda h, sl=sl: h.tensor_scalar(out=xb[:, sl, :], in0=ust[:, sl, :],
                                                                   scalar1=st[:, 28:29], scalar2=None, op0=ALU.mult),
                        waits=[(cv, m7b)], inc=cv)
        elif mzero:
            m8 = rec.op("vector", lambda h, sl=sl: h.tensor_scalar(out=xb[:, sl, :], in0=xt[:, sl, :],
                                                                   scalar1=st[:, 28:29], scalar2=None, op0=ALU.mult),
                        waits=[(cv, m7)], inc=cv)
        else:
            m8 = rec.op("vector", lambda h, sl=sl: h.tensor_scalar(out=xb[:, sl, :], in0=xt[:, sl, :],
                                                                   scalar1=st[:, 24:25], scalar2=st[:, 28:29],
                                                                   op0=ALU.subtract, op1=ALU.mult),
                        waits=[(cv, m7)], inc=cv)
        if u_keep is not None:
            if mzero:
                m8b = rec.op("vector", lambda h, sl=sl, i=i: h.tensor_scalar(out=u_keep[:, i, :], in0=xt[:, sl, :],
                                                                             scalar1=st[:, 28:29], scalar2=None,
                                                                             op0=ALU.mult), waits=[(cv, m7)], inc=cv)
            else:
                m8b = rec.op("vector", lambda h, sl=sl, i=i: h.tensor_scalar(out=u_keep[:, i, :], in0=xt[:, sl, :],
                                                                             scalar1=st[:, 24:25], scalar2=st[:, 28:29],
                                                                             op0=ALU.subtract, op1=ALU.mult),
                             waits=[(cv, m7)], inc=cv)
            m8 = m8b
        free_marks[i] = m8
        prev_tail = m8
        rec.op("sync", lambda h, i=i, sl=sl: h.dma_start(out=scr_hbm[i * 128:(i + 1) * 128, :], in_=xb[:, sl, :]),
               waits=[(cv, m8)], inc=cw, k=16)
    wall = cw.n
    tm = []
    for k in range(NDC):
        tm.append(rec.op("sync", lambda h, k=k: h.dma_start_transpose(out=xnT[:, k, :],
                                                                      in_=scr_hbm[:, k * 128:(k + 1) * 128]),
                         waits=[(cw, wall)], inc=tr, k=16))
    return tr, tm[-1]



def make_off_evac(evac, tt0):
    def e2(rec, o, tt, ps, pm):
        return evac(rec, o, tt + tt0, ps, pm)
    return e2


def mm_group(rec, pe_chan, mms, waits=()):
    n = len(mms)
    mark = None
    for i, fn in enumerate(mms):
        first, last = i == 0, i == n - 1
        mark = rec.op("tensor", (lambda h, fn=fn, first=first, last=last: fn(h, first, last)),
                      waits=waits if first else (), inc=pe_chan if last else None)
    return mark


def proj_state(ctx, scope, tag, nps=4, nk=NDC, cw=512, ksplit=1):
    return {
        "wbuf": ctx.sb(tag + "_w", [128, 2, nk // ksplit, cw], BF16, scope=scope),
        "w_free": [None, None],
        "pss": [ctx.ps(tag + "_ps", [128, 512], F32, scope=scope) for _ in range(nps)],
        "ps_free": [None] * nps,
        "nk": nk, "cw": cw,
    }


def emit_proj(ctx, rec, xnT, w_hbm, n_out, evac, *, tag, ready, pst, bias=None, consts=None,
              pe=None, wld=None, ntt=NTT, t0=0, ksplit=1, xwaits=()):
    wld = wld or ctx.chan(tag + "_wl")
    pe = pe or ctx.chan(tag + "_pe")
    nk = pst["nk"]
    CW = pst["cw"]
    NO = ceil_div(n_out, CW)
    nkh = nk // ksplit if ksplit > 1 else nk
    wbuf = pst["wbuf"]
    pss = pst["pss"]
    ps_free = pst["ps_free"]
    w_free = pst["w_free"]
    assert ksplit == 1 or len(pss) >= ntt
    for o in range(NO):
        o0, o1 = o * CW, min((o + 1) * CW, n_out)
        ow = o1 - o0
        pm_of = {}
        for kh in range(ksplit):
            sl = (o * ksplit + kh) % 2
            ws = [w_free[sl]] if w_free[sl] is not None else []
            k0 = kh * nkh
            wm = rec.op("sync", lambda h, o0=o0, sl=sl, ow=ow, k0=k0:
                        h.dma_start(out=wbuf[:, sl, 0:nkh, 0:ow],
                                    in_=w_hbm[k0 * 128:(k0 + nkh) * 128, o0:o0 + ow]
                                    .rearrange("(k p) n -> p k n", p=128)),
                        waits=ws + ctx.wwait(w_hbm), inc=wld, k=16)
            last_pm = None
            for tt in range(ntt):
                pi = (o * ntt + tt) % len(pss) if ksplit == 1 else tt
                ps = pss[pi]
                waits = [(wld, wm), ready] + list(xwaits)
                if kh == 0 and ps_free[pi] is not None:
                    waits.append(ps_free[pi])
                nmm = nkh + (1 if (bias is not None and kh == ksplit - 1) else 0)
                mark = None
                for k in range(nkh):
                    first = (kh == 0 and k == 0)
                    lastm = (kh == ksplit - 1 and k == nmm - 1)
                    lastk = (k == nkh - 1)
                    mark = rec.op("tensor", lambda h, k=k, tt=tt, sl=sl, ps=ps, ow=ow, k0=k0,
                                  first=first, lastm=lastm:
                                  h.matmul(out=ps[:, 0:ow], lhsT=xnT[:, k0 + k, t0 + tt * 128:t0 + (tt + 1) * 128],
                                           rhs=wbuf[:, sl, k, 0:ow], start=first, stop=lastm),
                                  waits=waits if k == 0 else (), inc=pe if (lastm or lastk) else None)
                if bias is not None and kh == ksplit - 1:
                    bsb, boff = bias
                    mark = rec.op("tensor", lambda h, ps=ps, ow=ow, o0=o0, bsb=bsb, boff=boff:
                                  h.matmul(out=ps[:, 0:ow], lhsT=consts["ones1"][0:1, 0:128],
                                           rhs=bsb[0:1, boff + o0:boff + o0 + ow], start=False, stop=True),
                                  inc=pe)
                last_pm = mark
                if kh == ksplit - 1:
                    pm_of[tt] = mark
                    if ksplit == 1:
                        ps_free[pi] = evac(rec, o, tt, ps[:, 0:ow], mark)
            w_free[sl] = (pe, last_pm)
        if ksplit > 1:
            for tt in range(ntt):
                ps_free[tt] = evac(rec, o, tt, pss[tt][:, 0:ow], pm_of[tt])



def emit_l0_attn(ctx, rec, io, W, C):
    nc = ctx.nc
    HN = 16
    bias_scope = ExitStack()
    bias_sb = ctx.sb("a_bias", [1, 4 * DM], BF16, scope=bias_scope, side="right")
    kv_scope = ExitStack()
    ktok = ctx.sb("a_ktok", [128, NTT, DM], BF16, scope=kv_scope, side="right")
    vtok = ctx.sb("a_vtok", [128, NTT, DM], BF16, scope=kv_scope, side="right")

    cv = ctx.chan("l0_v")
    ca = ctx.chan("l0_a")
    cw = ctx.chan("l0_sw")
    pe = ctx.chan("l0_pe")
    wld = ctx.chan("l0_wl")
    trq = ctx.chan("l0_trq")

    scopeA = ExitStack()
    xnT = ctx.sb("a_xnT", [128, NDC, TOK], BF16, scope=scopeA)
    elu_tmp = ctx.sb("a_elut", [128, 4, 512], F32, scope=scopeA)
    elu_e = ctx.sb("a_elue", [128, 4, 512], F32, scope=scopeA)
    qstage = ctx.sb("a_qst", [128, 4, 512], BF16, scope=scopeA)
    pst = proj_state(ctx, scopeA, "l0p")
    trc, trm = emit_ln_phase(ctx, rec, io["src"], io["scr_a"], xnT, kind="ln", tag="l0ln", scope=scopeA)
    ready = (trc, trm)
    wldg = ctx.chan("l0_wlg")
    bm_ = rec.op("gpsimd", lambda h: h.dma_start(out=bias_sb[:], in_=W["l0_biases"]), inc=wldg, k=16)
    bias_rdy = (wldg, bm_)

    elu_slot_mark = {}

    def elu_evac(dst_fn, store_fn):
        def evac(rec, o, tt, ps, pm):
            i = (o * NTT + tt) % 4
            ws = [(pe, pm)]
            if i in elu_slot_mark:
                ws.append(elu_slot_mark[i])
            if ("q", i) in elu_slot_mark:
                ws.append(elu_slot_mark[("q", i)])
            ow = ps.shape[-1]
            m1 = rec.op("vector", lambda h, ps=ps, i=i, ow=ow: h.tensor_scalar(
                out=elu_tmp[:, i, 0:ow], in0=ps, scalar1=0.0, scalar2=None, op0=ALU.min),
                waits=ws, inc=cv)
            m2 = rec.op("scalar", lambda h, i=i, ow=ow: h.activation(
                out=elu_e[:, i, 0:ow], in_=elu_tmp[:, i, 0:ow], func=AF.Exp),
                waits=[(cv, m1)], inc=ca)
            m3 = rec.op("vector", lambda h, ps=ps, i=i, o=o, tt=tt, ow=ow: h.scalar_tensor_tensor(
                out=dst_fn(o, tt, i), in0=ps, scalar=0.0, in1=elu_e[:, i, 0:ow],
                op0=ALU.max, op1=ALU.add), waits=[(ca, m2)], inc=cv)
            elu_slot_mark[i] = (cv, m3)
            sf = store_fn(o, tt, i)
            if sf is not None:
                hbm_ap, src_ap = sf
                sm = rec.op("sync", lambda h, hbm_ap=hbm_ap, src_ap=src_ap:
                            h.dma_start(out=hbm_ap, in_=src_ap),
                            waits=[(cv, m3), ready], inc=cw, k=16)
                elu_slot_mark[("q", i)] = (cw, sm)
            return (cv, m3)
        return evac

    kevac = elu_evac(lambda o, tt, i: ktok[:, tt, o * 512:(o + 1) * 512],
                     lambda o, tt, i: ((io["scr_b"][tt * 128:(tt + 1) * 128, :], ktok[:, tt, :])
                                       if o == 3 else None))
    emit_proj(ctx, rec, xnT, W["Wk"], DM, kevac, tag="l0k", ready=ready, pst=pst, xwaits=[bias_rdy],
              bias=(bias_sb, DM), consts=C, pe=pe, wld=wld)
    kdone = (cv, cv.n)

    def vevac(rec, o, tt, ps, pm):
        m = rec.op("scalar", lambda h, ps=ps, o=o, tt=tt: h.activation(
            out=vtok[:, tt, o * 512:(o + 1) * 512], in_=ps, func=AF.Copy), waits=[(pe, pm)], inc=ca)
        return (ca, m)
    emit_proj(ctx, rec, xnT, W["Wv"], DM, vevac, tag="l0v", ready=ready, pst=pst, xwaits=[bias_rdy],
              bias=(bias_sb, 2 * DM), consts=C, pe=pe, wld=wld)
    vdone = (ca, ca.n)

    qevac = elu_evac(lambda o, tt, i: qstage[:, i, :],
                     lambda o, tt, i: (io["scr_a"][tt * 128:(tt + 1) * 128, o * 512:(o + 1) * 512],
                                       qstage[:, i, :]))
    emit_proj(ctx, rec, xnT, W["Wq"], DM, qevac, tag="l0q", ready=ready, pst=pst, xwaits=[bias_rdy],
              bias=(bias_sb, 0), consts=C, pe=pe, wld=wld)
    q_stores = cw.n
    rec.barrier(ctx.chans)
    scopeA.close()

    scopeB = ExitStack()
    kfT = ctx.sb("a_kfT", [128, NDC, TOK], BF16, scope=scopeB)
    qfT = ctx.sb("a_qfT", [128, NDC, TOK], BF16, scope=scopeB)
    sendb = ctx.sb("a_send", [128, HN * 129], BF16, scope=scopeB)
    seedb = ctx.sb("a_seed", [128, HN * 129], BF16, scope=scopeB)
    KVrun = ctx.sb("a_kvr", [128, HN * 129], F32, scope=scopeB)
    KVbf = ctx.sb("a_kvb", [128, HN * 129], BF16, scope=scopeB)
    At = ctx.sb("a_At", [128, 2, 128], BF16, scope=scopeB)
    den = ctx.sb("a_den", [128, 4], F32, scope=scopeB)
    attno = ctx.sb("a_attno", [128, NTT, DM], BF16, scope=scopeB)

    st_scope = ExitStack()
    psst = [ctx.ps("l0_psst", [128, 132], F32, scope=st_scope) for _ in range(2)]
    st_free = [None, None]
    for hh in range(HN):
        psp = psst[hh % 2]
        mms = []
        for n in range(NTT):
            mms.append(lambda h, s, e, n=n, hh=hh, psp=psp: h.matmul(
                out=psp[:, 0:128], lhsT=ktok[:, n, hh * 128:(hh + 1) * 128],
                rhs=vtok[:, n, hh * 128:(hh + 1) * 128], start=s, stop=False))
            mms.append(lambda h, s, e, n=n, hh=hh, psp=psp: h.matmul(
                out=psp[:, 128:129], lhsT=ktok[:, n, hh * 128:(hh + 1) * 128],
                rhs=C["onescol"][0:128, 0:1], start=False, stop=e))
        ws = [kdone, vdone] + ([st_free[hh % 2]] if st_free[hh % 2] else [])
        pm = mm_group(rec, pe, mms, waits=ws)
        em = rec.op("scalar", lambda h, hh=hh, psp=psp: h.activation(
            out=sendb[:, hh * 129:(hh + 1) * 129], in_=psp[:, 0:129], func=AF.Copy),
            waits=[(pe, pm)], inc=ca)
        st_free[hh % 2] = (ca, em)
    st_all = ca.n

    sh_pay = io["sh_l0"]
    sh_flag = io["sh_l0f"]
    snd = ctx.chan("l0_snd")
    rcv = ctx.chan("l0_rcv")
    seeded = ctx.chan("l0_seeded")
    snd_v = st_all

    def g_exchange(h):
        with h.register("a_pp") as pp:
            h.reg_load(pp, C["par_sb"][0:1, 0:1])
            with h.If_eq(pp, 0):
                h.wait_ge(ca.sem, snd_v)
                h.dma_start(out=sh_pay[:], in_=sendb[:]).then_inc(snd.sem, 16)
                h.wait_ge(snd.sem, 16)
                h.dma_start(out=sh_flag[:], in_=C["nonce_sb"][0:1, 0:4]).then_inc(snd.sem, 16)
            with h.Else():
                with h.register("a_pc") as pc, h.register("a_pex") as pex, h.register("a_f0") as f0, \
                     h.register("a_nn") as nn:
                    h.reg_load(nn, C["nonce_sb"][0:1, 0:1])
                    h.reg_mov(pc, 1)
                    h.reg_mov(pex, 0)
                    with h.While(pc):
                        h.reg_alu(pex, pex, 16, ALU.add)
                        h.dma_start(out=C["flag_sb"][0:1, 0:4], in_=sh_flag[0:1, :]).then_inc(snd.sem, 16)
                        h.wait_ge(snd.sem, pex)
                        h.reg_load(f0, C["flag_sb"][0:1, 0:1])
                        h.reg_alu(pc, f0, nn, ALU.not_equal)
                h.dma_start(out=seedb[:], in_=sh_pay[:]).then_inc(rcv.sem, 16)
    rec.raw("gpsimd", g_exchange)

    seeded.reserve(1)

    def v_seed(h):
        with h.register("a_pv") as pv:
            h.reg_load(pv, C["par_sb"][0:1, 0:1])
            with h.If_eq(pv, 0):
                h.memset(KVrun[:], 0.0).then_inc(seeded.sem, 1)
            with h.Else():
                h.wait_ge(rcv.sem, 16)
                h.tensor_copy(out=KVrun[:], in_=seedb[:]).then_inc(seeded.sem, 1)
    rec.raw("vector", v_seed)
    mkb = rec.op("vector", lambda h: h.tensor_copy(out=KVbf[:], in_=KVrun[:]),
                 waits=[(seeded, 1)], inc=cv)

    for k in range(NDC):
        rec.op("sync", lambda h, k=k: h.dma_start_transpose(out=kfT[:, k, :],
                                                            in_=io["scr_b"][:, k * 128:(k + 1) * 128]),
               waits=[(cw, q_stores)], inc=trq, k=16)
        rec.op("sync", lambda h, k=k: h.dma_start_transpose(out=qfT[:, k, :],
                                                            in_=io["scr_a"][:, k * 128:(k + 1) * 128]),
               waits=[(cw, q_stores)], inc=trq, k=16)
    trdone = (trq, trq.n)
    rec.barrier(ctx.chans)
    st_scope.close()

    ap_scope = ExitStack()
    psA = [ctx.ps("l0_psA", [128, 128], F32, scope=ap_scope) for _ in range(2)]
    psY = [ctx.ps("l0_psY", [128, 132], F32, scope=ap_scope) for _ in range(2)]
    psK = [ctx.ps("l0_psK", [128, 132], F32, scope=ap_scope) for _ in range(2)]
    psA_free = [None, None]
    psY_free = [None, None]
    psK_free = [None, None]
    At_free = [None, None]
    kv_mark = {}
    for hh in range(HN):
        for n in range(NTT):
            sl = (hh * NTT + n) % 2
            wsA = [trdone] + ([psA_free[sl]] if psA_free[sl] else [])
            pmA = mm_group(rec, pe, [lambda h, s, e, hh=hh, n=n, sl=sl: h.matmul(
                out=psA[sl][:, 0:128], lhsT=kfT[:, hh, n * 128:(n + 1) * 128],
                rhs=qfT[:, hh, n * 128:(n + 1) * 128], start=s, stop=e)], waits=wsA)
            wsAt = [(pe, pmA)] + ([At_free[sl]] if At_free[sl] else [])
            mAt = rec.op("vector", lambda h, sl=sl: h.tensor_tensor(
                out=At[:, sl, :], in0=psA[sl][:, 0:128], in1=C["triU"][:, 0:128], op=ALU.mult),
                waits=wsAt, inc=cv)
            psA_free[sl] = (cv, mAt)
            wsY = [(cv, mAt), (seeded, 1), (cv, mkb)]
            if hh in kv_mark:
                wsY.append(kv_mark[hh])
            if psY_free[sl]:
                wsY.append(psY_free[sl])
            mms = [
                lambda h, s, e, sl=sl, hh=hh, n=n: h.matmul(
                    out=psY[sl][:, 0:128], lhsT=At[:, sl, :],
                    rhs=vtok[:, n, hh * 128:(hh + 1) * 128], start=s, stop=False),
                lambda h, s, e, sl=sl, hh=hh, n=n: h.matmul(
                    out=psY[sl][:, 128:129], lhsT=At[:, sl, :],
                    rhs=C["onescol"][0:128, 0:1], start=False, stop=False),
                lambda h, s, e, sl=sl, hh=hh, n=n: h.matmul(
                    out=psY[sl][:, 0:129], lhsT=qfT[:, hh, n * 128:(n + 1) * 128],
                    rhs=KVbf[:, hh * 129:(hh + 1) * 129], start=False, stop=e),
            ]
            pmY = mm_group(rec, pe, mms, waits=wsY)
            At_free[sl] = (pe, pmY)
            m1 = rec.op("vector", lambda h, sl=sl: h.tensor_scalar(
                out=den[:, sl:sl + 1], in0=psY[sl][:, 128:129], scalar1=1e-6, scalar2=None,
                op0=ALU.add), waits=[(pe, pmY)], inc=cv)
            m2 = rec.op("vector", lambda h, sl=sl: h.reciprocal(
                out=den[:, 2 + sl:3 + sl], in_=den[:, sl:sl + 1]), waits=[(cv, m1)], inc=cv)
            m3 = rec.op("vector", lambda h, sl=sl, hh=hh, n=n: h.tensor_scalar(
                out=attno[:, n, hh * 128:(hh + 1) * 128], in0=psY[sl][:, 0:128],
                scalar1=den[:, 2 + sl:3 + sl], scalar2=None, op0=ALU.mult),
                waits=[(cv, m2)], inc=cv)
            psY_free[sl] = (cv, m3)
            if n < NTT - 1:
                wsK = [kdone, vdone] + ([psK_free[sl]] if psK_free[sl] else [])
                pmK = mm_group(rec, pe, [
                    lambda h, s, e, sl=sl, hh=hh, n=n: h.matmul(
                        out=psK[sl][:, 0:128], lhsT=ktok[:, n, hh * 128:(hh + 1) * 128],
                        rhs=vtok[:, n, hh * 128:(hh + 1) * 128], start=s, stop=False),
                    lambda h, s, e, sl=sl, hh=hh, n=n: h.matmul(
                        out=psK[sl][:, 128:129], lhsT=ktok[:, n, hh * 128:(hh + 1) * 128],
                        rhs=C["onescol"][0:128, 0:1], start=False, stop=e)], waits=wsK)
                mk1 = rec.op("vector", lambda h, sl=sl, hh=hh: h.tensor_tensor(
                    out=KVrun[:, hh * 129:(hh + 1) * 129], in0=KVrun[:, hh * 129:(hh + 1) * 129],
                    in1=psK[sl][:, 0:129], op=ALU.add), waits=[(pe, pmK), (pe, pmY), (seeded, 1)], inc=cv)
                mk2 = rec.op("vector", lambda h, hh=hh: h.tensor_copy(
                    out=KVbf[:, hh * 129:(hh + 1) * 129], in_=KVrun[:, hh * 129:(hh + 1) * 129]),
                    waits=[(cv, mk1)], inc=cv)
                psK_free[sl] = (cv, mk1)
                kv_mark[hh] = (cv, mk2)

    attn_all = cv.n
    for tt in range(NTT):
        rec.op("sync", lambda h, tt=tt: h.dma_start(out=io["scr_b"][tt * 128:(tt + 1) * 128, :],
                                                    in_=attno[:, tt, :]),
               waits=[(cv, attn_all), (trq, trq.n)], inc=cw, k=16)
    ao_stores = cw.n
    rec.barrier(ctx.chans)
    ap_scope.close()
    scopeB.close()
    kv_scope.close()

    scopeC = ExitStack()
    aoT = ctx.sb("a_aoT", [128, NDC, TOK], BF16, scope=scopeC)
    of32 = ctx.sb("a_of32", [128, NTT, DM], F32, scope=scopeC)
    pst_o = proj_state(ctx, scopeC, "l0po")
    for k in range(NDC):
        rec.op("sync", lambda h, k=k: h.dma_start_transpose(out=aoT[:, k, :],
                                                            in_=io["scr_b"][:, k * 128:(k + 1) * 128]),
               waits=[(cw, ao_stores)], inc=trq, k=16)
    aoT_ready = (trq, trq.n)

    def oevac(rec, o, tt, ps, pm):
        m = rec.op("scalar", lambda h, ps=ps, o=o, tt=tt: h.activation(
            out=of32[:, tt, o * 512:(o + 1) * 512], in_=ps, func=AF.Copy), waits=[(pe, pm)], inc=ca)
        return (ca, m)
    emit_proj(ctx, rec, aoT, W["Wo"], DM, oevac, tag="l0o", ready=aoT_ready, pst=pst_o, xwaits=[bias_rdy],
              bias=(bias_sb, 3 * DM), consts=C, pe=pe, wld=wld)
    odone = (ca, ca.n)

    emit_ln_residual_store(ctx, rec, of32, io["src"], io["dst"], odone, tag="l0fin", scope=scopeC)
    rec.barrier(ctx.chans)
    scopeC.close()
    bias_scope.close()


def emit_ln_residual_store(ctx, rec, of32, src_x, dst_x, ready, *, tag, eps=EPS_LN, scope=None,
                           ntt=NTT, row0=0, pre_add_hbm=None):
    ld = ctx.chan(tag + "_ld")
    cv = ctx.chan(tag + "_v")
    ca = ctx.chan(tag + "_a")
    st = ctx.chan(tag + "_st")
    xt = ctx.sb(tag + "_x", [128, 2, DM], F32, scope=scope)
    ot = ctx.sb(tag + "_o", [128, 2, DM], F32, scope=scope)
    sst = ctx.sb(tag + "_s", [128, 32], F32, scope=scope)
    ut = ctx.sb(tag + "_u", [128, 2, DM], F32, scope=scope) if pre_add_hbm is not None else None
    free = [0, 0]
    prev_tail = 0
    for i in range(ntt):
        sl = i % 2
        r0 = row0 + i * 128
        lm = rec.op("sync", lambda h, r0=r0, sl=sl: h.dma_start(out=xt[:, sl, :],
                                                                in_=src_x[r0:r0 + 128, :]),
                    waits=[(cv, free[sl])], inc=ld, k=16)
        if pre_add_hbm is not None:
            lmu = rec.op("sync", lambda h, r0=r0, sl=sl: h.dma_start(out=ut[:, sl, :],
                                                                     in_=pre_add_hbm[r0:r0 + 128, :]),
                         waits=[(cv, free[sl])], inc=ld, k=16)
            pa = rec.op("vector", lambda h, i=i, sl=sl: h.tensor_tensor(
                out=of32[:, i, :], in0=of32[:, i, :], in1=ut[:, sl, :], op=ALU.add),
                waits=[ready, (ld, lmu), (cv, free[sl]), (cv, prev_tail)], inc=cv)
            stats_w = [(cv, pa)]
        else:
            stats_w = [ready, (cv, free[sl]), (cv, prev_tail)]
        m1 = None
        for g in range(4):
            m1 = rec.op("vector", lambda h, i=i, g=g: h.bn_stats(out=sst[:, g * 6:(g + 1) * 6],
                                                                 in_=of32[:, i, g * 512:(g + 1) * 512]),
                        waits=stats_w if g == 0 else [(cv, m1)], inc=cv)
        m2 = rec.op("vector", lambda h: h.bn_aggr(out=sst[:, 24:26], in_=sst[:, 0:24]),
                    waits=[(cv, m1)], inc=cv)
        m3 = rec.op("vector", lambda h: h.tensor_scalar(out=sst[:, 26:27], in0=sst[:, 25:26],
                                                        scalar1=float(eps), scalar2=None, op0=ALU.add),
                    waits=[(cv, m2)], inc=cv)
        m4 = rec.op("scalar", lambda h: h.activation(out=sst[:, 27:28], in_=sst[:, 26:27], func=AF.Sqrt),
                    waits=[(cv, m3)], inc=ca)
        m5 = rec.op("vector", lambda h: h.reciprocal(out=sst[:, 26:27], in_=sst[:, 27:28]),
                    waits=[(ca, m4)], inc=cv)
        m6 = rec.op("vector", lambda h, i=i, sl=sl: h.tensor_scalar(
            out=ot[:, sl, :], in0=of32[:, i, :], scalar1=sst[:, 24:25], scalar2=sst[:, 26:27],
            op0=ALU.subtract, op1=ALU.mult), waits=[(cv, m5)], inc=cv)
        m7 = rec.op("vector", lambda h, sl=sl: h.tensor_tensor(out=ot[:, sl, :], in0=ot[:, sl, :],
                                                               in1=xt[:, sl, :], op=ALU.add),
                    waits=[(cv, m6), (ld, lm)], inc=cv)
        free[sl] = m7
        prev_tail = m6
        rec.op("sync", lambda h, r0=r0, sl=sl: h.dma_start(out=dst_x[r0:r0 + 128, :],
                                                           in_=ot[:, sl, :]),
               waits=[(cv, m7)], inc=st, k=16)
    return st


def emit_l0_ffn(ctx, rec, io, W, C):
    nc = ctx.nc
    NI = 8192
    NIC = NI // 512
    cv = ctx.chan("f0_v")
    ca = ctx.chan("f0_a")
    cp = ctx.chan("f0_p")
    cw = ctx.chan("f0_sw")
    pe = ctx.chan("f0_pe")
    wld = ctx.chan("f0_wl")
    trh = ctx.chan("f0_tr")

    blk = ExitStack()
    xnT = ctx.sb("f_xnT", [128, NDC, TOK], BF16, side="right", scope=blk)
    of32 = ctx.sb("f_of32", [128, NTT, DM], F32, side="right", scope=blk)
    ln_scope = ExitStack()
    trc, trm = emit_ln_phase(ctx, rec, io["src"], io["scr_a"], xnT, kind="ln", tag="f0ln", scope=ln_scope)
    ready = (trc, trm)
    rec.barrier(ctx.chans)
    ln_scope.close()

    main_scope = ExitStack()
    wg = ctx.sb("f_wg", [128, 2, NDC, 512], BF16, scope=main_scope)
    wu = ctx.sb("f_wu", [128, 2, NDC, 512], BF16, scope=main_scope)
    wd = ctx.sb("f_wd", [128, 2, 4, DM // 4], BF16, scope=main_scope)
    sg = ctx.sb("f_sg", [128, 2, 512], F32, scope=main_scope)
    hbuf = ctx.sb("f_hbuf", [128, 4, 512], F32, scope=main_scope)
    ab = ctx.sb("f_ab", [128, 1, 512], F32, scope=main_scope)
    thr = ctx.sb("f_thr", [128, 6, 128], F32, scope=main_scope)
    hsst = ctx.sb("f_hsst", [128, 4, 512], BF16, scope=main_scope)
    hsT = ctx.sb("f_hsT", [128, 4, TOK], BF16, scope=main_scope)
    psg = [ctx.ps("f_psg", [128, 512], F32, scope=main_scope) for _ in range(2)]
    psu = [ctx.ps("f_psu", [128, 512], F32, scope=main_scope) for _ in range(2)]
    psd = [ctx.ps("f_psd", [128, 512], F32, scope=main_scope) for _ in range(4)]
    zm = rec.op("vector", lambda h: h.memset(of32[:, 0:4, :], 0.0), inc=cv)
    zm = rec.op("vector", lambda h: h.memset(of32[:, 4:8, :], 0.0), waits=[(cv, zm)], inc=cv)
    of_zero = (cv, zm)

    wfree = {"g": [None, None], "u": [None, None], "d": [None, None]}
    psg_free = [None, None]
    psu_free = [None, None]
    psd_free = [None] * 4
    sg_free = [None] * 4
    hb_free = [None] * 4
    hs_free = [None] * 4
    hsT_free = [None]
    ab_free = [None, None]
    thr_free = None
    of_add = {}
    st = {}

    def emit_gu(ic):
        nonlocal thr_free
        sl = ic % 2
        wmg = rec.op("sync", lambda h, ic=ic, sl=sl: h.dma_start(
            out=wg[:, sl, :, :], in_=w_chunk_ap(W["w_gate"], ic * 512, (ic + 1) * 512)),
            waits=([wfree["g"][sl]] if wfree["g"][sl] else []) + ctx.wwait(W["w_gate"]),
            inc=wld, k=16)
        wmu = rec.op("sync", lambda h, ic=ic, sl=sl: h.dma_start(
            out=wu[:, sl, :, :], in_=w_chunk_ap(W["w_up"], ic * 512, (ic + 1) * 512)),
            waits=([wfree["u"][sl]] if wfree["u"][sl] else []) + ctx.wwait(W["w_up"]),
            inc=wld, k=16)
        hs_marks = []
        for tt in range(NTT):
            pg = tt % 2
            wsg_ = [(wld, wmg), ready] + ([psg_free[pg]] if psg_free[pg] else [])
            gm = mm_group(rec, pe, [
                lambda h, s, e, k=k, tt=tt, sl=sl, pg=pg: h.matmul(
                    out=psg[pg][:], lhsT=xnT[:, k, tt * 128:(tt + 1) * 128],
                    rhs=wg[:, sl, k, :], start=s, stop=e) for k in range(NDC)], waits=wsg_)
            wsu_ = [(wld, wmu)] + ([psu_free[pg]] if psu_free[pg] else [])
            um = mm_group(rec, pe, [
                lambda h, s, e, k=k, tt=tt, sl=sl, pg=pg: h.matmul(
                    out=psu[pg][:], lhsT=xnT[:, k, tt * 128:(tt + 1) * 128],
                    rhs=wu[:, sl, k, :], start=s, stop=e) for k in range(NDC)], waits=wsu_)
            sgi = tt % 2
            wss = [(pe, gm)] + ([sg_free[sgi]] if sg_free[sgi] else [])
            sm = rec.op("scalar", lambda h, sgi=sgi, pg=pg: h.activation(
                out=sg[:, sgi, :], in_=psg[pg][:], func=AF.Silu), waits=wss, inc=ca)
            psg_free[pg] = (ca, sm)
            hsl = (ic * NTT + tt) % 4
            hw = [(ca, sm), (pe, um)] + ([hb_free[hsl]] if hb_free[hsl] else [])
            hm = rec.op("vector", lambda h, sgi=sgi, pg=pg, hsl=hsl: h.tensor_tensor(
                out=hbuf[:, hsl, :], in0=sg[:, sgi, :], in1=psu[pg][:], op=ALU.mult),
                waits=hw, inc=cv)
            psu_free[pg] = (cv, hm)
            sg_free[sgi] = (cv, hm)
            hv = hbuf[:, hsl, :]
            h4 = hv.rearrange("p (g f) -> p g f", f=4)
            asl = 0
            am = rec.op("scalar", lambda h, hv=hv, asl=asl: h.activation(
                out=ab[:, asl, :], in_=hv, func=AF.Abs),
                waits=[(cv, hm)] + ([ab_free[asl]] if ab_free[asl] else []), inc=ca)
            a4 = ab[:, asl, :].rearrange("p (g f) -> p g f", f=4)
            t_ = thr
            m_ = rec.op("vector", lambda h, a4=a4: h.tensor_tensor(
                out=t_[:, 0, :], in0=a4[:, :, 0], in1=a4[:, :, 1], op=ALU.max),
                waits=[(ca, am)] + ([thr_free] if thr_free else []), inc=cv)
            m_ = rec.op("vector", lambda h, a4=a4: h.tensor_tensor(
                out=t_[:, 1, :], in0=a4[:, :, 0], in1=a4[:, :, 1], op=ALU.min),
                waits=[(cv, m_)], inc=cv)
            m_ = rec.op("vector", lambda h, a4=a4: h.tensor_tensor(
                out=t_[:, 2, :], in0=a4[:, :, 2], in1=a4[:, :, 3], op=ALU.max),
                waits=[(cv, m_)], inc=cv)
            m_ = rec.op("vector", lambda h, a4=a4: h.tensor_tensor(
                out=t_[:, 3, :], in0=a4[:, :, 2], in1=a4[:, :, 3], op=ALU.min),
                waits=[(cv, m_)], inc=cv)
            m_ = rec.op("vector", lambda h: h.tensor_tensor(
                out=t_[:, 4, :], in0=t_[:, 0, :], in1=t_[:, 2, :], op=ALU.min),
                waits=[(cv, m_)], inc=cv)
            m_ = rec.op("vector", lambda h: h.tensor_tensor(
                out=t_[:, 5, :], in0=t_[:, 1, :], in1=t_[:, 3, :], op=ALU.max),
                waits=[(cv, m_)], inc=cv)
            m_ = rec.op("vector", lambda h: h.tensor_tensor(
                out=t_[:, 4, :], in0=t_[:, 4, :], in1=t_[:, 5, :], op=ALU.max),
                waits=[(cv, m_)], inc=cv)
            km = []
            for qi in range(4):
                m_ = rec.op("vector", lambda h, qi=qi, a4=a4: h.tensor_tensor(
                    out=t_[:, qi, :], in0=a4[:, :, qi], in1=t_[:, 4, :], op=ALU.is_ge),
                    waits=[(cv, m_)], inc=cv)
                km.append(m_)
            hsv = hsst[:, hsl, :].rearrange("p (g f) -> p g f", f=4)
            pm_ = None
            for qi in range(4):
                wsp = [(cv, km[qi])] + ([hs_free[hsl]] if (hs_free[hsl] and qi == 0) else [])
                pm_ = rec.op("gpsimd", lambda h, qi=qi, h4=h4, hsv=hsv: h.tensor_tensor(
                    out=hsv[:, :, qi], in0=h4[:, :, qi], in1=t_[:, qi, :], op=ALU.mult),
                    waits=wsp + ([(cp, pm_)] if pm_ else []), inc=cp)
            thr_free = (cp, pm_)
            hb_free[hsl] = (cp, pm_)
            ab_free[asl] = (cv, km[3])
            sm2 = rec.op("sync", lambda h, tt=tt, hsl=hsl, ic=ic: h.dma_start(
                out=io["scr_big"][tt * 128:(tt + 1) * 128, ic * 512:(ic + 1) * 512],
                in_=hsst[:, hsl, :]), waits=[(cp, pm_)], inc=cw, k=16)
            hs_marks.append(sm2)
            hs_free[hsl] = (cw, sm2)
        st[ic] = hs_marks[-1]

    def emit_dn(ic):
        sl = ic % 2
        tms = []
        for j4 in range(4):
            wst = [(cw, st[ic])] + ([hsT_free[0]] if (hsT_free[0] and j4 == 0) else [])
            tmk = rec.op("sync", lambda h, ic=ic, j4=j4: h.dma_start_transpose(
                out=hsT[:, j4, :],
                in_=io["scr_big"][0:TOK, ic * 512 + j4 * 128:ic * 512 + (j4 + 1) * 128]),
                waits=wst, inc=trh, k=16)
            tms.append(tmk)
        tr_ic = tms[-1]
        for dh in range(4):
            wmd = rec.op("sync", lambda h, ic=ic, sl=sl, dh=dh: h.dma_start(
                out=wd[:, sl, :, :],
                in_=W["w_down"][ic * 512:(ic + 1) * 512, dh * 512:(dh + 1) * 512]
                .rearrange("(k p) n -> p k n", p=128)),
                waits=(([wfree["d"][sl]] if wfree["d"][sl] else []) if dh == 0 else [(pe, pe.n)])
                + ctx.wwait(W["w_down"]), inc=wld, k=16)
            for tt in range(NTT):
                if True:
                    ob = dh
                    pdi = tt % 4
                    wsd_ = [(trh, tr_ic), (wld, wmd)] + ([psd_free[pdi]] if psd_free[pdi] else [])
                    dm = mm_group(rec, pe, [
                        lambda h, s, e, j4=j4, tt=tt, sl=sl, pdi=pdi: h.matmul(
                            out=psd[pdi][:], lhsT=hsT[:, j4, tt * 128:(tt + 1) * 128],
                            rhs=wd[:, sl, j4, 0:512], start=s, stop=e)
                        for j4 in range(4)], waits=wsd_)
                    wse_ = [(pe, dm), of_zero] + ([of_add[(tt, ob)]] if (tt, ob) in of_add else [])
                    em = rec.op("vector", lambda h, tt=tt, ob=ob, pdi=pdi: h.tensor_tensor(
                        out=of32[:, tt, ob * 512:(ob + 1) * 512],
                        in0=of32[:, tt, ob * 512:(ob + 1) * 512],
                        in1=psd[pdi][:], op=ALU.add), waits=wse_, inc=cv)
                    psd_free[pdi] = (cv, em)
                    of_add[(tt, ob)] = (cv, em)
        hsT_free[0] = (pe, pe.n)
        wfree["d"][sl] = (pe, pe.n)
        wfree["g"][sl] = (pe, pe.n)
        wfree["u"][sl] = (pe, pe.n)

    for ic in range(NIC + 1):
        if ic < NIC:
            emit_gu(ic)
        if ic >= 1:
            emit_dn(ic - 1)
        dn_done = (cv, cv.n)
    rec.barrier(ctx.chans)
    main_scope.close()
    fin_scope = ExitStack()
    emit_ln_residual_store(ctx, rec, of32, io["src"], io["dst"], dn_done, tag="f0fin",
                           scope=fin_scope)
    rec.barrier(ctx.chans)
    fin_scope.close()
    blk.close()



def emit_l1_attn(ctx, rec, io, W, C):
    nc = ctx.nc
    NH, DH, DS, DIN = 48, 64, 128, 3072
    NCC = DIN // 128
    cv = ctx.chan("m_v")
    ca = ctx.chan("m_a")
    cp = ctx.chan("m_p")
    cw = ctx.chan("m_sw")
    pe = ctx.chan("m_pe")
    wld = ctx.chan("m_wl")
    trm = ctx.chan("m_tr")
    snd = ctx.chan("m_snd")
    rcv1 = ctx.chan("m_rcv1")
    rcv2 = ctx.chan("m_rcv2")
    sdd = ctx.chan("m_seed")
    poll1 = ctx.chan("m_poll1")
    poll2 = ctx.chan("m_poll2")

    blk = ExitStack()
    small = ctx.sb("m_small", [128, 80], F32, side="right", scope=blk)
    LcT = ctx.sb("m_LcT", [128, NTT, NH], F32, side="right", scope=blk)
    eUT = ctx.sb("m_eUT", [128, NTT, NH], F32, side="right", scope=blk)
    expLcT = ctx.sb("m_expLcT", [128, NTT, NH], F32, side="right", scope=blk)
    dAe = ctx.sb("m_dAe", [128, NTT * NH], F32, side="right", scope=blk)
    Db = ctx.sb("m_Db", [128, NH], F32, side="right", scope=blk)
    T0c = ctx.sb("m_T0", [128, 128], F32, side="right", scope=blk)
    eyeb = ctx.sb("m_eyeb", [128, 128], BF16, side="right", scope=blk)
    eyef = ctx.sb("m_eyef", [128, 128], F32, side="right", scope=blk)
    zro = ctx.sb("m_zro", [128, 128], F32, side="right", scope=blk)
    ctail = ctx.sb("m_ctail", [128, NCC, 3], BF16, side="right", scope=blk)
    convst = ctx.sb("m_cvst", [128, NCC, 3], BF16, side="right", scope=blk)
    scope2 = ExitStack()
    xn3T = ctx.sb("m_xnT", [128, NDC, TOK], BF16, side="right", scope=scope2)
    scope3 = ExitStack()
    Lc = ctx.sb("m_Lc", [128, TOK], F32, side="right", scope=scope3)
    Bf = ctx.sb("m_Bf", [128, TOK], BF16, side="right", scope=scope3)
    Cf = ctx.sb("m_Cf", [128, TOK], BF16, side="right", scope=scope3)
    Btok = ctx.sb("m_Btok", [128, NTT, DS], BF16, side="right", scope=scope3)
    Ssb = ctx.sb("m_Ssb", [128, NH * DH], F32, side="right", scope=scope3)
    Sbf = ctx.sb("m_Sbf", [128, NH * DH], BF16, side="right", scope=scope3)
    sendS = ctx.sb("m_sendS", [128, NH * DH], BF16, side="right", scope=scope3)
    ssm0 = ctx.sb("m_ssm0", [128, NH * DH], F32, side="right", scope=scope3)
    scope4 = ExitStack()
    xcf = ctx.sb("m_xcf", [128, NCC, TOK], BF16, side="right", scope=scope4)
    scope3b = ExitStack()
    dtl = ctx.sb("m_dt", [128, TOK], F32, side="right", scope=scope3b)
    ldec = ctx.sb("m_ldec", [128, TOK], F32, side="right", scope=scope3b)
    eU = ctx.sb("m_eU", [128, TOK], F32, side="right", scope=scope3b)
    expLc = ctx.sb("m_expLc", [128, TOK], F32, side="right", scope=scope3b)

    cst = ctx.chan("m_cst")
    rec.op("sync", lambda h: h.dma_start(out=T0c[:], in_=W["c_T0"]), inc=cst, k=16)
    gcst = ctx.chan("m_gcst")
    rec.op("gpsimd", lambda h: h.dma_start(out=eyeb[:], in_=W["c_eyef"]), inc=gcst, k=16)
    rec.op("sync", lambda h: h.dma_start(out=eyef[:], in_=W["c_eyef"]), inc=cst, k=16)
    rec.op("sync", lambda h: h.dma_start(out=small[0:48, 0:1], in_=W["dt_bias_r"]), inc=cst, k=16)
    rec.op("sync", lambda h: h.dma_start(out=small[0:48, 49:50], in_=W["A_log_c"]), inc=cst, k=16)
    cvm_ = rec.op("gpsimd", lambda h: h.dma_start(
        out=convst[:], in_=W["conv_state"].rearrange("(cc p) k -> p cc k", p=128)), inc=gcst, k=16)
    rec.op("sync", lambda h: h.dma_start(out=ssm0[:], in_=W["ssm0_t"]), inc=cst, k=16)
    mz = rec.op("vector", lambda h: h.memset(zro[:], 0.0), inc=cv)
    mo = rec.op("vector", lambda h: h.memset(small[:, 48:49], 1.0), waits=[(cst, cst.n)], inc=cv)
    cst_all = (cst, cst.n)

    ln_scope = ExitStack()
    trc0, trm0 = emit_ln_phase(ctx, rec, io["src"], io["scr_a"], xn3T, kind="ln", tag="m_ln",
                               scope=ln_scope, u_hbm=io["scr_u"], rms2=True)
    wcol = ctx.sb("m_wcol", [128, NDC], F32, scope=ln_scope)
    wcm = rec.op("sync", lambda h: h.dma_start(out=wcol[:], in_=W["norm_w_c"]), inc=cst, k=16)
    sc_m = None
    for k in range(NDC):
        sc_m = rec.op("vector", lambda h, k=k: h.tensor_scalar(
            out=xn3T[:, k, :], in0=xn3T[:, k, :], scalar1=wcol[:, k:k + 1], scalar2=None,
            op0=ALU.mult), waits=[(trc0, trm0), (cst, wcm)] if k == 0 else [(cv, sc_m)], inc=cv)
    xn3_ready = (cv, sc_m)
    rec.barrier(ctx.chans)
    ln_scope.close()

    fp_scope = ExitStack()
    wti = ctx.sb("m_wti", [128, 2, NDC, 128], BF16, scope=fp_scope)
    psf = [ctx.ps("m_psf", [128, 512], F32, scope=fp_scope) for _ in range(4)]
    wfree = [None, None]
    psfree = [None] * 4
    units = [("xc", cc, DIN + cc * 128) for cc in range(NCC)]
    units += [("B", 0, 2 * DIN), ("C", 0, 2 * DIN + DS), ("dt", 0, 2 * DIN + 2 * DS)]
    for ui, (kind, cc, col0) in enumerate(units):
        sl = ui % 2
        ncol = 48 if kind == "dt" else 128
        ws = [wfree[sl]] if wfree[sl] else []
        wm = rec.op("sync", lambda h, sl=sl, col0=col0, ncol=ncol: h.dma_start(
            out=wti[:, sl, :, 0:ncol],
            in_=W["W_inT"][:, col0:col0 + ncol].rearrange("(k p) n -> p k n", p=128)),
            waits=ws + ctx.wwait(W["W_inT"]), inc=wld, k=16)
        for ts in range(2):
            pi = (ui * 2 + ts) % 4
            waits = [(wld, wm), xn3_ready, cst_all] + ([psfree[pi]] if psfree[pi] else [])
            mark = None
            for k in range(NDC):
                mark = rec.op("tensor", lambda h, k=k, ts=ts, sl=sl, pi=pi, ncol=ncol: h.matmul(
                    out=psf[pi][0:ncol, 0:512], lhsT=wti[:, sl, k, 0:ncol],
                    rhs=xn3T[:, k, ts * 512:(ts + 1) * 512],
                    start=(k == 0), stop=(k == NDC - 1)), waits=waits if k == 0 else (),
                    inc=pe if k == NDC - 1 else None)
            if kind == "xc":
                em = rec.op("vector", lambda h, pi=pi, cc=cc, ts=ts: h.tensor_copy(
                    out=xcf[:, cc, ts * 512:(ts + 1) * 512], in_=psf[pi][:, 0:512]),
                    waits=[(pe, mark)], inc=cv)
            elif kind == "B":
                em = rec.op("vector", lambda h, pi=pi, ts=ts: h.tensor_copy(
                    out=Bf[:, ts * 512:(ts + 1) * 512], in_=psf[pi][:, 0:512]),
                    waits=[(pe, mark)], inc=cv)
            elif kind == "C":
                em = rec.op("vector", lambda h, pi=pi, ts=ts: h.tensor_copy(
                    out=Cf[:, ts * 512:(ts + 1) * 512], in_=psf[pi][:, 0:512]),
                    waits=[(pe, mark)], inc=cv)
            else:
                em = rec.op("vector", lambda h, pi=pi, ts=ts: h.tensor_copy(
                    out=dtl[0:48, ts * 512:(ts + 1) * 512], in_=psf[pi][0:48, 0:512]),
                    waits=[(pe, mark)], inc=cv)
            psfree[pi] = (cv, em)
        wfree[sl] = (pe, mark)
    xc_done = (cv, cv.n)
    rec.barrier(ctx.chans)
    fp_scope.close()
    if DBG_L1STOP <= 2:
        rec.barrier(ctx.chans)
        return

    def g_tail(h):
        with h.register("t_pp") as pp:
            h.reg_load(pp, C["par_sb"][0:1, 0:1])
            with h.If_eq(pp, 0):
                h.wait_ge(cv.sem, xc_done[1])
                h.dma_start(out=io["sh_l1t"][:, 0:NCC * 3].rearrange("p (a b) -> p a b", b=3),
                            in_=xcf[:, :, TOK - 3:TOK]).then_inc(snd.sem, 16)
                h.wait_ge(snd.sem, 16)
                h.dma_start(out=io["sh_l1tf"][:], in_=C["nonce_sb"][0:1, 0:4]).then_inc(snd.sem, 16)
            with h.Else():
                with h.register("t_pc") as pc, h.register("t_pex") as pex, h.register("t_f0") as f0, \
                     h.register("t_nn") as nn:
                    h.reg_load(nn, C["nonce_sb"][0:1, 0:1])
                    h.reg_mov(pc, 1)
                    h.reg_mov(pex, 0)
                    with h.While(pc):
                        h.reg_alu(pex, pex, 16, ALU.add)
                        h.dma_start(out=C["flag_sb"][0:1, 0:4], in_=io["sh_l1tf"][0:1, :]).then_inc(poll1.sem, 16)
                        h.wait_ge(poll1.sem, pex)
                        h.reg_load(f0, C["flag_sb"][0:1, 0:1])
                        h.reg_alu(pc, f0, nn, ALU.not_equal)
                h.dma_start(out=ctail[:],
                            in_=io["sh_l1t"][:, 0:NCC * 3].rearrange("p (a b) -> p a b", b=3)).then_inc(rcv1.sem, 16)
    rec.raw("gpsimd", g_tail)
    sdd.reserve(1)

    def v_tail(h):
        with h.register("t_pv") as pv:
            h.reg_load(pv, C["par_sb"][0:1, 0:1])
            with h.If_eq(pv, 0):
                h.wait_ge(gcst.sem, cvm_)
                h.tensor_copy(out=ctail[:], in_=convst[:]).then_inc(sdd.sem, 1)
            with h.Else():
                h.wait_ge(rcv1.sem, 16)
                h.memset(small[:, 79:80], 0.0).then_inc(sdd.sem, 1)
    rec.raw("vector", v_tail)
    tail_ready = (sdd, 1)
    if DBG_L1STOP <= 3:
        rec.barrier(ctx.chans)
        return

    d0 = rec.op("vector", lambda h: h.tensor_scalar(
        out=dtl[0:48, :], in0=dtl[0:48, :], scalar1=small[0:48, 0:1], scalar2=None, op0=ALU.add),
        waits=[xc_done, cst_all], inc=cv)
    d1 = rec.op("scalar", lambda h: h.activation(out=ldec[0:48, :], in_=dtl[0:48, :], func=AF.Abs),
                waits=[(cv, d0)], inc=ca)
    d2 = rec.op("scalar", lambda h: h.activation(out=eU[0:48, :], in_=ldec[0:48, :], func=AF.Exp,
                                                 scale=-1.0), waits=[(ca, d1)], inc=ca)
    d3 = rec.op("scalar", lambda h: h.activation(out=eU[0:48, :], in_=eU[0:48, :], func=AF.Ln,
                                                 bias=small[0:48, 48:49]), waits=[(ca, d2)], inc=ca)
    d4 = rec.op("vector", lambda h: h.tensor_scalar(
        out=Lc[0:48, :], in0=dtl[0:48, :], scalar1=0.0, scalar2=None, op0=ALU.max),
        waits=[(ca, d3), (ca, d1)], inc=cv)
    d5 = rec.op("vector", lambda h: h.tensor_tensor(
        out=dtl[0:48, :], in0=Lc[0:48, :], in1=eU[0:48, :], op=ALU.add), waits=[(cv, d4)], inc=cv)
    if DBG_L1STOP <= 31:
        rec.barrier(ctx.chans)
        return
    a0 = rec.op("scalar", lambda h: h.activation(out=small[0:48, 50:51], in_=small[0:48, 49:50],
                                                 func=AF.Exp), waits=[cst_all], inc=ca)
    a1 = rec.op("vector", lambda h: h.tensor_scalar(
        out=small[0:48, 50:51], in0=small[0:48, 50:51], scalar1=-1.0, scalar2=None, op0=ALU.mult),
        waits=[(ca, a0)], inc=cv)
    l0 = rec.op("vector", lambda h: h.tensor_scalar(
        out=ldec[0:48, :], in0=dtl[0:48, :], scalar1=small[0:48, 50:51], scalar2=None, op0=ALU.mult),
        waits=[(cv, a1), (cv, d5)], inc=cv)
    sc = l0
    for n in range(NTT):
        sc = rec.op("vector", lambda h, n=n: h.tensor_tensor_scan(
            out=Lc[0:48, n * 128:(n + 1) * 128], data0=ldec[0:48, n * 128:(n + 1) * 128],
            data1=zro[0:48, 0:128], initial=0.0, op0=ALU.add, op1=ALU.add),
            waits=[(cv, sc), (cv, mz)], inc=cv)
    if DBG_L1STOP <= 32:
        rec.barrier(ctx.chans)
        return
    e0 = rec.op("scalar", lambda h: h.activation(out=expLc[0:48, :], in_=Lc[0:48, :], func=AF.Exp),
                waits=[(cv, sc)], inc=ca)
    eu = sc
    for n in range(NTT):
        eu = rec.op("vector", lambda h, n=n: h.tensor_scalar(
            out=eU[0:48, n * 128:(n + 1) * 128], in0=Lc[0:48, n * 128:(n + 1) * 128],
            scalar1=Lc[0:48, n * 128 + 127:n * 128 + 128], scalar2=None, op0=ALU.subtract),
            waits=[(cv, eu), (ca, e0)], inc=cv)
    e1 = rec.op("scalar", lambda h: h.activation(out=eU[0:48, :], in_=eU[0:48, :], func=AF.Exp,
                                                 scale=-1.0), waits=[(cv, eu)], inc=ca)
    if DBG_L1STOP <= 33:
        rec.barrier(ctx.chans)
        return
    dec_done = (ca, ca.n)

    tr_scope = ExitStack()
    pst_r = ctx.ps("m_pstr", [128, 64], F32, scope=tr_scope)
    ps_bc = ctx.ps("m_psbc", [128, 512], F32, scope=tr_scope)
    tm_ = None
    te_free = None
    for (srcT, dstT) in ((Lc, LcT), (eU, eUT), (expLc, expLcT)):
        for n in range(NTT):
            wsx = [dec_done] + ([te_free] if te_free else [])
            tm_ = rec.op("tensor", lambda h, srcT=srcT, n=n: h.transpose(
                out=pst_r[:, 0:48], in_=srcT[0:48, n * 128:(n + 1) * 128], identity=eyef[0:48, 0:48]),
                waits=wsx, inc=pe)
            tm_ = rec.op("scalar", lambda h, dstT=dstT, n=n: h.activation(
                out=dstT[:, n, :], in_=pst_r[:, 0:48], func=AF.Copy), waits=[(pe, tm_)], inc=ca)
            te_free = (ca, tm_)
    if DBG_L1STOP <= 34:
        rec.barrier(ctx.chans)
        return
    row1 = ctx.sb("m_row1", [1, 384], F32, scope=tr_scope)
    ec = rec.op("sync", lambda h: h.dma_start(
        out=io["scr_small"][0:48, 0:8],
        in_=Lc[0:48, :].rearrange("p (n c) -> p n c", c=128)[:, :, 127:128].rearrange("p n c -> p (n c)"),
        allow_slow_non_contiguous=True), waits=[dec_done], inc=cw, k=16)
    er = rec.op("sync", lambda h: h.dma_start(
        out=row1[:].rearrange("p (a b) -> p a b", a=8),
        in_=io["scr_small"][0:48, 0:8].transpose((1, 0)).unsqueeze(0),
        allow_slow_non_contiguous=True), waits=[(cw, ec)], inc=cw, k=16)
    bm1 = rec.op("tensor", lambda h: h.matmul(out=ps_bc[:, 0:384], lhsT=C["onesf"][0:1, 0:128],
                                              rhs=row1[:], start=True, stop=True),
                 waits=[(cw, er), cst_all], inc=pe)
    bm2 = rec.op("scalar", lambda h: h.activation(out=dAe[:, 0:384], in_=ps_bc[:, 0:384],
                                                  func=AF.Exp), waits=[(pe, bm1)], inc=ca)
    if DBG_L1STOP <= 35:
        rec.barrier(ctx.chans)
        return
    drow = ctx.sb("m_drow", [1, 48], F32, scope=tr_scope)
    dm0 = rec.op("sync", lambda h: h.dma_start(out=drow[:], in_=W["D_row"]), inc=cst, k=16)
    dm1 = rec.op("tensor", lambda h: h.matmul(out=ps_bc[:, 384:432], lhsT=C["onesf"][0:1, 0:128],
                                              rhs=drow[:], start=True, stop=True),
                 waits=[(cst, dm0), (ca, bm2), cst_all], inc=pe)
    dm2 = rec.op("scalar", lambda h: h.activation(out=Db[:], in_=ps_bc[:, 384:432], func=AF.Copy),
                 waits=[(pe, dm1)], inc=ca)
    bc_done = (ca, ca.n)
    rec.barrier(ctx.chans)
    tr_scope.close()
    scope3b.close()
    if DBG_L1STOP <= 4:
        rec.barrier(ctx.chans)
        return

    cvp_scope = ExitStack()
    cpad = ctx.sb("m_cpad", [128, 2, TOK + 3], F32, scope=cvp_scope)
    ctmp = ctx.sb("m_ctmp", [128, 2, TOK], F32, scope=cvp_scope)
    xsil = ctx.sb("m_xsil", [128, 2, TOK], BF16, scope=cvp_scope)
    cw4 = ctx.sb("m_cw4", [128, NCC, 4], F32, scope=cvp_scope)
    cb1 = ctx.sb("m_cb1", [128, NCC], F32, scope=cvp_scope)
    cwm = rec.op("sync", lambda h: h.dma_start(
        out=cw4[:], in_=W["conv_w"].rearrange("(cc p) k -> p cc k", p=128)), inc=cst, k=16)
    cbm = rec.op("sync", lambda h: h.dma_start(out=cb1[:], in_=W["conv_b_c"]), inc=cst, k=16)
    cp_free = [(cv, 0), (cv, 0)]
    for cc in range(NCC):
        sl = cc % 2
        m0 = rec.op("vector", lambda h, cc=cc, sl=sl: h.tensor_copy(
            out=cpad[:, sl, 0:3], in_=ctail[:, cc, :]),
            waits=[tail_ready, xc_done, (cst, max(cwm, cbm)), cp_free[sl]], inc=cv)
        m1 = rec.op("vector", lambda h, cc=cc, sl=sl: h.tensor_copy(
            out=cpad[:, sl, 3:TOK + 3], in_=xcf[:, cc, :]), waits=[(cv, m0)], inc=cv)
        m2 = rec.op("vector", lambda h, cc=cc, sl=sl: h.tensor_scalar(
            out=ctmp[:, sl, :], in0=cpad[:, sl, 0:TOK], scalar1=cw4[:, cc, 0:1], scalar2=None,
            op0=ALU.mult), waits=[(cv, m1)], inc=cv)
        for kk in range(1, 4):
            m2 = rec.op("vector", lambda h, cc=cc, sl=sl, kk=kk: h.scalar_tensor_tensor(
                out=ctmp[:, sl, :], in0=cpad[:, sl, kk:kk + TOK], scalar=cw4[:, cc, kk:kk + 1],
                in1=ctmp[:, sl, :], op0=ALU.mult, op1=ALU.add), waits=[(cv, m2)], inc=cv)
        m3 = rec.op("vector", lambda h, cc=cc, sl=sl: h.tensor_scalar(
            out=ctmp[:, sl, :], in0=ctmp[:, sl, :], scalar1=cb1[:, cc:cc + 1], scalar2=None,
            op0=ALU.add), waits=[(cv, m2)], inc=cv)
        m4 = rec.op("scalar", lambda h, cc=cc, sl=sl: h.activation(
            out=xsil[:, sl, :], in_=ctmp[:, sl, :], func=AF.Silu), waits=[(cv, m3)], inc=ca)
        sm_ = rec.op("sync", lambda h, cc=cc, sl=sl: h.dma_start(
            out=io["scr_m"][cc * 128:(cc + 1) * 128, 0:TOK], in_=xsil[:, sl, :]),
            waits=[(ca, m4)], inc=cw, k=16)
        cp_free[sl] = (cw, sm_)
    xf_st = cw.n
    rec.barrier(ctx.chans)
    scope4.close()
    cvp_scope.close()
    if DBG_L1STOP <= 5:
        rec.barrier(ctx.chans)
        return

    yt_scope = ExitStack()
    Ytok = ctx.sb("m_Ytok", [128, NTT, DIN], BF16, scope=yt_scope)
    xt_scope = ExitStack()
    Xchunk = ctx.sb("m_Xch", [128, 2, DIN], BF16, scope=xt_scope)
    rec.op("sync", lambda h: h.dma_start(out=io["scr_m"][DIN:DIN + 128, 0:TOK], in_=Bf[:]),
           waits=[xc_done], inc=cw, k=16)
    bf_st = cw.n
    for tt in range(NTT):
        rec.op("sync", lambda h, tt=tt: h.dma_start_transpose(
            out=Btok[:, tt, :], in_=io["scr_m"][DIN:DIN + 128, tt * 128:(tt + 1) * 128]),
            waits=[(cw, bf_st)], inc=trm, k=16)
    tok_ready = (trm, trm.n)
    hself = ctx.sb("m_hself", [48, NH * 128], F32, scope=xt_scope)
    hsm = rec.op("sync", lambda h: h.dma_start(out=hself[:], in_=W["c_hsel"]), inc=cst, k=16)
    if DBG_L1STOP <= 6:
        rec.barrier(ctx.chans)
        return


    sp_scope = ExitStack()
    psS = [ctx.ps("m_psS", [128, 64], F32, scope=sp_scope) for _ in range(2)]
    xeu = ctx.sb("m_xeu", [128, 2, DH], BF16, scope=sp_scope)
    s0m = rec.op("vector", lambda h: h.tensor_copy(out=Ssb[:], in_=ssm0[:]),
                 waits=[cst_all], inc=cv)

    def state_pass(seeded_mark, tag):
        psS_free = [None, None]
        xeu_free = [None, None]
        xch_last = [None, None]
        rm = {}
        for n in range(NTT):
            sx = n % 2
            wsl = [(cw, xf_st)] + ([xch_last[sx]] if xch_last[sx] else [])
            xlm = rec.op("sync", lambda h, n=n, sx=sx: h.dma_start_transpose(
                out=Xchunk[:, sx, :], in_=io["scr_m"][0:DIN, n * 128:(n + 1) * 128]),
                waits=wsl, inc=trm, k=16)
            hh_marks = []
            for hh in range(NH):
                sl = (n * NH + hh) % 2
                wsx = [tok_ready, bc_done, seeded_mark, (trm, xlm)] + ([xeu_free[sl]] if xeu_free[sl] else [])
                xm = rec.op("vector", lambda h, n=n, hh=hh, sl=sl, sx=sx: h.tensor_scalar(
                    out=xeu[:, sl, :], in0=Xchunk[:, sx, hh * DH:(hh + 1) * DH],
                    scalar1=eUT[:, n, hh:hh + 1], scalar2=None, op0=ALU.mult),
                    waits=wsx, inc=cv)
                hh_marks.append(xm)
                wsp = [(cv, xm)] + ([psS_free[sl]] if psS_free[sl] else [])
                pm = rec.op("tensor", lambda h, n=n, sl=sl: h.matmul(
                    out=psS[sl][:], lhsT=Btok[:, n, :], rhs=xeu[:, sl, :], start=True, stop=True),
                    waits=wsp, inc=pe)
                wsr = [(pe, pm)] + ([rm[hh]] if hh in rm else [])
                rmk = rec.op("vector", lambda h, n=n, hh=hh, sl=sl: h.scalar_tensor_tensor(
                    out=Ssb[:, hh * DH:(hh + 1) * DH], in0=Ssb[:, hh * DH:(hh + 1) * DH],
                    scalar=dAe[:, n * NH + hh:n * NH + hh + 1], in1=psS[sl][:],
                    op0=ALU.mult, op1=ALU.add), waits=wsr, inc=cv)
                rm[hh] = (cv, rmk)
                psS_free[sl] = (cv, rmk)
                xeu_free[sl] = (pe, pm)
            xch_last[sx] = (cv, cv.n)
        return cv.n
    sp_end = state_pass((cv, s0m), "sp")
    sendm = rec.op("vector", lambda h: h.tensor_copy(out=sendS[:], in_=Ssb[:]),
                   waits=[(cv, sp_end)], inc=cv)
    if DBG_L1STOP <= 7:
        rec.barrier(ctx.chans)
        return

    def g_state_x(h):
        with h.register("s_pp") as pp:
            h.reg_load(pp, C["par_sb"][0:1, 0:1])
            with h.If_eq(pp, 0):
                h.wait_ge(cv.sem, sendm)
                h.dma_start(out=io["sh_l1s"][:], in_=sendS[:]).then_inc(snd.sem, 16)
                h.wait_ge(snd.sem, 48)
                h.dma_start(out=io["sh_l1sf"][:], in_=C["nonce_sb"][0:1, 0:4]).then_inc(snd.sem, 16)
            with h.Else():
                h.wait_ge(cv.sem, sendm)
                with h.register("s_pc") as pc, h.register("s_pex") as pex, h.register("s_f0") as f0, \
                     h.register("s_nn") as nn:
                    h.reg_load(nn, C["nonce_sb"][0:1, 0:1])
                    h.reg_mov(pc, 1)
                    h.reg_mov(pex, 0)
                    with h.While(pc):
                        h.reg_alu(pex, pex, 16, ALU.add)
                        h.dma_start(out=C["flag_sb"][0:1, 0:4], in_=io["sh_l1sf"][0:1, :]).then_inc(poll2.sem, 16)
                        h.wait_ge(poll2.sem, pex)
                        h.reg_load(f0, C["flag_sb"][0:1, 0:1])
                        h.reg_alu(pc, f0, nn, ALU.not_equal)
                h.dma_start(out=sendS[:], in_=io["sh_l1s"][:]).then_inc(rcv2.sem, 16)
    rec.raw("gpsimd", g_state_x)
    sdd.reserve(1)

    def v_seed2(h):
        with h.register("s_pv") as pv:
            h.reg_load(pv, C["par_sb"][0:1, 0:1])
            with h.If_eq(pv, 0):
                h.tensor_copy(out=Ssb[:], in_=ssm0[:]).then_inc(sdd.sem, 1)
            with h.Else():
                h.wait_ge(rcv2.sem, 16)
                h.tensor_copy(out=Ssb[:], in_=sendS[:]).then_inc(sdd.sem, 1)
    rec.raw("vector", v_seed2)
    seed2 = (sdd, 2)
    sb0 = rec.op("vector", lambda h: h.tensor_copy(out=Sbf[:], in_=Ssb[:]), waits=[seed2], inc=cv)
    rec.barrier(ctx.chans)
    sp_scope.close()
    if DBG_L1STOP <= 8:
        rec.barrier(ctx.chans)
        return

    yl_scope = ExitStack()
    dtile = ctx.sb("m_dtile", [128, 4, 128], F32, scope=yl_scope)
    dexp = ctx.sb("m_dexp", [128, 4, 128], BF16, scope=yl_scope)
    Mt = ctx.sb("m_Mt", [128, 4, 128], BF16, scope=yl_scope)
    BCs = ctx.sb("m_BCs", [128, 128], F32, scope=yl_scope)
    xeu2 = ctx.sb("m_xeu2", [128, 4, DH], BF16, scope=yl_scope)
    xdd = ctx.sb("m_xdd", [128, 4, DH], BF16, scope=yl_scope)
    ytmp = ctx.sb("m_ytmp", [128, 4, DH], F32, scope=yl_scope)
    psBC = ctx.ps("m_psBC", [128, 128], F32, scope=yl_scope)
    psM = [ctx.ps("m_psM", [128, 128], F32, scope=yl_scope) for _ in range(2)]
    psYi = [ctx.ps("m_psYi", [128, 64], F32, scope=yl_scope) for _ in range(2)]
    psYe = [ctx.ps("m_psYe", [128, 64], F32, scope=yl_scope) for _ in range(2)]
    psS2 = ctx.ps("m_psS2", [128, 64], F32, scope=yl_scope)
    psM_free = [None, None]
    psYi_free = [None, None]
    psYe_free = [None, None]
    dt_free = [None] * 4
    de_free = [None] * 4
    mt_free = [None] * 4
    xe_free = [None] * 4
    xd_free = [None] * 4
    yt_free = [None] * 4
    bc_ev = None
    ps2_free = None
    srm = {}

    xch2_last = [None, None]

    def do_chunk(n):
        nonlocal bc_ev, ps2_free
        sx = n % 2
        wsl = [(cw, xf_st)] + ([xch2_last[sx]] if xch2_last[sx] else [])
        xlm = rec.op("sync", lambda h, n=n, sx=sx: h.dma_start_transpose(
            out=Xchunk[:, sx, :], in_=io["scr_m"][0:DIN, n * 128:(n + 1) * 128]),
            waits=wsl, inc=trm, k=16)
        bcw = [tok_ready, xc_done] + ([bc_ev] if bc_ev else [])
        bm = rec.op("tensor", lambda h, n=n: h.matmul(
            out=psBC[:], lhsT=Bf[:, n * 128:(n + 1) * 128], rhs=Cf[:, n * 128:(n + 1) * 128],
            start=True, stop=True), waits=bcw, inc=pe)
        bce = rec.op("scalar", lambda h: h.activation(out=BCs[:], in_=psBC[:], func=AF.Copy),
                     waits=[(pe, bm)], inc=ca)
        bc_ev = (ca, bce)
        for hh in range(NH):
            sl = (n * NH + hh) % 2
            s4 = (n * NH + hh) % 4
            wsm = [(cv, sc), (cst, hsm)] + ([psM_free[sl]] if psM_free[sl] else [])
            mm0 = rec.op("tensor", lambda h, n=n, hh=hh, sl=sl: h.matmul(
                out=psM[sl][:], lhsT=hself[0:48, hh * 128:(hh + 1) * 128],
                rhs=Lc[0:48, n * 128:(n + 1) * 128], start=True, stop=True), waits=wsm, inc=pe)
            wsd = [(pe, mm0), cst_all, bc_done] + ([dt_free[s4]] if dt_free[s4] else [])
            md = rec.op("vector", lambda h, n=n, hh=hh, sl=sl, s4=s4: h.scalar_tensor_tensor(
                out=dtile[:, s4, :], in0=psM[sl][:], scalar=LcT[:, n, hh:hh + 1], in1=T0c[:],
                op0=ALU.subtract, op1=ALU.add), waits=wsd, inc=cv)
            psM_free[sl] = (cv, md)
            wse = [(cv, md)] + ([de_free[s4]] if de_free[s4] else [])
            me = rec.op("scalar", lambda h, s4=s4: h.activation(
                out=dexp[:, s4, :], in_=dtile[:, s4, :], func=AF.Exp), waits=wse, inc=ca)
            dt_free[s4] = (ca, me)
            wst = [(ca, me), bc_ev] + ([mt_free[s4]] if mt_free[s4] else [])
            mt = rec.op("gpsimd", lambda h, s4=s4: h.tensor_tensor(
                out=Mt[:, s4, :], in0=BCs[:], in1=dexp[:, s4, :], op=ALU.mult), waits=wst, inc=cp)
            de_free[s4] = (cp, mt)
            wxd = [tok_ready, bc_done, (trm, xlm)] + ([xd_free[s4]] if xd_free[s4] else [])
            xd = rec.op("vector", lambda h, hh=hh, s4=s4, sx=sx: h.tensor_scalar(
                out=xdd[:, s4, :], in0=Xchunk[:, sx, hh * DH:(hh + 1) * DH],
                scalar1=Db[:, hh:hh + 1], scalar2=None, op0=ALU.mult), waits=wxd, inc=cv)
            wyi = [(cp, mt), (cv, xd)] + ([psYi_free[sl]] if psYi_free[sl] else [])
            yi1 = rec.op("tensor", lambda h, hh=hh, sl=sl, s4=s4, sx=sx: h.matmul(
                out=psYi[sl][:], lhsT=Mt[:, s4, :], rhs=Xchunk[:, sx, hh * DH:(hh + 1) * DH],
                start=True, stop=False), waits=wyi)
            yi2 = rec.op("tensor", lambda h, sl=sl, s4=s4: h.matmul(
                out=psYi[sl][:], lhsT=eyeb[:], rhs=xdd[:, s4, :], start=False, stop=True), inc=pe)
            mt_free[s4] = (pe, yi2)
            xd_free[s4] = (pe, yi2)
            wye = [(cv, sb0), xc_done] + ([srm[hh]] if hh in srm else []) + \
                  ([psYe_free[sl]] if psYe_free[sl] else [])
            ye = rec.op("tensor", lambda h, n=n, hh=hh, sl=sl: h.matmul(
                out=psYe[sl][:], lhsT=Cf[:, n * 128:(n + 1) * 128],
                rhs=Sbf[:, hh * DH:(hh + 1) * DH], start=True, stop=True), waits=wye, inc=pe)
            wv1 = [(pe, ye)] + ([yt_free[s4]] if yt_free[s4] else [])
            yv1 = rec.op("vector", lambda h, n=n, hh=hh, sl=sl, s4=s4: h.tensor_scalar(
                out=ytmp[:, s4, :], in0=psYe[sl][:], scalar1=expLcT[:, n, hh:hh + 1],
                scalar2=None, op0=ALU.mult), waits=wv1, inc=cv)
            psYe_free[sl] = (cv, yv1)
            yv2 = rec.op("vector", lambda h, n=n, hh=hh, sl=sl, s4=s4: h.tensor_tensor(
                out=Ytok[:, n, hh * DH:(hh + 1) * DH], in0=ytmp[:, s4, :], in1=psYi[sl][:],
                op=ALU.add), waits=[(cv, yv1), (pe, yi2)], inc=cv)
            psYi_free[sl] = (cv, yv2)
            yt_free[s4] = (cv, yv2)
            if n < NTT - 1:
                wxe = [tok_ready, bc_done] + ([xe_free[s4]] if xe_free[s4] else [])
                xm = rec.op("vector", lambda h, n=n, hh=hh, s4=s4, sx=sx: h.tensor_scalar(
                    out=xeu2[:, s4, :], in0=Xchunk[:, sx, hh * DH:(hh + 1) * DH],
                    scalar1=eUT[:, n, hh:hh + 1], scalar2=None, op0=ALU.mult), waits=wxe, inc=cv)
                wsp2 = [(cv, xm), (pe, ye)] + ([ps2_free] if ps2_free else [])
                pm2 = rec.op("tensor", lambda h, n=n, s4=s4: h.matmul(
                    out=psS2[:], lhsT=Btok[:, n, :], rhs=xeu2[:, s4, :], start=True, stop=True),
                    waits=wsp2, inc=pe)
                xe_free[s4] = (pe, pm2)
                rmk = rec.op("vector", lambda h, n=n, hh=hh: h.scalar_tensor_tensor(
                    out=Ssb[:, hh * DH:(hh + 1) * DH], in0=Ssb[:, hh * DH:(hh + 1) * DH],
                    scalar=dAe[:, n * NH + hh:n * NH + hh + 1], in1=psS2[:],
                    op0=ALU.mult, op1=ALU.add), waits=[(pe, pm2), seed2, bc_done], inc=cv)
                ps2_free = (cv, rmk)
                sbm = rec.op("vector", lambda h, hh=hh: h.tensor_copy(
                    out=Sbf[:, hh * DH:(hh + 1) * DH], in_=Ssb[:, hh * DH:(hh + 1) * DH]),
                    waits=[(cv, rmk)], inc=cv)
                srm[hh] = (cv, sbm)

    for n in range(NTT):
        do_chunk(n)
        xch2_last[n % 2] = (pe, pe.n)
    y_done = (cv, cv.n)
    if DBG_L1STOP <= 9:
        rec.barrier(ctx.chans)
        return
    rec.barrier(ctx.chans)
    yl_scope.close()
    xt_scope.close()
    scope3.close()

    zg_scope = ExitStack()
    zs = ctx.sb("m_zs", [128, NTT, DIN], BF16, scope=zg_scope)
    wib = ctx.sb("m_wib", [128, DIN], F32, scope=zg_scope)
    wirow = ctx.sb("m_wirow", [1, DIN], F32, scope=zg_scope)
    gst = ctx.sb("m_gst", [128, 40], F32, scope=zg_scope)
    gtmp = ctx.sb("m_gtmp", [128, 2, DIN], F32, scope=zg_scope)
    pst_z = proj_state(ctx, zg_scope, "m_z", nps=4, nk=NDC, cw=512, ksplit=2)
    psw = pst_z["pss"][0]

    def zevac(rec, o, tt, ps, pm):
        m = rec.op("scalar", lambda h, ps=ps, o=o, tt=tt: h.activation(
            out=zs[:, tt, o * 512:(o + 1) * 512], in_=ps, func=AF.Silu), waits=[(pe, pm)], inc=ca)
        return (ca, m)
    emit_proj(ctx, rec, xn3T, W["W_inT"], DIN, make_off_evac(zevac, 0), tag="m_zp0",
              ready=xn3_ready, pst=pst_z, consts=C, pe=pe, wld=wld, ksplit=2, ntt=4, t0=0)
    emit_proj(ctx, rec, xn3T, W["W_inT"], DIN, make_off_evac(zevac, 4), tag="m_zp1",
              ready=xn3_ready, pst=pst_z, consts=C, pe=pe, wld=wld, ksplit=2, ntt=4, t0=512)
    z_done = (ca, ca.n)
    wm0 = rec.op("sync", lambda h: h.dma_start(out=wirow[:], in_=W["inner_w_row"]), inc=cst, k=16)
    wbm = None
    for g6 in range(6):
        wbm = rec.op("tensor", lambda h, g6=g6: h.matmul(
            out=psw[:, 0:512], lhsT=C["onesf"][0:1, 0:128], rhs=wirow[0:1, g6 * 512:(g6 + 1) * 512],
            start=True, stop=True), waits=[(cst, wm0), z_done] if g6 == 0 else [(ca, wbm)], inc=pe)
        wbm = rec.op("scalar", lambda h, g6=g6: h.activation(
            out=wib[:, g6 * 512:(g6 + 1) * 512], in_=psw[:, 0:512], func=AF.Copy),
            waits=[(pe, wbm)], inc=ca)
    wib_done = (ca, wbm)
    gt_free = [None, None]
    ym_prev = 0
    for tt in range(NTT):
        sl = tt % 2
        g1 = None
        for g6 in range(6):
            g1 = rec.op("vector", lambda h, tt=tt, g6=g6: h.bn_stats(
                out=gst[:, g6 * 6:(g6 + 1) * 6], in_=Ytok[:, tt, g6 * 512:(g6 + 1) * 512]),
                waits=([y_done, (cv, ym_prev)] if g6 == 0 else [(cv, g1)]), inc=cv)
        g2 = rec.op("vector", lambda h: h.bn_aggr(out=gst[:, 36:38], in_=gst[:, 0:36]),
                    waits=[(cv, g1)], inc=cv)
        g3 = rec.op("vector", lambda h: h.tensor_scalar(out=gst[:, 38:39], in0=gst[:, 36:37],
                                                        scalar1=gst[:, 36:37], scalar2=None,
                                                        op0=ALU.mult), waits=[(cv, g2)], inc=cv)
        g4 = rec.op("vector", lambda h: h.tensor_tensor(out=gst[:, 38:39], in0=gst[:, 38:39],
                                                        in1=gst[:, 37:38], op=ALU.add),
                    waits=[(cv, g3)], inc=cv)
        g5 = rec.op("vector", lambda h: h.tensor_scalar(out=gst[:, 38:39], in0=gst[:, 38:39],
                                                        scalar1=1e-5, scalar2=None, op0=ALU.add),
                    waits=[(cv, g4)], inc=cv)
        g6m = rec.op("scalar", lambda h: h.activation(out=gst[:, 39:40], in_=gst[:, 38:39],
                                                      func=AF.Sqrt), waits=[(cv, g5)], inc=ca)
        g7 = rec.op("vector", lambda h: h.reciprocal(out=gst[:, 38:39], in_=gst[:, 39:40]),
                    waits=[(ca, g6m)], inc=cv)
        wsg = [(cv, g7)] + ([gt_free[sl]] if gt_free[sl] else [])
        g8 = rec.op("vector", lambda h, tt=tt, sl=sl: h.tensor_scalar(
            out=gtmp[:, sl, :], in0=Ytok[:, tt, :], scalar1=gst[:, 38:39], scalar2=None,
            op0=ALU.mult), waits=wsg, inc=cv)
        g9 = rec.op("vector", lambda h, sl=sl: h.tensor_tensor(
            out=gtmp[:, sl, :], in0=gtmp[:, sl, :], in1=wib[:], op=ALU.mult),
            waits=[(cv, g8), wib_done], inc=cv)
        g10 = rec.op("vector", lambda h, tt=tt, sl=sl: h.tensor_tensor(
            out=gtmp[:, sl, :], in0=gtmp[:, sl, :], in1=zs[:, tt, :], op=ALU.mult),
            waits=[(cv, g9), z_done], inc=cv)
        g11 = rec.op("vector", lambda h, tt=tt, sl=sl: h.tensor_copy(
            out=zs[:, tt, :], in_=gtmp[:, sl, :]), waits=[(cv, g10)], inc=cv)
        ym_prev = g11
        gt_free[sl] = (cv, g11)
        rec.op("sync", lambda h, tt=tt, sl=sl: h.dma_start(
            out=io["scr_big"][tt * 128:(tt + 1) * 128, 0:DIN], in_=zs[:, tt, :]),
            waits=[(cv, g11)], inc=cw, k=16)
    yg_st = cw.n
    if DBG_L1STOP <= 10:
        rec.barrier(ctx.chans)
        return
    rec.barrier(ctx.chans)
    zg_scope.close()
    yt_scope.close()
    scope2.close()

    fin_scope = ExitStack()
    ygT = ctx.sb("m_ygT", [128, NCC, TOK], BF16, scope=fin_scope)
    of32 = ctx.sb("m_of32", [128, NTT, DM], F32, scope=fin_scope)
    pst_o = proj_state(ctx, fin_scope, "m_o", nps=4, nk=NCC, cw=512, ksplit=2)
    for cc in range(NCC):
        rec.op("sync", lambda h, cc=cc: h.dma_start_transpose(
            out=ygT[:, cc, :], in_=io["scr_big"][0:TOK, cc * 128:(cc + 1) * 128]),
            waits=[(cw, yg_st)], inc=trm, k=16)
    ygT_ready = (trm, trm.n)

    def oevac(rec, o, tt, ps, pm):
        m = rec.op("scalar", lambda h, ps=ps, o=o, tt=tt: h.activation(
            out=of32[:, tt, o * 512:(o + 1) * 512], in_=ps, func=AF.Copy), waits=[(pe, pm)], inc=ca)
        return (ca, m)
    emit_proj(ctx, rec, ygT, W["out_projT"], DM, make_off_evac(oevac, 0), tag="m_op0",
              ready=ygT_ready, pst=pst_o, consts=C, pe=pe, wld=wld, ksplit=2, ntt=4, t0=0)
    emit_proj(ctx, rec, ygT, W["out_projT"], DM, make_off_evac(oevac, 4), tag="m_op1",
              ready=ygT_ready, pst=pst_o, consts=C, pe=pe, wld=wld, ksplit=2, ntt=4, t0=512)
    odone = (ca, ca.n)
    emit_ln_residual_store(ctx, rec, of32, io["src"], io["dst"], odone, tag="m_fin",
                           scope=fin_scope, pre_add_hbm=io["scr_u"])
    rec.barrier(ctx.chans)
    fin_scope.close()
    blk.close()



NSP = 20480
NJC = NSP // 128
NJRB = 10


def emit_l1_ffn(ctx, rec, io, W, C):
    nc = ctx.nc
    cv = ctx.chan("q_v")
    ca = ctx.chan("q_a")
    cw = ctx.chan("q_sw")
    pe = ctx.chan("q_pe")
    wld = ctx.chan("q_wl")
    cst = ctx.chan("q_cst")

    blk = ExitStack()
    of32 = ctx.sb("q_of32", [128, NTT, DM], F32, side="right", scope=blk)
    cwT = ctx.sb("q_cwT", [8, TOK], BF16, side="right", scope=blk)
    lgts = ctx.sb("q_lg", [128, NTT, 8], F32, side="right", scope=blk)
    lst = ctx.sb("q_lst", [128, 24], F32, side="right", scope=blk)
    scope2 = ExitStack()
    xn4T = ctx.sb("q_xnT", [128, NDC, TOK], BF16, side="right", scope=scope2)

    trc0, trm0 = emit_ln_phase(ctx, rec, io["src"], io["scr_a"], xn4T, kind="ln", tag="q_ln",
                               scope=scope2)
    ready = (trc0, trm0)
    for k in range(NDC):
        rec.op("sync", lambda h, k=k: h.dma_start(
            out=io["scr_m"][k * 128:(k + 1) * 128, 0:TOK], in_=xn4T[:, k, :]),
            waits=[ready], inc=cw, k=16)
    gsrc_ready = cw.n
    if DBG_L1FSTOP <= 1:
        rec.barrier(ctx.chans)
        return

    wp_scope = ExitStack()
    pst_w = proj_state(ctx, wp_scope, "q_w", nps=4, nk=NDC, cw=512)

    def wevac(rec, o, tt, ps, pm):
        m = rec.op("scalar", lambda h, ps=ps, o=o, tt=tt: h.activation(
            out=of32[:, tt, o * 512:(o + 1) * 512], in_=ps, func=AF.Copy), waits=[(pe, pm)], inc=ca)
        return (ca, m)
    emit_proj(ctx, rec, xn4T, W["WT"], DM, wevac, tag="q_wp", ready=ready, pst=pst_w,
              consts=C, pe=pe, wld=wld)
    w_done = (ca, ca.n)
    if DBG_L1FSTOP <= 2:
        rec.barrier(ctx.chans)
        return

    gwb = ctx.sb("q_gw", [128, NDC, 8], BF16, scope=wp_scope)
    wldg = ctx.chan("q_wlg")
    gm0 = rec.op("gpsimd", lambda h: h.dma_start(
        out=gwb[:], in_=W["gate_wT"].rearrange("(k p) n -> p k n", p=128)), inc=wldg, k=16)
    psl = pst_w["pss"][0]
    lg_ev = None
    for tt in range(NTT):
        mark = None
        for k in range(NDC):
            w0 = [(wldg, gm0), ready, w_done] + ([(cv, lg_ev)] if lg_ev else [])
            mark = rec.op("tensor", lambda h, k=k, tt=tt: h.matmul(
                out=psl[:, 0:8], lhsT=xn4T[:, k, tt * 128:(tt + 1) * 128], rhs=gwb[:, k, :],
                start=(k == 0), stop=(k == NDC - 1)),
                waits=w0 if k == 0 else (), inc=pe if k == NDC - 1 else None)
        lg_ev = rec.op("vector", lambda h, tt=tt: h.tensor_copy(out=lgts[:, tt, :], in_=psl[:, 0:8]),
                       waits=[(pe, mark)] + ([(pe, mark)] if lg_ev is None else [(cv, lg_ev)]), inc=cv)
    if DBG_L1FSTOP <= 3:
        rec.barrier(ctx.chans)
        return
    c1m = rec.op("vector", lambda h: h.tensor_scalar(
        out=lgts[:].rearrange("p a b -> p (a b)"), in0=lgts[:].rearrange("p a b -> p (a b)"),
        scalar1=50.0, scalar2=-50.0, op0=ALU.min, op1=ALU.max), waits=[(cv, lg_ev)], inc=cv)
    cwf = ctx.sb("q_cwf", [128, NTT, 8], F32, scope=wp_scope)
    mx8 = ctx.sb("q_mx8", [128, 8], F32, scope=wp_scope)
    cw_ev = None
    for tt in range(NTT):
        m1 = rec.op("vector", lambda h, tt=tt: h.max(out=mx8[:], in_=lgts[:, tt, :]),
                    waits=[(cv, c1m)] + ([(cv, cw_ev)] if cw_ev else []), inc=cv)
        m2 = rec.op("vector", lambda h: h.tensor_scalar(
            out=lst[:, 0:1], in0=mx8[:, 0:1], scalar1=mx8[:, 1:2], scalar2=None, op0=ALU.subtract),
            waits=[(cv, m1)], inc=cv)
        m3 = rec.op("scalar", lambda h: h.activation(out=lst[:, 1:2], in_=lst[:, 0:1],
                                                     func=AF.Sigmoid), waits=[(cv, m2)], inc=ca)
        m4 = rec.op("vector", lambda h: h.tensor_scalar(
            out=lst[:, 2:3], in0=lst[:, 1:2], scalar1=-1.0, scalar2=1.0, op0=ALU.mult, op1=ALU.add),
            waits=[(ca, m3)], inc=cv)
        m5 = rec.op("vector", lambda h, tt=tt: h.tensor_scalar(
            out=cwf[:, tt, :], in0=lgts[:, tt, :], scalar1=mx8[:, 0:1], scalar2=None,
            op0=ALU.is_equal), waits=[(cv, m4)], inc=cv)
        m6 = rec.op("vector", lambda h, tt=tt: h.tensor_scalar(
            out=cwf[:, tt, :], in0=cwf[:, tt, :], scalar1=lst[:, 1:2], scalar2=None, op0=ALU.mult),
            waits=[(cv, m5)], inc=cv)
        m7 = rec.op("vector", lambda h, tt=tt: h.tensor_scalar(
            out=mx8[:, 0:8], in0=lgts[:, tt, :], scalar1=mx8[:, 1:2], scalar2=None,
            op0=ALU.is_equal), waits=[(cv, m6)], inc=cv)
        m8 = rec.op("vector", lambda h, tt=tt: h.scalar_tensor_tensor(
            out=cwf[:, tt, :], in0=mx8[:, 0:8], scalar=lst[:, 2:3], in1=cwf[:, tt, :],
            op0=ALU.mult, op1=ALU.add), waits=[(cv, m7)], inc=cv)
        cw_ev = m8
    psq = pst_w["pss"][1]
    ct_ev = None
    pe_tr_prev = None
    for tt in range(NTT):
        wst = [(cv, cw_ev)] + ([ct_ev] if ct_ev else [])
        t1 = rec.op("tensor", lambda h, tt=tt: h.transpose(
            out=psq[0:8, 0:128], in_=cwf[:, tt, :], identity=C["eyef"][:]), waits=wst, inc=pe)
        t2 = rec.op("scalar", lambda h, tt=tt: h.activation(
            out=cwT[0:8, tt * 128:(tt + 1) * 128], in_=psq[0:8, 0:128], func=AF.Copy),
            waits=[(pe, t1)], inc=ca)
        ct_ev = (ca, t2)
    cwT_done = (ca, ca.n)
    if DBG_L1FSTOP <= 4:
        rec.barrier(ctx.chans)
        return

    wgt = ctx.sb("q_wgt", [8, NSP], BF16, side="right", scope=scope2)
    eawb = ctx.sb("q_eaw", [8, 64], F32, scope=wp_scope)
    eawt = ctx.sb("q_eawt", [8, 72], F32, scope=wp_scope)
    PTb = ctx.sb("q_PT", [64, 8], BF16, scope=wp_scope)
    em0 = rec.op("sync", lambda h: h.dma_start(out=eawb[:], in_=W["eaw"]), inc=cst, k=16)
    e1 = rec.op("vector", lambda h: h.tensor_reduce(out=eawt[0:8, 64:65], in_=eawb[0:8, :],
                                                    axis=mybir.AxisListType.X, op=ALU.max),
                waits=[(cst, em0)], inc=cv)
    e2 = rec.op("vector", lambda h: h.tensor_scalar(out=eawt[0:8, 0:64], in0=eawb[0:8, :],
                                                    scalar1=eawt[0:8, 64:65], scalar2=None,
                                                    op0=ALU.subtract), waits=[(cv, e1)], inc=cv)
    e3 = rec.op("scalar", lambda h: h.activation(out=eawt[0:8, 0:64], in_=eawt[0:8, 0:64],
                                                 func=AF.Exp), waits=[(cv, e2)], inc=ca)
    e4 = rec.op("vector", lambda h: h.tensor_reduce(out=eawt[0:8, 65:66], in_=eawt[0:8, 0:64],
                                                    axis=mybir.AxisListType.X, op=ALU.add),
                waits=[(ca, e3)], inc=cv)
    e5 = rec.op("vector", lambda h: h.reciprocal(out=eawt[0:8, 66:67], in_=eawt[0:8, 65:66]),
                waits=[(cv, e4)], inc=cv)
    e6 = rec.op("vector", lambda h: h.tensor_scalar(out=eawt[0:8, 0:64], in0=eawt[0:8, 0:64],
                                                    scalar1=eawt[0:8, 66:67], scalar2=None,
                                                    op0=ALU.mult), waits=[(cv, e5)], inc=cv)
    t3 = rec.op("tensor", lambda h: h.transpose(out=psq[0:64, 0:8], in_=eawt[0:8, 0:64],
                                                identity=C["eyef"][0:8, 0:8]),
                waits=[(cv, e6), cwT_done], inc=pe)
    t4 = rec.op("scalar", lambda h: h.activation(out=PTb[:], in_=psq[0:64, 0:8], func=AF.Copy),
                waits=[(pe, t3)], inc=ca)
    atb = ctx.sb("q_atb", [64, 2, 512], BF16, scope=wp_scope)
    imb = ctx.sb("q_imb", [8, 2, 512], F32, scope=wp_scope)
    sgb = ctx.sb("q_sgb", [8, 2, 512], F32, scope=wp_scope)
    psd = pst_w["pss"][2]
    at_free = [None, None]
    im_free = [None, None]
    wg_prev = None
    for jc in range(NSP // 512):
        sl = jc % 2
        wa = [(ca, t4)] + ([at_free[sl]] if at_free[sl] else [])
        am = rec.op("sync", lambda h, jc=jc, sl=sl: h.dma_start(
            out=atb[:, sl, :], in_=W["atoms_p"][:, jc * 512:(jc + 1) * 512]),
            waits=wa + ctx.wwait(W["atoms_p"]), inc=wld, k=16)
        im = rec.op("sync", lambda h, jc=jc, sl=sl: h.dma_start(
            out=imb[0:8, sl, :], in_=W["imp_p"][:, jc * 512:(jc + 1) * 512]),
            waits=[im_free[sl]] if im_free[sl] else [], inc=cst, k=16)
        dm = rec.op("tensor", lambda h, sl=sl: h.matmul(
            out=psd[0:8, 0:512], lhsT=PTb[:], rhs=atb[:, sl, :], start=True, stop=True),
            waits=[(wld, am), (ca, t4)] + ([wg_prev] if wg_prev else []), inc=pe)
        sg = rec.op("scalar", lambda h, sl=sl: h.activation(
            out=sgb[0:8, sl, :], in_=imb[0:8, sl, :], func=AF.Sigmoid), waits=[(cst, im)], inc=ca)
        wg = rec.op("vector", lambda h, jc=jc, sl=sl: h.scalar_tensor_tensor(
            out=wgt[0:8, jc * 512:(jc + 1) * 512], in0=psd[0:8, 0:512], scalar=1.0,
            in1=sgb[0:8, sl, :], op0=ALU.mult, op1=ALU.mult), waits=[(pe, dm), (ca, sg)], inc=cv)
        at_free[sl] = (pe, dm)
        im_free[sl] = (ca, sg)
        wg_prev = (cv, wg)
    wgt_done = (cv, cv.n)
    rec.barrier(ctx.chans)
    wp_scope.close()
    if DBG_L1FSTOP <= 5:
        rec.barrier(ctx.chans)
        return

    sc_scope = ExitStack()
    gidx = ctx.sb("q_gidx", [128, NJC], I32, scope=sc_scope)
    Rb = ctx.sb("q_Rb", [128, 2, NJRB, 128], BF16, scope=sc_scope)
    xg = ctx.sb("q_xg", [128, 4, TOK], BF16, scope=sc_scope)
    Gb = ctx.sb("q_Gb", [128, 4, TOK], BF16, scope=sc_scope)
    psV = [ctx.ps("q_psV", [128, 512], F32, scope=sc_scope) for _ in range(2)]
    psO = [ctx.ps("q_psO", [128, 512], F32, scope=sc_scope) for _ in range(4)]
    gi0 = rec.op("sync", lambda h: h.dma_start(out=gidx[:], in_=W["gidx"]), inc=cst, k=16)
    rb_slab_free = [None, None]
    xg_free = [None] * 4
    G_free = [None] * 4
    psV_free = [None, None]
    psO_ev = [None] * 4
    sc_done = []
    for rb in range(16):
        rsl = rb % 2
        wrb = [rb_slab_free[rsl]] if rb_slab_free[rsl] else []
        rb0 = rec.op("gpsimd", lambda h, rb=rb, rsl=rsl: h.dma_start(
            out=Rb[:, rsl, :, :],
            in_=W["Rb"][rb * NJRB * 128:(rb + 1) * NJRB * 128, :].rearrange("(j p) r -> p j r", p=128)),
            waits=wrb, inc=wldg, k=16)
        pmO = [None] * 8
        for jj in range(NJRB):
            j = rb * NJRB + jj
            sl = j % 4
            wgx = [(cst, gi0), (cw, gsrc_ready)] + ([xg_free[sl]] if xg_free[sl] else [])
            gx = rec.op("gpsimd", lambda h, j=j, sl=sl: h.indirect_dma_start(
                out=xg[:, sl, :], out_offset=None, in_=io["scr_m"][0:DM, 0:TOK],
                in_offset=bass.IndirectOffsetOnAxis(ap=gidx[:, j:j + 1], axis=0)),
                waits=wgx, inc=wldg, k=16)
            for ts in range(2):
                pv2 = j % 2
                wv = [wgt_done, cwT_done] + ([psV_free[pv2]] if psV_free[pv2] else [])
                vm = rec.op("tensor", lambda h, j=j, ts=ts, pv2=pv2: h.matmul(
                    out=psV[pv2][:], lhsT=wgt[0:8, j * 128:(j + 1) * 128],
                    rhs=cwT[0:8, ts * 512:(ts + 1) * 512], start=True, stop=True),
                    waits=wv, inc=pe)
                gm = rec.op("vector", lambda h, j=j, ts=ts, sl=sl, pv2=pv2: h.tensor_tensor(
                    out=Gb[:, sl, ts * 512:(ts + 1) * 512], in0=xg[:, sl, ts * 512:(ts + 1) * 512],
                    in1=psV[pv2][:], op=ALU.mult), waits=[(wldg, gx), (pe, vm)] +
                    ([G_free[sl]] if G_free[sl] and ts == 0 else []), inc=cv)
                psV_free[pv2] = (cv, gm)
                for t4i in range(4):
                    if DBG_NOSCAT:
                        continue
                    tt = ts * 4 + t4i
                    first = (jj == 0)
                    last = (jj == NJRB - 1)
                    wsc = [(cv, gm), (wldg, rb0)]
                    if first and ts == 0 and psO_ev[t4i] is not None:
                        wsc.append(psO_ev[t4i])
                    needs_inc = last or (ts == 1 and t4i == 3)
                    pmO[tt] = rec.op("tensor", lambda h, j=j, tt=tt, t4i=t4i, ts=ts, sl=sl,
                                     first=first, last=last: h.matmul(
                        out=psO[t4i][:, ts * 128:ts * 128 + 128],
                        lhsT=Gb[:, sl, tt * 128:(tt + 1) * 128], rhs=Rb[:, rsl, jj, :],
                        start=(first and ts == 0), stop=last),
                        waits=wsc, inc=pe if needs_inc else None)
            G_free[sl] = (pe, pmO[7]) if not DBG_NOSCAT else (cv, gm)
            xg_free[sl] = (cv, gm)
        for tt in range(8):
            if DBG_NOSCAT or DBG_NOEVAC:
                continue
            bank = tt % 4
            ts = tt // 4
            evm = rec.op("vector", lambda h, tt=tt, bank=bank, ts=ts, rb=rb: h.tensor_tensor(
                out=of32[:, tt, rb * 128:(rb + 1) * 128], in0=of32[:, tt, rb * 128:(rb + 1) * 128],
                in1=psO[bank][:, ts * 128:ts * 128 + 128], op=ALU.add),
                waits=[(pe, pmO[4 + bank]), w_done], inc=cv)
            psO_ev[bank] = (cv, evm)
            sc_done.append(evm)
        rb_slab_free[rsl] = (pe, pmO[7])
    scat_done = (cv, cv.n)
    if DBG_L1FSTOP <= 6:
        rec.barrier(ctx.chans)
        return
    rec.barrier(ctx.chans)
    sc_scope.close()
    scope2.close()

    emit_ln_residual_store(ctx, rec, of32, io["src"], io["dst"], scat_done, tag="q_fin", scope=blk)
    rec.barrier(ctx.chans)
    blk.close()



L0A_WEIGHTS = {
    "Wq": [DM, DM], "Wk": [DM, DM], "Wv": [DM, DM], "Wo": [DM, DM],
    "l0_biases": [1, 4 * DM],
}
L0F_WEIGHTS = {
    "w_gate": [DM, 8192], "w_up": [DM, 8192], "w_down": [8192, DM],
}
L1F_WEIGHTS_BASE = {
    "WT": [DM, DM], "gate_wT": [DM, 8], "eaw": [8, 64],
}
L1A_WEIGHTS = {
    "W_inT": [DM, 6448], "out_projT": [3072, DM], "conv_w": [3072, 4],
    "conv_b_c": [128, 24], "dt_bias_r": [48, 1], "A_log_c": [48, 1], "D_row": [1, 48],
    "inner_w_row": [1, 3072], "norm_w_c": [128, 16], "conv_state": [3072, 3],
    "ssm0_t": [128, 3072], "c_hsel": [48, 48 * 128], "c_T0": [128, 128], "c_eyef": [128, 128],
}


def build_program(blocks, dbg=False):
    nc = bass.Bass("TRN2", target_bir_lowering=False)
    es = ExitStack()
    xin = nc.dram_tensor("xin", [TOK, DM], F32, kind="ExternalInput").ap()
    xout = nc.dram_tensor("xout", [TOK, DM], F32, kind="ExternalOutput").ap()
    scr_a = nc.dram_tensor("scr_a", [TOK, DM], BF16, kind="Internal").ap()
    scr_b = nc.dram_tensor("scr_b", [TOK, DM], BF16, kind="Internal").ap()
    xmid = [nc.dram_tensor(f"xmid{i}", [TOK, DM], F32, kind="Internal").ap()
            for i in range(len(blocks) - 1)]
    sh_l0 = nc.dram_tensor("sh_l0", [128, 16 * 129], BF16, addr_space="Shared", kind="Internal").ap()
    sh_l0f = nc.dram_tensor("sh_l0f", [1, 4], I32, addr_space="Shared", kind="Internal").ap()

    scr_big = nc.dram_tensor("scr_big", [TOK, 8192], BF16, kind="Internal").ap()
    scr_m = nc.dram_tensor("scr_m", [3328, TOK], BF16, kind="Internal").ap()
    scr_u = nc.dram_tensor("scr_u", [TOK, DM], F32, kind="Internal").ap()
    scr_small = nc.dram_tensor("scr_small", [64, 64], F32, kind="Internal").ap()
    sh_l1t = nc.dram_tensor("sh_l1t", [128, 128], BF16, addr_space="Shared", kind="Internal").ap()
    sh_l1tf = nc.dram_tensor("sh_l1tf", [1, 4], I32, addr_space="Shared", kind="Internal").ap()
    sh_l1s = nc.dram_tensor("sh_l1s", [128, 3072], BF16, addr_space="Shared", kind="Internal").ap()
    sh_l1sf = nc.dram_tensor("sh_l1sf", [1, 4], I32, addr_space="Shared", kind="Internal").ap()
    wext = {}
    need = {}
    if "l0a" in blocks:
        need.update(L0A_WEIGHTS)
    if "l0f" in blocks:
        need.update(L0F_WEIGHTS)
    if "l1a" in blocks:
        need.update(L1A_WEIGHTS)
    if "l1f" in blocks:
        need.update(L1F_WEIGHTS_BASE)
        need["atoms_p"] = [64, NSP]
        need["imp_p"] = [8, NSP]
        need["gidx"] = [128, NSP // 128]
    for name, shape in need.items():
        dt = F32
        if name == "gidx":
            dt = I32
        wext[name] = nc.dram_tensor(name, shape, dt, kind="ExternalInput").ap()
    if "l1f" in blocks:
        wext["Rb"] = nc.dram_tensor("Rb", [NSP, 128], BF16, kind="ExternalInput").ap()
    CAST_NAMES = [n for n in ("Wk", "Wv", "Wq", "Wo", "w_gate", "w_up", "w_down",
                              "W_inT", "out_projT", "WT", "atoms_p") if n in wext]
    wtwin = {}
    for n in CAST_NAMES:
        wtwin[n] = nc.dram_tensor(n + "_bf", need[n], BF16, kind="Internal").ap()
    c_par = nc.dram_tensor("c_par", [1, 4], I32, kind="ExternalInput").ap()
    c_nonce = nc.dram_tensor("c_nonce", [1, 4], I32, kind="ExternalInput").ap()
    c_ones = nc.dram_tensor("c_ones", [1, 128], F32, kind="ExternalInput").ap()
    c_onescol = nc.dram_tensor("c_onescol", [128, 1], F32, kind="ExternalInput").ap()
    c_triu = nc.dram_tensor("c_triu", [128, 128], F32, kind="ExternalInput").ap()
    c_eye = nc.dram_tensor("c_eye", [128, 128], F32, kind="ExternalInput").ap()

    with nc.Block() as block:
        ctx = Ctx(nc, es, block)
        rec = Rec(ctx)
        C = {
            "par_sb": ctx.sb("c_par", [1, 4], I32),
            "nonce_sb": ctx.sb("c_nonce", [1, 4], I32),
            "flag_sb": ctx.sb("c_flag", [1, 4], I32),
            "ones1": ctx.sb("c_ones1", [1, 128], BF16),
            "onescol": ctx.sb("c_onescol", [128, 1], BF16),
            "triU": ctx.sb("c_triu", [128, 128], F32),
            "onesf": ctx.sb("c_onesf", [1, 128], F32),
            "eyef": ctx.sb("c_eyeg", [128, 128], F32),
        }
        C["nonce_i_sb"] = C["nonce_sb"]
        cst = ctx.chan("cst")
        rec.op("sync", lambda h: h.dma_start(out=C["par_sb"][:], in_=c_par[:]), inc=cst, k=16)
        rec.op("sync", lambda h: h.dma_start(out=C["nonce_sb"][:], in_=c_nonce[:]), inc=cst, k=16)
        gq = ctx.chan("cstg")
        rec.op("gpsimd", lambda h: h.dma_start(out=C["ones1"][:], in_=c_ones[:]), inc=gq, k=16)
        rec.op("gpsimd", lambda h: h.dma_start(out=C["onescol"][:], in_=c_onescol[:]), inc=gq, k=16)
        rec.op("sync", lambda h: h.dma_start(out=C["triU"][:], in_=c_triu[:]), inc=cst, k=16)
        rec.op("sync", lambda h: h.dma_start(out=C["onesf"][:], in_=c_ones[:]), inc=cst, k=16)
        rec.op("sync", lambda h: h.dma_start(out=C["eyef"][:], in_=c_eye[:]), inc=cst, k=16)
        rec.barrier([cst, ctx.chan("cstg")])
        wcst = ctx.chan("wcst")
        ctx.apmark = {}
        for n in CAST_NAMES:
            mm_ = rec.op("gpsimd", lambda h, dst=wtwin[n], srcp=wext[n]: h.dma_start(
                out=dst[:], in_=srcp[:]), inc=wcst, k=16)
            ctx.apmark[id(wtwin[n])] = (wcst, mm_)
            wext[n] = wtwin[n]

        cur = xin
        for j, b in enumerate(blocks):
            dst = xout if j == len(blocks) - 1 else xmid[j]
            io = {"src": cur, "dst": dst, "scr_a": scr_a, "scr_b": scr_b, "scr_big": scr_big,
                  "scr_m": scr_m, "scr_u": scr_u, "scr_small": scr_small,
                  "sh_l0": sh_l0, "sh_l0f": sh_l0f, "sh_l1t": sh_l1t, "sh_l1tf": sh_l1tf,
                  "sh_l1s": sh_l1s, "sh_l1sf": sh_l1sf}
            if b == "l0a":
                emit_l0_attn(ctx, rec, io, wext, C)
            elif b == "l0f":
                emit_l0_ffn(ctx, rec, io, wext, C)
            elif b == "l1a":
                emit_l1_attn(ctx, rec, io, wext, C)
            elif b == "l1f":
                emit_l1_ffn(ctx, rec, io, wext, C)
            else:
                raise NotImplementedError(b)
            cur = dst

        if dbg:
            dbg_hs = nc.dram_tensor("dbg_hs", [TOK, 8192], BF16, kind="ExternalOutput").ap()
            dbg_xn = nc.dram_tensor("dbg_xn", [TOK, DM], BF16, kind="ExternalOutput").ap()
            dch = ctx.chan("dbg")
            rec.op("sync", lambda h: h.dma_start(out=dbg_hs[:], in_=scr_big[:]), inc=dch, k=16)
            rec.op("sync", lambda h: h.dma_start(out=dbg_xn[:], in_=scr_a[:]), inc=dch, k=16)
            if DBG_XNT is not None:
                dbg_xnT = nc.dram_tensor("dbg_xnT", [128, NDC * TOK], BF16, kind="ExternalOutput").ap()
                rec.op("sync", lambda h: h.dma_start(out=dbg_xnT[:],
                                                     in_=DBG_XNT[0][:].rearrange("p a b -> p (a b)")),
                       inc=dch, k=16)
        rec.barrier(ctx.chans)
        rec.flush()
    return nc


def host_consts():
    tri = np.triu(np.ones((128, 128), np.float32))
    return {
        "c_ones": np.ones((1, 128), np.float32),
        "c_onescol": np.ones((128, 1), np.float32),
        "c_triu": tri,
        "c_eye": np.eye(128, dtype=np.float32),
    }


def run_blocks(blocks, inputs, x_full, dbg=False):
    consts = host_consts()
    nonce = np.random.randint(1, 2**31 - 1, size=(1, 4)).astype(np.int32)
    in_maps = []
    need = {}
    if "l0f" in blocks:
        need["w_gate"] = np.ascontiguousarray(inputs["L0_ffn_w_gate"])
        need["w_up"] = np.ascontiguousarray(inputs["L0_ffn_w_up"])
        need["w_down"] = np.ascontiguousarray(inputs["L0_ffn_w_down"])
    if "l1f" in blocks:
        import ml_dtypes as _md
        mi_ = np.asarray(inputs["L1_ffn_mask_idx"]).astype(np.int64)
        rows_ = mi_ // DM
        cols_ = mi_ % DM
        global NSP, NJC, NJRB
        cnt = np.bincount(rows_ // 128, minlength=16)
        NJRB = max(10, int(np.ceil(cnt.max() / 128.0)))
        NSP = 16 * NJRB * 128
        NJC = NSP // 128
        gidx_ = np.zeros((16, NJRB * 128), np.int32)
        Rbm_ = np.zeros((16, NJRB * 128, 128), np.float32)
        srcx = np.full((16, NJRB * 128), -1, np.int64)
        for rb in range(16):
            sel = np.nonzero((rows_ >= rb * 128) & (rows_ < (rb + 1) * 128))[0]
            srcx[rb, :len(sel)] = sel
            gidx_[rb, :len(sel)] = cols_[sel]
            Rbm_[rb, np.arange(len(sel)), rows_[sel] - rb * 128] = 1.0
        flat = srcx.reshape(-1)
        valid = flat >= 0
        atoms_p = np.zeros((64, NSP), np.float32)
        atoms_p[:, valid] = np.asarray(inputs["L1_ffn_atoms"], np.float32)[:, flat[valid]]
        imp_p = np.zeros((8, NSP), np.float32)
        imp_p[:, valid] = np.asarray(inputs["L1_ffn_importance"], np.float32)[:, flat[valid]]
        need["atoms_p"] = atoms_p
        need["imp_p"] = imp_p
        need["gidx"] = np.ascontiguousarray(gidx_.reshape(NJC, 128).T)
        need["Rb"] = Rbm_.reshape(NSP, 128).astype(_md.bfloat16)
        need["WT"] = np.ascontiguousarray(np.asarray(inputs["L1_ffn_W"], np.float32).T)
        need["gate_wT"] = np.ascontiguousarray(np.asarray(inputs["L1_ffn_gate_w"], np.float32).T)
        need["eaw"] = np.ascontiguousarray(inputs["L1_ffn_expert_atom_weights"], np.float32)
    if "l1a" in blocks:
        NH = 48
        need["W_inT"] = np.ascontiguousarray(inputs["L1_attn_in_proj_w"].T.astype(np.float32))
        need["out_projT"] = np.ascontiguousarray(inputs["L1_attn_out_proj_w"].T.astype(np.float32))
        need["conv_w"] = np.ascontiguousarray(inputs["L1_attn_conv_w"], np.float32)
        need["conv_b_c"] = np.ascontiguousarray(inputs["L1_attn_conv_b"].reshape(24, 128).T, np.float32)
        need["dt_bias_r"] = np.ascontiguousarray(inputs["L1_attn_dt_bias"].reshape(48, 1), np.float32)
        need["A_log_c"] = np.ascontiguousarray(inputs["L1_attn_A_log"].reshape(48, 1), np.float32)
        need["D_row"] = np.ascontiguousarray(inputs["L1_attn_D_param"].reshape(1, 48), np.float32)
        need["inner_w_row"] = np.ascontiguousarray(inputs["L1_attn_inner_norm_w"].reshape(1, 3072), np.float32)
        need["norm_w_c"] = np.ascontiguousarray(inputs["L1_attn_norm_w"].reshape(16, 128).T, np.float32)
        need["conv_state"] = np.ascontiguousarray(inputs["L1_attn_conv_state"][0], np.float32)
        need["ssm0_t"] = np.ascontiguousarray(
            np.transpose(np.asarray(inputs["L1_attn_ssm_state"], np.float32)[0], (1, 0, 2)).reshape(128, 3072))
        hsel = np.zeros((48, 48 * 128), np.float32)
        for hh in range(NH):
            hsel[hh, hh * 128:(hh + 1) * 128] = 1.0
        need["c_hsel"] = hsel
        tri = np.triu(np.ones((128, 128), np.float32))
        need["c_T0"] = np.where(tri > 0, 0.0, -1e9).astype(np.float32)
        need["c_eyef"] = np.eye(128, dtype=np.float32)
    if "l0a" in blocks:
        need["Wq"] = np.ascontiguousarray(inputs["L0_attn_Wq"])
        need["Wk"] = np.ascontiguousarray(inputs["L0_attn_Wk"])
        need["Wv"] = np.ascontiguousarray(inputs["L0_attn_Wv"])
        need["Wo"] = np.ascontiguousarray(inputs["L0_attn_Wo"])
        need["l0_biases"] = np.concatenate([
            inputs["L0_attn_bq"], inputs["L0_attn_bk"], inputs["L0_attn_bv"],
            inputs["L0_attn_bo"]]).reshape(1, 4 * DM).astype(np.float32)
    nc = build_program(blocks, dbg=dbg)
    for c in range(NCORE):
        b, hh = c // 2, c % 2
        m = dict(need)
        m["xin"] = np.ascontiguousarray(x_full[b, hh * TOK:(hh + 1) * TOK, :], np.float32)
        m["c_par"] = np.full((1, 4), hh, np.int32)
        m["c_nonce"] = nonce
        m.update(consts)
        in_maps.append(m)
    res = run_bass_kernel_spmd(nc, in_maps, core_ids=list(range(NCORE)))
    out = np.empty((4, 2048, DM), np.float32)
    for c in range(NCORE):
        b, hh = c // 2, c % 2
        out[b, hh * TOK:(hh + 1) * TOK, :] = res.results[c]["xout"]
    if dbg:
        return out, res
    return out


def kernel(**inputs):
    x = np.asarray(inputs["L0_attn_x"], np.float32)
    return run_blocks(["l0a", "l0f", "l1a", "l1f"], inputs, x)


# version 08
# speedup vs baseline: 1.1522x; 1.1122x over previous
import os
import numpy as np
import ml_dtypes
from contextlib import ExitStack

DBG_NOMASK = os.environ.get("DBG_NOMASK", "0") == "1"
DBG_L1STOP = int(os.environ.get("DBG_L1STOP", "99"))
DBG_L1FSTOP = int(os.environ.get("DBG_L1FSTOP", "99"))
DBG_NOSCAT = os.environ.get("DBG_NOSCAT", "0") == "1"
DBG_NOEVAC = os.environ.get("DBG_NOEVAC", "0") == "1"
DBG_XNT = None

import concourse.bass as bass
import concourse.mybir as mybir
from concourse.bass_utils import run_bass_kernel_spmd

F32 = mybir.dt.float32
BF16 = mybir.dt.bfloat16
I32 = mybir.dt.int32
AF = mybir.ActivationFunctionType
ALU = mybir.AluOpType
BT = ml_dtypes.bfloat16

NCORE = 8
TOK = 1024
DM = 2048
NTT = TOK // 128
NDC = DM // 128
MAGIC = 1234567891
EPS_LN = 1e-5

ENGS = ("sync", "vector", "scalar", "tensor", "gpsimd")


def ceil_div(a, b):
    return (a + b - 1) // b


class Ch:

    def __init__(self, es, nc, name):
        self.sem = es.enter_context(nc.semaphore(name))
        self.n = 0
        self._waited = {}

    def reserve(self, k):
        self.n += k
        return self.n

    def wait_emit(self, eng, v):
        if v <= 0:
            return
        key = id(eng)
        if self._waited.get(key, -1) >= v:
            return
        eng.wait_ge(self.sem, v)
        self._waited[key] = v


class Rec:

    def __init__(self, ctx):
        self.ctx = ctx
        self.q = {e: [] for e in ENGS}

    def op(self, eng, fn, waits=(), inc=None, k=1):
        ws = [(c, v) for (c, v) in waits]
        mark = None
        sem = None
        if inc is not None:
            mark = inc.reserve(k)
            sem = inc.sem
        kk = k

        def run(h):
            for c, v in ws:
                c.wait_emit(h, v)
            inst = fn(h)
            if sem is not None:
                inst.then_inc(sem, kk)
        self.q[eng].append(run)
        return mark

    def raw(self, eng, fn):
        self.q[eng].append(fn)

    def barrier(self, chans):
        ws = [(c, c.n) for c in chans]
        for e in ENGS:
            def run(h, ws=ws):
                for c, v in ws:
                    c.wait_emit(h, v)
            self.q[e].append(run)

    def flush(self):
        b = self.ctx.block
        for name in ENGS:
            fns = self.q[name]
            if fns:
                getattr(b, name)(lambda h, fns=fns: [f(h) for f in fns])
        self.q = {e: [] for e in ENGS}


class Ctx:
    def __init__(self, nc, es, block):
        self.nc = nc
        self.es = es
        self.block = block
        self._chans = {}
        self._uid = 0

    def chan(self, name):
        if name not in self._chans:
            self._chans[name] = Ch(self.es, self.nc, name)
        return self._chans[name]

    @property
    def chans(self):
        return list(self._chans.values())

    def uid(self):
        self._uid += 1
        return self._uid

    def sb(self, name, shape, dtype, scope=None, side=None):
        s = scope if scope is not None else self.es
        return s.enter_context(self.nc.sbuf_tensor(f"{name}_{self.uid()}", shape, dtype, side=side))

    def ps(self, name, shape, dtype=F32, scope=None):
        s = scope if scope is not None else self.es
        return s.enter_context(self.nc.psum_tensor(f"{name}_{self.uid()}", shape, dtype))



def w_chunk_ap(w_hbm, o0, o1):
    return w_hbm[:, o0:o1].rearrange("(k p) n -> p k n", p=128)


def emit_ln_phase(ctx, rec, src_x, scr_hbm, xnT, *, kind="ln", eps=EPS_LN,
                  u_keep=None, tag="ln", scope=None, u_hbm=None, rms2=False):
    nc = ctx.nc
    ld = ctx.chan(tag + "_ld")
    cv = ctx.chan(tag + "_v")
    ca = ctx.chan(tag + "_a")
    cw = ctx.chan(tag + "_w")
    tr = ctx.chan(tag + "_t")
    xt = ctx.sb("lnx", [128, 2, DM], F32, scope=scope)
    xb = ctx.sb("lnxb", [128, 2, DM], BF16, scope=scope)
    st = ctx.sb("lnst", [128, 32], F32, scope=scope)
    ust = ctx.sb("lnu", [128, 2, DM], F32, scope=scope) if rms2 else None
    cu = ctx.chan(tag + "_u")
    free_marks = [0] * NTT
    prev_tail = 0
    for i in range(NTT):
        sl = i % 2
        lm = rec.op("sync", lambda h, i=i, sl=sl: h.dma_start(out=xt[:, sl, :], in_=src_x[i * 128:(i + 1) * 128, :]),
                    waits=[(cv, free_marks[i - 2])] if i >= 2 else [], inc=ld, k=16)
        m1 = None
        for g in range(4):
            m1 = rec.op("vector", lambda h, sl=sl, g=g: h.bn_stats(out=st[:, g * 6:(g + 1) * 6],
                                                                   in_=xt[:, sl, g * 512:(g + 1) * 512]),
                        waits=[(ld, lm), (cv, prev_tail)] if g == 0 else [(cv, m1)], inc=cv)
        m2 = rec.op("vector", lambda h: h.bn_aggr(out=st[:, 24:26], in_=st[:, 0:24]),
                    waits=[(cv, m1)], inc=cv)
        if kind == "rms":
            m3 = rec.op("vector", lambda h: h.tensor_scalar(out=st[:, 26:27], in0=st[:, 24:25], scalar1=st[:, 24:25],
                                                            scalar2=None, op0=ALU.mult), waits=[(cv, m2)], inc=cv)
            m4 = rec.op("vector", lambda h: h.tensor_tensor(out=st[:, 27:28], in0=st[:, 26:27], in1=st[:, 25:26],
                                                            op=ALU.add), waits=[(cv, m3)], inc=cv)
            vcol = st[:, 27:28]
            mzero = True
        else:
            m4 = m2
            vcol = st[:, 25:26]
            mzero = False
        m5 = rec.op("vector", lambda h, vcol=vcol: h.tensor_scalar(out=st[:, 28:29], in0=vcol, scalar1=float(eps),
                                                                   scalar2=None, op0=ALU.add), waits=[(cv, m4)], inc=cv)
        m6 = rec.op("scalar", lambda h: h.activation(out=st[:, 29:30], in_=st[:, 28:29], func=AF.Sqrt),
                    waits=[(cv, m5)], inc=ca)
        m7 = rec.op("vector", lambda h: h.reciprocal(out=st[:, 28:29], in_=st[:, 29:30]),
                    waits=[(ca, m6)], inc=cv)
        if rms2:
            m8u = rec.op("vector", lambda h, sl=sl: h.tensor_scalar(out=ust[:, sl, :], in0=xt[:, sl, :],
                                                                    scalar1=st[:, 24:25], scalar2=st[:, 28:29],
                                                                    op0=ALU.subtract, op1=ALU.mult),
                         waits=[(cv, m7)], inc=cv)
            rec.op("sync", lambda h, i=i, sl=sl: h.dma_start(out=u_hbm[i * 128:(i + 1) * 128, :],
                                                             in_=ust[:, sl, :]),
                   waits=[(cv, m8u)], inc=cu, k=16)
            m1b = None
            for g in range(4):
                m1b = rec.op("vector", lambda h, sl=sl, g=g: h.bn_stats(out=st[:, g * 6:(g + 1) * 6],
                                                                        in_=ust[:, sl, g * 512:(g + 1) * 512]),
                             waits=[(cv, m8u)] if g == 0 else [(cv, m1b)], inc=cv)
            m2b = rec.op("vector", lambda h: h.bn_aggr(out=st[:, 24:26], in_=st[:, 0:24]),
                         waits=[(cv, m1b)], inc=cv)
            m3b = rec.op("vector", lambda h: h.tensor_scalar(out=st[:, 26:27], in0=st[:, 24:25], scalar1=st[:, 24:25],
                                                             scalar2=None, op0=ALU.mult), waits=[(cv, m2b)], inc=cv)
            m4b = rec.op("vector", lambda h: h.tensor_tensor(out=st[:, 27:28], in0=st[:, 26:27], in1=st[:, 25:26],
                                                             op=ALU.add), waits=[(cv, m3b)], inc=cv)
            m5b = rec.op("vector", lambda h: h.tensor_scalar(out=st[:, 28:29], in0=st[:, 27:28], scalar1=float(eps),
                                                             scalar2=None, op0=ALU.add), waits=[(cv, m4b)], inc=cv)
            m6b = rec.op("scalar", lambda h: h.activation(out=st[:, 29:30], in_=st[:, 28:29], func=AF.Sqrt),
                         waits=[(cv, m5b)], inc=ca)
            m7b = rec.op("vector", lambda h: h.reciprocal(out=st[:, 28:29], in_=st[:, 29:30]),
                         waits=[(ca, m6b)], inc=cv)
            m8 = rec.op("vector", lambda h, sl=sl: h.tensor_scalar(out=xb[:, sl, :], in0=ust[:, sl, :],
                                                                   scalar1=st[:, 28:29], scalar2=None, op0=ALU.mult),
                        waits=[(cv, m7b)], inc=cv)
        elif mzero:
            m8 = rec.op("vector", lambda h, sl=sl: h.tensor_scalar(out=xb[:, sl, :], in0=xt[:, sl, :],
                                                                   scalar1=st[:, 28:29], scalar2=None, op0=ALU.mult),
                        waits=[(cv, m7)], inc=cv)
        else:
            m8 = rec.op("vector", lambda h, sl=sl: h.tensor_scalar(out=xb[:, sl, :], in0=xt[:, sl, :],
                                                                   scalar1=st[:, 24:25], scalar2=st[:, 28:29],
                                                                   op0=ALU.subtract, op1=ALU.mult),
                        waits=[(cv, m7)], inc=cv)
        if u_keep is not None:
            if mzero:
                m8b = rec.op("vector", lambda h, sl=sl, i=i: h.tensor_scalar(out=u_keep[:, i, :], in0=xt[:, sl, :],
                                                                             scalar1=st[:, 28:29], scalar2=None,
                                                                             op0=ALU.mult), waits=[(cv, m7)], inc=cv)
            else:
                m8b = rec.op("vector", lambda h, sl=sl, i=i: h.tensor_scalar(out=u_keep[:, i, :], in0=xt[:, sl, :],
                                                                             scalar1=st[:, 24:25], scalar2=st[:, 28:29],
                                                                             op0=ALU.subtract, op1=ALU.mult),
                             waits=[(cv, m7)], inc=cv)
            m8 = m8b
        free_marks[i] = m8
        prev_tail = m8
        rec.op("sync", lambda h, i=i, sl=sl: h.dma_start(out=scr_hbm[i * 128:(i + 1) * 128, :], in_=xb[:, sl, :]),
               waits=[(cv, m8)], inc=cw, k=16)
    wall = cw.n
    tm = []
    for k in range(NDC):
        tm.append(rec.op("sync", lambda h, k=k: h.dma_start_transpose(out=xnT[:, k, :],
                                                                      in_=scr_hbm[:, k * 128:(k + 1) * 128]),
                         waits=[(cw, wall)], inc=tr, k=16))
    return tr, tm[-1]



def make_off_evac(evac, tt0):
    def e2(rec, o, tt, ps, pm):
        return evac(rec, o, tt + tt0, ps, pm)
    return e2


def mm_group(rec, pe_chan, mms, waits=()):
    n = len(mms)
    mark = None
    for i, fn in enumerate(mms):
        first, last = i == 0, i == n - 1
        mark = rec.op("tensor", (lambda h, fn=fn, first=first, last=last: fn(h, first, last)),
                      waits=waits if first else (), inc=pe_chan if last else None)
    return mark


def proj_state(ctx, scope, tag, nps=4, nk=NDC, cw=512, ksplit=1):
    return {
        "wbuf": ctx.sb(tag + "_w", [128, 2, nk // ksplit, cw], BF16, scope=scope),
        "w_free": [None, None],
        "pss": [ctx.ps(tag + "_ps", [128, 512], F32, scope=scope) for _ in range(nps)],
        "ps_free": [None] * nps,
        "nk": nk, "cw": cw,
    }


def emit_proj(ctx, rec, xnT, w_hbm, n_out, evac, *, tag, ready, pst, bias=None, consts=None,
              pe=None, wld=None, ntt=NTT, t0=0, ksplit=1):
    wld = wld or ctx.chan(tag + "_wl")
    pe = pe or ctx.chan(tag + "_pe")
    nk = pst["nk"]
    CW = pst["cw"]
    NO = ceil_div(n_out, CW)
    nkh = nk // ksplit if ksplit > 1 else nk
    wbuf = pst["wbuf"]
    pss = pst["pss"]
    ps_free = pst["ps_free"]
    w_free = pst["w_free"]
    assert ksplit == 1 or len(pss) >= ntt
    for o in range(NO):
        o0, o1 = o * CW, min((o + 1) * CW, n_out)
        ow = o1 - o0
        pm_of = {}
        for kh in range(ksplit):
            sl = (o * ksplit + kh) % 2
            ws = [w_free[sl]] if w_free[sl] is not None else []
            k0 = kh * nkh
            wm = rec.op("gpsimd", lambda h, o0=o0, sl=sl, ow=ow, k0=k0:
                        h.dma_start(out=wbuf[:, sl, 0:nkh, 0:ow],
                                    in_=w_hbm[k0 * 128:(k0 + nkh) * 128, o0:o0 + ow]
                                    .rearrange("(k p) n -> p k n", p=128)),
                        waits=ws, inc=wld, k=16)
            last_pm = None
            for tt in range(ntt):
                pi = (o * ntt + tt) % len(pss) if ksplit == 1 else tt
                ps = pss[pi]
                waits = [(wld, wm), ready]
                if kh == 0 and ps_free[pi] is not None:
                    waits.append(ps_free[pi])
                nmm = nkh + (1 if (bias is not None and kh == ksplit - 1) else 0)
                mark = None
                for k in range(nkh):
                    first = (kh == 0 and k == 0)
                    lastm = (kh == ksplit - 1 and k == nmm - 1)
                    lastk = (k == nkh - 1)
                    mark = rec.op("tensor", lambda h, k=k, tt=tt, sl=sl, ps=ps, ow=ow, k0=k0,
                                  first=first, lastm=lastm:
                                  h.matmul(out=ps[:, 0:ow], lhsT=xnT[:, k0 + k, t0 + tt * 128:t0 + (tt + 1) * 128],
                                           rhs=wbuf[:, sl, k, 0:ow], start=first, stop=lastm),
                                  waits=waits if k == 0 else (), inc=pe if (lastm or lastk) else None)
                if bias is not None and kh == ksplit - 1:
                    bsb, boff = bias
                    mark = rec.op("tensor", lambda h, ps=ps, ow=ow, o0=o0, bsb=bsb, boff=boff:
                                  h.matmul(out=ps[:, 0:ow], lhsT=consts["ones1"][0:1, 0:128],
                                           rhs=bsb[0:1, boff + o0:boff + o0 + ow], start=False, stop=True),
                                  inc=pe)
                last_pm = mark
                if kh == ksplit - 1:
                    pm_of[tt] = mark
                    if ksplit == 1:
                        ps_free[pi] = evac(rec, o, tt, ps[:, 0:ow], mark)
            w_free[sl] = (pe, last_pm)
        if ksplit > 1:
            for tt in range(ntt):
                ps_free[tt] = evac(rec, o, tt, pss[tt][:, 0:ow], pm_of[tt])



def emit_l0_attn(ctx, rec, io, W, C):
    nc = ctx.nc
    HN = 16
    bias_scope = ExitStack()
    bias_sb = ctx.sb("a_bias", [1, 4 * DM], BF16, scope=bias_scope, side="right")
    kv_scope = ExitStack()
    ktok = ctx.sb("a_ktok", [128, NTT, DM], BF16, scope=kv_scope, side="right")
    vtok = ctx.sb("a_vtok", [128, NTT, DM], BF16, scope=kv_scope, side="right")

    cv = ctx.chan("l0_v")
    ca = ctx.chan("l0_a")
    cw = ctx.chan("l0_sw")
    pe = ctx.chan("l0_pe")
    wld = ctx.chan("l0_wl")
    trq = ctx.chan("l0_trq")

    scopeA = ExitStack()
    xnT = ctx.sb("a_xnT", [128, NDC, TOK], BF16, scope=scopeA)
    elu_tmp = ctx.sb("a_elut", [128, 4, 512], F32, scope=scopeA)
    elu_e = ctx.sb("a_elue", [128, 4, 512], F32, scope=scopeA)
    qstage = ctx.sb("a_qst", [128, 4, 512], BF16, scope=scopeA)
    pst = proj_state(ctx, scopeA, "l0p")
    trc, trm = emit_ln_phase(ctx, rec, io["src"], io["scr_a"], xnT, kind="ln", tag="l0ln", scope=scopeA)
    ready = (trc, trm)
    rec.op("gpsimd", lambda h: h.dma_start(out=bias_sb[:], in_=W["l0_biases"]), inc=wld, k=16)

    elu_slot_mark = {}

    def elu_evac(dst_fn, store_fn):
        def evac(rec, o, tt, ps, pm):
            i = (o * NTT + tt) % 4
            ws = [(pe, pm)]
            if i in elu_slot_mark:
                ws.append(elu_slot_mark[i])
            if ("q", i) in elu_slot_mark:
                ws.append(elu_slot_mark[("q", i)])
            ow = ps.shape[-1]
            m1 = rec.op("vector", lambda h, ps=ps, i=i, ow=ow: h.tensor_scalar(
                out=elu_tmp[:, i, 0:ow], in0=ps, scalar1=0.0, scalar2=None, op0=ALU.min),
                waits=ws, inc=cv)
            m2 = rec.op("scalar", lambda h, i=i, ow=ow: h.activation(
                out=elu_e[:, i, 0:ow], in_=elu_tmp[:, i, 0:ow], func=AF.Exp),
                waits=[(cv, m1)], inc=ca)
            m3 = rec.op("vector", lambda h, ps=ps, i=i, o=o, tt=tt, ow=ow: h.scalar_tensor_tensor(
                out=dst_fn(o, tt, i), in0=ps, scalar=0.0, in1=elu_e[:, i, 0:ow],
                op0=ALU.max, op1=ALU.add), waits=[(ca, m2)], inc=cv)
            elu_slot_mark[i] = (cv, m3)
            sf = store_fn(o, tt, i)
            if sf is not None:
                hbm_ap, src_ap = sf
                sm = rec.op("sync", lambda h, hbm_ap=hbm_ap, src_ap=src_ap:
                            h.dma_start(out=hbm_ap, in_=src_ap),
                            waits=[(cv, m3), ready], inc=cw, k=16)
                elu_slot_mark[("q", i)] = (cw, sm)
            return (cv, m3)
        return evac

    kevac = elu_evac(lambda o, tt, i: ktok[:, tt, o * 512:(o + 1) * 512],
                     lambda o, tt, i: ((io["scr_b"][tt * 128:(tt + 1) * 128, :], ktok[:, tt, :])
                                       if o == 3 else None))
    emit_proj(ctx, rec, xnT, W["Wk"], DM, kevac, tag="l0k", ready=ready, pst=pst,
              bias=(bias_sb, DM), consts=C, pe=pe, wld=wld)
    kdone = (cv, cv.n)

    def vevac(rec, o, tt, ps, pm):
        m = rec.op("scalar", lambda h, ps=ps, o=o, tt=tt: h.activation(
            out=vtok[:, tt, o * 512:(o + 1) * 512], in_=ps, func=AF.Copy), waits=[(pe, pm)], inc=ca)
        return (ca, m)
    emit_proj(ctx, rec, xnT, W["Wv"], DM, vevac, tag="l0v", ready=ready, pst=pst,
              bias=(bias_sb, 2 * DM), consts=C, pe=pe, wld=wld)
    vdone = (ca, ca.n)

    qevac = elu_evac(lambda o, tt, i: qstage[:, i, :],
                     lambda o, tt, i: (io["scr_a"][tt * 128:(tt + 1) * 128, o * 512:(o + 1) * 512],
                                       qstage[:, i, :]))
    emit_proj(ctx, rec, xnT, W["Wq"], DM, qevac, tag="l0q", ready=ready, pst=pst,
              bias=(bias_sb, 0), consts=C, pe=pe, wld=wld)
    q_stores = cw.n
    rec.barrier(ctx.chans)
    scopeA.close()

    scopeB = ExitStack()
    kfT = ctx.sb("a_kfT", [128, NDC, TOK], BF16, scope=scopeB)
    qfT = ctx.sb("a_qfT", [128, NDC, TOK], BF16, scope=scopeB)
    sendb = ctx.sb("a_send", [128, HN * 129], BF16, scope=scopeB)
    seedb = ctx.sb("a_seed", [128, HN * 129], BF16, scope=scopeB)
    KVrun = ctx.sb("a_kvr", [128, HN * 129], F32, scope=scopeB)
    KVbf = ctx.sb("a_kvb", [128, HN * 129], BF16, scope=scopeB)
    At = ctx.sb("a_At", [128, 2, 128], BF16, scope=scopeB)
    den = ctx.sb("a_den", [128, 4], F32, scope=scopeB)
    attno = ctx.sb("a_attno", [128, NTT, DM], BF16, scope=scopeB)

    st_scope = ExitStack()
    psst = [ctx.ps("l0_psst", [128, 132], F32, scope=st_scope) for _ in range(2)]
    st_free = [None, None]
    for hh in range(HN):
        psp = psst[hh % 2]
        mms = []
        for n in range(NTT):
            mms.append(lambda h, s, e, n=n, hh=hh, psp=psp: h.matmul(
                out=psp[:, 0:128], lhsT=ktok[:, n, hh * 128:(hh + 1) * 128],
                rhs=vtok[:, n, hh * 128:(hh + 1) * 128], start=s, stop=False))
            mms.append(lambda h, s, e, n=n, hh=hh, psp=psp: h.matmul(
                out=psp[:, 128:129], lhsT=ktok[:, n, hh * 128:(hh + 1) * 128],
                rhs=C["onescol"][0:128, 0:1], start=False, stop=e))
        ws = [kdone, vdone] + ([st_free[hh % 2]] if st_free[hh % 2] else [])
        pm = mm_group(rec, pe, mms, waits=ws)
        em = rec.op("scalar", lambda h, hh=hh, psp=psp: h.activation(
            out=sendb[:, hh * 129:(hh + 1) * 129], in_=psp[:, 0:129], func=AF.Copy),
            waits=[(pe, pm)], inc=ca)
        st_free[hh % 2] = (ca, em)
    st_all = ca.n

    sh_pay = io["sh_l0"]
    sh_flag = io["sh_l0f"]
    snd = ctx.chan("l0_snd")
    rcv = ctx.chan("l0_rcv")
    seeded = ctx.chan("l0_seeded")
    snd_v = st_all

    def g_exchange(h):
        with h.register("a_pp") as pp:
            h.reg_load(pp, C["par_sb"][0:1, 0:1])
            with h.If_eq(pp, 0):
                h.wait_ge(ca.sem, snd_v)
                h.dma_start(out=sh_pay[:], in_=sendb[:]).then_inc(snd.sem, 16)
                h.wait_ge(snd.sem, 16)
                h.dma_start(out=sh_flag[:], in_=C["nonce_sb"][0:1, 0:4]).then_inc(snd.sem, 16)
            with h.Else():
                with h.register("a_pc") as pc, h.register("a_pex") as pex, h.register("a_f0") as f0, \
                     h.register("a_nn") as nn:
                    h.reg_load(nn, C["nonce_sb"][0:1, 0:1])
                    h.reg_mov(pc, 1)
                    h.reg_mov(pex, 0)
                    with h.While(pc):
                        h.reg_alu(pex, pex, 16, ALU.add)
                        h.dma_start(out=C["flag_sb"][0:1, 0:4], in_=sh_flag[0:1, :]).then_inc(snd.sem, 16)
                        h.wait_ge(snd.sem, pex)
                        h.reg_load(f0, C["flag_sb"][0:1, 0:1])
                        h.reg_alu(pc, f0, nn, ALU.not_equal)
                h.dma_start(out=seedb[:], in_=sh_pay[:]).then_inc(rcv.sem, 16)
    rec.raw("gpsimd", g_exchange)

    seeded.reserve(1)

    def v_seed(h):
        with h.register("a_pv") as pv:
            h.reg_load(pv, C["par_sb"][0:1, 0:1])
            with h.If_eq(pv, 0):
                h.memset(KVrun[:], 0.0).then_inc(seeded.sem, 1)
            with h.Else():
                h.wait_ge(rcv.sem, 16)
                h.tensor_copy(out=KVrun[:], in_=seedb[:]).then_inc(seeded.sem, 1)
    rec.raw("vector", v_seed)
    mkb = rec.op("vector", lambda h: h.tensor_copy(out=KVbf[:], in_=KVrun[:]),
                 waits=[(seeded, 1)], inc=cv)

    for k in range(NDC):
        rec.op("sync", lambda h, k=k: h.dma_start_transpose(out=kfT[:, k, :],
                                                            in_=io["scr_b"][:, k * 128:(k + 1) * 128]),
               waits=[(cw, q_stores)], inc=trq, k=16)
        rec.op("sync", lambda h, k=k: h.dma_start_transpose(out=qfT[:, k, :],
                                                            in_=io["scr_a"][:, k * 128:(k + 1) * 128]),
               waits=[(cw, q_stores)], inc=trq, k=16)
    trdone = (trq, trq.n)
    rec.barrier(ctx.chans)
    st_scope.close()

    ap_scope = ExitStack()
    psA = [ctx.ps("l0_psA", [128, 128], F32, scope=ap_scope) for _ in range(2)]
    psY = [ctx.ps("l0_psY", [128, 132], F32, scope=ap_scope) for _ in range(2)]
    psK = [ctx.ps("l0_psK", [128, 132], F32, scope=ap_scope) for _ in range(2)]
    psA_free = [None, None]
    psY_free = [None, None]
    psK_free = [None, None]
    At_free = [None, None]
    kv_mark = {}
    for hh in range(HN):
        for n in range(NTT):
            sl = (hh * NTT + n) % 2
            wsA = [trdone] + ([psA_free[sl]] if psA_free[sl] else [])
            pmA = mm_group(rec, pe, [lambda h, s, e, hh=hh, n=n, sl=sl: h.matmul(
                out=psA[sl][:, 0:128], lhsT=kfT[:, hh, n * 128:(n + 1) * 128],
                rhs=qfT[:, hh, n * 128:(n + 1) * 128], start=s, stop=e)], waits=wsA)
            wsAt = [(pe, pmA)] + ([At_free[sl]] if At_free[sl] else [])
            mAt = rec.op("vector", lambda h, sl=sl: h.tensor_tensor(
                out=At[:, sl, :], in0=psA[sl][:, 0:128], in1=C["triU"][:, 0:128], op=ALU.mult),
                waits=wsAt, inc=cv)
            psA_free[sl] = (cv, mAt)
            wsY = [(cv, mAt), (seeded, 1), (cv, mkb)]
            if hh in kv_mark:
                wsY.append(kv_mark[hh])
            if psY_free[sl]:
                wsY.append(psY_free[sl])
            mms = [
                lambda h, s, e, sl=sl, hh=hh, n=n: h.matmul(
                    out=psY[sl][:, 0:128], lhsT=At[:, sl, :],
                    rhs=vtok[:, n, hh * 128:(hh + 1) * 128], start=s, stop=False),
                lambda h, s, e, sl=sl, hh=hh, n=n: h.matmul(
                    out=psY[sl][:, 128:129], lhsT=At[:, sl, :],
                    rhs=C["onescol"][0:128, 0:1], start=False, stop=False),
                lambda h, s, e, sl=sl, hh=hh, n=n: h.matmul(
                    out=psY[sl][:, 0:129], lhsT=qfT[:, hh, n * 128:(n + 1) * 128],
                    rhs=KVbf[:, hh * 129:(hh + 1) * 129], start=False, stop=e),
            ]
            pmY = mm_group(rec, pe, mms, waits=wsY)
            At_free[sl] = (pe, pmY)
            m1 = rec.op("vector", lambda h, sl=sl: h.tensor_scalar(
                out=den[:, sl:sl + 1], in0=psY[sl][:, 128:129], scalar1=1e-6, scalar2=None,
                op0=ALU.add), waits=[(pe, pmY)], inc=cv)
            m2 = rec.op("vector", lambda h, sl=sl: h.reciprocal(
                out=den[:, 2 + sl:3 + sl], in_=den[:, sl:sl + 1]), waits=[(cv, m1)], inc=cv)
            m3 = rec.op("vector", lambda h, sl=sl, hh=hh, n=n: h.tensor_scalar(
                out=attno[:, n, hh * 128:(hh + 1) * 128], in0=psY[sl][:, 0:128],
                scalar1=den[:, 2 + sl:3 + sl], scalar2=None, op0=ALU.mult),
                waits=[(cv, m2)], inc=cv)
            psY_free[sl] = (cv, m3)
            if n < NTT - 1:
                wsK = [kdone, vdone] + ([psK_free[sl]] if psK_free[sl] else [])
                pmK = mm_group(rec, pe, [
                    lambda h, s, e, sl=sl, hh=hh, n=n: h.matmul(
                        out=psK[sl][:, 0:128], lhsT=ktok[:, n, hh * 128:(hh + 1) * 128],
                        rhs=vtok[:, n, hh * 128:(hh + 1) * 128], start=s, stop=False),
                    lambda h, s, e, sl=sl, hh=hh, n=n: h.matmul(
                        out=psK[sl][:, 128:129], lhsT=ktok[:, n, hh * 128:(hh + 1) * 128],
                        rhs=C["onescol"][0:128, 0:1], start=False, stop=e)], waits=wsK)
                mk1 = rec.op("vector", lambda h, sl=sl, hh=hh: h.tensor_tensor(
                    out=KVrun[:, hh * 129:(hh + 1) * 129], in0=KVrun[:, hh * 129:(hh + 1) * 129],
                    in1=psK[sl][:, 0:129], op=ALU.add), waits=[(pe, pmK), (pe, pmY), (seeded, 1)], inc=cv)
                mk2 = rec.op("vector", lambda h, hh=hh: h.tensor_copy(
                    out=KVbf[:, hh * 129:(hh + 1) * 129], in_=KVrun[:, hh * 129:(hh + 1) * 129]),
                    waits=[(cv, mk1)], inc=cv)
                psK_free[sl] = (cv, mk1)
                kv_mark[hh] = (cv, mk2)

    attn_all = cv.n
    for tt in range(NTT):
        rec.op("sync", lambda h, tt=tt: h.dma_start(out=io["scr_b"][tt * 128:(tt + 1) * 128, :],
                                                    in_=attno[:, tt, :]),
               waits=[(cv, attn_all), (trq, trq.n)], inc=cw, k=16)
    ao_stores = cw.n
    rec.barrier(ctx.chans)
    ap_scope.close()
    scopeB.close()
    kv_scope.close()

    scopeC = ExitStack()
    aoT = ctx.sb("a_aoT", [128, NDC, TOK], BF16, scope=scopeC)
    of32 = ctx.sb("a_of32", [128, NTT, DM], F32, scope=scopeC)
    pst_o = proj_state(ctx, scopeC, "l0po")
    for k in range(NDC):
        rec.op("sync", lambda h, k=k: h.dma_start_transpose(out=aoT[:, k, :],
                                                            in_=io["scr_b"][:, k * 128:(k + 1) * 128]),
               waits=[(cw, ao_stores)], inc=trq, k=16)
    aoT_ready = (trq, trq.n)

    def oevac(rec, o, tt, ps, pm):
        m = rec.op("scalar", lambda h, ps=ps, o=o, tt=tt: h.activation(
            out=of32[:, tt, o * 512:(o + 1) * 512], in_=ps, func=AF.Copy), waits=[(pe, pm)], inc=ca)
        return (ca, m)
    emit_proj(ctx, rec, aoT, W["Wo"], DM, oevac, tag="l0o", ready=aoT_ready, pst=pst_o,
              bias=(bias_sb, 3 * DM), consts=C, pe=pe, wld=wld)
    odone = (ca, ca.n)

    emit_ln_residual_store(ctx, rec, of32, io["src"], io["dst"], odone, tag="l0fin", scope=scopeC)
    rec.barrier(ctx.chans)
    scopeC.close()
    bias_scope.close()


def emit_ln_residual_store(ctx, rec, of32, src_x, dst_x, ready, *, tag, eps=EPS_LN, scope=None,
                           ntt=NTT, row0=0, pre_add_hbm=None):
    ld = ctx.chan(tag + "_ld")
    cv = ctx.chan(tag + "_v")
    ca = ctx.chan(tag + "_a")
    st = ctx.chan(tag + "_st")
    xt = ctx.sb(tag + "_x", [128, 2, DM], F32, scope=scope)
    ot = ctx.sb(tag + "_o", [128, 2, DM], F32, scope=scope)
    sst = ctx.sb(tag + "_s", [128, 32], F32, scope=scope)
    ut = ctx.sb(tag + "_u", [128, 2, DM], F32, scope=scope) if pre_add_hbm is not None else None
    free = [0, 0]
    prev_tail = 0
    for i in range(ntt):
        sl = i % 2
        r0 = row0 + i * 128
        lm = rec.op("sync", lambda h, r0=r0, sl=sl: h.dma_start(out=xt[:, sl, :],
                                                                in_=src_x[r0:r0 + 128, :]),
                    waits=[(cv, free[sl])], inc=ld, k=16)
        if pre_add_hbm is not None:
            lmu = rec.op("sync", lambda h, r0=r0, sl=sl: h.dma_start(out=ut[:, sl, :],
                                                                     in_=pre_add_hbm[r0:r0 + 128, :]),
                         waits=[(cv, free[sl])], inc=ld, k=16)
            pa = rec.op("vector", lambda h, i=i, sl=sl: h.tensor_tensor(
                out=of32[:, i, :], in0=of32[:, i, :], in1=ut[:, sl, :], op=ALU.add),
                waits=[ready, (ld, lmu), (cv, free[sl]), (cv, prev_tail)], inc=cv)
            stats_w = [(cv, pa)]
        else:
            stats_w = [ready, (cv, free[sl]), (cv, prev_tail)]
        m1 = None
        for g in range(4):
            m1 = rec.op("vector", lambda h, i=i, g=g: h.bn_stats(out=sst[:, g * 6:(g + 1) * 6],
                                                                 in_=of32[:, i, g * 512:(g + 1) * 512]),
                        waits=stats_w if g == 0 else [(cv, m1)], inc=cv)
        m2 = rec.op("vector", lambda h: h.bn_aggr(out=sst[:, 24:26], in_=sst[:, 0:24]),
                    waits=[(cv, m1)], inc=cv)
        m3 = rec.op("vector", lambda h: h.tensor_scalar(out=sst[:, 26:27], in0=sst[:, 25:26],
                                                        scalar1=float(eps), scalar2=None, op0=ALU.add),
                    waits=[(cv, m2)], inc=cv)
        m4 = rec.op("scalar", lambda h: h.activation(out=sst[:, 27:28], in_=sst[:, 26:27], func=AF.Sqrt),
                    waits=[(cv, m3)], inc=ca)
        m5 = rec.op("vector", lambda h: h.reciprocal(out=sst[:, 26:27], in_=sst[:, 27:28]),
                    waits=[(ca, m4)], inc=cv)
        m6 = rec.op("vector", lambda h, i=i, sl=sl: h.tensor_scalar(
            out=ot[:, sl, :], in0=of32[:, i, :], scalar1=sst[:, 24:25], scalar2=sst[:, 26:27],
            op0=ALU.subtract, op1=ALU.mult), waits=[(cv, m5)], inc=cv)
        m7 = rec.op("vector", lambda h, sl=sl: h.tensor_tensor(out=ot[:, sl, :], in0=ot[:, sl, :],
                                                               in1=xt[:, sl, :], op=ALU.add),
                    waits=[(cv, m6), (ld, lm)], inc=cv)
        free[sl] = m7
        prev_tail = m6
        rec.op("sync", lambda h, r0=r0, sl=sl: h.dma_start(out=dst_x[r0:r0 + 128, :],
                                                           in_=ot[:, sl, :]),
               waits=[(cv, m7)], inc=st, k=16)
    return st


def emit_l0_ffn(ctx, rec, io, W, C):
    nc = ctx.nc
    NI = 8192
    NIC = NI // 512
    cv = ctx.chan("f0_v")
    ca = ctx.chan("f0_a")
    cp = ctx.chan("f0_p")
    cw = ctx.chan("f0_sw")
    pe = ctx.chan("f0_pe")
    wld = ctx.chan("f0_wl")
    trh = ctx.chan("f0_tr")

    blk = ExitStack()
    xnT = ctx.sb("f_xnT", [128, NDC, TOK], BF16, side="right", scope=blk)
    of32 = ctx.sb("f_of32", [128, NTT, DM], F32, side="right", scope=blk)
    ln_scope = ExitStack()
    trc, trm = emit_ln_phase(ctx, rec, io["src"], io["scr_a"], xnT, kind="ln", tag="f0ln", scope=ln_scope)
    ready = (trc, trm)
    rec.barrier(ctx.chans)
    ln_scope.close()

    main_scope = ExitStack()
    wg = ctx.sb("f_wg", [128, 2, NDC, 512], BF16, scope=main_scope)
    wu = ctx.sb("f_wu", [128, 2, NDC, 512], BF16, scope=main_scope)
    wd = ctx.sb("f_wd", [128, 2, 4, DM // 4], BF16, scope=main_scope)
    sg = ctx.sb("f_sg", [128, 2, 512], F32, scope=main_scope)
    hbuf = ctx.sb("f_hbuf", [128, 4, 512], F32, scope=main_scope)
    ab = ctx.sb("f_ab", [128, 1, 512], F32, scope=main_scope)
    thr = ctx.sb("f_thr", [128, 6, 128], F32, scope=main_scope)
    hsst = ctx.sb("f_hsst", [128, 4, 512], BF16, scope=main_scope)
    hsT = ctx.sb("f_hsT", [128, 4, TOK], BF16, scope=main_scope)
    psg = [ctx.ps("f_psg", [128, 512], F32, scope=main_scope) for _ in range(2)]
    psu = [ctx.ps("f_psu", [128, 512], F32, scope=main_scope) for _ in range(2)]
    psd = [ctx.ps("f_psd", [128, 512], F32, scope=main_scope) for _ in range(4)]
    zm = rec.op("vector", lambda h: h.memset(of32[:, 0:4, :], 0.0), inc=cv)
    zm = rec.op("vector", lambda h: h.memset(of32[:, 4:8, :], 0.0), waits=[(cv, zm)], inc=cv)
    of_zero = (cv, zm)

    wfree = {"g": [None, None], "u": [None, None], "d": [None, None]}
    psg_free = [None, None]
    psu_free = [None, None]
    psd_free = [None] * 4
    sg_free = [None] * 4
    hb_free = [None] * 4
    hs_free = [None] * 4
    hsT_free = [None]
    ab_free = [None, None]
    thr_free = None
    of_add = {}
    st = {}

    def emit_gu(ic):
        nonlocal thr_free
        sl = ic % 2
        wmg = rec.op("gpsimd", lambda h, ic=ic, sl=sl: h.dma_start(
            out=wg[:, sl, :, :], in_=w_chunk_ap(W["w_gate"], ic * 512, (ic + 1) * 512)),
            waits=[wfree["g"][sl]] if wfree["g"][sl] else [], inc=wld, k=16)
        wmu = rec.op("gpsimd", lambda h, ic=ic, sl=sl: h.dma_start(
            out=wu[:, sl, :, :], in_=w_chunk_ap(W["w_up"], ic * 512, (ic + 1) * 512)),
            waits=[wfree["u"][sl]] if wfree["u"][sl] else [], inc=wld, k=16)
        hs_marks = []
        for tt in range(NTT):
            pg = tt % 2
            wsg_ = [(wld, wmg), (wld, wmu), ready] + \
                ([psg_free[pg]] if psg_free[pg] else []) + ([psu_free[pg]] if psu_free[pg] else [])
            gm = um = None
            for k in range(NDC):
                gm = rec.op("tensor", lambda h, k=k, tt=tt, sl=sl, pg=pg: h.matmul(
                    out=psg[pg][:], lhsT=xnT[:, k, tt * 128:(tt + 1) * 128],
                    rhs=wg[:, sl, k, :], start=(k == 0), stop=(k == NDC - 1)),
                    waits=wsg_ if k == 0 else (), inc=pe if k == NDC - 1 else None)
                um = rec.op("tensor", lambda h, k=k, tt=tt, sl=sl, pg=pg: h.matmul(
                    out=psu[pg][:], lhsT=xnT[:, k, tt * 128:(tt + 1) * 128],
                    rhs=wu[:, sl, k, :], start=(k == 0), stop=(k == NDC - 1)),
                    waits=(), inc=pe if k == NDC - 1 else None)
            sgi = tt % 2
            wss = [(pe, gm)] + ([sg_free[sgi]] if sg_free[sgi] else [])
            sm = rec.op("scalar", lambda h, sgi=sgi, pg=pg: h.activation(
                out=sg[:, sgi, :], in_=psg[pg][:], func=AF.Silu), waits=wss, inc=ca)
            psg_free[pg] = (ca, sm)
            hsl = (ic * NTT + tt) % 4
            hw = [(ca, sm), (pe, um)] + ([hb_free[hsl]] if hb_free[hsl] else [])
            hm = rec.op("vector", lambda h, sgi=sgi, pg=pg, hsl=hsl: h.tensor_tensor(
                out=hbuf[:, hsl, :], in0=sg[:, sgi, :], in1=psu[pg][:], op=ALU.mult),
                waits=hw, inc=cv)
            psu_free[pg] = (cv, hm)
            sg_free[sgi] = (cv, hm)
            hv = hbuf[:, hsl, :]
            h4 = hv.rearrange("p (g f) -> p g f", f=4)
            asl = 0
            am = rec.op("scalar", lambda h, hv=hv, asl=asl: h.activation(
                out=ab[:, asl, :], in_=hv, func=AF.Abs),
                waits=[(cv, hm)] + ([ab_free[asl]] if ab_free[asl] else []), inc=ca)
            a4 = ab[:, asl, :].rearrange("p (g f) -> p g f", f=4)
            t_ = thr
            m_ = rec.op("vector", lambda h, a4=a4: h.tensor_tensor(
                out=t_[:, 0, :], in0=a4[:, :, 0], in1=a4[:, :, 1], op=ALU.max),
                waits=[(ca, am)] + ([thr_free] if thr_free else []), inc=cv)
            m_ = rec.op("vector", lambda h, a4=a4: h.tensor_tensor(
                out=t_[:, 1, :], in0=a4[:, :, 0], in1=a4[:, :, 1], op=ALU.min),
                waits=[(cv, m_)], inc=cv)
            m_ = rec.op("vector", lambda h, a4=a4: h.tensor_tensor(
                out=t_[:, 2, :], in0=a4[:, :, 2], in1=a4[:, :, 3], op=ALU.max),
                waits=[(cv, m_)], inc=cv)
            m_ = rec.op("vector", lambda h, a4=a4: h.tensor_tensor(
                out=t_[:, 3, :], in0=a4[:, :, 2], in1=a4[:, :, 3], op=ALU.min),
                waits=[(cv, m_)], inc=cv)
            m_ = rec.op("vector", lambda h: h.tensor_tensor(
                out=t_[:, 4, :], in0=t_[:, 0, :], in1=t_[:, 2, :], op=ALU.min),
                waits=[(cv, m_)], inc=cv)
            m_ = rec.op("vector", lambda h: h.tensor_tensor(
                out=t_[:, 5, :], in0=t_[:, 1, :], in1=t_[:, 3, :], op=ALU.max),
                waits=[(cv, m_)], inc=cv)
            m_ = rec.op("vector", lambda h: h.tensor_tensor(
                out=t_[:, 4, :], in0=t_[:, 4, :], in1=t_[:, 5, :], op=ALU.max),
                waits=[(cv, m_)], inc=cv)
            km = []
            for qi in range(4):
                m_ = rec.op("vector", lambda h, qi=qi, a4=a4: h.tensor_tensor(
                    out=t_[:, qi, :], in0=a4[:, :, qi], in1=t_[:, 4, :], op=ALU.is_ge),
                    waits=[(cv, m_)], inc=cv)
                km.append(m_)
            hsv = hsst[:, hsl, :].rearrange("p (g f) -> p g f", f=4)
            pm_ = None
            for qi in range(4):
                wsp = [(cv, km[qi])] + ([hs_free[hsl]] if (hs_free[hsl] and qi == 0) else [])
                pm_ = rec.op("gpsimd", lambda h, qi=qi, h4=h4, hsv=hsv: h.tensor_tensor(
                    out=hsv[:, :, qi], in0=h4[:, :, qi], in1=t_[:, qi, :], op=ALU.mult),
                    waits=wsp + ([(cp, pm_)] if pm_ else []), inc=cp)
            thr_free = (cp, pm_)
            hb_free[hsl] = (cp, pm_)
            ab_free[asl] = (cv, km[3])
            sm2 = rec.op("sync", lambda h, tt=tt, hsl=hsl, ic=ic: h.dma_start(
                out=io["scr_big"][tt * 128:(tt + 1) * 128, ic * 512:(ic + 1) * 512],
                in_=hsst[:, hsl, :]), waits=[(cp, pm_)], inc=cw, k=16)
            hs_marks.append(sm2)
            hs_free[hsl] = (cw, sm2)
        st[ic] = hs_marks[-1]

    def emit_dn(ic):
        sl = ic % 2
        tms = []
        for j4 in range(4):
            wst = [(cw, st[ic])] + ([hsT_free[0]] if (hsT_free[0] and j4 == 0) else [])
            tmk = rec.op("sync", lambda h, ic=ic, j4=j4: h.dma_start_transpose(
                out=hsT[:, j4, :],
                in_=io["scr_big"][0:TOK, ic * 512 + j4 * 128:ic * 512 + (j4 + 1) * 128]),
                waits=wst, inc=trh, k=16)
            tms.append(tmk)
        tr_ic = tms[-1]
        for dh in range(4):
            wmd = rec.op("gpsimd", lambda h, ic=ic, sl=sl, dh=dh: h.dma_start(
                out=wd[:, sl, :, :],
                in_=W["w_down"][ic * 512:(ic + 1) * 512, dh * 512:(dh + 1) * 512]
                .rearrange("(k p) n -> p k n", p=128)),
                waits=([wfree["d"][sl]] if wfree["d"][sl] else []) if dh == 0 else [(pe, pe.n)],
                inc=wld, k=16)
            for tt in range(NTT):
                if True:
                    ob = dh
                    pdi = tt % 4
                    wsd_ = [(trh, tr_ic), (wld, wmd)] + ([psd_free[pdi]] if psd_free[pdi] else [])
                    dm = mm_group(rec, pe, [
                        lambda h, s, e, j4=j4, tt=tt, sl=sl, pdi=pdi: h.matmul(
                            out=psd[pdi][:], lhsT=hsT[:, j4, tt * 128:(tt + 1) * 128],
                            rhs=wd[:, sl, j4, 0:512], start=s, stop=e)
                        for j4 in range(4)], waits=wsd_)
                    wse_ = [(pe, dm), of_zero] + ([of_add[(tt, ob)]] if (tt, ob) in of_add else [])
                    em = rec.op("vector", lambda h, tt=tt, ob=ob, pdi=pdi: h.tensor_tensor(
                        out=of32[:, tt, ob * 512:(ob + 1) * 512],
                        in0=of32[:, tt, ob * 512:(ob + 1) * 512],
                        in1=psd[pdi][:], op=ALU.add), waits=wse_, inc=cv)
                    psd_free[pdi] = (cv, em)
                    of_add[(tt, ob)] = (cv, em)
        hsT_free[0] = (pe, pe.n)
        wfree["d"][sl] = (pe, pe.n)
        wfree["g"][sl] = (pe, pe.n)
        wfree["u"][sl] = (pe, pe.n)

    for ic in range(NIC + 1):
        if ic < NIC:
            emit_gu(ic)
        if ic >= 1:
            emit_dn(ic - 1)
        dn_done = (cv, cv.n)
    rec.barrier(ctx.chans)
    main_scope.close()
    fin_scope = ExitStack()
    emit_ln_residual_store(ctx, rec, of32, io["src"], io["dst"], dn_done, tag="f0fin",
                           scope=fin_scope)
    rec.barrier(ctx.chans)
    fin_scope.close()
    blk.close()



def emit_l1_attn(ctx, rec, io, W, C):
    nc = ctx.nc
    NH, DH, DS, DIN = 48, 64, 128, 3072
    NCC = DIN // 128
    cv = ctx.chan("m_v")
    ca = ctx.chan("m_a")
    cp = ctx.chan("m_p")
    cw = ctx.chan("m_sw")
    pe = ctx.chan("m_pe")
    wld = ctx.chan("m_wl")
    trm = ctx.chan("m_tr")
    snd = ctx.chan("m_snd")
    rcv1 = ctx.chan("m_rcv1")
    rcv2 = ctx.chan("m_rcv2")
    sdd = ctx.chan("m_seed")
    poll1 = ctx.chan("m_poll1")
    poll2 = ctx.chan("m_poll2")

    blk = ExitStack()
    small = ctx.sb("m_small", [128, 80], F32, side="right", scope=blk)
    LcT = ctx.sb("m_LcT", [128, NTT, NH], F32, side="right", scope=blk)
    eUT = ctx.sb("m_eUT", [128, NTT, NH], F32, side="right", scope=blk)
    expLcT = ctx.sb("m_expLcT", [128, NTT, NH], F32, side="right", scope=blk)
    dAe = ctx.sb("m_dAe", [128, NTT * NH], F32, side="right", scope=blk)
    Db = ctx.sb("m_Db", [128, NH], F32, side="right", scope=blk)
    T0c = ctx.sb("m_T0", [128, 128], F32, side="right", scope=blk)
    eyeb = ctx.sb("m_eyeb", [128, 128], BF16, side="right", scope=blk)
    eyef = ctx.sb("m_eyef", [128, 128], F32, side="right", scope=blk)
    zro = ctx.sb("m_zro", [128, 128], F32, side="right", scope=blk)
    ctail = ctx.sb("m_ctail", [128, NCC, 3], BF16, side="right", scope=blk)
    convst = ctx.sb("m_cvst", [128, NCC, 3], BF16, side="right", scope=blk)
    scope2 = ExitStack()
    xn3T = ctx.sb("m_xnT", [128, NDC, TOK], BF16, side="right", scope=scope2)
    scope3 = ExitStack()
    Lc = ctx.sb("m_Lc", [128, TOK], F32, side="right", scope=scope3)
    Bf = ctx.sb("m_Bf", [128, TOK], BF16, side="right", scope=scope3)
    Cf = ctx.sb("m_Cf", [128, TOK], BF16, side="right", scope=scope3)
    Btok = ctx.sb("m_Btok", [128, NTT, DS], BF16, side="right", scope=scope3)
    Ssb = ctx.sb("m_Ssb", [128, NH * DH], F32, side="right", scope=scope3)
    Sbf = ctx.sb("m_Sbf", [128, NH * DH], BF16, side="right", scope=scope3)
    sendS = ctx.sb("m_sendS", [128, NH * DH], BF16, side="right", scope=scope3)
    ssm0 = ctx.sb("m_ssm0", [128, NH * DH], F32, side="right", scope=scope3)
    scope4 = ExitStack()
    xcf = ctx.sb("m_xcf", [128, NCC, TOK], BF16, side="right", scope=scope4)
    scope3b = ExitStack()
    dtl = ctx.sb("m_dt", [128, TOK], F32, side="right", scope=scope3b)
    ldec = ctx.sb("m_ldec", [128, TOK], F32, side="right", scope=scope3b)
    eU = ctx.sb("m_eU", [128, TOK], F32, side="right", scope=scope3b)
    expLc = ctx.sb("m_expLc", [128, TOK], F32, side="right", scope=scope3b)

    cst = ctx.chan("m_cst")
    rec.op("sync", lambda h: h.dma_start(out=T0c[:], in_=W["c_T0"]), inc=cst, k=16)
    rec.op("gpsimd", lambda h: h.dma_start(out=eyeb[:], in_=W["c_eyef"]), inc=cst, k=16)
    rec.op("sync", lambda h: h.dma_start(out=eyef[:], in_=W["c_eyef"]), inc=cst, k=16)
    rec.op("sync", lambda h: h.dma_start(out=small[0:48, 0:1], in_=W["dt_bias_r"]), inc=cst, k=16)
    rec.op("sync", lambda h: h.dma_start(out=small[0:48, 49:50], in_=W["A_log_c"]), inc=cst, k=16)
    rec.op("gpsimd", lambda h: h.dma_start(
        out=convst[:], in_=W["conv_state"].rearrange("(cc p) k -> p cc k", p=128)), inc=cst, k=16)
    rec.op("sync", lambda h: h.dma_start(out=ssm0[:], in_=W["ssm0_t"]), inc=cst, k=16)
    mz = rec.op("vector", lambda h: h.memset(zro[:], 0.0), inc=cv)
    mo = rec.op("vector", lambda h: h.memset(small[:, 48:49], 1.0), waits=[(cst, cst.n)], inc=cv)
    cst_all = (cst, cst.n)

    ln_scope = ExitStack()
    trc0, trm0 = emit_ln_phase(ctx, rec, io["src"], io["scr_a"], xn3T, kind="ln", tag="m_ln",
                               scope=ln_scope, u_hbm=io["scr_u"], rms2=True)
    wcol = ctx.sb("m_wcol", [128, NDC], F32, scope=ln_scope)
    wcm = rec.op("sync", lambda h: h.dma_start(out=wcol[:], in_=W["norm_w_c"]), inc=cst, k=16)
    sc_m = None
    for k in range(NDC):
        sc_m = rec.op("vector", lambda h, k=k: h.tensor_scalar(
            out=xn3T[:, k, :], in0=xn3T[:, k, :], scalar1=wcol[:, k:k + 1], scalar2=None,
            op0=ALU.mult), waits=[(trc0, trm0), (cst, wcm)] if k == 0 else [(cv, sc_m)], inc=cv)
    xn3_ready = (cv, sc_m)
    rec.barrier(ctx.chans)
    ln_scope.close()

    fp_scope = ExitStack()
    wti = ctx.sb("m_wti", [128, 2, NDC, 128], BF16, scope=fp_scope)
    psf = [ctx.ps("m_psf", [128, 512], F32, scope=fp_scope) for _ in range(4)]
    wfree = [None, None]
    psfree = [None] * 4
    units = [("xc", cc, DIN + cc * 128) for cc in range(NCC)]
    units += [("B", 0, 2 * DIN), ("C", 0, 2 * DIN + DS), ("dt", 0, 2 * DIN + 2 * DS)]
    for ui, (kind, cc, col0) in enumerate(units):
        sl = ui % 2
        ncol = 48 if kind == "dt" else 128
        ws = [wfree[sl]] if wfree[sl] else []
        wm = rec.op("gpsimd", lambda h, sl=sl, col0=col0, ncol=ncol: h.dma_start(
            out=wti[:, sl, :, 0:ncol],
            in_=W["W_inT"][:, col0:col0 + ncol].rearrange("(k p) n -> p k n", p=128)),
            waits=ws, inc=wld, k=16)
        for ts in range(2):
            pi = (ui * 2 + ts) % 4
            waits = [(wld, wm), xn3_ready, cst_all] + ([psfree[pi]] if psfree[pi] else [])
            mark = None
            for k in range(NDC):
                mark = rec.op("tensor", lambda h, k=k, ts=ts, sl=sl, pi=pi, ncol=ncol: h.matmul(
                    out=psf[pi][0:ncol, 0:512], lhsT=wti[:, sl, k, 0:ncol],
                    rhs=xn3T[:, k, ts * 512:(ts + 1) * 512],
                    start=(k == 0), stop=(k == NDC - 1)), waits=waits if k == 0 else (),
                    inc=pe if k == NDC - 1 else None)
            if kind == "xc":
                em = rec.op("vector", lambda h, pi=pi, cc=cc, ts=ts: h.tensor_copy(
                    out=xcf[:, cc, ts * 512:(ts + 1) * 512], in_=psf[pi][:, 0:512]),
                    waits=[(pe, mark)], inc=cv)
            elif kind == "B":
                em = rec.op("vector", lambda h, pi=pi, ts=ts: h.tensor_copy(
                    out=Bf[:, ts * 512:(ts + 1) * 512], in_=psf[pi][:, 0:512]),
                    waits=[(pe, mark)], inc=cv)
            elif kind == "C":
                em = rec.op("vector", lambda h, pi=pi, ts=ts: h.tensor_copy(
                    out=Cf[:, ts * 512:(ts + 1) * 512], in_=psf[pi][:, 0:512]),
                    waits=[(pe, mark)], inc=cv)
            else:
                em = rec.op("vector", lambda h, pi=pi, ts=ts: h.tensor_copy(
                    out=dtl[0:48, ts * 512:(ts + 1) * 512], in_=psf[pi][0:48, 0:512]),
                    waits=[(pe, mark)], inc=cv)
            psfree[pi] = (cv, em)
        wfree[sl] = (pe, mark)
    xc_done = (cv, cv.n)
    rec.barrier(ctx.chans)
    fp_scope.close()
    if DBG_L1STOP <= 2:
        rec.barrier(ctx.chans)
        return

    def g_tail(h):
        with h.register("t_pp") as pp:
            h.reg_load(pp, C["par_sb"][0:1, 0:1])
            with h.If_eq(pp, 0):
                h.wait_ge(cv.sem, xc_done[1])
                h.dma_start(out=io["sh_l1t"][:, 0:NCC * 3].rearrange("p (a b) -> p a b", b=3),
                            in_=xcf[:, :, TOK - 3:TOK]).then_inc(snd.sem, 16)
                h.wait_ge(snd.sem, 16)
                h.dma_start(out=io["sh_l1tf"][:], in_=C["nonce_sb"][0:1, 0:4]).then_inc(snd.sem, 16)
            with h.Else():
                with h.register("t_pc") as pc, h.register("t_pex") as pex, h.register("t_f0") as f0, \
                     h.register("t_nn") as nn:
                    h.reg_load(nn, C["nonce_sb"][0:1, 0:1])
                    h.reg_mov(pc, 1)
                    h.reg_mov(pex, 0)
                    with h.While(pc):
                        h.reg_alu(pex, pex, 16, ALU.add)
                        h.dma_start(out=C["flag_sb"][0:1, 0:4], in_=io["sh_l1tf"][0:1, :]).then_inc(poll1.sem, 16)
                        h.wait_ge(poll1.sem, pex)
                        h.reg_load(f0, C["flag_sb"][0:1, 0:1])
                        h.reg_alu(pc, f0, nn, ALU.not_equal)
                h.dma_start(out=ctail[:],
                            in_=io["sh_l1t"][:, 0:NCC * 3].rearrange("p (a b) -> p a b", b=3)).then_inc(rcv1.sem, 16)
    rec.raw("gpsimd", g_tail)
    sdd.reserve(1)

    def v_tail(h):
        with h.register("t_pv") as pv:
            h.reg_load(pv, C["par_sb"][0:1, 0:1])
            with h.If_eq(pv, 0):
                h.tensor_copy(out=ctail[:], in_=convst[:]).then_inc(sdd.sem, 1)
            with h.Else():
                h.wait_ge(rcv1.sem, 16)
                h.memset(small[:, 79:80], 0.0).then_inc(sdd.sem, 1)
    rec.raw("vector", v_tail)
    tail_ready = (sdd, 1)
    if DBG_L1STOP <= 3:
        rec.barrier(ctx.chans)
        return

    d0 = rec.op("vector", lambda h: h.tensor_scalar(
        out=dtl[0:48, :], in0=dtl[0:48, :], scalar1=small[0:48, 0:1], scalar2=None, op0=ALU.add),
        waits=[xc_done, cst_all], inc=cv)
    d1 = rec.op("scalar", lambda h: h.activation(out=ldec[0:48, :], in_=dtl[0:48, :], func=AF.Abs),
                waits=[(cv, d0)], inc=ca)
    d2 = rec.op("scalar", lambda h: h.activation(out=eU[0:48, :], in_=ldec[0:48, :], func=AF.Exp,
                                                 scale=-1.0), waits=[(ca, d1)], inc=ca)
    d3 = rec.op("scalar", lambda h: h.activation(out=eU[0:48, :], in_=eU[0:48, :], func=AF.Ln,
                                                 bias=small[0:48, 48:49]), waits=[(ca, d2)], inc=ca)
    d4 = rec.op("vector", lambda h: h.tensor_scalar(
        out=Lc[0:48, :], in0=dtl[0:48, :], scalar1=0.0, scalar2=None, op0=ALU.max),
        waits=[(ca, d3), (ca, d1)], inc=cv)
    d5 = rec.op("vector", lambda h: h.tensor_tensor(
        out=dtl[0:48, :], in0=Lc[0:48, :], in1=eU[0:48, :], op=ALU.add), waits=[(cv, d4)], inc=cv)
    if DBG_L1STOP <= 31:
        rec.barrier(ctx.chans)
        return
    a0 = rec.op("scalar", lambda h: h.activation(out=small[0:48, 50:51], in_=small[0:48, 49:50],
                                                 func=AF.Exp), waits=[cst_all], inc=ca)
    a1 = rec.op("vector", lambda h: h.tensor_scalar(
        out=small[0:48, 50:51], in0=small[0:48, 50:51], scalar1=-1.0, scalar2=None, op0=ALU.mult),
        waits=[(ca, a0)], inc=cv)
    l0 = rec.op("vector", lambda h: h.tensor_scalar(
        out=ldec[0:48, :], in0=dtl[0:48, :], scalar1=small[0:48, 50:51], scalar2=None, op0=ALU.mult),
        waits=[(cv, a1), (cv, d5)], inc=cv)
    sc = l0
    for n in range(NTT):
        sc = rec.op("vector", lambda h, n=n: h.tensor_tensor_scan(
            out=Lc[0:48, n * 128:(n + 1) * 128], data0=ldec[0:48, n * 128:(n + 1) * 128],
            data1=zro[0:48, 0:128], initial=0.0, op0=ALU.add, op1=ALU.add),
            waits=[(cv, sc), (cv, mz)], inc=cv)
    if DBG_L1STOP <= 32:
        rec.barrier(ctx.chans)
        return
    e0 = rec.op("scalar", lambda h: h.activation(out=expLc[0:48, :], in_=Lc[0:48, :], func=AF.Exp),
                waits=[(cv, sc)], inc=ca)
    eu = sc
    for n in range(NTT):
        eu = rec.op("vector", lambda h, n=n: h.tensor_scalar(
            out=eU[0:48, n * 128:(n + 1) * 128], in0=Lc[0:48, n * 128:(n + 1) * 128],
            scalar1=Lc[0:48, n * 128 + 127:n * 128 + 128], scalar2=None, op0=ALU.subtract),
            waits=[(cv, eu), (ca, e0)], inc=cv)
    e1 = rec.op("scalar", lambda h: h.activation(out=eU[0:48, :], in_=eU[0:48, :], func=AF.Exp,
                                                 scale=-1.0), waits=[(cv, eu)], inc=ca)
    if DBG_L1STOP <= 33:
        rec.barrier(ctx.chans)
        return
    dec_done = (ca, ca.n)

    tr_scope = ExitStack()
    pst_r = ctx.ps("m_pstr", [128, 64], F32, scope=tr_scope)
    ps_bc = ctx.ps("m_psbc", [128, 512], F32, scope=tr_scope)
    tm_ = None
    te_free = None
    for (srcT, dstT) in ((Lc, LcT), (eU, eUT), (expLc, expLcT)):
        for n in range(NTT):
            wsx = [dec_done] + ([te_free] if te_free else [])
            tm_ = rec.op("tensor", lambda h, srcT=srcT, n=n: h.transpose(
                out=pst_r[:, 0:48], in_=srcT[0:48, n * 128:(n + 1) * 128], identity=eyef[0:48, 0:48]),
                waits=wsx, inc=pe)
            tm_ = rec.op("scalar", lambda h, dstT=dstT, n=n: h.activation(
                out=dstT[:, n, :], in_=pst_r[:, 0:48], func=AF.Copy), waits=[(pe, tm_)], inc=ca)
            te_free = (ca, tm_)
    if DBG_L1STOP <= 34:
        rec.barrier(ctx.chans)
        return
    row1 = ctx.sb("m_row1", [1, 384], F32, scope=tr_scope)
    ec = rec.op("sync", lambda h: h.dma_start(
        out=io["scr_small"][0:48, 0:8],
        in_=Lc[0:48, :].rearrange("p (n c) -> p n c", c=128)[:, :, 127:128].rearrange("p n c -> p (n c)"),
        allow_slow_non_contiguous=True), waits=[dec_done], inc=cw, k=16)
    er = rec.op("sync", lambda h: h.dma_start(
        out=row1[:].rearrange("p (a b) -> p a b", a=8),
        in_=io["scr_small"][0:48, 0:8].transpose((1, 0)).unsqueeze(0),
        allow_slow_non_contiguous=True), waits=[(cw, ec)], inc=cw, k=16)
    bm1 = rec.op("tensor", lambda h: h.matmul(out=ps_bc[:, 0:384], lhsT=C["onesf"][0:1, 0:128],
                                              rhs=row1[:], start=True, stop=True),
                 waits=[(cw, er), cst_all], inc=pe)
    bm2 = rec.op("scalar", lambda h: h.activation(out=dAe[:, 0:384], in_=ps_bc[:, 0:384],
                                                  func=AF.Exp), waits=[(pe, bm1)], inc=ca)
    if DBG_L1STOP <= 35:
        rec.barrier(ctx.chans)
        return
    drow = ctx.sb("m_drow", [1, 48], F32, scope=tr_scope)
    dm0 = rec.op("sync", lambda h: h.dma_start(out=drow[:], in_=W["D_row"]), inc=cst, k=16)
    dm1 = rec.op("tensor", lambda h: h.matmul(out=ps_bc[:, 384:432], lhsT=C["onesf"][0:1, 0:128],
                                              rhs=drow[:], start=True, stop=True),
                 waits=[(cst, dm0), (ca, bm2), cst_all], inc=pe)
    dm2 = rec.op("scalar", lambda h: h.activation(out=Db[:], in_=ps_bc[:, 384:432], func=AF.Copy),
                 waits=[(pe, dm1)], inc=ca)
    bc_done = (ca, ca.n)
    rec.barrier(ctx.chans)
    tr_scope.close()
    scope3b.close()
    if DBG_L1STOP <= 4:
        rec.barrier(ctx.chans)
        return

    cvp_scope = ExitStack()
    cpad = ctx.sb("m_cpad", [128, 2, TOK + 3], F32, scope=cvp_scope)
    ctmp = ctx.sb("m_ctmp", [128, 2, TOK], F32, scope=cvp_scope)
    xsil = ctx.sb("m_xsil", [128, 2, TOK], BF16, scope=cvp_scope)
    cw4 = ctx.sb("m_cw4", [128, NCC, 4], F32, scope=cvp_scope)
    cb1 = ctx.sb("m_cb1", [128, NCC], F32, scope=cvp_scope)
    cwm = rec.op("sync", lambda h: h.dma_start(
        out=cw4[:], in_=W["conv_w"].rearrange("(cc p) k -> p cc k", p=128)), inc=cst, k=16)
    cbm = rec.op("sync", lambda h: h.dma_start(out=cb1[:], in_=W["conv_b_c"]), inc=cst, k=16)
    cp_free = [(cv, 0), (cv, 0)]
    for cc in range(NCC):
        sl = cc % 2
        m0 = rec.op("vector", lambda h, cc=cc, sl=sl: h.tensor_copy(
            out=cpad[:, sl, 0:3], in_=ctail[:, cc, :]),
            waits=[tail_ready, xc_done, (cst, max(cwm, cbm)), cp_free[sl]], inc=cv)
        m1 = rec.op("vector", lambda h, cc=cc, sl=sl: h.tensor_copy(
            out=cpad[:, sl, 3:TOK + 3], in_=xcf[:, cc, :]), waits=[(cv, m0)], inc=cv)
        m2 = rec.op("vector", lambda h, cc=cc, sl=sl: h.tensor_scalar(
            out=ctmp[:, sl, :], in0=cpad[:, sl, 0:TOK], scalar1=cw4[:, cc, 0:1], scalar2=None,
            op0=ALU.mult), waits=[(cv, m1)], inc=cv)
        for kk in range(1, 4):
            m2 = rec.op("vector", lambda h, cc=cc, sl=sl, kk=kk: h.scalar_tensor_tensor(
                out=ctmp[:, sl, :], in0=cpad[:, sl, kk:kk + TOK], scalar=cw4[:, cc, kk:kk + 1],
                in1=ctmp[:, sl, :], op0=ALU.mult, op1=ALU.add), waits=[(cv, m2)], inc=cv)
        m3 = rec.op("vector", lambda h, cc=cc, sl=sl: h.tensor_scalar(
            out=ctmp[:, sl, :], in0=ctmp[:, sl, :], scalar1=cb1[:, cc:cc + 1], scalar2=None,
            op0=ALU.add), waits=[(cv, m2)], inc=cv)
        m4 = rec.op("scalar", lambda h, cc=cc, sl=sl: h.activation(
            out=xsil[:, sl, :], in_=ctmp[:, sl, :], func=AF.Silu), waits=[(cv, m3)], inc=ca)
        sm_ = rec.op("sync", lambda h, cc=cc, sl=sl: h.dma_start(
            out=io["scr_m"][cc * 128:(cc + 1) * 128, 0:TOK], in_=xsil[:, sl, :]),
            waits=[(ca, m4)], inc=cw, k=16)
        cp_free[sl] = (cw, sm_)
    xf_st = cw.n
    rec.barrier(ctx.chans)
    scope4.close()
    cvp_scope.close()
    if DBG_L1STOP <= 5:
        rec.barrier(ctx.chans)
        return

    yt_scope = ExitStack()
    Ytok = ctx.sb("m_Ytok", [128, NTT, DIN], BF16, scope=yt_scope)
    xt_scope = ExitStack()
    Xchunk = ctx.sb("m_Xch", [128, 2, DIN], BF16, scope=xt_scope)
    rec.op("sync", lambda h: h.dma_start(out=io["scr_m"][DIN:DIN + 128, 0:TOK], in_=Bf[:]),
           waits=[xc_done], inc=cw, k=16)
    bf_st = cw.n
    for tt in range(NTT):
        rec.op("sync", lambda h, tt=tt: h.dma_start_transpose(
            out=Btok[:, tt, :], in_=io["scr_m"][DIN:DIN + 128, tt * 128:(tt + 1) * 128]),
            waits=[(cw, bf_st)], inc=trm, k=16)
    tok_ready = (trm, trm.n)
    hself = ctx.sb("m_hself", [48, NH * 128], F32, scope=xt_scope)
    hsm = rec.op("sync", lambda h: h.dma_start(out=hself[:], in_=W["c_hsel"]), inc=cst, k=16)
    if DBG_L1STOP <= 6:
        rec.barrier(ctx.chans)
        return


    sp_scope = ExitStack()
    psS = [ctx.ps("m_psS", [128, 64], F32, scope=sp_scope) for _ in range(2)]
    xeu = ctx.sb("m_xeu", [128, 2, DH], BF16, scope=sp_scope)
    s0m = rec.op("vector", lambda h: h.tensor_copy(out=Ssb[:], in_=ssm0[:]),
                 waits=[cst_all], inc=cv)

    def state_pass(seeded_mark, tag):
        psS_free = [None, None]
        xeu_free = [None, None]
        xch_last = [None, None]
        rm = {}
        for n in range(NTT):
            sx = n % 2
            wsl = [(cw, xf_st)] + ([xch_last[sx]] if xch_last[sx] else [])
            xlm = rec.op("sync", lambda h, n=n, sx=sx: h.dma_start_transpose(
                out=Xchunk[:, sx, :], in_=io["scr_m"][0:DIN, n * 128:(n + 1) * 128]),
                waits=wsl, inc=trm, k=16)
            hh_marks = []
            for hh in range(NH):
                sl = (n * NH + hh) % 2
                wsx = [tok_ready, bc_done, seeded_mark, (trm, xlm)] + ([xeu_free[sl]] if xeu_free[sl] else [])
                xm = rec.op("vector", lambda h, n=n, hh=hh, sl=sl, sx=sx: h.tensor_scalar(
                    out=xeu[:, sl, :], in0=Xchunk[:, sx, hh * DH:(hh + 1) * DH],
                    scalar1=eUT[:, n, hh:hh + 1], scalar2=None, op0=ALU.mult),
                    waits=wsx, inc=cv)
                hh_marks.append(xm)
                wsp = [(cv, xm)] + ([psS_free[sl]] if psS_free[sl] else [])
                pm = rec.op("tensor", lambda h, n=n, sl=sl: h.matmul(
                    out=psS[sl][:], lhsT=Btok[:, n, :], rhs=xeu[:, sl, :], start=True, stop=True),
                    waits=wsp, inc=pe)
                wsr = [(pe, pm)] + ([rm[hh]] if hh in rm else [])
                rmk = rec.op("vector", lambda h, n=n, hh=hh, sl=sl: h.scalar_tensor_tensor(
                    out=Ssb[:, hh * DH:(hh + 1) * DH], in0=Ssb[:, hh * DH:(hh + 1) * DH],
                    scalar=dAe[:, n * NH + hh:n * NH + hh + 1], in1=psS[sl][:],
                    op0=ALU.mult, op1=ALU.add), waits=wsr, inc=cv)
                rm[hh] = (cv, rmk)
                psS_free[sl] = (cv, rmk)
                xeu_free[sl] = (pe, pm)
            xch_last[sx] = (cv, cv.n)
        return cv.n
    sp_end = state_pass((cv, s0m), "sp")
    sendm = rec.op("vector", lambda h: h.tensor_copy(out=sendS[:], in_=Ssb[:]),
                   waits=[(cv, sp_end)], inc=cv)
    if DBG_L1STOP <= 7:
        rec.barrier(ctx.chans)
        return

    def g_state_x(h):
        with h.register("s_pp") as pp:
            h.reg_load(pp, C["par_sb"][0:1, 0:1])
            with h.If_eq(pp, 0):
                h.wait_ge(cv.sem, sendm)
                h.dma_start(out=io["sh_l1s"][:], in_=sendS[:]).then_inc(snd.sem, 16)
                h.wait_ge(snd.sem, 48)
                h.dma_start(out=io["sh_l1sf"][:], in_=C["nonce_sb"][0:1, 0:4]).then_inc(snd.sem, 16)
            with h.Else():
                h.wait_ge(cv.sem, sendm)
                with h.register("s_pc") as pc, h.register("s_pex") as pex, h.register("s_f0") as f0, \
                     h.register("s_nn") as nn:
                    h.reg_load(nn, C["nonce_sb"][0:1, 0:1])
                    h.reg_mov(pc, 1)
                    h.reg_mov(pex, 0)
                    with h.While(pc):
                        h.reg_alu(pex, pex, 16, ALU.add)
                        h.dma_start(out=C["flag_sb"][0:1, 0:4], in_=io["sh_l1sf"][0:1, :]).then_inc(poll2.sem, 16)
                        h.wait_ge(poll2.sem, pex)
                        h.reg_load(f0, C["flag_sb"][0:1, 0:1])
                        h.reg_alu(pc, f0, nn, ALU.not_equal)
                h.dma_start(out=sendS[:], in_=io["sh_l1s"][:]).then_inc(rcv2.sem, 16)
    rec.raw("gpsimd", g_state_x)
    sdd.reserve(1)

    def v_seed2(h):
        with h.register("s_pv") as pv:
            h.reg_load(pv, C["par_sb"][0:1, 0:1])
            with h.If_eq(pv, 0):
                h.tensor_copy(out=Ssb[:], in_=ssm0[:]).then_inc(sdd.sem, 1)
            with h.Else():
                h.wait_ge(rcv2.sem, 16)
                h.tensor_copy(out=Ssb[:], in_=sendS[:]).then_inc(sdd.sem, 1)
    rec.raw("vector", v_seed2)
    seed2 = (sdd, 2)
    sb0 = rec.op("vector", lambda h: h.tensor_copy(out=Sbf[:], in_=Ssb[:]), waits=[seed2], inc=cv)
    rec.barrier(ctx.chans)
    sp_scope.close()
    if DBG_L1STOP <= 8:
        rec.barrier(ctx.chans)
        return

    yl_scope = ExitStack()
    dtile = ctx.sb("m_dtile", [128, 4, 128], F32, scope=yl_scope)
    dexp = ctx.sb("m_dexp", [128, 4, 128], BF16, scope=yl_scope)
    Mt = ctx.sb("m_Mt", [128, 4, 128], BF16, scope=yl_scope)
    BCs = ctx.sb("m_BCs", [128, 128], F32, scope=yl_scope)
    xeu2 = ctx.sb("m_xeu2", [128, 4, DH], BF16, scope=yl_scope)
    xdd = ctx.sb("m_xdd", [128, 4, DH], BF16, scope=yl_scope)
    ytmp = ctx.sb("m_ytmp", [128, 4, DH], F32, scope=yl_scope)
    psBC = ctx.ps("m_psBC", [128, 128], F32, scope=yl_scope)
    psM = [ctx.ps("m_psM", [128, 128], F32, scope=yl_scope) for _ in range(2)]
    psYi = [ctx.ps("m_psYi", [128, 64], F32, scope=yl_scope) for _ in range(2)]
    psYe = [ctx.ps("m_psYe", [128, 64], F32, scope=yl_scope) for _ in range(2)]
    psS2 = ctx.ps("m_psS2", [128, 64], F32, scope=yl_scope)
    psM_free = [None, None]
    psYi_free = [None, None]
    psYe_free = [None, None]
    dt_free = [None] * 4
    de_free = [None] * 4
    mt_free = [None] * 4
    xe_free = [None] * 4
    xd_free = [None] * 4
    yt_free = [None] * 4
    bc_ev = None
    ps2_free = None
    srm = {}

    xch2_last = [None, None]

    def do_chunk(n):
        nonlocal bc_ev, ps2_free
        sx = n % 2
        wsl = [(cw, xf_st)] + ([xch2_last[sx]] if xch2_last[sx] else [])
        xlm = rec.op("sync", lambda h, n=n, sx=sx: h.dma_start_transpose(
            out=Xchunk[:, sx, :], in_=io["scr_m"][0:DIN, n * 128:(n + 1) * 128]),
            waits=wsl, inc=trm, k=16)
        bcw = [tok_ready, xc_done] + ([bc_ev] if bc_ev else [])
        bm = rec.op("tensor", lambda h, n=n: h.matmul(
            out=psBC[:], lhsT=Bf[:, n * 128:(n + 1) * 128], rhs=Cf[:, n * 128:(n + 1) * 128],
            start=True, stop=True), waits=bcw, inc=pe)
        bce = rec.op("scalar", lambda h: h.activation(out=BCs[:], in_=psBC[:], func=AF.Copy),
                     waits=[(pe, bm)], inc=ca)
        bc_ev = (ca, bce)
        for hh in range(NH):
            sl = (n * NH + hh) % 2
            s4 = (n * NH + hh) % 4
            wsm = [(cv, sc), (cst, hsm)] + ([psM_free[sl]] if psM_free[sl] else [])
            mm0 = rec.op("tensor", lambda h, n=n, hh=hh, sl=sl: h.matmul(
                out=psM[sl][:], lhsT=hself[0:48, hh * 128:(hh + 1) * 128],
                rhs=Lc[0:48, n * 128:(n + 1) * 128], start=True, stop=True), waits=wsm, inc=pe)
            wsd = [(pe, mm0), cst_all, bc_done] + ([dt_free[s4]] if dt_free[s4] else [])
            md = rec.op("vector", lambda h, n=n, hh=hh, sl=sl, s4=s4: h.scalar_tensor_tensor(
                out=dtile[:, s4, :], in0=psM[sl][:], scalar=LcT[:, n, hh:hh + 1], in1=T0c[:],
                op0=ALU.subtract, op1=ALU.add), waits=wsd, inc=cv)
            psM_free[sl] = (cv, md)
            wse = [(cv, md)] + ([de_free[s4]] if de_free[s4] else [])
            me = rec.op("scalar", lambda h, s4=s4: h.activation(
                out=dexp[:, s4, :], in_=dtile[:, s4, :], func=AF.Exp), waits=wse, inc=ca)
            dt_free[s4] = (ca, me)
            wst = [(ca, me), bc_ev] + ([mt_free[s4]] if mt_free[s4] else [])
            mt = rec.op("gpsimd", lambda h, s4=s4: h.tensor_tensor(
                out=Mt[:, s4, :], in0=BCs[:], in1=dexp[:, s4, :], op=ALU.mult), waits=wst, inc=cp)
            de_free[s4] = (cp, mt)
            wxd = [tok_ready, bc_done, (trm, xlm)] + ([xd_free[s4]] if xd_free[s4] else [])
            xd = rec.op("vector", lambda h, hh=hh, s4=s4, sx=sx: h.tensor_scalar(
                out=xdd[:, s4, :], in0=Xchunk[:, sx, hh * DH:(hh + 1) * DH],
                scalar1=Db[:, hh:hh + 1], scalar2=None, op0=ALU.mult), waits=wxd, inc=cv)
            wyi = [(cp, mt), (cv, xd)] + ([psYi_free[sl]] if psYi_free[sl] else [])
            yi1 = rec.op("tensor", lambda h, hh=hh, sl=sl, s4=s4, sx=sx: h.matmul(
                out=psYi[sl][:], lhsT=Mt[:, s4, :], rhs=Xchunk[:, sx, hh * DH:(hh + 1) * DH],
                start=True, stop=False), waits=wyi)
            yi2 = rec.op("tensor", lambda h, sl=sl, s4=s4: h.matmul(
                out=psYi[sl][:], lhsT=eyeb[:], rhs=xdd[:, s4, :], start=False, stop=True), inc=pe)
            mt_free[s4] = (pe, yi2)
            xd_free[s4] = (pe, yi2)
            wye = [(cv, sb0), xc_done] + ([srm[hh]] if hh in srm else []) + \
                  ([psYe_free[sl]] if psYe_free[sl] else [])
            ye = rec.op("tensor", lambda h, n=n, hh=hh, sl=sl: h.matmul(
                out=psYe[sl][:], lhsT=Cf[:, n * 128:(n + 1) * 128],
                rhs=Sbf[:, hh * DH:(hh + 1) * DH], start=True, stop=True), waits=wye, inc=pe)
            wv1 = [(pe, ye)] + ([yt_free[s4]] if yt_free[s4] else [])
            yv1 = rec.op("vector", lambda h, n=n, hh=hh, sl=sl, s4=s4: h.tensor_scalar(
                out=ytmp[:, s4, :], in0=psYe[sl][:], scalar1=expLcT[:, n, hh:hh + 1],
                scalar2=None, op0=ALU.mult), waits=wv1, inc=cv)
            psYe_free[sl] = (cv, yv1)
            yv2 = rec.op("vector", lambda h, n=n, hh=hh, sl=sl, s4=s4: h.tensor_tensor(
                out=Ytok[:, n, hh * DH:(hh + 1) * DH], in0=ytmp[:, s4, :], in1=psYi[sl][:],
                op=ALU.add), waits=[(cv, yv1), (pe, yi2)], inc=cv)
            psYi_free[sl] = (cv, yv2)
            yt_free[s4] = (cv, yv2)
            if n < NTT - 1:
                wxe = [tok_ready, bc_done] + ([xe_free[s4]] if xe_free[s4] else [])
                xm = rec.op("vector", lambda h, n=n, hh=hh, s4=s4, sx=sx: h.tensor_scalar(
                    out=xeu2[:, s4, :], in0=Xchunk[:, sx, hh * DH:(hh + 1) * DH],
                    scalar1=eUT[:, n, hh:hh + 1], scalar2=None, op0=ALU.mult), waits=wxe, inc=cv)
                wsp2 = [(cv, xm), (pe, ye)] + ([ps2_free] if ps2_free else [])
                pm2 = rec.op("tensor", lambda h, n=n, s4=s4: h.matmul(
                    out=psS2[:], lhsT=Btok[:, n, :], rhs=xeu2[:, s4, :], start=True, stop=True),
                    waits=wsp2, inc=pe)
                xe_free[s4] = (pe, pm2)
                rmk = rec.op("vector", lambda h, n=n, hh=hh: h.scalar_tensor_tensor(
                    out=Ssb[:, hh * DH:(hh + 1) * DH], in0=Ssb[:, hh * DH:(hh + 1) * DH],
                    scalar=dAe[:, n * NH + hh:n * NH + hh + 1], in1=psS2[:],
                    op0=ALU.mult, op1=ALU.add), waits=[(pe, pm2), seed2, bc_done], inc=cv)
                ps2_free = (cv, rmk)
                sbm = rec.op("vector", lambda h, hh=hh: h.tensor_copy(
                    out=Sbf[:, hh * DH:(hh + 1) * DH], in_=Ssb[:, hh * DH:(hh + 1) * DH]),
                    waits=[(cv, rmk)], inc=cv)
                srm[hh] = (cv, sbm)

    for n in range(NTT):
        do_chunk(n)
        xch2_last[n % 2] = (pe, pe.n)
    y_done = (cv, cv.n)
    if DBG_L1STOP <= 9:
        rec.barrier(ctx.chans)
        return
    rec.barrier(ctx.chans)
    yl_scope.close()
    xt_scope.close()
    scope3.close()

    zg_scope = ExitStack()
    zs = ctx.sb("m_zs", [128, NTT, DIN], BF16, scope=zg_scope)
    wib = ctx.sb("m_wib", [128, DIN], F32, scope=zg_scope)
    wirow = ctx.sb("m_wirow", [1, DIN], F32, scope=zg_scope)
    gst = ctx.sb("m_gst", [128, 40], F32, scope=zg_scope)
    gtmp = ctx.sb("m_gtmp", [128, 2, DIN], F32, scope=zg_scope)
    pst_z = proj_state(ctx, zg_scope, "m_z", nps=4, nk=NDC, cw=512, ksplit=2)
    psw = pst_z["pss"][0]

    def zevac(rec, o, tt, ps, pm):
        m = rec.op("scalar", lambda h, ps=ps, o=o, tt=tt: h.activation(
            out=zs[:, tt, o * 512:(o + 1) * 512], in_=ps, func=AF.Silu), waits=[(pe, pm)], inc=ca)
        return (ca, m)
    emit_proj(ctx, rec, xn3T, W["W_inT"], DIN, make_off_evac(zevac, 0), tag="m_zp0",
              ready=xn3_ready, pst=pst_z, consts=C, pe=pe, wld=wld, ksplit=2, ntt=4, t0=0)
    emit_proj(ctx, rec, xn3T, W["W_inT"], DIN, make_off_evac(zevac, 4), tag="m_zp1",
              ready=xn3_ready, pst=pst_z, consts=C, pe=pe, wld=wld, ksplit=2, ntt=4, t0=512)
    z_done = (ca, ca.n)
    wm0 = rec.op("sync", lambda h: h.dma_start(out=wirow[:], in_=W["inner_w_row"]), inc=cst, k=16)
    wbm = None
    for g6 in range(6):
        wbm = rec.op("tensor", lambda h, g6=g6: h.matmul(
            out=psw[:, 0:512], lhsT=C["onesf"][0:1, 0:128], rhs=wirow[0:1, g6 * 512:(g6 + 1) * 512],
            start=True, stop=True), waits=[(cst, wm0), z_done] if g6 == 0 else [(ca, wbm)], inc=pe)
        wbm = rec.op("scalar", lambda h, g6=g6: h.activation(
            out=wib[:, g6 * 512:(g6 + 1) * 512], in_=psw[:, 0:512], func=AF.Copy),
            waits=[(pe, wbm)], inc=ca)
    wib_done = (ca, wbm)
    gt_free = [None, None]
    ym_prev = 0
    for tt in range(NTT):
        sl = tt % 2
        g1 = None
        for g6 in range(6):
            g1 = rec.op("vector", lambda h, tt=tt, g6=g6: h.bn_stats(
                out=gst[:, g6 * 6:(g6 + 1) * 6], in_=Ytok[:, tt, g6 * 512:(g6 + 1) * 512]),
                waits=([y_done, (cv, ym_prev)] if g6 == 0 else [(cv, g1)]), inc=cv)
        g2 = rec.op("vector", lambda h: h.bn_aggr(out=gst[:, 36:38], in_=gst[:, 0:36]),
                    waits=[(cv, g1)], inc=cv)
        g3 = rec.op("vector", lambda h: h.tensor_scalar(out=gst[:, 38:39], in0=gst[:, 36:37],
                                                        scalar1=gst[:, 36:37], scalar2=None,
                                                        op0=ALU.mult), waits=[(cv, g2)], inc=cv)
        g4 = rec.op("vector", lambda h: h.tensor_tensor(out=gst[:, 38:39], in0=gst[:, 38:39],
                                                        in1=gst[:, 37:38], op=ALU.add),
                    waits=[(cv, g3)], inc=cv)
        g5 = rec.op("vector", lambda h: h.tensor_scalar(out=gst[:, 38:39], in0=gst[:, 38:39],
                                                        scalar1=1e-5, scalar2=None, op0=ALU.add),
                    waits=[(cv, g4)], inc=cv)
        g6m = rec.op("scalar", lambda h: h.activation(out=gst[:, 39:40], in_=gst[:, 38:39],
                                                      func=AF.Sqrt), waits=[(cv, g5)], inc=ca)
        g7 = rec.op("vector", lambda h: h.reciprocal(out=gst[:, 38:39], in_=gst[:, 39:40]),
                    waits=[(ca, g6m)], inc=cv)
        wsg = [(cv, g7)] + ([gt_free[sl]] if gt_free[sl] else [])
        g8 = rec.op("vector", lambda h, tt=tt, sl=sl: h.tensor_scalar(
            out=gtmp[:, sl, :], in0=Ytok[:, tt, :], scalar1=gst[:, 38:39], scalar2=None,
            op0=ALU.mult), waits=wsg, inc=cv)
        g9 = rec.op("vector", lambda h, sl=sl: h.tensor_tensor(
            out=gtmp[:, sl, :], in0=gtmp[:, sl, :], in1=wib[:], op=ALU.mult),
            waits=[(cv, g8), wib_done], inc=cv)
        g10 = rec.op("vector", lambda h, tt=tt, sl=sl: h.tensor_tensor(
            out=gtmp[:, sl, :], in0=gtmp[:, sl, :], in1=zs[:, tt, :], op=ALU.mult),
            waits=[(cv, g9), z_done], inc=cv)
        g11 = rec.op("vector", lambda h, tt=tt, sl=sl: h.tensor_copy(
            out=zs[:, tt, :], in_=gtmp[:, sl, :]), waits=[(cv, g10)], inc=cv)
        ym_prev = g11
        gt_free[sl] = (cv, g11)
        rec.op("sync", lambda h, tt=tt, sl=sl: h.dma_start(
            out=io["scr_big"][tt * 128:(tt + 1) * 128, 0:DIN], in_=zs[:, tt, :]),
            waits=[(cv, g11)], inc=cw, k=16)
    yg_st = cw.n
    if DBG_L1STOP <= 10:
        rec.barrier(ctx.chans)
        return
    rec.barrier(ctx.chans)
    zg_scope.close()
    yt_scope.close()
    scope2.close()

    fin_scope = ExitStack()
    ygT = ctx.sb("m_ygT", [128, NCC, TOK], BF16, scope=fin_scope)
    of32 = ctx.sb("m_of32", [128, NTT, DM], F32, scope=fin_scope)
    pst_o = proj_state(ctx, fin_scope, "m_o", nps=4, nk=NCC, cw=512, ksplit=2)
    for cc in range(NCC):
        rec.op("sync", lambda h, cc=cc: h.dma_start_transpose(
            out=ygT[:, cc, :], in_=io["scr_big"][0:TOK, cc * 128:(cc + 1) * 128]),
            waits=[(cw, yg_st)], inc=trm, k=16)
    ygT_ready = (trm, trm.n)

    def oevac(rec, o, tt, ps, pm):
        m = rec.op("scalar", lambda h, ps=ps, o=o, tt=tt: h.activation(
            out=of32[:, tt, o * 512:(o + 1) * 512], in_=ps, func=AF.Copy), waits=[(pe, pm)], inc=ca)
        return (ca, m)
    emit_proj(ctx, rec, ygT, W["out_projT"], DM, make_off_evac(oevac, 0), tag="m_op0",
              ready=ygT_ready, pst=pst_o, consts=C, pe=pe, wld=wld, ksplit=2, ntt=4, t0=0)
    emit_proj(ctx, rec, ygT, W["out_projT"], DM, make_off_evac(oevac, 4), tag="m_op1",
              ready=ygT_ready, pst=pst_o, consts=C, pe=pe, wld=wld, ksplit=2, ntt=4, t0=512)
    odone = (ca, ca.n)
    emit_ln_residual_store(ctx, rec, of32, io["src"], io["dst"], odone, tag="m_fin",
                           scope=fin_scope, pre_add_hbm=io["scr_u"])
    rec.barrier(ctx.chans)
    fin_scope.close()
    blk.close()



NSP = 20480
NJC = NSP // 128
NJRB = 10


def emit_l1_ffn(ctx, rec, io, W, C):
    nc = ctx.nc
    cv = ctx.chan("q_v")
    ca = ctx.chan("q_a")
    cw = ctx.chan("q_sw")
    pe = ctx.chan("q_pe")
    wld = ctx.chan("q_wl")
    cst = ctx.chan("q_cst")

    blk = ExitStack()
    of32 = ctx.sb("q_of32", [128, NTT, DM], F32, side="right", scope=blk)
    cwT = ctx.sb("q_cwT", [8, TOK], BF16, side="right", scope=blk)
    lgts = ctx.sb("q_lg", [128, NTT, 8], F32, side="right", scope=blk)
    lst = ctx.sb("q_lst", [128, 24], F32, side="right", scope=blk)
    scope2 = ExitStack()
    xn4T = ctx.sb("q_xnT", [128, NDC, TOK], BF16, side="right", scope=scope2)

    trc0, trm0 = emit_ln_phase(ctx, rec, io["src"], io["scr_a"], xn4T, kind="ln", tag="q_ln",
                               scope=scope2)
    ready = (trc0, trm0)
    for k in range(NDC):
        rec.op("sync", lambda h, k=k: h.dma_start(
            out=io["scr_m"][k * 128:(k + 1) * 128, 0:TOK], in_=xn4T[:, k, :]),
            waits=[ready], inc=cw, k=16)
    gsrc_ready = cw.n
    if DBG_L1FSTOP <= 1:
        rec.barrier(ctx.chans)
        return

    wp_scope = ExitStack()
    pst_w = proj_state(ctx, wp_scope, "q_w", nps=4, nk=NDC, cw=512)

    def wevac(rec, o, tt, ps, pm):
        m = rec.op("scalar", lambda h, ps=ps, o=o, tt=tt: h.activation(
            out=of32[:, tt, o * 512:(o + 1) * 512], in_=ps, func=AF.Copy), waits=[(pe, pm)], inc=ca)
        return (ca, m)
    emit_proj(ctx, rec, xn4T, W["WT"], DM, wevac, tag="q_wp", ready=ready, pst=pst_w,
              consts=C, pe=pe, wld=wld)
    w_done = (ca, ca.n)
    if DBG_L1FSTOP <= 2:
        rec.barrier(ctx.chans)
        return

    gwb = ctx.sb("q_gw", [128, NDC, 8], BF16, scope=wp_scope)
    gm0 = rec.op("gpsimd", lambda h: h.dma_start(
        out=gwb[:], in_=W["gate_wT"].rearrange("(k p) n -> p k n", p=128)), inc=wld, k=16)
    psl = pst_w["pss"][0]
    lg_ev = None
    for tt in range(NTT):
        mark = None
        for k in range(NDC):
            w0 = [(wld, gm0), ready, w_done] + ([(cv, lg_ev)] if lg_ev else [])
            mark = rec.op("tensor", lambda h, k=k, tt=tt: h.matmul(
                out=psl[:, 0:8], lhsT=xn4T[:, k, tt * 128:(tt + 1) * 128], rhs=gwb[:, k, :],
                start=(k == 0), stop=(k == NDC - 1)),
                waits=w0 if k == 0 else (), inc=pe if k == NDC - 1 else None)
        lg_ev = rec.op("vector", lambda h, tt=tt: h.tensor_copy(out=lgts[:, tt, :], in_=psl[:, 0:8]),
                       waits=[(pe, mark)] + ([(pe, mark)] if lg_ev is None else [(cv, lg_ev)]), inc=cv)
    if DBG_L1FSTOP <= 3:
        rec.barrier(ctx.chans)
        return
    c1m = rec.op("vector", lambda h: h.tensor_scalar(
        out=lgts[:].rearrange("p a b -> p (a b)"), in0=lgts[:].rearrange("p a b -> p (a b)"),
        scalar1=50.0, scalar2=-50.0, op0=ALU.min, op1=ALU.max), waits=[(cv, lg_ev)], inc=cv)
    cwf = ctx.sb("q_cwf", [128, NTT, 8], F32, scope=wp_scope)
    mx8 = ctx.sb("q_mx8", [128, 8], F32, scope=wp_scope)
    cw_ev = None
    for tt in range(NTT):
        m1 = rec.op("vector", lambda h, tt=tt: h.max(out=mx8[:], in_=lgts[:, tt, :]),
                    waits=[(cv, c1m)] + ([(cv, cw_ev)] if cw_ev else []), inc=cv)
        m2 = rec.op("vector", lambda h: h.tensor_scalar(
            out=lst[:, 0:1], in0=mx8[:, 0:1], scalar1=mx8[:, 1:2], scalar2=None, op0=ALU.subtract),
            waits=[(cv, m1)], inc=cv)
        m3 = rec.op("scalar", lambda h: h.activation(out=lst[:, 1:2], in_=lst[:, 0:1],
                                                     func=AF.Sigmoid), waits=[(cv, m2)], inc=ca)
        m4 = rec.op("vector", lambda h: h.tensor_scalar(
            out=lst[:, 2:3], in0=lst[:, 1:2], scalar1=-1.0, scalar2=1.0, op0=ALU.mult, op1=ALU.add),
            waits=[(ca, m3)], inc=cv)
        m5 = rec.op("vector", lambda h, tt=tt: h.tensor_scalar(
            out=cwf[:, tt, :], in0=lgts[:, tt, :], scalar1=mx8[:, 0:1], scalar2=None,
            op0=ALU.is_equal), waits=[(cv, m4)], inc=cv)
        m6 = rec.op("vector", lambda h, tt=tt: h.tensor_scalar(
            out=cwf[:, tt, :], in0=cwf[:, tt, :], scalar1=lst[:, 1:2], scalar2=None, op0=ALU.mult),
            waits=[(cv, m5)], inc=cv)
        m7 = rec.op("vector", lambda h, tt=tt: h.tensor_scalar(
            out=mx8[:, 0:8], in0=lgts[:, tt, :], scalar1=mx8[:, 1:2], scalar2=None,
            op0=ALU.is_equal), waits=[(cv, m6)], inc=cv)
        m8 = rec.op("vector", lambda h, tt=tt: h.scalar_tensor_tensor(
            out=cwf[:, tt, :], in0=mx8[:, 0:8], scalar=lst[:, 2:3], in1=cwf[:, tt, :],
            op0=ALU.mult, op1=ALU.add), waits=[(cv, m7)], inc=cv)
        cw_ev = m8
    psq = pst_w["pss"][1]
    ct_ev = None
    pe_tr_prev = None
    for tt in range(NTT):
        wst = [(cv, cw_ev)] + ([ct_ev] if ct_ev else [])
        t1 = rec.op("tensor", lambda h, tt=tt: h.transpose(
            out=psq[0:8, 0:128], in_=cwf[:, tt, :], identity=C["eyef"][:]), waits=wst, inc=pe)
        t2 = rec.op("scalar", lambda h, tt=tt: h.activation(
            out=cwT[0:8, tt * 128:(tt + 1) * 128], in_=psq[0:8, 0:128], func=AF.Copy),
            waits=[(pe, t1)], inc=ca)
        ct_ev = (ca, t2)
    cwT_done = (ca, ca.n)
    if DBG_L1FSTOP <= 4:
        rec.barrier(ctx.chans)
        return

    wgt = ctx.sb("q_wgt", [8, NSP], BF16, side="right", scope=scope2)
    eawb = ctx.sb("q_eaw", [8, 64], F32, scope=wp_scope)
    eawt = ctx.sb("q_eawt", [8, 72], F32, scope=wp_scope)
    PTb = ctx.sb("q_PT", [64, 8], BF16, scope=wp_scope)
    em0 = rec.op("sync", lambda h: h.dma_start(out=eawb[:], in_=W["eaw"]), inc=cst, k=16)
    e1 = rec.op("vector", lambda h: h.tensor_reduce(out=eawt[0:8, 64:65], in_=eawb[0:8, :],
                                                    axis=mybir.AxisListType.X, op=ALU.max),
                waits=[(cst, em0)], inc=cv)
    e2 = rec.op("vector", lambda h: h.tensor_scalar(out=eawt[0:8, 0:64], in0=eawb[0:8, :],
                                                    scalar1=eawt[0:8, 64:65], scalar2=None,
                                                    op0=ALU.subtract), waits=[(cv, e1)], inc=cv)
    e3 = rec.op("scalar", lambda h: h.activation(out=eawt[0:8, 0:64], in_=eawt[0:8, 0:64],
                                                 func=AF.Exp), waits=[(cv, e2)], inc=ca)
    e4 = rec.op("vector", lambda h: h.tensor_reduce(out=eawt[0:8, 65:66], in_=eawt[0:8, 0:64],
                                                    axis=mybir.AxisListType.X, op=ALU.add),
                waits=[(ca, e3)], inc=cv)
    e5 = rec.op("vector", lambda h: h.reciprocal(out=eawt[0:8, 66:67], in_=eawt[0:8, 65:66]),
                waits=[(cv, e4)], inc=cv)
    e6 = rec.op("vector", lambda h: h.tensor_scalar(out=eawt[0:8, 0:64], in0=eawt[0:8, 0:64],
                                                    scalar1=eawt[0:8, 66:67], scalar2=None,
                                                    op0=ALU.mult), waits=[(cv, e5)], inc=cv)
    t3 = rec.op("tensor", lambda h: h.transpose(out=psq[0:64, 0:8], in_=eawt[0:8, 0:64],
                                                identity=C["eyef"][0:8, 0:8]),
                waits=[(cv, e6), cwT_done], inc=pe)
    t4 = rec.op("scalar", lambda h: h.activation(out=PTb[:], in_=psq[0:64, 0:8], func=AF.Copy),
                waits=[(pe, t3)], inc=ca)
    atb = ctx.sb("q_atb", [64, 2, 512], BF16, scope=wp_scope)
    imb = ctx.sb("q_imb", [8, 2, 512], F32, scope=wp_scope)
    sgb = ctx.sb("q_sgb", [8, 2, 512], F32, scope=wp_scope)
    psd = pst_w["pss"][2]
    at_free = [None, None]
    im_free = [None, None]
    wg_prev = None
    for jc in range(NSP // 512):
        sl = jc % 2
        wa = [(ca, t4)] + ([at_free[sl]] if at_free[sl] else [])
        am = rec.op("gpsimd", lambda h, jc=jc, sl=sl: h.dma_start(
            out=atb[:, sl, :], in_=W["atoms_p"][:, jc * 512:(jc + 1) * 512]), waits=wa, inc=wld, k=16)
        im = rec.op("sync", lambda h, jc=jc, sl=sl: h.dma_start(
            out=imb[0:8, sl, :], in_=W["imp_p"][:, jc * 512:(jc + 1) * 512]),
            waits=[im_free[sl]] if im_free[sl] else [], inc=cst, k=16)
        dm = rec.op("tensor", lambda h, sl=sl: h.matmul(
            out=psd[0:8, 0:512], lhsT=PTb[:], rhs=atb[:, sl, :], start=True, stop=True),
            waits=[(wld, am), (ca, t4)] + ([wg_prev] if wg_prev else []), inc=pe)
        sg = rec.op("scalar", lambda h, sl=sl: h.activation(
            out=sgb[0:8, sl, :], in_=imb[0:8, sl, :], func=AF.Sigmoid), waits=[(cst, im)], inc=ca)
        wg = rec.op("vector", lambda h, jc=jc, sl=sl: h.scalar_tensor_tensor(
            out=wgt[0:8, jc * 512:(jc + 1) * 512], in0=psd[0:8, 0:512], scalar=1.0,
            in1=sgb[0:8, sl, :], op0=ALU.mult, op1=ALU.mult), waits=[(pe, dm), (ca, sg)], inc=cv)
        at_free[sl] = (pe, dm)
        im_free[sl] = (ca, sg)
        wg_prev = (cv, wg)
    wgt_done = (cv, cv.n)
    rec.barrier(ctx.chans)
    wp_scope.close()
    if DBG_L1FSTOP <= 5:
        rec.barrier(ctx.chans)
        return

    sc_scope = ExitStack()
    gidx = ctx.sb("q_gidx", [128, NJC], I32, scope=sc_scope)
    Rb = ctx.sb("q_Rb", [128, 2, NJRB, 128], BF16, scope=sc_scope)
    xg = ctx.sb("q_xg", [128, 4, TOK], BF16, scope=sc_scope)
    Gb = ctx.sb("q_Gb", [128, 4, TOK], BF16, scope=sc_scope)
    psV = [ctx.ps("q_psV", [128, 512], F32, scope=sc_scope) for _ in range(2)]
    psO = [ctx.ps("q_psO", [128, 512], F32, scope=sc_scope) for _ in range(4)]
    gi0 = rec.op("sync", lambda h: h.dma_start(out=gidx[:], in_=W["gidx"]), inc=cst, k=16)
    rb_slab_free = [None, None]
    xg_free = [None] * 4
    G_free = [None] * 4
    psV_free = [None, None]
    psO_ev = [None] * 4
    sc_done = []
    for rb in range(16):
        rsl = rb % 2
        wrb = [rb_slab_free[rsl]] if rb_slab_free[rsl] else []
        rb0 = rec.op("gpsimd", lambda h, rb=rb, rsl=rsl: h.dma_start(
            out=Rb[:, rsl, :, :],
            in_=W["Rb"][rb * NJRB * 128:(rb + 1) * NJRB * 128, :].rearrange("(j p) r -> p j r", p=128)),
            waits=wrb, inc=wld, k=16)
        pmO = [None] * 8
        for jj in range(NJRB):
            j = rb * NJRB + jj
            sl = j % 4
            wgx = [(cst, gi0), (cw, gsrc_ready)] + ([xg_free[sl]] if xg_free[sl] else [])
            gx = rec.op("gpsimd", lambda h, j=j, sl=sl: h.indirect_dma_start(
                out=xg[:, sl, :], out_offset=None, in_=io["scr_m"][0:DM, 0:TOK],
                in_offset=bass.IndirectOffsetOnAxis(ap=gidx[:, j:j + 1], axis=0)),
                waits=wgx, inc=wld, k=16)
            for ts in range(2):
                pv2 = j % 2
                wv = [wgt_done, cwT_done] + ([psV_free[pv2]] if psV_free[pv2] else [])
                vm = rec.op("tensor", lambda h, j=j, ts=ts, pv2=pv2: h.matmul(
                    out=psV[pv2][:], lhsT=wgt[0:8, j * 128:(j + 1) * 128],
                    rhs=cwT[0:8, ts * 512:(ts + 1) * 512], start=True, stop=True),
                    waits=wv, inc=pe)
                gm = rec.op("vector", lambda h, j=j, ts=ts, sl=sl, pv2=pv2: h.tensor_tensor(
                    out=Gb[:, sl, ts * 512:(ts + 1) * 512], in0=xg[:, sl, ts * 512:(ts + 1) * 512],
                    in1=psV[pv2][:], op=ALU.mult), waits=[(wld, gx), (pe, vm)] +
                    ([G_free[sl]] if G_free[sl] and ts == 0 else []), inc=cv)
                psV_free[pv2] = (cv, gm)
                for t4i in range(4):
                    if DBG_NOSCAT:
                        continue
                    tt = ts * 4 + t4i
                    first = (jj == 0)
                    last = (jj == NJRB - 1)
                    wsc = [(cv, gm), (wld, rb0)]
                    if first and ts == 0 and psO_ev[t4i] is not None:
                        wsc.append(psO_ev[t4i])
                    needs_inc = last or (ts == 1 and t4i == 3)
                    pmO[tt] = rec.op("tensor", lambda h, j=j, tt=tt, t4i=t4i, ts=ts, sl=sl,
                                     first=first, last=last: h.matmul(
                        out=psO[t4i][:, ts * 128:ts * 128 + 128],
                        lhsT=Gb[:, sl, tt * 128:(tt + 1) * 128], rhs=Rb[:, rsl, jj, :],
                        start=(first and ts == 0), stop=last),
                        waits=wsc, inc=pe if needs_inc else None)
            G_free[sl] = (pe, pmO[7]) if not DBG_NOSCAT else (cv, gm)
            xg_free[sl] = (cv, gm)
        for tt in range(8):
            if DBG_NOSCAT or DBG_NOEVAC:
                continue
            bank = tt % 4
            ts = tt // 4
            evm = rec.op("vector", lambda h, tt=tt, bank=bank, ts=ts, rb=rb: h.tensor_tensor(
                out=of32[:, tt, rb * 128:(rb + 1) * 128], in0=of32[:, tt, rb * 128:(rb + 1) * 128],
                in1=psO[bank][:, ts * 128:ts * 128 + 128], op=ALU.add),
                waits=[(pe, pmO[4 + bank]), w_done], inc=cv)
            psO_ev[bank] = (cv, evm)
            sc_done.append(evm)
        rb_slab_free[rsl] = (pe, pmO[7])
    scat_done = (cv, cv.n)
    if DBG_L1FSTOP <= 6:
        rec.barrier(ctx.chans)
        return
    rec.barrier(ctx.chans)
    sc_scope.close()
    scope2.close()

    emit_ln_residual_store(ctx, rec, of32, io["src"], io["dst"], scat_done, tag="q_fin", scope=blk)
    rec.barrier(ctx.chans)
    blk.close()



L0A_WEIGHTS = {
    "Wq": [DM, DM], "Wk": [DM, DM], "Wv": [DM, DM], "Wo": [DM, DM],
    "l0_biases": [1, 4 * DM],
}
L0F_WEIGHTS = {
    "w_gate": [DM, 8192], "w_up": [DM, 8192], "w_down": [8192, DM],
}
L1F_WEIGHTS_BASE = {
    "WT": [DM, DM], "gate_wT": [DM, 8], "eaw": [8, 64],
}
L1A_WEIGHTS = {
    "W_inT": [DM, 6448], "out_projT": [3072, DM], "conv_w": [3072, 4],
    "conv_b_c": [128, 24], "dt_bias_r": [48, 1], "A_log_c": [48, 1], "D_row": [1, 48],
    "inner_w_row": [1, 3072], "norm_w_c": [128, 16], "conv_state": [3072, 3],
    "ssm0_t": [128, 3072], "c_hsel": [48, 48 * 128], "c_T0": [128, 128], "c_eyef": [128, 128],
}


def build_program(blocks, dbg=False):
    nc = bass.Bass("TRN2", target_bir_lowering=False)
    es = ExitStack()
    xin = nc.dram_tensor("xin", [TOK, DM], F32, kind="ExternalInput").ap()
    xout = nc.dram_tensor("xout", [TOK, DM], F32, kind="ExternalOutput").ap()
    scr_a = nc.dram_tensor("scr_a", [TOK, DM], BF16, kind="Internal").ap()
    scr_b = nc.dram_tensor("scr_b", [TOK, DM], BF16, kind="Internal").ap()
    xmid = [nc.dram_tensor(f"xmid{i}", [TOK, DM], F32, kind="Internal").ap()
            for i in range(len(blocks) - 1)]
    sh_l0 = nc.dram_tensor("sh_l0", [128, 16 * 129], BF16, addr_space="Shared", kind="Internal").ap()
    sh_l0f = nc.dram_tensor("sh_l0f", [1, 4], I32, addr_space="Shared", kind="Internal").ap()

    scr_big = nc.dram_tensor("scr_big", [TOK, 8192], BF16, kind="Internal").ap()
    scr_m = nc.dram_tensor("scr_m", [3328, TOK], BF16, kind="Internal").ap()
    scr_u = nc.dram_tensor("scr_u", [TOK, DM], F32, kind="Internal").ap()
    scr_small = nc.dram_tensor("scr_small", [64, 64], F32, kind="Internal").ap()
    sh_l1t = nc.dram_tensor("sh_l1t", [128, 128], BF16, addr_space="Shared", kind="Internal").ap()
    sh_l1tf = nc.dram_tensor("sh_l1tf", [1, 4], I32, addr_space="Shared", kind="Internal").ap()
    sh_l1s = nc.dram_tensor("sh_l1s", [128, 3072], BF16, addr_space="Shared", kind="Internal").ap()
    sh_l1sf = nc.dram_tensor("sh_l1sf", [1, 4], I32, addr_space="Shared", kind="Internal").ap()
    wext = {}
    need = {}
    if "l0a" in blocks:
        need.update(L0A_WEIGHTS)
    if "l0f" in blocks:
        need.update(L0F_WEIGHTS)
    if "l1a" in blocks:
        need.update(L1A_WEIGHTS)
    if "l1f" in blocks:
        need.update(L1F_WEIGHTS_BASE)
        need["atoms_p"] = [64, NSP]
        need["imp_p"] = [8, NSP]
        need["gidx"] = [128, NSP // 128]
    for name, shape in need.items():
        dt = F32
        if name == "gidx":
            dt = I32
        wext[name] = nc.dram_tensor(name, shape, dt, kind="ExternalInput").ap()
    if "l1f" in blocks:
        wext["Rb"] = nc.dram_tensor("Rb", [NSP, 128], BF16, kind="ExternalInput").ap()
    c_par = nc.dram_tensor("c_par", [1, 4], I32, kind="ExternalInput").ap()
    c_nonce = nc.dram_tensor("c_nonce", [1, 4], I32, kind="ExternalInput").ap()
    c_ones = nc.dram_tensor("c_ones", [1, 128], F32, kind="ExternalInput").ap()
    c_onescol = nc.dram_tensor("c_onescol", [128, 1], F32, kind="ExternalInput").ap()
    c_triu = nc.dram_tensor("c_triu", [128, 128], F32, kind="ExternalInput").ap()
    c_eye = nc.dram_tensor("c_eye", [128, 128], F32, kind="ExternalInput").ap()

    with nc.Block() as block:
        ctx = Ctx(nc, es, block)
        rec = Rec(ctx)
        C = {
            "par_sb": ctx.sb("c_par", [1, 4], I32),
            "nonce_sb": ctx.sb("c_nonce", [1, 4], I32),
            "flag_sb": ctx.sb("c_flag", [1, 4], I32),
            "ones1": ctx.sb("c_ones1", [1, 128], BF16),
            "onescol": ctx.sb("c_onescol", [128, 1], BF16),
            "triU": ctx.sb("c_triu", [128, 128], F32),
            "onesf": ctx.sb("c_onesf", [1, 128], F32),
            "eyef": ctx.sb("c_eyeg", [128, 128], F32),
        }
        C["nonce_i_sb"] = C["nonce_sb"]
        cst = ctx.chan("cst")
        rec.op("sync", lambda h: h.dma_start(out=C["par_sb"][:], in_=c_par[:]), inc=cst, k=16)
        rec.op("sync", lambda h: h.dma_start(out=C["nonce_sb"][:], in_=c_nonce[:]), inc=cst, k=16)
        rec.op("gpsimd", lambda h: h.dma_start(out=C["ones1"][:], in_=c_ones[:]), inc=cst, k=16)
        rec.op("gpsimd", lambda h: h.dma_start(out=C["onescol"][:], in_=c_onescol[:]), inc=cst, k=16)
        rec.op("sync", lambda h: h.dma_start(out=C["triU"][:], in_=c_triu[:]), inc=cst, k=16)
        rec.op("sync", lambda h: h.dma_start(out=C["onesf"][:], in_=c_ones[:]), inc=cst, k=16)
        rec.op("sync", lambda h: h.dma_start(out=C["eyef"][:], in_=c_eye[:]), inc=cst, k=16)
        rec.barrier([cst])

        cur = xin
        for j, b in enumerate(blocks):
            dst = xout if j == len(blocks) - 1 else xmid[j]
            io = {"src": cur, "dst": dst, "scr_a": scr_a, "scr_b": scr_b, "scr_big": scr_big,
                  "scr_m": scr_m, "scr_u": scr_u, "scr_small": scr_small,
                  "sh_l0": sh_l0, "sh_l0f": sh_l0f, "sh_l1t": sh_l1t, "sh_l1tf": sh_l1tf,
                  "sh_l1s": sh_l1s, "sh_l1sf": sh_l1sf}
            if b == "l0a":
                emit_l0_attn(ctx, rec, io, wext, C)
            elif b == "l0f":
                emit_l0_ffn(ctx, rec, io, wext, C)
            elif b == "l1a":
                emit_l1_attn(ctx, rec, io, wext, C)
            elif b == "l1f":
                emit_l1_ffn(ctx, rec, io, wext, C)
            else:
                raise NotImplementedError(b)
            cur = dst

        if dbg:
            dbg_hs = nc.dram_tensor("dbg_hs", [TOK, 8192], BF16, kind="ExternalOutput").ap()
            dbg_xn = nc.dram_tensor("dbg_xn", [TOK, DM], BF16, kind="ExternalOutput").ap()
            dch = ctx.chan("dbg")
            rec.op("sync", lambda h: h.dma_start(out=dbg_hs[:], in_=scr_big[:]), inc=dch, k=16)
            rec.op("sync", lambda h: h.dma_start(out=dbg_xn[:], in_=scr_a[:]), inc=dch, k=16)
            if DBG_XNT is not None:
                dbg_xnT = nc.dram_tensor("dbg_xnT", [128, NDC * TOK], BF16, kind="ExternalOutput").ap()
                rec.op("sync", lambda h: h.dma_start(out=dbg_xnT[:],
                                                     in_=DBG_XNT[0][:].rearrange("p a b -> p (a b)")),
                       inc=dch, k=16)
        rec.barrier(ctx.chans)
        rec.flush()
    return nc


def host_consts():
    tri = np.triu(np.ones((128, 128), np.float32))
    return {
        "c_ones": np.ones((1, 128), np.float32),
        "c_onescol": np.ones((128, 1), np.float32),
        "c_triu": tri,
        "c_eye": np.eye(128, dtype=np.float32),
    }


def run_blocks(blocks, inputs, x_full, dbg=False):
    consts = host_consts()
    nonce = np.random.randint(1, 2**31 - 1, size=(1, 4)).astype(np.int32)
    in_maps = []
    need = {}
    if "l0f" in blocks:
        need["w_gate"] = np.ascontiguousarray(inputs["L0_ffn_w_gate"])
        need["w_up"] = np.ascontiguousarray(inputs["L0_ffn_w_up"])
        need["w_down"] = np.ascontiguousarray(inputs["L0_ffn_w_down"])
    if "l1f" in blocks:
        import ml_dtypes as _md
        mi_ = np.asarray(inputs["L1_ffn_mask_idx"]).astype(np.int64)
        rows_ = mi_ // DM
        cols_ = mi_ % DM
        global NSP, NJC, NJRB
        cnt = np.bincount(rows_ // 128, minlength=16)
        NJRB = max(10, int(np.ceil(cnt.max() / 128.0)))
        NSP = 16 * NJRB * 128
        NJC = NSP // 128
        gidx_ = np.zeros((16, NJRB * 128), np.int32)
        Rbm_ = np.zeros((16, NJRB * 128, 128), np.float32)
        srcx = np.full((16, NJRB * 128), -1, np.int64)
        for rb in range(16):
            sel = np.nonzero((rows_ >= rb * 128) & (rows_ < (rb + 1) * 128))[0]
            srcx[rb, :len(sel)] = sel
            gidx_[rb, :len(sel)] = cols_[sel]
            Rbm_[rb, np.arange(len(sel)), rows_[sel] - rb * 128] = 1.0
        flat = srcx.reshape(-1)
        valid = flat >= 0
        atoms_p = np.zeros((64, NSP), np.float32)
        atoms_p[:, valid] = np.asarray(inputs["L1_ffn_atoms"], np.float32)[:, flat[valid]]
        imp_p = np.zeros((8, NSP), np.float32)
        imp_p[:, valid] = np.asarray(inputs["L1_ffn_importance"], np.float32)[:, flat[valid]]
        need["atoms_p"] = atoms_p
        need["imp_p"] = imp_p
        need["gidx"] = np.ascontiguousarray(gidx_.reshape(NJC, 128).T)
        need["Rb"] = Rbm_.reshape(NSP, 128).astype(_md.bfloat16)
        need["WT"] = np.ascontiguousarray(np.asarray(inputs["L1_ffn_W"], np.float32).T)
        need["gate_wT"] = np.ascontiguousarray(np.asarray(inputs["L1_ffn_gate_w"], np.float32).T)
        need["eaw"] = np.ascontiguousarray(inputs["L1_ffn_expert_atom_weights"], np.float32)
    if "l1a" in blocks:
        NH = 48
        need["W_inT"] = np.ascontiguousarray(inputs["L1_attn_in_proj_w"].T.astype(np.float32))
        need["out_projT"] = np.ascontiguousarray(inputs["L1_attn_out_proj_w"].T.astype(np.float32))
        need["conv_w"] = np.ascontiguousarray(inputs["L1_attn_conv_w"], np.float32)
        need["conv_b_c"] = np.ascontiguousarray(inputs["L1_attn_conv_b"].reshape(24, 128).T, np.float32)
        need["dt_bias_r"] = np.ascontiguousarray(inputs["L1_attn_dt_bias"].reshape(48, 1), np.float32)
        need["A_log_c"] = np.ascontiguousarray(inputs["L1_attn_A_log"].reshape(48, 1), np.float32)
        need["D_row"] = np.ascontiguousarray(inputs["L1_attn_D_param"].reshape(1, 48), np.float32)
        need["inner_w_row"] = np.ascontiguousarray(inputs["L1_attn_inner_norm_w"].reshape(1, 3072), np.float32)
        need["norm_w_c"] = np.ascontiguousarray(inputs["L1_attn_norm_w"].reshape(16, 128).T, np.float32)
        need["conv_state"] = np.ascontiguousarray(inputs["L1_attn_conv_state"][0], np.float32)
        need["ssm0_t"] = np.ascontiguousarray(
            np.transpose(np.asarray(inputs["L1_attn_ssm_state"], np.float32)[0], (1, 0, 2)).reshape(128, 3072))
        hsel = np.zeros((48, 48 * 128), np.float32)
        for hh in range(NH):
            hsel[hh, hh * 128:(hh + 1) * 128] = 1.0
        need["c_hsel"] = hsel
        tri = np.triu(np.ones((128, 128), np.float32))
        need["c_T0"] = np.where(tri > 0, 0.0, -1e9).astype(np.float32)
        need["c_eyef"] = np.eye(128, dtype=np.float32)
    if "l0a" in blocks:
        need["Wq"] = np.ascontiguousarray(inputs["L0_attn_Wq"])
        need["Wk"] = np.ascontiguousarray(inputs["L0_attn_Wk"])
        need["Wv"] = np.ascontiguousarray(inputs["L0_attn_Wv"])
        need["Wo"] = np.ascontiguousarray(inputs["L0_attn_Wo"])
        need["l0_biases"] = np.concatenate([
            inputs["L0_attn_bq"], inputs["L0_attn_bk"], inputs["L0_attn_bv"],
            inputs["L0_attn_bo"]]).reshape(1, 4 * DM).astype(np.float32)
    nc = build_program(blocks, dbg=dbg)
    for c in range(NCORE):
        b, hh = c // 2, c % 2
        m = dict(need)
        m["xin"] = np.ascontiguousarray(x_full[b, hh * TOK:(hh + 1) * TOK, :], np.float32)
        m["c_par"] = np.full((1, 4), hh, np.int32)
        m["c_nonce"] = nonce
        m.update(consts)
        in_maps.append(m)
    res = run_bass_kernel_spmd(nc, in_maps, core_ids=list(range(NCORE)))
    out = np.empty((4, 2048, DM), np.float32)
    for c in range(NCORE):
        b, hh = c // 2, c % 2
        out[b, hh * TOK:(hh + 1) * TOK, :] = res.results[c]["xout"]
    if dbg:
        return out, res
    return out


def kernel(**inputs):
    x = np.asarray(inputs["L0_attn_x"], np.float32)
    return run_blocks(["l0a", "l0f", "l1a", "l1f"], inputs, x)


# version 09 (final)
# speedup vs baseline: 1.1849x; 1.0285x over previous
import os
import numpy as np
import ml_dtypes
from contextlib import ExitStack

DBG_NOMASK = os.environ.get("DBG_NOMASK", "0") == "1"
DBG_L1STOP = int(os.environ.get("DBG_L1STOP", "99"))
DBG_L1FSTOP = int(os.environ.get("DBG_L1FSTOP", "99"))
DBG_NOSCAT = os.environ.get("DBG_NOSCAT", "0") == "1"
DBG_NOEVAC = os.environ.get("DBG_NOEVAC", "0") == "1"
DBG_XNT = None

import concourse.bass as bass
import concourse.mybir as mybir
from concourse.bass_utils import run_bass_kernel_spmd

F32 = mybir.dt.float32
BF16 = mybir.dt.bfloat16
I32 = mybir.dt.int32
AF = mybir.ActivationFunctionType
ALU = mybir.AluOpType
BT = ml_dtypes.bfloat16

NCORE = 8
TOK = 1024
DM = 2048
NTT = TOK // 128
NDC = DM // 128
MAGIC = 1234567891
EPS_LN = 1e-5

ENGS = ("sync", "vector", "scalar", "tensor", "gpsimd")


def ceil_div(a, b):
    return (a + b - 1) // b


class Ch:

    def __init__(self, es, nc, name):
        self.sem = es.enter_context(nc.semaphore(name))
        self.n = 0
        self._waited = {}

    def reserve(self, k):
        self.n += k
        return self.n

    def wait_emit(self, eng, v):
        if v <= 0:
            return
        key = id(eng)
        if self._waited.get(key, -1) >= v:
            return
        eng.wait_ge(self.sem, v)
        self._waited[key] = v


class Rec:

    def __init__(self, ctx):
        self.ctx = ctx
        self.q = {e: [] for e in ENGS}

    def op(self, eng, fn, waits=(), inc=None, k=1):
        ws = [(c, v) for (c, v) in waits]
        mark = None
        sem = None
        if inc is not None:
            mark = inc.reserve(k)
            sem = inc.sem
        kk = k

        def run(h):
            for c, v in ws:
                c.wait_emit(h, v)
            inst = fn(h)
            if sem is not None:
                inst.then_inc(sem, kk)
        self.q[eng].append(run)
        return mark

    def raw(self, eng, fn):
        self.q[eng].append(fn)

    def barrier(self, chans):
        ws = [(c, c.n) for c in chans]
        for e in ENGS:
            def run(h, ws=ws):
                for c, v in ws:
                    c.wait_emit(h, v)
            self.q[e].append(run)

    def flush(self):
        b = self.ctx.block
        for name in ENGS:
            fns = self.q[name]
            if fns:
                getattr(b, name)(lambda h, fns=fns: [f(h) for f in fns])
        self.q = {e: [] for e in ENGS}


class Ctx:
    def __init__(self, nc, es, block):
        self.nc = nc
        self.es = es
        self.block = block
        self._chans = {}
        self._uid = 0

    def chan(self, name):
        if name not in self._chans:
            self._chans[name] = Ch(self.es, self.nc, name)
        return self._chans[name]

    @property
    def chans(self):
        return list(self._chans.values())

    def uid(self):
        self._uid += 1
        return self._uid

    def sb(self, name, shape, dtype, scope=None, side=None):
        s = scope if scope is not None else self.es
        return s.enter_context(self.nc.sbuf_tensor(f"{name}_{self.uid()}", shape, dtype, side=side))

    def ps(self, name, shape, dtype=F32, scope=None):
        s = scope if scope is not None else self.es
        return s.enter_context(self.nc.psum_tensor(f"{name}_{self.uid()}", shape, dtype))



def w_chunk_ap(w_hbm, o0, o1):
    return w_hbm[:, o0:o1].rearrange("(k p) n -> p k n", p=128)


def emit_ln_phase(ctx, rec, src_x, scr_hbm, xnT, *, kind="ln", eps=EPS_LN,
                  u_keep=None, tag="ln", scope=None, u_hbm=None, rms2=False):
    nc = ctx.nc
    ld = ctx.chan(tag + "_ld")
    cv = ctx.chan(tag + "_v")
    ca = ctx.chan(tag + "_a")
    cw = ctx.chan(tag + "_w")
    tr = ctx.chan(tag + "_t")
    xt = ctx.sb("lnx", [128, 2, DM], F32, scope=scope)
    xb = ctx.sb("lnxb", [128, 2, DM], BF16, scope=scope)
    st = ctx.sb("lnst", [128, 32], F32, scope=scope)
    ust = ctx.sb("lnu", [128, 2, DM], F32, scope=scope) if rms2 else None
    cu = ctx.chan(tag + "_u")
    free_marks = [0] * NTT
    prev_tail = 0
    for i in range(NTT):
        sl = i % 2
        lm = rec.op("sync", lambda h, i=i, sl=sl: h.dma_start(out=xt[:, sl, :], in_=src_x[i * 128:(i + 1) * 128, :]),
                    waits=[(cv, free_marks[i - 2])] if i >= 2 else [], inc=ld, k=16)
        m1 = None
        for g in range(4):
            m1 = rec.op("vector", lambda h, sl=sl, g=g: h.bn_stats(out=st[:, g * 6:(g + 1) * 6],
                                                                   in_=xt[:, sl, g * 512:(g + 1) * 512]),
                        waits=[(ld, lm), (cv, prev_tail)] if g == 0 else [(cv, m1)], inc=cv)
        m2 = rec.op("vector", lambda h: h.bn_aggr(out=st[:, 24:26], in_=st[:, 0:24]),
                    waits=[(cv, m1)], inc=cv)
        if kind == "rms":
            m3 = rec.op("vector", lambda h: h.tensor_scalar(out=st[:, 26:27], in0=st[:, 24:25], scalar1=st[:, 24:25],
                                                            scalar2=None, op0=ALU.mult), waits=[(cv, m2)], inc=cv)
            m4 = rec.op("vector", lambda h: h.tensor_tensor(out=st[:, 27:28], in0=st[:, 26:27], in1=st[:, 25:26],
                                                            op=ALU.add), waits=[(cv, m3)], inc=cv)
            vcol = st[:, 27:28]
            mzero = True
        else:
            m4 = m2
            vcol = st[:, 25:26]
            mzero = False
        m5 = rec.op("vector", lambda h, vcol=vcol: h.tensor_scalar(out=st[:, 28:29], in0=vcol, scalar1=float(eps),
                                                                   scalar2=None, op0=ALU.add), waits=[(cv, m4)], inc=cv)
        m6 = rec.op("scalar", lambda h: h.activation(out=st[:, 29:30], in_=st[:, 28:29], func=AF.Sqrt),
                    waits=[(cv, m5)], inc=ca)
        m7 = rec.op("vector", lambda h: h.reciprocal(out=st[:, 28:29], in_=st[:, 29:30]),
                    waits=[(ca, m6)], inc=cv)
        if rms2:
            m8u = rec.op("vector", lambda h, sl=sl: h.tensor_scalar(out=ust[:, sl, :], in0=xt[:, sl, :],
                                                                    scalar1=st[:, 24:25], scalar2=st[:, 28:29],
                                                                    op0=ALU.subtract, op1=ALU.mult),
                         waits=[(cv, m7)], inc=cv)
            rec.op("sync", lambda h, i=i, sl=sl: h.dma_start(out=u_hbm[i * 128:(i + 1) * 128, :],
                                                             in_=ust[:, sl, :]),
                   waits=[(cv, m8u)], inc=cu, k=16)
            m1b = None
            for g in range(4):
                m1b = rec.op("vector", lambda h, sl=sl, g=g: h.bn_stats(out=st[:, g * 6:(g + 1) * 6],
                                                                        in_=ust[:, sl, g * 512:(g + 1) * 512]),
                             waits=[(cv, m8u)] if g == 0 else [(cv, m1b)], inc=cv)
            m2b = rec.op("vector", lambda h: h.bn_aggr(out=st[:, 24:26], in_=st[:, 0:24]),
                         waits=[(cv, m1b)], inc=cv)
            m3b = rec.op("vector", lambda h: h.tensor_scalar(out=st[:, 26:27], in0=st[:, 24:25], scalar1=st[:, 24:25],
                                                             scalar2=None, op0=ALU.mult), waits=[(cv, m2b)], inc=cv)
            m4b = rec.op("vector", lambda h: h.tensor_tensor(out=st[:, 27:28], in0=st[:, 26:27], in1=st[:, 25:26],
                                                             op=ALU.add), waits=[(cv, m3b)], inc=cv)
            m5b = rec.op("vector", lambda h: h.tensor_scalar(out=st[:, 28:29], in0=st[:, 27:28], scalar1=float(eps),
                                                             scalar2=None, op0=ALU.add), waits=[(cv, m4b)], inc=cv)
            m6b = rec.op("scalar", lambda h: h.activation(out=st[:, 29:30], in_=st[:, 28:29], func=AF.Sqrt),
                         waits=[(cv, m5b)], inc=ca)
            m7b = rec.op("vector", lambda h: h.reciprocal(out=st[:, 28:29], in_=st[:, 29:30]),
                         waits=[(ca, m6b)], inc=cv)
            m8 = rec.op("vector", lambda h, sl=sl: h.tensor_scalar(out=xb[:, sl, :], in0=ust[:, sl, :],
                                                                   scalar1=st[:, 28:29], scalar2=None, op0=ALU.mult),
                        waits=[(cv, m7b)], inc=cv)
        elif mzero:
            m8 = rec.op("vector", lambda h, sl=sl: h.tensor_scalar(out=xb[:, sl, :], in0=xt[:, sl, :],
                                                                   scalar1=st[:, 28:29], scalar2=None, op0=ALU.mult),
                        waits=[(cv, m7)], inc=cv)
        else:
            m8 = rec.op("vector", lambda h, sl=sl: h.tensor_scalar(out=xb[:, sl, :], in0=xt[:, sl, :],
                                                                   scalar1=st[:, 24:25], scalar2=st[:, 28:29],
                                                                   op0=ALU.subtract, op1=ALU.mult),
                        waits=[(cv, m7)], inc=cv)
        if u_keep is not None:
            if mzero:
                m8b = rec.op("vector", lambda h, sl=sl, i=i: h.tensor_scalar(out=u_keep[:, i, :], in0=xt[:, sl, :],
                                                                             scalar1=st[:, 28:29], scalar2=None,
                                                                             op0=ALU.mult), waits=[(cv, m7)], inc=cv)
            else:
                m8b = rec.op("vector", lambda h, sl=sl, i=i: h.tensor_scalar(out=u_keep[:, i, :], in0=xt[:, sl, :],
                                                                             scalar1=st[:, 24:25], scalar2=st[:, 28:29],
                                                                             op0=ALU.subtract, op1=ALU.mult),
                             waits=[(cv, m7)], inc=cv)
            m8 = m8b
        free_marks[i] = m8
        prev_tail = m8
        rec.op("sync", lambda h, i=i, sl=sl: h.dma_start(out=scr_hbm[i * 128:(i + 1) * 128, :], in_=xb[:, sl, :]),
               waits=[(cv, m8)], inc=cw, k=16)
    wall = cw.n
    tm = []
    for k in range(NDC):
        tm.append(rec.op("sync", lambda h, k=k: h.dma_start_transpose(out=xnT[:, k, :],
                                                                      in_=scr_hbm[:, k * 128:(k + 1) * 128]),
                         waits=[(cw, wall)], inc=tr, k=16))
    return tr, tm[-1]



def make_off_evac(evac, tt0):
    def e2(rec, o, tt, ps, pm):
        return evac(rec, o, tt + tt0, ps, pm)
    return e2


def mm_group(rec, pe_chan, mms, waits=()):
    n = len(mms)
    mark = None
    for i, fn in enumerate(mms):
        first, last = i == 0, i == n - 1
        mark = rec.op("tensor", (lambda h, fn=fn, first=first, last=last: fn(h, first, last)),
                      waits=waits if first else (), inc=pe_chan if last else None)
    return mark


def proj_state(ctx, scope, tag, nps=4, nk=NDC, cw=512, ksplit=1):
    return {
        "wbuf": ctx.sb(tag + "_w", [128, 2, nk // ksplit, cw], BF16, scope=scope),
        "w_free": [None, None],
        "pss": [ctx.ps(tag + "_ps", [128, 512], F32, scope=scope) for _ in range(nps)],
        "ps_free": [None] * nps,
        "nk": nk, "cw": cw,
    }


def emit_proj(ctx, rec, xnT, w_hbm, n_out, evac, *, tag, ready, pst, bias=None, consts=None,
              pe=None, wld=None, ntt=NTT, t0=0, ksplit=1):
    wld = wld or ctx.chan(tag + "_wl")
    pe = pe or ctx.chan(tag + "_pe")
    nk = pst["nk"]
    CW = pst["cw"]
    NO = ceil_div(n_out, CW)
    nkh = nk // ksplit if ksplit > 1 else nk
    wbuf = pst["wbuf"]
    pss = pst["pss"]
    ps_free = pst["ps_free"]
    w_free = pst["w_free"]
    assert ksplit == 1 or len(pss) >= ntt
    for o in range(NO):
        o0, o1 = o * CW, min((o + 1) * CW, n_out)
        ow = o1 - o0
        pm_of = {}
        for kh in range(ksplit):
            sl = (o * ksplit + kh) % 2
            ws = [w_free[sl]] if w_free[sl] is not None else []
            k0 = kh * nkh
            wm = rec.op("gpsimd", lambda h, o0=o0, sl=sl, ow=ow, k0=k0:
                        h.dma_start(out=wbuf[:, sl, 0:nkh, 0:ow],
                                    in_=w_hbm[k0 * 128:(k0 + nkh) * 128, o0:o0 + ow]
                                    .rearrange("(k p) n -> p k n", p=128)),
                        waits=ws, inc=wld, k=16)
            last_pm = None
            for tt in range(ntt):
                pi = (o * ntt + tt) % len(pss) if ksplit == 1 else tt
                ps = pss[pi]
                waits = [(wld, wm), ready]
                if kh == 0 and ps_free[pi] is not None:
                    waits.append(ps_free[pi])
                nmm = nkh + (1 if (bias is not None and kh == ksplit - 1) else 0)
                mark = None
                for k in range(nkh):
                    first = (kh == 0 and k == 0)
                    lastm = (kh == ksplit - 1 and k == nmm - 1)
                    lastk = (k == nkh - 1)
                    mark = rec.op("tensor", lambda h, k=k, tt=tt, sl=sl, ps=ps, ow=ow, k0=k0,
                                  first=first, lastm=lastm:
                                  h.matmul(out=ps[:, 0:ow], lhsT=xnT[:, k0 + k, t0 + tt * 128:t0 + (tt + 1) * 128],
                                           rhs=wbuf[:, sl, k, 0:ow], start=first, stop=lastm),
                                  waits=waits if k == 0 else (), inc=pe if (lastm or lastk) else None)
                if bias is not None and kh == ksplit - 1:
                    bsb, boff = bias
                    mark = rec.op("tensor", lambda h, ps=ps, ow=ow, o0=o0, bsb=bsb, boff=boff:
                                  h.matmul(out=ps[:, 0:ow], lhsT=consts["ones1"][0:1, 0:128],
                                           rhs=bsb[0:1, boff + o0:boff + o0 + ow], start=False, stop=True),
                                  inc=pe)
                last_pm = mark
                if kh == ksplit - 1:
                    pm_of[tt] = mark
                    if ksplit == 1:
                        ps_free[pi] = evac(rec, o, tt, ps[:, 0:ow], mark)
            w_free[sl] = (pe, last_pm)
        if ksplit > 1:
            for tt in range(ntt):
                ps_free[tt] = evac(rec, o, tt, pss[tt][:, 0:ow], pm_of[tt])



def emit_l0_attn(ctx, rec, io, W, C):
    nc = ctx.nc
    HN = 16
    bias_scope = ExitStack()
    bias_sb = ctx.sb("a_bias", [1, 4 * DM], BF16, scope=bias_scope, side="right")
    kv_scope = ExitStack()
    ktok = ctx.sb("a_ktok", [128, NTT, DM], BF16, scope=kv_scope, side="right")
    vtok = ctx.sb("a_vtok", [128, NTT, DM], BF16, scope=kv_scope, side="right")

    cv = ctx.chan("l0_v")
    ca = ctx.chan("l0_a")
    cw = ctx.chan("l0_sw")
    pe = ctx.chan("l0_pe")
    wld = ctx.chan("l0_wl")
    trq = ctx.chan("l0_trq")

    scopeA = ExitStack()
    xnT = ctx.sb("a_xnT", [128, NDC, TOK], BF16, scope=scopeA)
    elu_tmp = ctx.sb("a_elut", [128, 4, 512], F32, scope=scopeA)
    elu_e = ctx.sb("a_elue", [128, 4, 512], F32, scope=scopeA)
    qstage = ctx.sb("a_qst", [128, 4, 512], BF16, scope=scopeA)
    pst = proj_state(ctx, scopeA, "l0p")
    trc, trm = emit_ln_phase(ctx, rec, io["src"], io["scr_a"], xnT, kind="ln", tag="l0ln", scope=scopeA)
    ready = (trc, trm)
    rec.op("gpsimd", lambda h: h.dma_start(out=bias_sb[:], in_=W["l0_biases"]), inc=wld, k=16)

    elu_slot_mark = {}

    def elu_evac(dst_fn, store_fn):
        def evac(rec, o, tt, ps, pm):
            i = (o * NTT + tt) % 4
            ws = [(pe, pm)]
            if i in elu_slot_mark:
                ws.append(elu_slot_mark[i])
            if ("q", i) in elu_slot_mark:
                ws.append(elu_slot_mark[("q", i)])
            ow = ps.shape[-1]
            m1 = rec.op("vector", lambda h, ps=ps, i=i, ow=ow: h.tensor_scalar(
                out=elu_tmp[:, i, 0:ow], in0=ps, scalar1=0.0, scalar2=None, op0=ALU.min),
                waits=ws, inc=cv)
            m2 = rec.op("scalar", lambda h, i=i, ow=ow: h.activation(
                out=elu_e[:, i, 0:ow], in_=elu_tmp[:, i, 0:ow], func=AF.Exp),
                waits=[(cv, m1)], inc=ca)
            m3 = rec.op("vector", lambda h, ps=ps, i=i, o=o, tt=tt, ow=ow: h.scalar_tensor_tensor(
                out=dst_fn(o, tt, i), in0=ps, scalar=0.0, in1=elu_e[:, i, 0:ow],
                op0=ALU.max, op1=ALU.add), waits=[(ca, m2)], inc=cv)
            elu_slot_mark[i] = (cv, m3)
            sf = store_fn(o, tt, i)
            if sf is not None:
                hbm_ap, src_ap = sf
                sm = rec.op("sync", lambda h, hbm_ap=hbm_ap, src_ap=src_ap:
                            h.dma_start(out=hbm_ap, in_=src_ap),
                            waits=[(cv, m3), ready], inc=cw, k=16)
                elu_slot_mark[("q", i)] = (cw, sm)
            return (cv, m3)
        return evac

    kevac = elu_evac(lambda o, tt, i: ktok[:, tt, o * 512:(o + 1) * 512],
                     lambda o, tt, i: ((io["scr_b"][tt * 128:(tt + 1) * 128, :], ktok[:, tt, :])
                                       if o == 3 else None))
    emit_proj(ctx, rec, xnT, W["Wk"], DM, kevac, tag="l0k", ready=ready, pst=pst,
              bias=(bias_sb, DM), consts=C, pe=pe, wld=wld)
    kdone = (cv, cv.n)

    def vevac(rec, o, tt, ps, pm):
        m = rec.op("scalar", lambda h, ps=ps, o=o, tt=tt: h.activation(
            out=vtok[:, tt, o * 512:(o + 1) * 512], in_=ps, func=AF.Copy), waits=[(pe, pm)], inc=ca)
        return (ca, m)
    emit_proj(ctx, rec, xnT, W["Wv"], DM, vevac, tag="l0v", ready=ready, pst=pst,
              bias=(bias_sb, 2 * DM), consts=C, pe=pe, wld=wld)
    vdone = (ca, ca.n)

    qevac = elu_evac(lambda o, tt, i: qstage[:, i, :],
                     lambda o, tt, i: (io["scr_a"][tt * 128:(tt + 1) * 128, o * 512:(o + 1) * 512],
                                       qstage[:, i, :]))
    emit_proj(ctx, rec, xnT, W["Wq"], DM, qevac, tag="l0q", ready=ready, pst=pst,
              bias=(bias_sb, 0), consts=C, pe=pe, wld=wld)
    q_stores = cw.n
    rec.barrier(ctx.chans)
    scopeA.close()

    scopeB = ExitStack()
    kfT = ctx.sb("a_kfT", [128, NDC, TOK], BF16, scope=scopeB)
    qfT = ctx.sb("a_qfT", [128, NDC, TOK], BF16, scope=scopeB)
    sendb = ctx.sb("a_send", [128, HN * 129], BF16, scope=scopeB)
    seedb = ctx.sb("a_seed", [128, HN * 129], BF16, scope=scopeB)
    KVrun = ctx.sb("a_kvr", [128, HN * 129], F32, scope=scopeB)
    KVbf = ctx.sb("a_kvb", [128, HN * 129], BF16, scope=scopeB)
    At = ctx.sb("a_At", [128, 2, 128], BF16, scope=scopeB)
    den = ctx.sb("a_den", [128, 4], F32, scope=scopeB)
    attno = ctx.sb("a_attno", [128, NTT, DM], BF16, scope=scopeB)

    st_scope = ExitStack()
    psst = [ctx.ps("l0_psst", [128, 132], F32, scope=st_scope) for _ in range(2)]
    st_free = [None, None]
    for hh in range(HN):
        psp = psst[hh % 2]
        mms = []
        for n in range(NTT):
            mms.append(lambda h, s, e, n=n, hh=hh, psp=psp: h.matmul(
                out=psp[:, 0:128], lhsT=ktok[:, n, hh * 128:(hh + 1) * 128],
                rhs=vtok[:, n, hh * 128:(hh + 1) * 128], start=s, stop=False))
            mms.append(lambda h, s, e, n=n, hh=hh, psp=psp: h.matmul(
                out=psp[:, 128:129], lhsT=ktok[:, n, hh * 128:(hh + 1) * 128],
                rhs=C["onescol"][0:128, 0:1], start=False, stop=e))
        ws = [kdone, vdone] + ([st_free[hh % 2]] if st_free[hh % 2] else [])
        pm = mm_group(rec, pe, mms, waits=ws)
        em = rec.op("scalar", lambda h, hh=hh, psp=psp: h.activation(
            out=sendb[:, hh * 129:(hh + 1) * 129], in_=psp[:, 0:129], func=AF.Copy),
            waits=[(pe, pm)], inc=ca)
        st_free[hh % 2] = (ca, em)
    st_all = ca.n

    sh_pay = io["sh_l0"]
    sh_flag = io["sh_l0f"]
    snd = ctx.chan("l0_snd")
    rcv = ctx.chan("l0_rcv")
    seeded = ctx.chan("l0_seeded")
    snd_v = st_all

    def g_exchange(h):
        with h.register("a_pp") as pp:
            h.reg_load(pp, C["par_sb"][0:1, 0:1])
            with h.If_eq(pp, 0):
                h.wait_ge(ca.sem, snd_v)
                h.dma_start(out=sh_pay[:], in_=sendb[:]).then_inc(snd.sem, 16)
                h.wait_ge(snd.sem, 16)
                h.dma_start(out=sh_flag[:], in_=C["nonce_sb"][0:1, 0:4]).then_inc(snd.sem, 16)
            with h.Else():
                with h.register("a_pc") as pc, h.register("a_pex") as pex, h.register("a_f0") as f0, \
                     h.register("a_nn") as nn:
                    h.reg_load(nn, C["nonce_sb"][0:1, 0:1])
                    h.reg_mov(pc, 1)
                    h.reg_mov(pex, 0)
                    with h.While(pc):
                        h.reg_alu(pex, pex, 16, ALU.add)
                        h.dma_start(out=C["flag_sb"][0:1, 0:4], in_=sh_flag[0:1, :]).then_inc(snd.sem, 16)
                        h.wait_ge(snd.sem, pex)
                        h.reg_load(f0, C["flag_sb"][0:1, 0:1])
                        h.reg_alu(pc, f0, nn, ALU.not_equal)
                h.dma_start(out=seedb[:], in_=sh_pay[:]).then_inc(rcv.sem, 16)
    rec.raw("gpsimd", g_exchange)

    seeded.reserve(1)

    def v_seed(h):
        with h.register("a_pv") as pv:
            h.reg_load(pv, C["par_sb"][0:1, 0:1])
            with h.If_eq(pv, 0):
                h.memset(KVrun[:], 0.0).then_inc(seeded.sem, 1)
            with h.Else():
                h.wait_ge(rcv.sem, 16)
                h.tensor_copy(out=KVrun[:], in_=seedb[:]).then_inc(seeded.sem, 1)
    rec.raw("vector", v_seed)
    mkb = rec.op("vector", lambda h: h.tensor_copy(out=KVbf[:], in_=KVrun[:]),
                 waits=[(seeded, 1)], inc=cv)

    for k in range(NDC):
        rec.op("sync", lambda h, k=k: h.dma_start_transpose(out=kfT[:, k, :],
                                                            in_=io["scr_b"][:, k * 128:(k + 1) * 128]),
               waits=[(cw, q_stores)], inc=trq, k=16)
        rec.op("sync", lambda h, k=k: h.dma_start_transpose(out=qfT[:, k, :],
                                                            in_=io["scr_a"][:, k * 128:(k + 1) * 128]),
               waits=[(cw, q_stores)], inc=trq, k=16)
    trdone = (trq, trq.n)
    rec.barrier(ctx.chans)
    st_scope.close()

    ap_scope = ExitStack()
    psA = [ctx.ps("l0_psA", [128, 128], F32, scope=ap_scope) for _ in range(2)]
    psY = [ctx.ps("l0_psY", [128, 132], F32, scope=ap_scope) for _ in range(2)]
    psK = [ctx.ps("l0_psK", [128, 132], F32, scope=ap_scope) for _ in range(2)]
    psA_free = [None, None]
    psY_free = [None, None]
    psK_free = [None, None]
    At_free = [None, None]
    kv_mark = {}
    for hh in range(HN):
        for n in range(NTT):
            sl = (hh * NTT + n) % 2
            wsA = [trdone] + ([psA_free[sl]] if psA_free[sl] else [])
            pmA = mm_group(rec, pe, [lambda h, s, e, hh=hh, n=n, sl=sl: h.matmul(
                out=psA[sl][:, 0:128], lhsT=kfT[:, hh, n * 128:(n + 1) * 128],
                rhs=qfT[:, hh, n * 128:(n + 1) * 128], start=s, stop=e)], waits=wsA)
            wsAt = [(pe, pmA)] + ([At_free[sl]] if At_free[sl] else [])
            mAt = rec.op("vector", lambda h, sl=sl: h.tensor_tensor(
                out=At[:, sl, :], in0=psA[sl][:, 0:128], in1=C["triU"][:, 0:128], op=ALU.mult),
                waits=wsAt, inc=cv)
            psA_free[sl] = (cv, mAt)
            wsY = [(cv, mAt), (seeded, 1), (cv, mkb)]
            if hh in kv_mark:
                wsY.append(kv_mark[hh])
            if psY_free[sl]:
                wsY.append(psY_free[sl])
            mms = [
                lambda h, s, e, sl=sl, hh=hh, n=n: h.matmul(
                    out=psY[sl][:, 0:128], lhsT=At[:, sl, :],
                    rhs=vtok[:, n, hh * 128:(hh + 1) * 128], start=s, stop=False),
                lambda h, s, e, sl=sl, hh=hh, n=n: h.matmul(
                    out=psY[sl][:, 128:129], lhsT=At[:, sl, :],
                    rhs=C["onescol"][0:128, 0:1], start=False, stop=False),
                lambda h, s, e, sl=sl, hh=hh, n=n: h.matmul(
                    out=psY[sl][:, 0:129], lhsT=qfT[:, hh, n * 128:(n + 1) * 128],
                    rhs=KVbf[:, hh * 129:(hh + 1) * 129], start=False, stop=e),
            ]
            pmY = mm_group(rec, pe, mms, waits=wsY)
            At_free[sl] = (pe, pmY)
            m1 = rec.op("vector", lambda h, sl=sl: h.tensor_scalar(
                out=den[:, sl:sl + 1], in0=psY[sl][:, 128:129], scalar1=1e-6, scalar2=None,
                op0=ALU.add), waits=[(pe, pmY)], inc=cv)
            m2 = rec.op("vector", lambda h, sl=sl: h.reciprocal(
                out=den[:, 2 + sl:3 + sl], in_=den[:, sl:sl + 1]), waits=[(cv, m1)], inc=cv)
            m3 = rec.op("vector", lambda h, sl=sl, hh=hh, n=n: h.tensor_scalar(
                out=attno[:, n, hh * 128:(hh + 1) * 128], in0=psY[sl][:, 0:128],
                scalar1=den[:, 2 + sl:3 + sl], scalar2=None, op0=ALU.mult),
                waits=[(cv, m2)], inc=cv)
            psY_free[sl] = (cv, m3)
            if n < NTT - 1:
                wsK = [kdone, vdone] + ([psK_free[sl]] if psK_free[sl] else [])
                pmK = mm_group(rec, pe, [
                    lambda h, s, e, sl=sl, hh=hh, n=n: h.matmul(
                        out=psK[sl][:, 0:128], lhsT=ktok[:, n, hh * 128:(hh + 1) * 128],
                        rhs=vtok[:, n, hh * 128:(hh + 1) * 128], start=s, stop=False),
                    lambda h, s, e, sl=sl, hh=hh, n=n: h.matmul(
                        out=psK[sl][:, 128:129], lhsT=ktok[:, n, hh * 128:(hh + 1) * 128],
                        rhs=C["onescol"][0:128, 0:1], start=False, stop=e)], waits=wsK)
                mk1 = rec.op("vector", lambda h, sl=sl, hh=hh: h.tensor_tensor(
                    out=KVrun[:, hh * 129:(hh + 1) * 129], in0=KVrun[:, hh * 129:(hh + 1) * 129],
                    in1=psK[sl][:, 0:129], op=ALU.add), waits=[(pe, pmK), (pe, pmY), (seeded, 1)], inc=cv)
                mk2 = rec.op("vector", lambda h, hh=hh: h.tensor_copy(
                    out=KVbf[:, hh * 129:(hh + 1) * 129], in_=KVrun[:, hh * 129:(hh + 1) * 129]),
                    waits=[(cv, mk1)], inc=cv)
                psK_free[sl] = (cv, mk1)
                kv_mark[hh] = (cv, mk2)

    attn_all = cv.n
    for tt in range(NTT):
        rec.op("sync", lambda h, tt=tt: h.dma_start(out=io["scr_b"][tt * 128:(tt + 1) * 128, :],
                                                    in_=attno[:, tt, :]),
               waits=[(cv, attn_all), (trq, trq.n)], inc=cw, k=16)
    ao_stores = cw.n
    rec.barrier(ctx.chans)
    ap_scope.close()
    scopeB.close()
    kv_scope.close()

    scopeC = ExitStack()
    aoT = ctx.sb("a_aoT", [128, NDC, TOK], BF16, scope=scopeC)
    of32 = ctx.sb("a_of32", [128, NTT, DM], F32, scope=scopeC)
    pst_o = proj_state(ctx, scopeC, "l0po")
    for k in range(NDC):
        rec.op("sync", lambda h, k=k: h.dma_start_transpose(out=aoT[:, k, :],
                                                            in_=io["scr_b"][:, k * 128:(k + 1) * 128]),
               waits=[(cw, ao_stores)], inc=trq, k=16)
    aoT_ready = (trq, trq.n)

    def oevac(rec, o, tt, ps, pm):
        m = rec.op("scalar", lambda h, ps=ps, o=o, tt=tt: h.activation(
            out=of32[:, tt, o * 512:(o + 1) * 512], in_=ps, func=AF.Copy), waits=[(pe, pm)], inc=ca)
        return (ca, m)
    emit_proj(ctx, rec, aoT, W["Wo"], DM, oevac, tag="l0o", ready=aoT_ready, pst=pst_o,
              bias=(bias_sb, 3 * DM), consts=C, pe=pe, wld=wld)
    odone = (ca, ca.n)

    emit_ln_residual_store(ctx, rec, of32, io["src"], io["dst"], odone, tag="l0fin", scope=scopeC)
    rec.barrier(ctx.chans)
    scopeC.close()
    bias_scope.close()


def emit_ln_residual_store(ctx, rec, of32, src_x, dst_x, ready, *, tag, eps=EPS_LN, scope=None,
                           ntt=NTT, row0=0, pre_add_hbm=None):
    ld = ctx.chan(tag + "_ld")
    cv = ctx.chan(tag + "_v")
    ca = ctx.chan(tag + "_a")
    st = ctx.chan(tag + "_st")
    xt = ctx.sb(tag + "_x", [128, 2, DM], F32, scope=scope)
    ot = ctx.sb(tag + "_o", [128, 2, DM], F32, scope=scope)
    sst = ctx.sb(tag + "_s", [128, 32], F32, scope=scope)
    ut = ctx.sb(tag + "_u", [128, 2, DM], F32, scope=scope) if pre_add_hbm is not None else None
    free = [0, 0]
    prev_tail = 0
    for i in range(ntt):
        sl = i % 2
        r0 = row0 + i * 128
        lm = rec.op("sync", lambda h, r0=r0, sl=sl: h.dma_start(out=xt[:, sl, :],
                                                                in_=src_x[r0:r0 + 128, :]),
                    waits=[(cv, free[sl])], inc=ld, k=16)
        if pre_add_hbm is not None:
            lmu = rec.op("sync", lambda h, r0=r0, sl=sl: h.dma_start(out=ut[:, sl, :],
                                                                     in_=pre_add_hbm[r0:r0 + 128, :]),
                         waits=[(cv, free[sl])], inc=ld, k=16)
            pa = rec.op("vector", lambda h, i=i, sl=sl: h.tensor_tensor(
                out=of32[:, i, :], in0=of32[:, i, :], in1=ut[:, sl, :], op=ALU.add),
                waits=[ready, (ld, lmu), (cv, free[sl]), (cv, prev_tail)], inc=cv)
            stats_w = [(cv, pa)]
        else:
            stats_w = [ready, (cv, free[sl]), (cv, prev_tail)]
        m1 = None
        for g in range(4):
            m1 = rec.op("vector", lambda h, i=i, g=g: h.bn_stats(out=sst[:, g * 6:(g + 1) * 6],
                                                                 in_=of32[:, i, g * 512:(g + 1) * 512]),
                        waits=stats_w if g == 0 else [(cv, m1)], inc=cv)
        m2 = rec.op("vector", lambda h: h.bn_aggr(out=sst[:, 24:26], in_=sst[:, 0:24]),
                    waits=[(cv, m1)], inc=cv)
        m3 = rec.op("vector", lambda h: h.tensor_scalar(out=sst[:, 26:27], in0=sst[:, 25:26],
                                                        scalar1=float(eps), scalar2=None, op0=ALU.add),
                    waits=[(cv, m2)], inc=cv)
        m4 = rec.op("scalar", lambda h: h.activation(out=sst[:, 27:28], in_=sst[:, 26:27], func=AF.Sqrt),
                    waits=[(cv, m3)], inc=ca)
        m5 = rec.op("vector", lambda h: h.reciprocal(out=sst[:, 26:27], in_=sst[:, 27:28]),
                    waits=[(ca, m4)], inc=cv)
        m6 = rec.op("vector", lambda h, i=i, sl=sl: h.tensor_scalar(
            out=ot[:, sl, :], in0=of32[:, i, :], scalar1=sst[:, 24:25], scalar2=sst[:, 26:27],
            op0=ALU.subtract, op1=ALU.mult), waits=[(cv, m5)], inc=cv)
        m7 = rec.op("vector", lambda h, sl=sl: h.tensor_tensor(out=ot[:, sl, :], in0=ot[:, sl, :],
                                                               in1=xt[:, sl, :], op=ALU.add),
                    waits=[(cv, m6), (ld, lm)], inc=cv)
        free[sl] = m7
        prev_tail = m6
        rec.op("sync", lambda h, r0=r0, sl=sl: h.dma_start(out=dst_x[r0:r0 + 128, :],
                                                           in_=ot[:, sl, :]),
               waits=[(cv, m7)], inc=st, k=16)
    return st


def emit_l0_ffn(ctx, rec, io, W, C):
    nc = ctx.nc
    NI = 8192
    NIC = NI // 512
    cv = ctx.chan("f0_v")
    ca = ctx.chan("f0_a")
    cp = ctx.chan("f0_p")
    cw = ctx.chan("f0_sw")
    pe = ctx.chan("f0_pe")
    wld = ctx.chan("f0_wl")
    trh = ctx.chan("f0_tr")

    blk = ExitStack()
    xnT = ctx.sb("f_xnT", [128, NDC, TOK], BF16, side="right", scope=blk)
    of32 = ctx.sb("f_of32", [128, NTT, DM], F32, side="right", scope=blk)
    ln_scope = ExitStack()
    trc, trm = emit_ln_phase(ctx, rec, io["src"], io["scr_a"], xnT, kind="ln", tag="f0ln", scope=ln_scope)
    ready = (trc, trm)
    rec.barrier(ctx.chans)
    ln_scope.close()

    main_scope = ExitStack()
    wg = ctx.sb("f_wg", [128, 2, NDC, 512], BF16, scope=main_scope)
    wu = ctx.sb("f_wu", [128, 2, NDC, 512], BF16, scope=main_scope)
    wd = ctx.sb("f_wd", [128, 2, 4, DM // 4], BF16, scope=main_scope)
    sg = ctx.sb("f_sg", [128, 2, 512], F32, scope=main_scope)
    hbuf = ctx.sb("f_hbuf", [128, 4, 512], F32, scope=main_scope)
    ab = ctx.sb("f_ab", [128, 1, 512], F32, scope=main_scope)
    thr = ctx.sb("f_thr", [128, 6, 128], F32, scope=main_scope)
    hsst = ctx.sb("f_hsst", [128, 4, 512], BF16, scope=main_scope)
    hsT = ctx.sb("f_hsT", [128, 4, TOK], BF16, scope=main_scope)
    psg = [ctx.ps("f_psg", [128, 512], F32, scope=main_scope) for _ in range(2)]
    psu = [ctx.ps("f_psu", [128, 512], F32, scope=main_scope) for _ in range(2)]
    psd = [ctx.ps("f_psd", [128, 512], F32, scope=main_scope) for _ in range(4)]
    zm = rec.op("vector", lambda h: h.memset(of32[:, 0:4, :], 0.0), inc=cv)
    zm = rec.op("vector", lambda h: h.memset(of32[:, 4:8, :], 0.0), waits=[(cv, zm)], inc=cv)
    of_zero = (cv, zm)

    wfree = {"g": [None, None], "u": [None, None], "d": [None, None]}
    psg_free = [None, None]
    psu_free = [None, None]
    psd_free = [None] * 4
    sg_free = [None] * 4
    hb_free = [None] * 4
    hs_free = [None] * 4
    hsT_free = [None]
    ab_free = [None, None]
    wd_last_slots = [None, None]
    thr_free = None
    of_add = {}
    st = {}

    def emit_gu(ic):
        nonlocal thr_free
        sl = ic % 2
        wmg = rec.op("gpsimd", lambda h, ic=ic, sl=sl: h.dma_start(
            out=wg[:, sl, :, :], in_=w_chunk_ap(W["w_gate"], ic * 512, (ic + 1) * 512)),
            waits=[wfree["g"][sl]] if wfree["g"][sl] else [], inc=wld, k=16)
        wmu = rec.op("gpsimd", lambda h, ic=ic, sl=sl: h.dma_start(
            out=wu[:, sl, :, :], in_=w_chunk_ap(W["w_up"], ic * 512, (ic + 1) * 512)),
            waits=[wfree["u"][sl]] if wfree["u"][sl] else [], inc=wld, k=16)
        hs_marks = []
        for tt in range(NTT):
            pg = tt % 2
            wsg_ = [(wld, wmg), ready] + ([psg_free[pg]] if psg_free[pg] else [])
            gm = mm_group(rec, pe, [
                lambda h, s, e, k=k, tt=tt, sl=sl, pg=pg: h.matmul(
                    out=psg[pg][:], lhsT=xnT[:, k, tt * 128:(tt + 1) * 128],
                    rhs=wg[:, sl, k, :], start=s, stop=e) for k in range(NDC)], waits=wsg_)
            wsu_ = [(wld, wmu)] + ([psu_free[pg]] if psu_free[pg] else [])
            um = mm_group(rec, pe, [
                lambda h, s, e, k=k, tt=tt, sl=sl, pg=pg: h.matmul(
                    out=psu[pg][:], lhsT=xnT[:, k, tt * 128:(tt + 1) * 128],
                    rhs=wu[:, sl, k, :], start=s, stop=e) for k in range(NDC)], waits=wsu_)
            sgi = tt % 2
            wss = [(pe, gm)] + ([sg_free[sgi]] if sg_free[sgi] else [])
            sm = rec.op("scalar", lambda h, sgi=sgi, pg=pg: h.activation(
                out=sg[:, sgi, :], in_=psg[pg][:], func=AF.Silu), waits=wss, inc=ca)
            psg_free[pg] = (ca, sm)
            hsl = (ic * NTT + tt) % 4
            hw = [(ca, sm), (pe, um)] + ([hb_free[hsl]] if hb_free[hsl] else [])
            hm = rec.op("vector", lambda h, sgi=sgi, pg=pg, hsl=hsl: h.tensor_tensor(
                out=hbuf[:, hsl, :], in0=sg[:, sgi, :], in1=psu[pg][:], op=ALU.mult),
                waits=hw, inc=cv)
            psu_free[pg] = (cv, hm)
            sg_free[sgi] = (cv, hm)
            hv = hbuf[:, hsl, :]
            h4 = hv.rearrange("p (g f) -> p g f", f=4)
            asl = 0
            am = rec.op("scalar", lambda h, hv=hv, asl=asl: h.activation(
                out=ab[:, asl, :], in_=hv, func=AF.Abs),
                waits=[(cv, hm)] + ([ab_free[asl]] if ab_free[asl] else []), inc=ca)
            a4 = ab[:, asl, :].rearrange("p (g f) -> p g f", f=4)
            t_ = thr
            m_ = rec.op("vector", lambda h, a4=a4: h.tensor_tensor(
                out=t_[:, 0, :], in0=a4[:, :, 0], in1=a4[:, :, 1], op=ALU.max),
                waits=[(ca, am)] + ([thr_free] if thr_free else []), inc=cv)
            m_ = rec.op("vector", lambda h, a4=a4: h.tensor_tensor(
                out=t_[:, 1, :], in0=a4[:, :, 0], in1=a4[:, :, 1], op=ALU.min),
                waits=[(cv, m_)], inc=cv)
            m_ = rec.op("vector", lambda h, a4=a4: h.tensor_tensor(
                out=t_[:, 2, :], in0=a4[:, :, 2], in1=a4[:, :, 3], op=ALU.max),
                waits=[(cv, m_)], inc=cv)
            m_ = rec.op("vector", lambda h, a4=a4: h.tensor_tensor(
                out=t_[:, 3, :], in0=a4[:, :, 2], in1=a4[:, :, 3], op=ALU.min),
                waits=[(cv, m_)], inc=cv)
            m_ = rec.op("vector", lambda h: h.tensor_tensor(
                out=t_[:, 4, :], in0=t_[:, 0, :], in1=t_[:, 2, :], op=ALU.min),
                waits=[(cv, m_)], inc=cv)
            m_ = rec.op("vector", lambda h: h.tensor_tensor(
                out=t_[:, 5, :], in0=t_[:, 1, :], in1=t_[:, 3, :], op=ALU.max),
                waits=[(cv, m_)], inc=cv)
            m_ = rec.op("vector", lambda h: h.tensor_tensor(
                out=t_[:, 4, :], in0=t_[:, 4, :], in1=t_[:, 5, :], op=ALU.max),
                waits=[(cv, m_)], inc=cv)
            km = []
            for qi in range(4):
                m_ = rec.op("vector", lambda h, qi=qi, a4=a4: h.tensor_tensor(
                    out=t_[:, qi, :], in0=a4[:, :, qi], in1=t_[:, 4, :], op=ALU.is_ge),
                    waits=[(cv, m_)], inc=cv)
                km.append(m_)
            hsv = hsst[:, hsl, :].rearrange("p (g f) -> p g f", f=4)
            pm_ = None
            for qi in range(4):
                wsp = [(cv, km[qi])] + ([hs_free[hsl]] if (hs_free[hsl] and qi == 0) else [])
                pm_ = rec.op("gpsimd", lambda h, qi=qi, h4=h4, hsv=hsv: h.tensor_tensor(
                    out=hsv[:, :, qi], in0=h4[:, :, qi], in1=t_[:, qi, :], op=ALU.mult),
                    waits=wsp + ([(cp, pm_)] if pm_ else []), inc=cp)
            thr_free = (cp, pm_)
            hb_free[hsl] = (cp, pm_)
            ab_free[asl] = (cv, km[3])
            sm2 = rec.op("sync", lambda h, tt=tt, hsl=hsl, ic=ic: h.dma_start(
                out=io["scr_big"][tt * 128:(tt + 1) * 128, ic * 512:(ic + 1) * 512],
                in_=hsst[:, hsl, :]), waits=[(cp, pm_)], inc=cw, k=16)
            hs_marks.append(sm2)
            hs_free[hsl] = (cw, sm2)
        st[ic] = hs_marks[-1]

    def emit_dn(ic):
        sl = ic % 2
        tms = []
        wd_last = wd_last_slots
        for j4 in range(4):
            wst = [(cw, st[ic])] + ([hsT_free[0]] if (hsT_free[0] and j4 == 0) else [])
            tmk = rec.op("sync", lambda h, ic=ic, j4=j4: h.dma_start_transpose(
                out=hsT[:, j4, :],
                in_=io["scr_big"][0:TOK, ic * 512 + j4 * 128:ic * 512 + (j4 + 1) * 128]),
                waits=wst, inc=trh, k=16)
            tms.append(tmk)
        tr_ic = tms[-1]
        for dh in range(4):
            wsl = (ic * 4 + dh) % 2
            wmd = rec.op("gpsimd", lambda h, ic=ic, wsl=wsl, dh=dh: h.dma_start(
                out=wd[:, wsl, :, :],
                in_=W["w_down"][ic * 512:(ic + 1) * 512, dh * 512:(dh + 1) * 512]
                .rearrange("(k p) n -> p k n", p=128)),
                waits=[wd_last[wsl]] if wd_last[wsl] else [],
                inc=wld, k=16)
            for tt in range(NTT):
                if True:
                    ob = dh
                    pdi = tt % 4
                    wsd_ = [(trh, tr_ic), (wld, wmd)] + ([psd_free[pdi]] if psd_free[pdi] else [])
                    dm = mm_group(rec, pe, [
                        lambda h, s, e, j4=j4, tt=tt, wsl=wsl, pdi=pdi: h.matmul(
                            out=psd[pdi][:], lhsT=hsT[:, j4, tt * 128:(tt + 1) * 128],
                            rhs=wd[:, wsl, j4, 0:512], start=s, stop=e)
                        for j4 in range(4)], waits=wsd_)
                    wse_ = [(pe, dm), of_zero] + ([of_add[(tt, ob)]] if (tt, ob) in of_add else [])
                    em = rec.op("vector", lambda h, tt=tt, ob=ob, pdi=pdi: h.tensor_tensor(
                        out=of32[:, tt, ob * 512:(ob + 1) * 512],
                        in0=of32[:, tt, ob * 512:(ob + 1) * 512],
                        in1=psd[pdi][:], op=ALU.add), waits=wse_, inc=cv)
                    psd_free[pdi] = (cv, em)
                    of_add[(tt, ob)] = (cv, em)
            wd_last[wsl] = (pe, dm)
        hsT_free[0] = (pe, pe.n)
        wfree["g"][sl] = (pe, pe.n)
        wfree["u"][sl] = (pe, pe.n)

    for ic in range(NIC + 1):
        if ic < NIC:
            emit_gu(ic)
        if ic >= 1:
            emit_dn(ic - 1)
        dn_done = (cv, cv.n)
    rec.barrier(ctx.chans)
    main_scope.close()
    fin_scope = ExitStack()
    emit_ln_residual_store(ctx, rec, of32, io["src"], io["dst"], dn_done, tag="f0fin",
                           scope=fin_scope)
    rec.barrier(ctx.chans)
    fin_scope.close()
    blk.close()



def emit_l1_attn(ctx, rec, io, W, C):
    nc = ctx.nc
    NH, DH, DS, DIN = 48, 64, 128, 3072
    NCC = DIN // 128
    cv = ctx.chan("m_v")
    ca = ctx.chan("m_a")
    cp = ctx.chan("m_p")
    cw = ctx.chan("m_sw")
    pe = ctx.chan("m_pe")
    wld = ctx.chan("m_wl")
    trm = ctx.chan("m_tr")
    snd = ctx.chan("m_snd")
    rcv1 = ctx.chan("m_rcv1")
    rcv2 = ctx.chan("m_rcv2")
    sdd = ctx.chan("m_seed")
    poll1 = ctx.chan("m_poll1")
    poll2 = ctx.chan("m_poll2")

    blk = ExitStack()
    small = ctx.sb("m_small", [128, 80], F32, side="right", scope=blk)
    LcT = ctx.sb("m_LcT", [128, NTT, NH], F32, side="right", scope=blk)
    eUT = ctx.sb("m_eUT", [128, NTT, NH], F32, side="right", scope=blk)
    expLcT = ctx.sb("m_expLcT", [128, NTT, NH], F32, side="right", scope=blk)
    dAe = ctx.sb("m_dAe", [128, NTT * NH], F32, side="right", scope=blk)
    Db = ctx.sb("m_Db", [128, NH], F32, side="right", scope=blk)
    T0c = ctx.sb("m_T0", [128, 128], F32, side="right", scope=blk)
    eyeb = ctx.sb("m_eyeb", [128, 128], BF16, side="right", scope=blk)
    eyef = ctx.sb("m_eyef", [128, 128], F32, side="right", scope=blk)
    zro = ctx.sb("m_zro", [128, 128], F32, side="right", scope=blk)
    ctail = ctx.sb("m_ctail", [128, NCC, 3], BF16, side="right", scope=blk)
    convst = ctx.sb("m_cvst", [128, NCC, 3], BF16, side="right", scope=blk)
    scope2 = ExitStack()
    xn3T = ctx.sb("m_xnT", [128, NDC, TOK], BF16, side="right", scope=scope2)
    scope3 = ExitStack()
    Lc = ctx.sb("m_Lc", [128, TOK], F32, side="right", scope=scope3)
    Bf = ctx.sb("m_Bf", [128, TOK], BF16, side="right", scope=scope3)
    Cf = ctx.sb("m_Cf", [128, TOK], BF16, side="right", scope=scope3)
    Btok = ctx.sb("m_Btok", [128, NTT, DS], BF16, side="right", scope=scope3)
    Ssb = ctx.sb("m_Ssb", [128, NH * DH], F32, side="right", scope=scope3)
    Sbf = ctx.sb("m_Sbf", [128, NH * DH], BF16, side="right", scope=scope3)
    sendS = ctx.sb("m_sendS", [128, NH * DH], BF16, side="right", scope=scope3)
    ssm0 = ctx.sb("m_ssm0", [128, NH * DH], F32, side="right", scope=scope3)
    scope4 = ExitStack()
    xcf = ctx.sb("m_xcf", [128, NCC, TOK], BF16, side="right", scope=scope4)
    scope3b = ExitStack()
    dtl = ctx.sb("m_dt", [128, TOK], F32, side="right", scope=scope3b)
    ldec = ctx.sb("m_ldec", [128, TOK], F32, side="right", scope=scope3b)
    eU = ctx.sb("m_eU", [128, TOK], F32, side="right", scope=scope3b)
    expLc = ctx.sb("m_expLc", [128, TOK], F32, side="right", scope=scope3b)

    cst = ctx.chan("m_cst")
    rec.op("sync", lambda h: h.dma_start(out=T0c[:], in_=W["c_T0"]), inc=cst, k=16)
    rec.op("gpsimd", lambda h: h.dma_start(out=eyeb[:], in_=W["c_eyef"]), inc=cst, k=16)
    rec.op("sync", lambda h: h.dma_start(out=eyef[:], in_=W["c_eyef"]), inc=cst, k=16)
    rec.op("sync", lambda h: h.dma_start(out=small[0:48, 0:1], in_=W["dt_bias_r"]), inc=cst, k=16)
    rec.op("sync", lambda h: h.dma_start(out=small[0:48, 49:50], in_=W["A_log_c"]), inc=cst, k=16)
    rec.op("gpsimd", lambda h: h.dma_start(
        out=convst[:], in_=W["conv_state"].rearrange("(cc p) k -> p cc k", p=128)), inc=cst, k=16)
    rec.op("sync", lambda h: h.dma_start(out=ssm0[:], in_=W["ssm0_t"]), inc=cst, k=16)
    mz = rec.op("vector", lambda h: h.memset(zro[:], 0.0), inc=cv)
    mo = rec.op("vector", lambda h: h.memset(small[:, 48:49], 1.0), waits=[(cst, cst.n)], inc=cv)
    cst_all = (cst, cst.n)

    ln_scope = ExitStack()
    trc0, trm0 = emit_ln_phase(ctx, rec, io["src"], io["scr_a"], xn3T, kind="ln", tag="m_ln",
                               scope=ln_scope, u_hbm=io["scr_u"], rms2=True)
    wcol = ctx.sb("m_wcol", [128, NDC], F32, scope=ln_scope)
    wcm = rec.op("sync", lambda h: h.dma_start(out=wcol[:], in_=W["norm_w_c"]), inc=cst, k=16)
    sc_m = None
    for k in range(NDC):
        sc_m = rec.op("vector", lambda h, k=k: h.tensor_scalar(
            out=xn3T[:, k, :], in0=xn3T[:, k, :], scalar1=wcol[:, k:k + 1], scalar2=None,
            op0=ALU.mult), waits=[(trc0, trm0), (cst, wcm)] if k == 0 else [(cv, sc_m)], inc=cv)
    xn3_ready = (cv, sc_m)
    rec.barrier(ctx.chans)
    ln_scope.close()

    fp_scope = ExitStack()
    wti = ctx.sb("m_wti", [128, 2, NDC, 128], BF16, scope=fp_scope)
    psf = [ctx.ps("m_psf", [128, 512], F32, scope=fp_scope) for _ in range(4)]
    wfree = [None, None]
    psfree = [None] * 4
    units = [("xc", cc, DIN + cc * 128) for cc in range(NCC)]
    units += [("B", 0, 2 * DIN), ("C", 0, 2 * DIN + DS), ("dt", 0, 2 * DIN + 2 * DS)]
    for ui, (kind, cc, col0) in enumerate(units):
        sl = ui % 2
        ncol = 48 if kind == "dt" else 128
        ws = [wfree[sl]] if wfree[sl] else []
        wm = rec.op("gpsimd", lambda h, sl=sl, col0=col0, ncol=ncol: h.dma_start(
            out=wti[:, sl, :, 0:ncol],
            in_=W["W_inT"][:, col0:col0 + ncol].rearrange("(k p) n -> p k n", p=128)),
            waits=ws, inc=wld, k=16)
        for ts in range(2):
            pi = (ui * 2 + ts) % 4
            waits = [(wld, wm), xn3_ready, cst_all] + ([psfree[pi]] if psfree[pi] else [])
            mark = None
            for k in range(NDC):
                mark = rec.op("tensor", lambda h, k=k, ts=ts, sl=sl, pi=pi, ncol=ncol: h.matmul(
                    out=psf[pi][0:ncol, 0:512], lhsT=wti[:, sl, k, 0:ncol],
                    rhs=xn3T[:, k, ts * 512:(ts + 1) * 512],
                    start=(k == 0), stop=(k == NDC - 1)), waits=waits if k == 0 else (),
                    inc=pe if k == NDC - 1 else None)
            if kind == "xc":
                em = rec.op("vector", lambda h, pi=pi, cc=cc, ts=ts: h.tensor_copy(
                    out=xcf[:, cc, ts * 512:(ts + 1) * 512], in_=psf[pi][:, 0:512]),
                    waits=[(pe, mark)], inc=cv)
            elif kind == "B":
                em = rec.op("vector", lambda h, pi=pi, ts=ts: h.tensor_copy(
                    out=Bf[:, ts * 512:(ts + 1) * 512], in_=psf[pi][:, 0:512]),
                    waits=[(pe, mark)], inc=cv)
            elif kind == "C":
                em = rec.op("vector", lambda h, pi=pi, ts=ts: h.tensor_copy(
                    out=Cf[:, ts * 512:(ts + 1) * 512], in_=psf[pi][:, 0:512]),
                    waits=[(pe, mark)], inc=cv)
            else:
                em = rec.op("vector", lambda h, pi=pi, ts=ts: h.tensor_copy(
                    out=dtl[0:48, ts * 512:(ts + 1) * 512], in_=psf[pi][0:48, 0:512]),
                    waits=[(pe, mark)], inc=cv)
            psfree[pi] = (cv, em)
        wfree[sl] = (pe, mark)
    xc_done = (cv, cv.n)
    rec.barrier(ctx.chans)
    fp_scope.close()
    if DBG_L1STOP <= 2:
        rec.barrier(ctx.chans)
        return

    def g_tail(h):
        with h.register("t_pp") as pp:
            h.reg_load(pp, C["par_sb"][0:1, 0:1])
            with h.If_eq(pp, 0):
                h.wait_ge(cv.sem, xc_done[1])
                h.dma_start(out=io["sh_l1t"][:, 0:NCC * 3].rearrange("p (a b) -> p a b", b=3),
                            in_=xcf[:, :, TOK - 3:TOK]).then_inc(snd.sem, 16)
                h.wait_ge(snd.sem, 16)
                h.dma_start(out=io["sh_l1tf"][:], in_=C["nonce_sb"][0:1, 0:4]).then_inc(snd.sem, 16)
            with h.Else():
                with h.register("t_pc") as pc, h.register("t_pex") as pex, h.register("t_f0") as f0, \
                     h.register("t_nn") as nn:
                    h.reg_load(nn, C["nonce_sb"][0:1, 0:1])
                    h.reg_mov(pc, 1)
                    h.reg_mov(pex, 0)
                    with h.While(pc):
                        h.reg_alu(pex, pex, 16, ALU.add)
                        h.dma_start(out=C["flag_sb"][0:1, 0:4], in_=io["sh_l1tf"][0:1, :]).then_inc(poll1.sem, 16)
                        h.wait_ge(poll1.sem, pex)
                        h.reg_load(f0, C["flag_sb"][0:1, 0:1])
                        h.reg_alu(pc, f0, nn, ALU.not_equal)
                h.dma_start(out=ctail[:],
                            in_=io["sh_l1t"][:, 0:NCC * 3].rearrange("p (a b) -> p a b", b=3)).then_inc(rcv1.sem, 16)
    rec.raw("gpsimd", g_tail)
    sdd.reserve(1)

    def v_tail(h):
        with h.register("t_pv") as pv:
            h.reg_load(pv, C["par_sb"][0:1, 0:1])
            with h.If_eq(pv, 0):
                h.tensor_copy(out=ctail[:], in_=convst[:]).then_inc(sdd.sem, 1)
            with h.Else():
                h.wait_ge(rcv1.sem, 16)
                h.memset(small[:, 79:80], 0.0).then_inc(sdd.sem, 1)
    rec.raw("vector", v_tail)
    tail_ready = (sdd, 1)
    if DBG_L1STOP <= 3:
        rec.barrier(ctx.chans)
        return

    d0 = rec.op("vector", lambda h: h.tensor_scalar(
        out=dtl[0:48, :], in0=dtl[0:48, :], scalar1=small[0:48, 0:1], scalar2=None, op0=ALU.add),
        waits=[xc_done, cst_all], inc=cv)
    d1 = rec.op("scalar", lambda h: h.activation(out=ldec[0:48, :], in_=dtl[0:48, :], func=AF.Abs),
                waits=[(cv, d0)], inc=ca)
    d2 = rec.op("scalar", lambda h: h.activation(out=eU[0:48, :], in_=ldec[0:48, :], func=AF.Exp,
                                                 scale=-1.0), waits=[(ca, d1)], inc=ca)
    d3 = rec.op("scalar", lambda h: h.activation(out=eU[0:48, :], in_=eU[0:48, :], func=AF.Ln,
                                                 bias=small[0:48, 48:49]), waits=[(ca, d2)], inc=ca)
    d4 = rec.op("vector", lambda h: h.tensor_scalar(
        out=Lc[0:48, :], in0=dtl[0:48, :], scalar1=0.0, scalar2=None, op0=ALU.max),
        waits=[(ca, d3), (ca, d1)], inc=cv)
    d5 = rec.op("vector", lambda h: h.tensor_tensor(
        out=dtl[0:48, :], in0=Lc[0:48, :], in1=eU[0:48, :], op=ALU.add), waits=[(cv, d4)], inc=cv)
    if DBG_L1STOP <= 31:
        rec.barrier(ctx.chans)
        return
    a0 = rec.op("scalar", lambda h: h.activation(out=small[0:48, 50:51], in_=small[0:48, 49:50],
                                                 func=AF.Exp), waits=[cst_all], inc=ca)
    a1 = rec.op("vector", lambda h: h.tensor_scalar(
        out=small[0:48, 50:51], in0=small[0:48, 50:51], scalar1=-1.0, scalar2=None, op0=ALU.mult),
        waits=[(ca, a0)], inc=cv)
    l0 = rec.op("vector", lambda h: h.tensor_scalar(
        out=ldec[0:48, :], in0=dtl[0:48, :], scalar1=small[0:48, 50:51], scalar2=None, op0=ALU.mult),
        waits=[(cv, a1), (cv, d5)], inc=cv)
    sc = l0
    for n in range(NTT):
        sc = rec.op("vector", lambda h, n=n: h.tensor_tensor_scan(
            out=Lc[0:48, n * 128:(n + 1) * 128], data0=ldec[0:48, n * 128:(n + 1) * 128],
            data1=zro[0:48, 0:128], initial=0.0, op0=ALU.add, op1=ALU.add),
            waits=[(cv, sc), (cv, mz)], inc=cv)
    if DBG_L1STOP <= 32:
        rec.barrier(ctx.chans)
        return
    e0 = rec.op("scalar", lambda h: h.activation(out=expLc[0:48, :], in_=Lc[0:48, :], func=AF.Exp),
                waits=[(cv, sc)], inc=ca)
    eu = sc
    for n in range(NTT):
        eu = rec.op("vector", lambda h, n=n: h.tensor_scalar(
            out=eU[0:48, n * 128:(n + 1) * 128], in0=Lc[0:48, n * 128:(n + 1) * 128],
            scalar1=Lc[0:48, n * 128 + 127:n * 128 + 128], scalar2=None, op0=ALU.subtract),
            waits=[(cv, eu), (ca, e0)], inc=cv)
    e1 = rec.op("scalar", lambda h: h.activation(out=eU[0:48, :], in_=eU[0:48, :], func=AF.Exp,
                                                 scale=-1.0), waits=[(cv, eu)], inc=ca)
    if DBG_L1STOP <= 33:
        rec.barrier(ctx.chans)
        return
    dec_done = (ca, ca.n)

    tr_scope = ExitStack()
    pst_r = ctx.ps("m_pstr", [128, 64], F32, scope=tr_scope)
    ps_bc = ctx.ps("m_psbc", [128, 512], F32, scope=tr_scope)
    tm_ = None
    te_free = None
    for (srcT, dstT) in ((Lc, LcT), (eU, eUT), (expLc, expLcT)):
        for n in range(NTT):
            wsx = [dec_done] + ([te_free] if te_free else [])
            tm_ = rec.op("tensor", lambda h, srcT=srcT, n=n: h.transpose(
                out=pst_r[:, 0:48], in_=srcT[0:48, n * 128:(n + 1) * 128], identity=eyef[0:48, 0:48]),
                waits=wsx, inc=pe)
            tm_ = rec.op("scalar", lambda h, dstT=dstT, n=n: h.activation(
                out=dstT[:, n, :], in_=pst_r[:, 0:48], func=AF.Copy), waits=[(pe, tm_)], inc=ca)
            te_free = (ca, tm_)
    if DBG_L1STOP <= 34:
        rec.barrier(ctx.chans)
        return
    row1 = ctx.sb("m_row1", [1, 384], F32, scope=tr_scope)
    ec = rec.op("sync", lambda h: h.dma_start(
        out=io["scr_small"][0:48, 0:8],
        in_=Lc[0:48, :].rearrange("p (n c) -> p n c", c=128)[:, :, 127:128].rearrange("p n c -> p (n c)"),
        allow_slow_non_contiguous=True), waits=[dec_done], inc=cw, k=16)
    er = rec.op("sync", lambda h: h.dma_start(
        out=row1[:].rearrange("p (a b) -> p a b", a=8),
        in_=io["scr_small"][0:48, 0:8].transpose((1, 0)).unsqueeze(0),
        allow_slow_non_contiguous=True), waits=[(cw, ec)], inc=cw, k=16)
    bm1 = rec.op("tensor", lambda h: h.matmul(out=ps_bc[:, 0:384], lhsT=C["onesf"][0:1, 0:128],
                                              rhs=row1[:], start=True, stop=True),
                 waits=[(cw, er), cst_all], inc=pe)
    bm2 = rec.op("scalar", lambda h: h.activation(out=dAe[:, 0:384], in_=ps_bc[:, 0:384],
                                                  func=AF.Exp), waits=[(pe, bm1)], inc=ca)
    if DBG_L1STOP <= 35:
        rec.barrier(ctx.chans)
        return
    drow = ctx.sb("m_drow", [1, 48], F32, scope=tr_scope)
    dm0 = rec.op("sync", lambda h: h.dma_start(out=drow[:], in_=W["D_row"]), inc=cst, k=16)
    dm1 = rec.op("tensor", lambda h: h.matmul(out=ps_bc[:, 384:432], lhsT=C["onesf"][0:1, 0:128],
                                              rhs=drow[:], start=True, stop=True),
                 waits=[(cst, dm0), (ca, bm2), cst_all], inc=pe)
    dm2 = rec.op("scalar", lambda h: h.activation(out=Db[:], in_=ps_bc[:, 384:432], func=AF.Copy),
                 waits=[(pe, dm1)], inc=ca)
    bc_done = (ca, ca.n)
    rec.barrier(ctx.chans)
    tr_scope.close()
    scope3b.close()
    if DBG_L1STOP <= 4:
        rec.barrier(ctx.chans)
        return

    cvp_scope = ExitStack()
    cpad = ctx.sb("m_cpad", [128, 2, TOK + 3], F32, scope=cvp_scope)
    ctmp = ctx.sb("m_ctmp", [128, 2, TOK], F32, scope=cvp_scope)
    xsil = ctx.sb("m_xsil", [128, 2, TOK], BF16, scope=cvp_scope)
    cw4 = ctx.sb("m_cw4", [128, NCC, 4], F32, scope=cvp_scope)
    cb1 = ctx.sb("m_cb1", [128, NCC], F32, scope=cvp_scope)
    cwm = rec.op("sync", lambda h: h.dma_start(
        out=cw4[:], in_=W["conv_w"].rearrange("(cc p) k -> p cc k", p=128)), inc=cst, k=16)
    cbm = rec.op("sync", lambda h: h.dma_start(out=cb1[:], in_=W["conv_b_c"]), inc=cst, k=16)
    cp_free = [(cv, 0), (cv, 0)]
    for cc in range(NCC):
        sl = cc % 2
        m0 = rec.op("vector", lambda h, cc=cc, sl=sl: h.tensor_copy(
            out=cpad[:, sl, 0:3], in_=ctail[:, cc, :]),
            waits=[tail_ready, xc_done, (cst, max(cwm, cbm)), cp_free[sl]], inc=cv)
        m1 = rec.op("vector", lambda h, cc=cc, sl=sl: h.tensor_copy(
            out=cpad[:, sl, 3:TOK + 3], in_=xcf[:, cc, :]), waits=[(cv, m0)], inc=cv)
        m2 = rec.op("vector", lambda h, cc=cc, sl=sl: h.tensor_scalar(
            out=ctmp[:, sl, :], in0=cpad[:, sl, 0:TOK], scalar1=cw4[:, cc, 0:1], scalar2=None,
            op0=ALU.mult), waits=[(cv, m1)], inc=cv)
        for kk in range(1, 4):
            m2 = rec.op("vector", lambda h, cc=cc, sl=sl, kk=kk: h.scalar_tensor_tensor(
                out=ctmp[:, sl, :], in0=cpad[:, sl, kk:kk + TOK], scalar=cw4[:, cc, kk:kk + 1],
                in1=ctmp[:, sl, :], op0=ALU.mult, op1=ALU.add), waits=[(cv, m2)], inc=cv)
        m3 = rec.op("vector", lambda h, cc=cc, sl=sl: h.tensor_scalar(
            out=ctmp[:, sl, :], in0=ctmp[:, sl, :], scalar1=cb1[:, cc:cc + 1], scalar2=None,
            op0=ALU.add), waits=[(cv, m2)], inc=cv)
        m4 = rec.op("scalar", lambda h, cc=cc, sl=sl: h.activation(
            out=xsil[:, sl, :], in_=ctmp[:, sl, :], func=AF.Silu), waits=[(cv, m3)], inc=ca)
        sm_ = rec.op("sync", lambda h, cc=cc, sl=sl: h.dma_start(
            out=io["scr_m"][cc * 128:(cc + 1) * 128, 0:TOK], in_=xsil[:, sl, :]),
            waits=[(ca, m4)], inc=cw, k=16)
        cp_free[sl] = (cw, sm_)
    xf_st = cw.n
    rec.barrier(ctx.chans)
    scope4.close()
    cvp_scope.close()
    if DBG_L1STOP <= 5:
        rec.barrier(ctx.chans)
        return

    yt_scope = ExitStack()
    Ytok = ctx.sb("m_Ytok", [128, NTT, DIN], BF16, scope=yt_scope)
    xt_scope = ExitStack()
    Xchunk = ctx.sb("m_Xch", [128, 2, DIN], BF16, scope=xt_scope)
    rec.op("sync", lambda h: h.dma_start(out=io["scr_m"][DIN:DIN + 128, 0:TOK], in_=Bf[:]),
           waits=[xc_done], inc=cw, k=16)
    bf_st = cw.n
    for tt in range(NTT):
        rec.op("sync", lambda h, tt=tt: h.dma_start_transpose(
            out=Btok[:, tt, :], in_=io["scr_m"][DIN:DIN + 128, tt * 128:(tt + 1) * 128]),
            waits=[(cw, bf_st)], inc=trm, k=16)
    tok_ready = (trm, trm.n)
    hself = ctx.sb("m_hself", [48, NH * 128], F32, scope=xt_scope)
    hsm = rec.op("sync", lambda h: h.dma_start(out=hself[:], in_=W["c_hsel"]), inc=cst, k=16)
    if DBG_L1STOP <= 6:
        rec.barrier(ctx.chans)
        return


    sp_scope = ExitStack()
    psS = [ctx.ps("m_psS", [128, 64], F32, scope=sp_scope) for _ in range(2)]
    xeu = ctx.sb("m_xeu", [128, 2, DH], BF16, scope=sp_scope)
    s0m = rec.op("vector", lambda h: h.tensor_copy(out=Ssb[:], in_=ssm0[:]),
                 waits=[cst_all], inc=cv)

    def state_pass(seeded_mark, tag):
        psS_free = [None, None]
        xeu_free = [None, None]
        xch_last = [None, None]
        rm = {}
        for n in range(NTT):
            sx = n % 2
            wsl = [(cw, xf_st)] + ([xch_last[sx]] if xch_last[sx] else [])
            xlm = rec.op("sync", lambda h, n=n, sx=sx: h.dma_start_transpose(
                out=Xchunk[:, sx, :], in_=io["scr_m"][0:DIN, n * 128:(n + 1) * 128]),
                waits=wsl, inc=trm, k=16)
            hh_marks = []
            for hh in range(NH):
                sl = (n * NH + hh) % 2
                wsx = [tok_ready, bc_done, seeded_mark, (trm, xlm)] + ([xeu_free[sl]] if xeu_free[sl] else [])
                xm = rec.op("vector", lambda h, n=n, hh=hh, sl=sl, sx=sx: h.tensor_scalar(
                    out=xeu[:, sl, :], in0=Xchunk[:, sx, hh * DH:(hh + 1) * DH],
                    scalar1=eUT[:, n, hh:hh + 1], scalar2=None, op0=ALU.mult),
                    waits=wsx, inc=cv)
                hh_marks.append(xm)
                wsp = [(cv, xm)] + ([psS_free[sl]] if psS_free[sl] else [])
                pm = rec.op("tensor", lambda h, n=n, sl=sl: h.matmul(
                    out=psS[sl][:], lhsT=Btok[:, n, :], rhs=xeu[:, sl, :], start=True, stop=True),
                    waits=wsp, inc=pe)
                wsr = [(pe, pm)] + ([rm[hh]] if hh in rm else [])
                rmk = rec.op("vector", lambda h, n=n, hh=hh, sl=sl: h.scalar_tensor_tensor(
                    out=Ssb[:, hh * DH:(hh + 1) * DH], in0=Ssb[:, hh * DH:(hh + 1) * DH],
                    scalar=dAe[:, n * NH + hh:n * NH + hh + 1], in1=psS[sl][:],
                    op0=ALU.mult, op1=ALU.add), waits=wsr, inc=cv)
                rm[hh] = (cv, rmk)
                psS_free[sl] = (cv, rmk)
                xeu_free[sl] = (pe, pm)
            xch_last[sx] = (cv, cv.n)
        return cv.n
    sp_end = state_pass((cv, s0m), "sp")
    sendm = rec.op("vector", lambda h: h.tensor_copy(out=sendS[:], in_=Ssb[:]),
                   waits=[(cv, sp_end)], inc=cv)
    if DBG_L1STOP <= 7:
        rec.barrier(ctx.chans)
        return

    def g_state_x(h):
        with h.register("s_pp") as pp:
            h.reg_load(pp, C["par_sb"][0:1, 0:1])
            with h.If_eq(pp, 0):
                h.wait_ge(cv.sem, sendm)
                h.dma_start(out=io["sh_l1s"][:], in_=sendS[:]).then_inc(snd.sem, 16)
                h.wait_ge(snd.sem, 48)
                h.dma_start(out=io["sh_l1sf"][:], in_=C["nonce_sb"][0:1, 0:4]).then_inc(snd.sem, 16)
            with h.Else():
                h.wait_ge(cv.sem, sendm)
                with h.register("s_pc") as pc, h.register("s_pex") as pex, h.register("s_f0") as f0, \
                     h.register("s_nn") as nn:
                    h.reg_load(nn, C["nonce_sb"][0:1, 0:1])
                    h.reg_mov(pc, 1)
                    h.reg_mov(pex, 0)
                    with h.While(pc):
                        h.reg_alu(pex, pex, 16, ALU.add)
                        h.dma_start(out=C["flag_sb"][0:1, 0:4], in_=io["sh_l1sf"][0:1, :]).then_inc(poll2.sem, 16)
                        h.wait_ge(poll2.sem, pex)
                        h.reg_load(f0, C["flag_sb"][0:1, 0:1])
                        h.reg_alu(pc, f0, nn, ALU.not_equal)
                h.dma_start(out=sendS[:], in_=io["sh_l1s"][:]).then_inc(rcv2.sem, 16)
    rec.raw("gpsimd", g_state_x)
    sdd.reserve(1)

    def v_seed2(h):
        with h.register("s_pv") as pv:
            h.reg_load(pv, C["par_sb"][0:1, 0:1])
            with h.If_eq(pv, 0):
                h.tensor_copy(out=Ssb[:], in_=ssm0[:]).then_inc(sdd.sem, 1)
            with h.Else():
                h.wait_ge(rcv2.sem, 16)
                h.tensor_copy(out=Ssb[:], in_=sendS[:]).then_inc(sdd.sem, 1)
    rec.raw("vector", v_seed2)
    seed2 = (sdd, 2)
    sb0 = rec.op("vector", lambda h: h.tensor_copy(out=Sbf[:], in_=Ssb[:]), waits=[seed2], inc=cv)
    rec.barrier(ctx.chans)
    sp_scope.close()
    if DBG_L1STOP <= 8:
        rec.barrier(ctx.chans)
        return

    yl_scope = ExitStack()
    dtile = ctx.sb("m_dtile", [128, 4, 128], F32, scope=yl_scope)
    dexp = ctx.sb("m_dexp", [128, 4, 128], BF16, scope=yl_scope)
    Mt = ctx.sb("m_Mt", [128, 4, 128], BF16, scope=yl_scope)
    BCs = ctx.sb("m_BCs", [128, 128], F32, scope=yl_scope)
    xeu2 = ctx.sb("m_xeu2", [128, 4, DH], BF16, scope=yl_scope)
    xdd = ctx.sb("m_xdd", [128, 4, DH], BF16, scope=yl_scope)
    ytmp = ctx.sb("m_ytmp", [128, 4, DH], F32, scope=yl_scope)
    psBC = ctx.ps("m_psBC", [128, 128], F32, scope=yl_scope)
    psM = [ctx.ps("m_psM", [128, 128], F32, scope=yl_scope) for _ in range(2)]
    psYi = [ctx.ps("m_psYi", [128, 64], F32, scope=yl_scope) for _ in range(2)]
    psYe = [ctx.ps("m_psYe", [128, 64], F32, scope=yl_scope) for _ in range(2)]
    psS2 = ctx.ps("m_psS2", [128, 64], F32, scope=yl_scope)
    psM_free = [None, None]
    psYi_free = [None, None]
    psYe_free = [None, None]
    dt_free = [None] * 4
    de_free = [None] * 4
    mt_free = [None] * 4
    xe_free = [None] * 4
    xd_free = [None] * 4
    yt_free = [None] * 4
    bc_ev = None
    ps2_free = None
    srm = {}

    xch2_last = [None, None]

    def do_chunk(n):
        nonlocal bc_ev, ps2_free
        sx = n % 2
        wsl = [(cw, xf_st)] + ([xch2_last[sx]] if xch2_last[sx] else [])
        xlm = rec.op("sync", lambda h, n=n, sx=sx: h.dma_start_transpose(
            out=Xchunk[:, sx, :], in_=io["scr_m"][0:DIN, n * 128:(n + 1) * 128]),
            waits=wsl, inc=trm, k=16)
        bcw = [tok_ready, xc_done] + ([bc_ev] if bc_ev else [])
        bm = rec.op("tensor", lambda h, n=n: h.matmul(
            out=psBC[:], lhsT=Bf[:, n * 128:(n + 1) * 128], rhs=Cf[:, n * 128:(n + 1) * 128],
            start=True, stop=True), waits=bcw, inc=pe)
        bce = rec.op("scalar", lambda h: h.activation(out=BCs[:], in_=psBC[:], func=AF.Copy),
                     waits=[(pe, bm)], inc=ca)
        bc_ev = (ca, bce)
        for hh in range(NH):
            sl = (n * NH + hh) % 2
            s4 = (n * NH + hh) % 4
            wsm = [(cv, sc), (cst, hsm)] + ([psM_free[sl]] if psM_free[sl] else [])
            mm0 = rec.op("tensor", lambda h, n=n, hh=hh, sl=sl: h.matmul(
                out=psM[sl][:], lhsT=hself[0:48, hh * 128:(hh + 1) * 128],
                rhs=Lc[0:48, n * 128:(n + 1) * 128], start=True, stop=True), waits=wsm, inc=pe)
            wsd = [(pe, mm0), cst_all, bc_done] + ([dt_free[s4]] if dt_free[s4] else [])
            md = rec.op("vector", lambda h, n=n, hh=hh, sl=sl, s4=s4: h.scalar_tensor_tensor(
                out=dtile[:, s4, :], in0=psM[sl][:], scalar=LcT[:, n, hh:hh + 1], in1=T0c[:],
                op0=ALU.subtract, op1=ALU.add), waits=wsd, inc=cv)
            psM_free[sl] = (cv, md)
            wse = [(cv, md)] + ([de_free[s4]] if de_free[s4] else [])
            me = rec.op("scalar", lambda h, s4=s4: h.activation(
                out=dexp[:, s4, :], in_=dtile[:, s4, :], func=AF.Exp), waits=wse, inc=ca)
            dt_free[s4] = (ca, me)
            wst = [(ca, me), bc_ev] + ([mt_free[s4]] if mt_free[s4] else [])
            mt = rec.op("gpsimd", lambda h, s4=s4: h.tensor_tensor(
                out=Mt[:, s4, :], in0=BCs[:], in1=dexp[:, s4, :], op=ALU.mult), waits=wst, inc=cp)
            de_free[s4] = (cp, mt)
            wxd = [tok_ready, bc_done, (trm, xlm)] + ([xd_free[s4]] if xd_free[s4] else [])
            xd = rec.op("vector", lambda h, hh=hh, s4=s4, sx=sx: h.tensor_scalar(
                out=xdd[:, s4, :], in0=Xchunk[:, sx, hh * DH:(hh + 1) * DH],
                scalar1=Db[:, hh:hh + 1], scalar2=None, op0=ALU.mult), waits=wxd, inc=cv)
            wyi = [(cp, mt), (cv, xd)] + ([psYi_free[sl]] if psYi_free[sl] else [])
            yi1 = rec.op("tensor", lambda h, hh=hh, sl=sl, s4=s4, sx=sx: h.matmul(
                out=psYi[sl][:], lhsT=Mt[:, s4, :], rhs=Xchunk[:, sx, hh * DH:(hh + 1) * DH],
                start=True, stop=False), waits=wyi)
            yi2 = rec.op("tensor", lambda h, sl=sl, s4=s4: h.matmul(
                out=psYi[sl][:], lhsT=eyeb[:], rhs=xdd[:, s4, :], start=False, stop=True), inc=pe)
            mt_free[s4] = (pe, yi2)
            xd_free[s4] = (pe, yi2)
            wye = [(cv, sb0), xc_done] + ([srm[hh]] if hh in srm else []) + \
                  ([psYe_free[sl]] if psYe_free[sl] else [])
            ye = rec.op("tensor", lambda h, n=n, hh=hh, sl=sl: h.matmul(
                out=psYe[sl][:], lhsT=Cf[:, n * 128:(n + 1) * 128],
                rhs=Sbf[:, hh * DH:(hh + 1) * DH], start=True, stop=True), waits=wye, inc=pe)
            wv1 = [(pe, ye)] + ([yt_free[s4]] if yt_free[s4] else [])
            yv1 = rec.op("vector", lambda h, n=n, hh=hh, sl=sl, s4=s4: h.tensor_scalar(
                out=ytmp[:, s4, :], in0=psYe[sl][:], scalar1=expLcT[:, n, hh:hh + 1],
                scalar2=None, op0=ALU.mult), waits=wv1, inc=cv)
            psYe_free[sl] = (cv, yv1)
            yv2 = rec.op("vector", lambda h, n=n, hh=hh, sl=sl, s4=s4: h.tensor_tensor(
                out=Ytok[:, n, hh * DH:(hh + 1) * DH], in0=ytmp[:, s4, :], in1=psYi[sl][:],
                op=ALU.add), waits=[(cv, yv1), (pe, yi2)], inc=cv)
            psYi_free[sl] = (cv, yv2)
            yt_free[s4] = (cv, yv2)
            if n < NTT - 1:
                wxe = [tok_ready, bc_done] + ([xe_free[s4]] if xe_free[s4] else [])
                xm = rec.op("vector", lambda h, n=n, hh=hh, s4=s4, sx=sx: h.tensor_scalar(
                    out=xeu2[:, s4, :], in0=Xchunk[:, sx, hh * DH:(hh + 1) * DH],
                    scalar1=eUT[:, n, hh:hh + 1], scalar2=None, op0=ALU.mult), waits=wxe, inc=cv)
                wsp2 = [(cv, xm), (pe, ye)] + ([ps2_free] if ps2_free else [])
                pm2 = rec.op("tensor", lambda h, n=n, s4=s4: h.matmul(
                    out=psS2[:], lhsT=Btok[:, n, :], rhs=xeu2[:, s4, :], start=True, stop=True),
                    waits=wsp2, inc=pe)
                xe_free[s4] = (pe, pm2)
                rmk = rec.op("vector", lambda h, n=n, hh=hh: h.scalar_tensor_tensor(
                    out=Ssb[:, hh * DH:(hh + 1) * DH], in0=Ssb[:, hh * DH:(hh + 1) * DH],
                    scalar=dAe[:, n * NH + hh:n * NH + hh + 1], in1=psS2[:],
                    op0=ALU.mult, op1=ALU.add), waits=[(pe, pm2), seed2, bc_done], inc=cv)
                ps2_free = (cv, rmk)
                sbm = rec.op("vector", lambda h, hh=hh: h.tensor_copy(
                    out=Sbf[:, hh * DH:(hh + 1) * DH], in_=Ssb[:, hh * DH:(hh + 1) * DH]),
                    waits=[(cv, rmk)], inc=cv)
                srm[hh] = (cv, sbm)

    for n in range(NTT):
        do_chunk(n)
        xch2_last[n % 2] = (pe, pe.n)
    y_done = (cv, cv.n)
    if DBG_L1STOP <= 9:
        rec.barrier(ctx.chans)
        return
    rec.barrier(ctx.chans)
    yl_scope.close()
    xt_scope.close()
    scope3.close()

    zg_scope = ExitStack()
    zs = ctx.sb("m_zs", [128, NTT, DIN], BF16, scope=zg_scope)
    wib = ctx.sb("m_wib", [128, DIN], F32, scope=zg_scope)
    wirow = ctx.sb("m_wirow", [1, DIN], F32, scope=zg_scope)
    gst = ctx.sb("m_gst", [128, 40], F32, scope=zg_scope)
    gtmp = ctx.sb("m_gtmp", [128, 2, DIN], F32, scope=zg_scope)
    pst_z = proj_state(ctx, zg_scope, "m_z", nps=4, nk=NDC, cw=512, ksplit=2)
    psw = pst_z["pss"][0]

    def zevac(rec, o, tt, ps, pm):
        m = rec.op("scalar", lambda h, ps=ps, o=o, tt=tt: h.activation(
            out=zs[:, tt, o * 512:(o + 1) * 512], in_=ps, func=AF.Silu), waits=[(pe, pm)], inc=ca)
        return (ca, m)
    emit_proj(ctx, rec, xn3T, W["W_inT"], DIN, make_off_evac(zevac, 0), tag="m_zp0",
              ready=xn3_ready, pst=pst_z, consts=C, pe=pe, wld=wld, ksplit=2, ntt=4, t0=0)
    emit_proj(ctx, rec, xn3T, W["W_inT"], DIN, make_off_evac(zevac, 4), tag="m_zp1",
              ready=xn3_ready, pst=pst_z, consts=C, pe=pe, wld=wld, ksplit=2, ntt=4, t0=512)
    z_done = (ca, ca.n)
    wm0 = rec.op("sync", lambda h: h.dma_start(out=wirow[:], in_=W["inner_w_row"]), inc=cst, k=16)
    wbm = None
    for g6 in range(6):
        wbm = rec.op("tensor", lambda h, g6=g6: h.matmul(
            out=psw[:, 0:512], lhsT=C["onesf"][0:1, 0:128], rhs=wirow[0:1, g6 * 512:(g6 + 1) * 512],
            start=True, stop=True), waits=[(cst, wm0), z_done] if g6 == 0 else [(ca, wbm)], inc=pe)
        wbm = rec.op("scalar", lambda h, g6=g6: h.activation(
            out=wib[:, g6 * 512:(g6 + 1) * 512], in_=psw[:, 0:512], func=AF.Copy),
            waits=[(pe, wbm)], inc=ca)
    wib_done = (ca, wbm)
    gt_free = [None, None]
    ym_prev = 0
    for tt in range(NTT):
        sl = tt % 2
        g1 = None
        for g6 in range(6):
            g1 = rec.op("vector", lambda h, tt=tt, g6=g6: h.bn_stats(
                out=gst[:, g6 * 6:(g6 + 1) * 6], in_=Ytok[:, tt, g6 * 512:(g6 + 1) * 512]),
                waits=([y_done, (cv, ym_prev)] if g6 == 0 else [(cv, g1)]), inc=cv)
        g2 = rec.op("vector", lambda h: h.bn_aggr(out=gst[:, 36:38], in_=gst[:, 0:36]),
                    waits=[(cv, g1)], inc=cv)
        g3 = rec.op("vector", lambda h: h.tensor_scalar(out=gst[:, 38:39], in0=gst[:, 36:37],
                                                        scalar1=gst[:, 36:37], scalar2=None,
                                                        op0=ALU.mult), waits=[(cv, g2)], inc=cv)
        g4 = rec.op("vector", lambda h: h.tensor_tensor(out=gst[:, 38:39], in0=gst[:, 38:39],
                                                        in1=gst[:, 37:38], op=ALU.add),
                    waits=[(cv, g3)], inc=cv)
        g5 = rec.op("vector", lambda h: h.tensor_scalar(out=gst[:, 38:39], in0=gst[:, 38:39],
                                                        scalar1=1e-5, scalar2=None, op0=ALU.add),
                    waits=[(cv, g4)], inc=cv)
        g6m = rec.op("scalar", lambda h: h.activation(out=gst[:, 39:40], in_=gst[:, 38:39],
                                                      func=AF.Sqrt), waits=[(cv, g5)], inc=ca)
        g7 = rec.op("vector", lambda h: h.reciprocal(out=gst[:, 38:39], in_=gst[:, 39:40]),
                    waits=[(ca, g6m)], inc=cv)
        wsg = [(cv, g7)] + ([gt_free[sl]] if gt_free[sl] else [])
        g8 = rec.op("vector", lambda h, tt=tt, sl=sl: h.tensor_scalar(
            out=gtmp[:, sl, :], in0=Ytok[:, tt, :], scalar1=gst[:, 38:39], scalar2=None,
            op0=ALU.mult), waits=wsg, inc=cv)
        g9 = rec.op("vector", lambda h, sl=sl: h.tensor_tensor(
            out=gtmp[:, sl, :], in0=gtmp[:, sl, :], in1=wib[:], op=ALU.mult),
            waits=[(cv, g8), wib_done], inc=cv)
        g10 = rec.op("vector", lambda h, tt=tt, sl=sl: h.tensor_tensor(
            out=gtmp[:, sl, :], in0=gtmp[:, sl, :], in1=zs[:, tt, :], op=ALU.mult),
            waits=[(cv, g9), z_done], inc=cv)
        g11 = rec.op("vector", lambda h, tt=tt, sl=sl: h.tensor_copy(
            out=zs[:, tt, :], in_=gtmp[:, sl, :]), waits=[(cv, g10)], inc=cv)
        ym_prev = g11
        gt_free[sl] = (cv, g11)
        rec.op("sync", lambda h, tt=tt, sl=sl: h.dma_start(
            out=io["scr_big"][tt * 128:(tt + 1) * 128, 0:DIN], in_=zs[:, tt, :]),
            waits=[(cv, g11)], inc=cw, k=16)
    yg_st = cw.n
    if DBG_L1STOP <= 10:
        rec.barrier(ctx.chans)
        return
    rec.barrier(ctx.chans)
    zg_scope.close()
    yt_scope.close()
    scope2.close()

    fin_scope = ExitStack()
    ygT = ctx.sb("m_ygT", [128, NCC, TOK], BF16, scope=fin_scope)
    of32 = ctx.sb("m_of32", [128, NTT, DM], F32, scope=fin_scope)
    pst_o = proj_state(ctx, fin_scope, "m_o", nps=4, nk=NCC, cw=512, ksplit=2)
    for cc in range(NCC):
        rec.op("sync", lambda h, cc=cc: h.dma_start_transpose(
            out=ygT[:, cc, :], in_=io["scr_big"][0:TOK, cc * 128:(cc + 1) * 128]),
            waits=[(cw, yg_st)], inc=trm, k=16)
    ygT_ready = (trm, trm.n)

    def oevac(rec, o, tt, ps, pm):
        m = rec.op("scalar", lambda h, ps=ps, o=o, tt=tt: h.activation(
            out=of32[:, tt, o * 512:(o + 1) * 512], in_=ps, func=AF.Copy), waits=[(pe, pm)], inc=ca)
        return (ca, m)
    emit_proj(ctx, rec, ygT, W["out_projT"], DM, make_off_evac(oevac, 0), tag="m_op0",
              ready=ygT_ready, pst=pst_o, consts=C, pe=pe, wld=wld, ksplit=2, ntt=4, t0=0)
    emit_proj(ctx, rec, ygT, W["out_projT"], DM, make_off_evac(oevac, 4), tag="m_op1",
              ready=ygT_ready, pst=pst_o, consts=C, pe=pe, wld=wld, ksplit=2, ntt=4, t0=512)
    odone = (ca, ca.n)
    emit_ln_residual_store(ctx, rec, of32, io["src"], io["dst"], odone, tag="m_fin",
                           scope=fin_scope, pre_add_hbm=io["scr_u"])
    rec.barrier(ctx.chans)
    fin_scope.close()
    blk.close()



NSP = 20480
NJC = NSP // 128
NJRB = 10


def emit_l1_ffn(ctx, rec, io, W, C):
    nc = ctx.nc
    cv = ctx.chan("q_v")
    ca = ctx.chan("q_a")
    cw = ctx.chan("q_sw")
    pe = ctx.chan("q_pe")
    wld = ctx.chan("q_wl")
    cst = ctx.chan("q_cst")

    blk = ExitStack()
    of32 = ctx.sb("q_of32", [128, NTT, DM], F32, side="right", scope=blk)
    cwT = ctx.sb("q_cwT", [8, TOK], BF16, side="right", scope=blk)
    lgts = ctx.sb("q_lg", [128, NTT, 8], F32, side="right", scope=blk)
    lst = ctx.sb("q_lst", [128, 24], F32, side="right", scope=blk)
    scope2 = ExitStack()
    xn4T = ctx.sb("q_xnT", [128, NDC, TOK], BF16, side="right", scope=scope2)

    trc0, trm0 = emit_ln_phase(ctx, rec, io["src"], io["scr_a"], xn4T, kind="ln", tag="q_ln",
                               scope=scope2)
    ready = (trc0, trm0)
    for k in range(NDC):
        rec.op("sync", lambda h, k=k: h.dma_start(
            out=io["scr_m"][k * 128:(k + 1) * 128, 0:TOK], in_=xn4T[:, k, :]),
            waits=[ready], inc=cw, k=16)
    gsrc_ready = cw.n
    if DBG_L1FSTOP <= 1:
        rec.barrier(ctx.chans)
        return

    wp_scope = ExitStack()
    pst_w = proj_state(ctx, wp_scope, "q_w", nps=4, nk=NDC, cw=512)

    def wevac(rec, o, tt, ps, pm):
        m = rec.op("scalar", lambda h, ps=ps, o=o, tt=tt: h.activation(
            out=of32[:, tt, o * 512:(o + 1) * 512], in_=ps, func=AF.Copy), waits=[(pe, pm)], inc=ca)
        return (ca, m)
    emit_proj(ctx, rec, xn4T, W["WT"], DM, wevac, tag="q_wp", ready=ready, pst=pst_w,
              consts=C, pe=pe, wld=wld)
    w_done = (ca, ca.n)
    if DBG_L1FSTOP <= 2:
        rec.barrier(ctx.chans)
        return

    gwb = ctx.sb("q_gw", [128, NDC, 8], BF16, scope=wp_scope)
    gm0 = rec.op("gpsimd", lambda h: h.dma_start(
        out=gwb[:], in_=W["gate_wT"].rearrange("(k p) n -> p k n", p=128)), inc=wld, k=16)
    psl = pst_w["pss"][0]
    lg_ev = None
    for tt in range(NTT):
        mark = None
        for k in range(NDC):
            w0 = [(wld, gm0), ready, w_done] + ([(cv, lg_ev)] if lg_ev else [])
            mark = rec.op("tensor", lambda h, k=k, tt=tt: h.matmul(
                out=psl[:, 0:8], lhsT=xn4T[:, k, tt * 128:(tt + 1) * 128], rhs=gwb[:, k, :],
                start=(k == 0), stop=(k == NDC - 1)),
                waits=w0 if k == 0 else (), inc=pe if k == NDC - 1 else None)
        lg_ev = rec.op("vector", lambda h, tt=tt: h.tensor_copy(out=lgts[:, tt, :], in_=psl[:, 0:8]),
                       waits=[(pe, mark)] + ([(pe, mark)] if lg_ev is None else [(cv, lg_ev)]), inc=cv)
    if DBG_L1FSTOP <= 3:
        rec.barrier(ctx.chans)
        return
    c1m = rec.op("vector", lambda h: h.tensor_scalar(
        out=lgts[:].rearrange("p a b -> p (a b)"), in0=lgts[:].rearrange("p a b -> p (a b)"),
        scalar1=50.0, scalar2=-50.0, op0=ALU.min, op1=ALU.max), waits=[(cv, lg_ev)], inc=cv)
    cwf = ctx.sb("q_cwf", [128, NTT, 8], F32, scope=wp_scope)
    mx8 = ctx.sb("q_mx8", [128, 8], F32, scope=wp_scope)
    cw_ev = None
    for tt in range(NTT):
        m1 = rec.op("vector", lambda h, tt=tt: h.max(out=mx8[:], in_=lgts[:, tt, :]),
                    waits=[(cv, c1m)] + ([(cv, cw_ev)] if cw_ev else []), inc=cv)
        m2 = rec.op("vector", lambda h: h.tensor_scalar(
            out=lst[:, 0:1], in0=mx8[:, 0:1], scalar1=mx8[:, 1:2], scalar2=None, op0=ALU.subtract),
            waits=[(cv, m1)], inc=cv)
        m3 = rec.op("scalar", lambda h: h.activation(out=lst[:, 1:2], in_=lst[:, 0:1],
                                                     func=AF.Sigmoid), waits=[(cv, m2)], inc=ca)
        m4 = rec.op("vector", lambda h: h.tensor_scalar(
            out=lst[:, 2:3], in0=lst[:, 1:2], scalar1=-1.0, scalar2=1.0, op0=ALU.mult, op1=ALU.add),
            waits=[(ca, m3)], inc=cv)
        m5 = rec.op("vector", lambda h, tt=tt: h.tensor_scalar(
            out=cwf[:, tt, :], in0=lgts[:, tt, :], scalar1=mx8[:, 0:1], scalar2=None,
            op0=ALU.is_equal), waits=[(cv, m4)], inc=cv)
        m6 = rec.op("vector", lambda h, tt=tt: h.tensor_scalar(
            out=cwf[:, tt, :], in0=cwf[:, tt, :], scalar1=lst[:, 1:2], scalar2=None, op0=ALU.mult),
            waits=[(cv, m5)], inc=cv)
        m7 = rec.op("vector", lambda h, tt=tt: h.tensor_scalar(
            out=mx8[:, 0:8], in0=lgts[:, tt, :], scalar1=mx8[:, 1:2], scalar2=None,
            op0=ALU.is_equal), waits=[(cv, m6)], inc=cv)
        m8 = rec.op("vector", lambda h, tt=tt: h.scalar_tensor_tensor(
            out=cwf[:, tt, :], in0=mx8[:, 0:8], scalar=lst[:, 2:3], in1=cwf[:, tt, :],
            op0=ALU.mult, op1=ALU.add), waits=[(cv, m7)], inc=cv)
        cw_ev = m8
    psq = pst_w["pss"][1]
    ct_ev = None
    pe_tr_prev = None
    for tt in range(NTT):
        wst = [(cv, cw_ev)] + ([ct_ev] if ct_ev else [])
        t1 = rec.op("tensor", lambda h, tt=tt: h.transpose(
            out=psq[0:8, 0:128], in_=cwf[:, tt, :], identity=C["eyef"][:]), waits=wst, inc=pe)
        t2 = rec.op("scalar", lambda h, tt=tt: h.activation(
            out=cwT[0:8, tt * 128:(tt + 1) * 128], in_=psq[0:8, 0:128], func=AF.Copy),
            waits=[(pe, t1)], inc=ca)
        ct_ev = (ca, t2)
    cwT_done = (ca, ca.n)
    if DBG_L1FSTOP <= 4:
        rec.barrier(ctx.chans)
        return

    wgt = ctx.sb("q_wgt", [8, NSP], BF16, side="right", scope=scope2)
    eawb = ctx.sb("q_eaw", [8, 64], F32, scope=wp_scope)
    eawt = ctx.sb("q_eawt", [8, 72], F32, scope=wp_scope)
    PTb = ctx.sb("q_PT", [64, 8], BF16, scope=wp_scope)
    em0 = rec.op("sync", lambda h: h.dma_start(out=eawb[:], in_=W["eaw"]), inc=cst, k=16)
    e1 = rec.op("vector", lambda h: h.tensor_reduce(out=eawt[0:8, 64:65], in_=eawb[0:8, :],
                                                    axis=mybir.AxisListType.X, op=ALU.max),
                waits=[(cst, em0)], inc=cv)
    e2 = rec.op("vector", lambda h: h.tensor_scalar(out=eawt[0:8, 0:64], in0=eawb[0:8, :],
                                                    scalar1=eawt[0:8, 64:65], scalar2=None,
                                                    op0=ALU.subtract), waits=[(cv, e1)], inc=cv)
    e3 = rec.op("scalar", lambda h: h.activation(out=eawt[0:8, 0:64], in_=eawt[0:8, 0:64],
                                                 func=AF.Exp), waits=[(cv, e2)], inc=ca)
    e4 = rec.op("vector", lambda h: h.tensor_reduce(out=eawt[0:8, 65:66], in_=eawt[0:8, 0:64],
                                                    axis=mybir.AxisListType.X, op=ALU.add),
                waits=[(ca, e3)], inc=cv)
    e5 = rec.op("vector", lambda h: h.reciprocal(out=eawt[0:8, 66:67], in_=eawt[0:8, 65:66]),
                waits=[(cv, e4)], inc=cv)
    e6 = rec.op("vector", lambda h: h.tensor_scalar(out=eawt[0:8, 0:64], in0=eawt[0:8, 0:64],
                                                    scalar1=eawt[0:8, 66:67], scalar2=None,
                                                    op0=ALU.mult), waits=[(cv, e5)], inc=cv)
    t3 = rec.op("tensor", lambda h: h.transpose(out=psq[0:64, 0:8], in_=eawt[0:8, 0:64],
                                                identity=C["eyef"][0:8, 0:8]),
                waits=[(cv, e6), cwT_done], inc=pe)
    t4 = rec.op("scalar", lambda h: h.activation(out=PTb[:], in_=psq[0:64, 0:8], func=AF.Copy),
                waits=[(pe, t3)], inc=ca)
    atb = ctx.sb("q_atb", [64, 2, 512], BF16, scope=wp_scope)
    imb = ctx.sb("q_imb", [8, 2, 512], F32, scope=wp_scope)
    sgb = ctx.sb("q_sgb", [8, 2, 512], F32, scope=wp_scope)
    psd = pst_w["pss"][2]
    at_free = [None, None]
    im_free = [None, None]
    wg_prev = None
    for jc in range(NSP // 512):
        sl = jc % 2
        wa = [(ca, t4)] + ([at_free[sl]] if at_free[sl] else [])
        am = rec.op("gpsimd", lambda h, jc=jc, sl=sl: h.dma_start(
            out=atb[:, sl, :], in_=W["atoms_p"][:, jc * 512:(jc + 1) * 512]), waits=wa, inc=wld, k=16)
        im = rec.op("sync", lambda h, jc=jc, sl=sl: h.dma_start(
            out=imb[0:8, sl, :], in_=W["imp_p"][:, jc * 512:(jc + 1) * 512]),
            waits=[im_free[sl]] if im_free[sl] else [], inc=cst, k=16)
        dm = rec.op("tensor", lambda h, sl=sl: h.matmul(
            out=psd[0:8, 0:512], lhsT=PTb[:], rhs=atb[:, sl, :], start=True, stop=True),
            waits=[(wld, am), (ca, t4)] + ([wg_prev] if wg_prev else []), inc=pe)
        sg = rec.op("scalar", lambda h, sl=sl: h.activation(
            out=sgb[0:8, sl, :], in_=imb[0:8, sl, :], func=AF.Sigmoid), waits=[(cst, im)], inc=ca)
        wg = rec.op("vector", lambda h, jc=jc, sl=sl: h.scalar_tensor_tensor(
            out=wgt[0:8, jc * 512:(jc + 1) * 512], in0=psd[0:8, 0:512], scalar=1.0,
            in1=sgb[0:8, sl, :], op0=ALU.mult, op1=ALU.mult), waits=[(pe, dm), (ca, sg)], inc=cv)
        at_free[sl] = (pe, dm)
        im_free[sl] = (ca, sg)
        wg_prev = (cv, wg)
    wgt_done = (cv, cv.n)
    rec.barrier(ctx.chans)
    wp_scope.close()
    if DBG_L1FSTOP <= 5:
        rec.barrier(ctx.chans)
        return

    sc_scope = ExitStack()
    gidx = ctx.sb("q_gidx", [128, NJC], I32, scope=sc_scope)
    Rb = ctx.sb("q_Rb", [128, 2, NJRB, 128], BF16, scope=sc_scope)
    xg = ctx.sb("q_xg", [128, 4, TOK], BF16, scope=sc_scope)
    Gb = ctx.sb("q_Gb", [128, 4, TOK], BF16, scope=sc_scope)
    psV = [ctx.ps("q_psV", [128, 512], F32, scope=sc_scope) for _ in range(2)]
    psO = [ctx.ps("q_psO", [128, 512], F32, scope=sc_scope) for _ in range(4)]
    gi0 = rec.op("sync", lambda h: h.dma_start(out=gidx[:], in_=W["gidx"]), inc=cst, k=16)
    rb_slab_free = [None, None]
    xg_free = [None] * 4
    G_free = [None] * 4
    psV_free = [None, None]
    psO_ev = [None] * 4
    sc_done = []
    for rb in range(16):
        rsl = rb % 2
        wrb = [rb_slab_free[rsl]] if rb_slab_free[rsl] else []
        rb0 = rec.op("gpsimd", lambda h, rb=rb, rsl=rsl: h.dma_start(
            out=Rb[:, rsl, :, :],
            in_=W["Rb"][rb * NJRB * 128:(rb + 1) * NJRB * 128, :].rearrange("(j p) r -> p j r", p=128)),
            waits=wrb, inc=wld, k=16)
        pmO = [None] * 8
        for jj in range(NJRB):
            j = rb * NJRB + jj
            sl = j % 4
            wgx = [(cst, gi0), (cw, gsrc_ready)] + ([xg_free[sl]] if xg_free[sl] else [])
            gx = rec.op("gpsimd", lambda h, j=j, sl=sl: h.indirect_dma_start(
                out=xg[:, sl, :], out_offset=None, in_=io["scr_m"][0:DM, 0:TOK],
                in_offset=bass.IndirectOffsetOnAxis(ap=gidx[:, j:j + 1], axis=0)),
                waits=wgx, inc=wld, k=16)
            for ts in range(2):
                pv2 = j % 2
                wv = [wgt_done, cwT_done] + ([psV_free[pv2]] if psV_free[pv2] else [])
                vm = rec.op("tensor", lambda h, j=j, ts=ts, pv2=pv2: h.matmul(
                    out=psV[pv2][:], lhsT=wgt[0:8, j * 128:(j + 1) * 128],
                    rhs=cwT[0:8, ts * 512:(ts + 1) * 512], start=True, stop=True),
                    waits=wv, inc=pe)
                gm = rec.op("vector", lambda h, j=j, ts=ts, sl=sl, pv2=pv2: h.tensor_tensor(
                    out=Gb[:, sl, ts * 512:(ts + 1) * 512], in0=xg[:, sl, ts * 512:(ts + 1) * 512],
                    in1=psV[pv2][:], op=ALU.mult), waits=[(wld, gx), (pe, vm)] +
                    ([G_free[sl]] if G_free[sl] and ts == 0 else []), inc=cv)
                psV_free[pv2] = (cv, gm)
                for t4i in range(4):
                    if DBG_NOSCAT:
                        continue
                    tt = ts * 4 + t4i
                    first = (jj == 0)
                    last = (jj == NJRB - 1)
                    wsc = [(cv, gm), (wld, rb0)]
                    if first and ts == 0 and psO_ev[t4i] is not None:
                        wsc.append(psO_ev[t4i])
                    needs_inc = last or (ts == 1 and t4i == 3)
                    pmO[tt] = rec.op("tensor", lambda h, j=j, tt=tt, t4i=t4i, ts=ts, sl=sl,
                                     first=first, last=last: h.matmul(
                        out=psO[t4i][:, ts * 128:ts * 128 + 128],
                        lhsT=Gb[:, sl, tt * 128:(tt + 1) * 128], rhs=Rb[:, rsl, jj, :],
                        start=(first and ts == 0), stop=last),
                        waits=wsc, inc=pe if needs_inc else None)
            G_free[sl] = (pe, pmO[7]) if not DBG_NOSCAT else (cv, gm)
            xg_free[sl] = (cv, gm)
        for tt in range(8):
            if DBG_NOSCAT or DBG_NOEVAC:
                continue
            bank = tt % 4
            ts = tt // 4
            evm = rec.op("vector", lambda h, tt=tt, bank=bank, ts=ts, rb=rb: h.tensor_tensor(
                out=of32[:, tt, rb * 128:(rb + 1) * 128], in0=of32[:, tt, rb * 128:(rb + 1) * 128],
                in1=psO[bank][:, ts * 128:ts * 128 + 128], op=ALU.add),
                waits=[(pe, pmO[4 + bank]), w_done], inc=cv)
            psO_ev[bank] = (cv, evm)
            sc_done.append(evm)
        rb_slab_free[rsl] = (pe, pmO[7])
    scat_done = (cv, cv.n)
    if DBG_L1FSTOP <= 6:
        rec.barrier(ctx.chans)
        return
    rec.barrier(ctx.chans)
    sc_scope.close()
    scope2.close()

    emit_ln_residual_store(ctx, rec, of32, io["src"], io["dst"], scat_done, tag="q_fin", scope=blk)
    rec.barrier(ctx.chans)
    blk.close()



L0A_WEIGHTS = {
    "Wq": [DM, DM], "Wk": [DM, DM], "Wv": [DM, DM], "Wo": [DM, DM],
    "l0_biases": [1, 4 * DM],
}
L0F_WEIGHTS = {
    "w_gate": [DM, 8192], "w_up": [DM, 8192], "w_down": [8192, DM],
}
L1F_WEIGHTS_BASE = {
    "WT": [DM, DM], "gate_wT": [DM, 8], "eaw": [8, 64],
}
L1A_WEIGHTS = {
    "W_inT": [DM, 6448], "out_projT": [3072, DM], "conv_w": [3072, 4],
    "conv_b_c": [128, 24], "dt_bias_r": [48, 1], "A_log_c": [48, 1], "D_row": [1, 48],
    "inner_w_row": [1, 3072], "norm_w_c": [128, 16], "conv_state": [3072, 3],
    "ssm0_t": [128, 3072], "c_hsel": [48, 48 * 128], "c_T0": [128, 128], "c_eyef": [128, 128],
}


def build_program(blocks, dbg=False):
    nc = bass.Bass("TRN2", target_bir_lowering=False)
    es = ExitStack()
    xin = nc.dram_tensor("xin", [TOK, DM], F32, kind="ExternalInput").ap()
    xout = nc.dram_tensor("xout", [TOK, DM], F32, kind="ExternalOutput").ap()
    scr_a = nc.dram_tensor("scr_a", [TOK, DM], BF16, kind="Internal").ap()
    scr_b = nc.dram_tensor("scr_b", [TOK, DM], BF16, kind="Internal").ap()
    xmid = [nc.dram_tensor(f"xmid{i}", [TOK, DM], F32, kind="Internal").ap()
            for i in range(len(blocks) - 1)]
    sh_l0 = nc.dram_tensor("sh_l0", [128, 16 * 129], BF16, addr_space="Shared", kind="Internal").ap()
    sh_l0f = nc.dram_tensor("sh_l0f", [1, 4], I32, addr_space="Shared", kind="Internal").ap()

    scr_big = nc.dram_tensor("scr_big", [TOK, 8192], BF16, kind="Internal").ap()
    scr_m = nc.dram_tensor("scr_m", [3328, TOK], BF16, kind="Internal").ap()
    scr_u = nc.dram_tensor("scr_u", [TOK, DM], F32, kind="Internal").ap()
    scr_small = nc.dram_tensor("scr_small", [64, 64], F32, kind="Internal").ap()
    sh_l1t = nc.dram_tensor("sh_l1t", [128, 128], BF16, addr_space="Shared", kind="Internal").ap()
    sh_l1tf = nc.dram_tensor("sh_l1tf", [1, 4], I32, addr_space="Shared", kind="Internal").ap()
    sh_l1s = nc.dram_tensor("sh_l1s", [128, 3072], BF16, addr_space="Shared", kind="Internal").ap()
    sh_l1sf = nc.dram_tensor("sh_l1sf", [1, 4], I32, addr_space="Shared", kind="Internal").ap()
    wext = {}
    need = {}
    if "l0a" in blocks:
        need.update(L0A_WEIGHTS)
    if "l0f" in blocks:
        need.update(L0F_WEIGHTS)
    if "l1a" in blocks:
        need.update(L1A_WEIGHTS)
    if "l1f" in blocks:
        need.update(L1F_WEIGHTS_BASE)
        need["atoms_p"] = [64, NSP]
        need["imp_p"] = [8, NSP]
        need["gidx"] = [128, NSP // 128]
    for name, shape in need.items():
        dt = F32
        if name == "gidx":
            dt = I32
        wext[name] = nc.dram_tensor(name, shape, dt, kind="ExternalInput").ap()
    if "l1f" in blocks:
        wext["Rb"] = nc.dram_tensor("Rb", [NSP, 128], BF16, kind="ExternalInput").ap()
    c_par = nc.dram_tensor("c_par", [1, 4], I32, kind="ExternalInput").ap()
    c_nonce = nc.dram_tensor("c_nonce", [1, 4], I32, kind="ExternalInput").ap()
    c_ones = nc.dram_tensor("c_ones", [1, 128], F32, kind="ExternalInput").ap()
    c_onescol = nc.dram_tensor("c_onescol", [128, 1], F32, kind="ExternalInput").ap()
    c_triu = nc.dram_tensor("c_triu", [128, 128], F32, kind="ExternalInput").ap()
    c_eye = nc.dram_tensor("c_eye", [128, 128], F32, kind="ExternalInput").ap()

    with nc.Block() as block:
        ctx = Ctx(nc, es, block)
        rec = Rec(ctx)
        C = {
            "par_sb": ctx.sb("c_par", [1, 4], I32),
            "nonce_sb": ctx.sb("c_nonce", [1, 4], I32),
            "flag_sb": ctx.sb("c_flag", [1, 4], I32),
            "ones1": ctx.sb("c_ones1", [1, 128], BF16),
            "onescol": ctx.sb("c_onescol", [128, 1], BF16),
            "triU": ctx.sb("c_triu", [128, 128], F32),
            "onesf": ctx.sb("c_onesf", [1, 128], F32),
            "eyef": ctx.sb("c_eyeg", [128, 128], F32),
        }
        C["nonce_i_sb"] = C["nonce_sb"]
        cst = ctx.chan("cst")
        rec.op("sync", lambda h: h.dma_start(out=C["par_sb"][:], in_=c_par[:]), inc=cst, k=16)
        rec.op("sync", lambda h: h.dma_start(out=C["nonce_sb"][:], in_=c_nonce[:]), inc=cst, k=16)
        rec.op("gpsimd", lambda h: h.dma_start(out=C["ones1"][:], in_=c_ones[:]), inc=cst, k=16)
        rec.op("gpsimd", lambda h: h.dma_start(out=C["onescol"][:], in_=c_onescol[:]), inc=cst, k=16)
        rec.op("sync", lambda h: h.dma_start(out=C["triU"][:], in_=c_triu[:]), inc=cst, k=16)
        rec.op("sync", lambda h: h.dma_start(out=C["onesf"][:], in_=c_ones[:]), inc=cst, k=16)
        rec.op("sync", lambda h: h.dma_start(out=C["eyef"][:], in_=c_eye[:]), inc=cst, k=16)
        rec.barrier([cst])

        cur = xin
        for j, b in enumerate(blocks):
            dst = xout if j == len(blocks) - 1 else xmid[j]
            io = {"src": cur, "dst": dst, "scr_a": scr_a, "scr_b": scr_b, "scr_big": scr_big,
                  "scr_m": scr_m, "scr_u": scr_u, "scr_small": scr_small,
                  "sh_l0": sh_l0, "sh_l0f": sh_l0f, "sh_l1t": sh_l1t, "sh_l1tf": sh_l1tf,
                  "sh_l1s": sh_l1s, "sh_l1sf": sh_l1sf}
            if b == "l0a":
                emit_l0_attn(ctx, rec, io, wext, C)
            elif b == "l0f":
                emit_l0_ffn(ctx, rec, io, wext, C)
            elif b == "l1a":
                emit_l1_attn(ctx, rec, io, wext, C)
            elif b == "l1f":
                emit_l1_ffn(ctx, rec, io, wext, C)
            else:
                raise NotImplementedError(b)
            cur = dst

        if dbg:
            dbg_hs = nc.dram_tensor("dbg_hs", [TOK, 8192], BF16, kind="ExternalOutput").ap()
            dbg_xn = nc.dram_tensor("dbg_xn", [TOK, DM], BF16, kind="ExternalOutput").ap()
            dch = ctx.chan("dbg")
            rec.op("sync", lambda h: h.dma_start(out=dbg_hs[:], in_=scr_big[:]), inc=dch, k=16)
            rec.op("sync", lambda h: h.dma_start(out=dbg_xn[:], in_=scr_a[:]), inc=dch, k=16)
            if DBG_XNT is not None:
                dbg_xnT = nc.dram_tensor("dbg_xnT", [128, NDC * TOK], BF16, kind="ExternalOutput").ap()
                rec.op("sync", lambda h: h.dma_start(out=dbg_xnT[:],
                                                     in_=DBG_XNT[0][:].rearrange("p a b -> p (a b)")),
                       inc=dch, k=16)
        rec.barrier(ctx.chans)
        rec.flush()
    return nc


def host_consts():
    tri = np.triu(np.ones((128, 128), np.float32))
    return {
        "c_ones": np.ones((1, 128), np.float32),
        "c_onescol": np.ones((128, 1), np.float32),
        "c_triu": tri,
        "c_eye": np.eye(128, dtype=np.float32),
    }


def run_blocks(blocks, inputs, x_full, dbg=False):
    consts = host_consts()
    nonce = np.random.randint(1, 2**31 - 1, size=(1, 4)).astype(np.int32)
    in_maps = []
    need = {}
    if "l0f" in blocks:
        need["w_gate"] = np.ascontiguousarray(inputs["L0_ffn_w_gate"])
        need["w_up"] = np.ascontiguousarray(inputs["L0_ffn_w_up"])
        need["w_down"] = np.ascontiguousarray(inputs["L0_ffn_w_down"])
    if "l1f" in blocks:
        import ml_dtypes as _md
        mi_ = np.asarray(inputs["L1_ffn_mask_idx"]).astype(np.int64)
        rows_ = mi_ // DM
        cols_ = mi_ % DM
        global NSP, NJC, NJRB
        cnt = np.bincount(rows_ // 128, minlength=16)
        NJRB = max(10, int(np.ceil(cnt.max() / 128.0)))
        NSP = 16 * NJRB * 128
        NJC = NSP // 128
        gidx_ = np.zeros((16, NJRB * 128), np.int32)
        Rbm_ = np.zeros((16, NJRB * 128, 128), np.float32)
        srcx = np.full((16, NJRB * 128), -1, np.int64)
        for rb in range(16):
            sel = np.nonzero((rows_ >= rb * 128) & (rows_ < (rb + 1) * 128))[0]
            srcx[rb, :len(sel)] = sel
            gidx_[rb, :len(sel)] = cols_[sel]
            Rbm_[rb, np.arange(len(sel)), rows_[sel] - rb * 128] = 1.0
        flat = srcx.reshape(-1)
        valid = flat >= 0
        atoms_p = np.zeros((64, NSP), np.float32)
        atoms_p[:, valid] = np.asarray(inputs["L1_ffn_atoms"], np.float32)[:, flat[valid]]
        imp_p = np.zeros((8, NSP), np.float32)
        imp_p[:, valid] = np.asarray(inputs["L1_ffn_importance"], np.float32)[:, flat[valid]]
        need["atoms_p"] = atoms_p
        need["imp_p"] = imp_p
        need["gidx"] = np.ascontiguousarray(gidx_.reshape(NJC, 128).T)
        need["Rb"] = Rbm_.reshape(NSP, 128).astype(_md.bfloat16)
        need["WT"] = np.ascontiguousarray(np.asarray(inputs["L1_ffn_W"], np.float32).T)
        need["gate_wT"] = np.ascontiguousarray(np.asarray(inputs["L1_ffn_gate_w"], np.float32).T)
        need["eaw"] = np.ascontiguousarray(inputs["L1_ffn_expert_atom_weights"], np.float32)
    if "l1a" in blocks:
        NH = 48
        need["W_inT"] = np.ascontiguousarray(inputs["L1_attn_in_proj_w"].T.astype(np.float32))
        need["out_projT"] = np.ascontiguousarray(inputs["L1_attn_out_proj_w"].T.astype(np.float32))
        need["conv_w"] = np.ascontiguousarray(inputs["L1_attn_conv_w"], np.float32)
        need["conv_b_c"] = np.ascontiguousarray(inputs["L1_attn_conv_b"].reshape(24, 128).T, np.float32)
        need["dt_bias_r"] = np.ascontiguousarray(inputs["L1_attn_dt_bias"].reshape(48, 1), np.float32)
        need["A_log_c"] = np.ascontiguousarray(inputs["L1_attn_A_log"].reshape(48, 1), np.float32)
        need["D_row"] = np.ascontiguousarray(inputs["L1_attn_D_param"].reshape(1, 48), np.float32)
        need["inner_w_row"] = np.ascontiguousarray(inputs["L1_attn_inner_norm_w"].reshape(1, 3072), np.float32)
        need["norm_w_c"] = np.ascontiguousarray(inputs["L1_attn_norm_w"].reshape(16, 128).T, np.float32)
        need["conv_state"] = np.ascontiguousarray(inputs["L1_attn_conv_state"][0], np.float32)
        need["ssm0_t"] = np.ascontiguousarray(
            np.transpose(np.asarray(inputs["L1_attn_ssm_state"], np.float32)[0], (1, 0, 2)).reshape(128, 3072))
        hsel = np.zeros((48, 48 * 128), np.float32)
        for hh in range(NH):
            hsel[hh, hh * 128:(hh + 1) * 128] = 1.0
        need["c_hsel"] = hsel
        tri = np.triu(np.ones((128, 128), np.float32))
        need["c_T0"] = np.where(tri > 0, 0.0, -1e9).astype(np.float32)
        need["c_eyef"] = np.eye(128, dtype=np.float32)
    if "l0a" in blocks:
        need["Wq"] = np.ascontiguousarray(inputs["L0_attn_Wq"])
        need["Wk"] = np.ascontiguousarray(inputs["L0_attn_Wk"])
        need["Wv"] = np.ascontiguousarray(inputs["L0_attn_Wv"])
        need["Wo"] = np.ascontiguousarray(inputs["L0_attn_Wo"])
        need["l0_biases"] = np.concatenate([
            inputs["L0_attn_bq"], inputs["L0_attn_bk"], inputs["L0_attn_bv"],
            inputs["L0_attn_bo"]]).reshape(1, 4 * DM).astype(np.float32)
    nc = build_program(blocks, dbg=dbg)
    for c in range(NCORE):
        b, hh = c // 2, c % 2
        m = dict(need)
        m["xin"] = np.ascontiguousarray(x_full[b, hh * TOK:(hh + 1) * TOK, :], np.float32)
        m["c_par"] = np.full((1, 4), hh, np.int32)
        m["c_nonce"] = nonce
        m.update(consts)
        in_maps.append(m)
    res = run_bass_kernel_spmd(nc, in_maps, core_ids=list(range(NCORE)))
    out = np.empty((4, 2048, DM), np.float32)
    for c in range(NCORE):
        b, hh = c // 2, c % 2
        out[b, hh * TOK:(hh + 1) * TOK, :] = res.results[c]["xout"]
    if dbg:
        return out, res
    return out


def kernel(**inputs):
    x = np.asarray(inputs["L0_attn_x"], np.float32)
    return run_blocks(["l0a", "l0f", "l1a", "l1f"], inputs, x)
